# version 01
# baseline (speedup 1.0000x reference)
import math
import numpy as np
import ml_dtypes

import concourse.bass as bass
import concourse.mybir as mybir
from concourse.alu_op_type import AluOpType as ALU
from concourse.bass_utils import run_bass_kernel_spmd

F32 = mybir.dt.float32
BF16 = mybir.dt.bfloat16
AF = mybir.ActivationFunctionType
AX = mybir.AxisListType

NCORES = 8
SEQ = 2048
D = 1024
NCH = 8
HEADS = 16
HD = 64
GRP = 512
NGRP = 4
FT1 = 512
FT2 = 256
NT2 = SEQ // FT2
QT = 128
NQT = 16

ALL_PHASES = ["ln1", "ffn0a", "qkv0", "attn0", "ln3", "ffn0b",
              "ffn1a", "qkv1", "attn1", "ffn1b"]
_PHASES = None
_SAME_ENGINE_SEM = True
_FFN0_NCYC = 3


class Buf:
    __slots__ = ("name", "w", "r")

    def __init__(self, name):
        self.name = name
        self.w = []
        self.r = []


class Op:
    __slots__ = ("fn", "waits", "stream", "seq", "eng", "tag")

    def __init__(self, fn, waits, stream, seq, eng, tag):
        self.fn = fn
        self.waits = waits
        self.stream = stream
        self.seq = seq
        self.eng = eng
        self.tag = tag


class Prog:
    ENGS = ("T", "V", "S", "G", "Q")
    DMA_SLOTS = {"Q": 12, "G": 6, "S": 4, "V": 2, "T": 2}

    def __init__(self, nc):
        self.nc = nc
        self.eng = {"T": nc.tensor, "V": nc.vector, "S": nc.scalar,
                    "G": nc.gpsimd, "Q": nc.sync}
        self.ops = []
        self.seq = {}
        self.waited = {e: {} for e in self.ENGS}
        self.needed = {}
        self.dmacnt = {}

    def op(self, eng, fn, reads=(), writes=(), dma=False, tag=""):
        if dma:
            i = self.dmacnt.get(eng, 0)
            self.dmacnt[eng] = i + 1
            nslots = self.DMA_SLOTS[eng]
            stream = f"{eng}d{i % nslots}"
            s = i // nslots
        else:
            stream = eng
            s = self.seq.get(stream, 0)
        deps = []
        for b in reads:
            deps += b.w
        for b in writes:
            deps += b.w
            deps += b.r
        if dma and s > 0:
            deps.append((stream, s - 1))
        waits = []
        wmap = self.waited[eng]
        for (ps, pq) in deps:
            if (not _SAME_ENGINE_SEM) and ps == eng and not dma:
                continue
            if wmap.get(ps, -1) < pq:
                wmap[ps] = pq
                self.needed.setdefault(ps, set()).add(pq)
                waits.append((ps, pq))
        self.seq[stream] = s + 1
        self.ops.append(Op(fn, waits, stream, s, eng, tag))
        if dma:
            self.needed.setdefault(stream, set()).add(s)
        ev = (stream, s)
        for b in writes:
            b.w = [ev]
            b.r = []
        for b in reads:
            b.r.append(ev)
        return ev

    def wait_event(self, eng, ev):
        (ps, pq) = ev
        self.needed.setdefault(ps, set()).add(pq)
        self.ops.append(Op(None, [(ps, pq)], None, None, eng, "finwait"))

    def emit(self):
        nc = self.nc
        rank = {}
        sems = {}
        for stream, seqs in self.needed.items():
            order = sorted(seqs)
            step = 16 if len(stream) > 1 else 1
            rank[stream] = {q: (i + 1) * step for i, q in enumerate(order)}
            sems[stream] = nc.alloc_semaphore(f"sem_{stream}")
        for o in self.ops:
            eng = self.eng[o.eng]
            for (ps, pq) in o.waits:
                eng.wait_ge(sems[ps], rank[ps][pq])
            if o.fn is None:
                continue
            inst = o.fn(eng)
            if o.stream in rank and o.seq in rank[o.stream]:
                step = 16 if len(o.stream) > 1 else 1
                inst.then_inc(sems[o.stream], step)
        return len(self.ops)


def _rope_perm():
    pi = np.zeros(D, dtype=np.int64)
    for h in range(HEADS):
        for i in range(32):
            pi[h * 64 + i] = h * 64 + 2 * i
            pi[h * 64 + 32 + i] = h * 64 + 2 * i + 1
    return pi


def _vec_tile(v):
    return np.ascontiguousarray(np.asarray(v, dtype=np.float32).reshape(-1, 128).T)


def prep_inputs(inp):
    f32 = np.float32
    pi = _rope_perm()
    g = {}
    g["l0wq"] = np.ascontiguousarray(inp["L0_attn_Wq"], dtype=f32)
    g["l0wk"] = np.ascontiguousarray(inp["L0_attn_Wk"], dtype=f32)
    g["l0wv"] = np.ascontiguousarray(inp["L0_attn_Wv"], dtype=f32)
    g["l0wo"] = np.ascontiguousarray(inp["L0_attn_Wo"], dtype=f32)
    for k in ("cn_w1", "cn_w2", "cn_w3", "cn_w4", "gc_w1", "gc_w2", "gc_w3"):
        g[k] = np.ascontiguousarray(inp["L0_ffn_" + k], dtype=f32)
    g["cn_b1"] = _vec_tile(inp["L0_ffn_cn_b1"])
    g["cn_b2"] = _vec_tile(inp["L0_ffn_cn_b2"])
    g["cn_b3"] = _vec_tile(inp["L0_ffn_cn_b3"])
    g["cn_b4"] = np.asarray(inp["L0_ffn_cn_b4"], dtype=f32).reshape(1, 1)
    g["gc_b1"] = _vec_tile(inp["L0_ffn_gc_b1"])
    g["gc_b2"] = _vec_tile(inp["L0_ffn_gc_b2"])
    g["gc_b3"] = np.asarray(inp["L0_ffn_gc_b3"], dtype=f32).reshape(1, 1)
    g["arange"] = np.asarray(inp["L0_ffn_adaptive_range"], dtype=f32).reshape(1, 1)
    g["l1wq"] = np.ascontiguousarray(np.asarray(inp["L1_attn_wq"]).T[:, pi], dtype=f32)
    g["l1wk"] = np.ascontiguousarray(np.asarray(inp["L1_attn_wk"]).T[:, pi], dtype=f32)
    g["l1wv"] = np.ascontiguousarray(np.asarray(inp["L1_attn_wv"]).T, dtype=f32)
    g["l1wo"] = np.ascontiguousarray(np.asarray(inp["L1_attn_wo"]).T, dtype=f32)
    g["l1bq"] = _vec_tile(np.asarray(inp["L1_attn_bq"])[pi])
    g["l1bk"] = _vec_tile(np.asarray(inp["L1_attn_bk"])[pi])
    g["l1bv"] = np.asarray(inp["L1_attn_bv"], dtype=f32).reshape(1, D)
    g["l1bo"] = _vec_tile(inp["L1_attn_bo"])
    g["wfcT"] = np.ascontiguousarray(np.asarray(inp["L1_ffn_W_fc"]).T, dtype=f32)
    g["wprT"] = np.ascontiguousarray(np.asarray(inp["L1_ffn_W_proj"]).T, dtype=f32)
    g["lnw"] = _vec_tile(inp["L1_ffn_ln_w"])
    g["lnb"] = _vec_tile(inp["L1_ffn_ln_b"])
    x = np.asarray(inp["L0_attn_x"], dtype=f32)
    cores = []
    for i in range(NCORES):
        m = dict(g)
        m["xT"] = np.ascontiguousarray(x[i].T)
        cores.append(m)
    return cores


def make_consts():
    c = {}
    c["ident"] = np.eye(128, dtype=ml_dtypes.bfloat16)
    ps = np.zeros((128, 128), dtype=np.float64)
    for h in range(2):
        for i in range(32):
            e = h * 64 + i
            o = h * 64 + 32 + i
            ps[o, e] = -1.0
            ps[e, o] = 1.0
    c["pswap"] = ps.astype(ml_dtypes.bfloat16)
    m = np.where(np.arange(128)[None, :] > np.arange(128)[:, None], -1e30, 0.0)
    c["maskdiag"] = m.astype(np.float32)
    c["ones_col"] = np.ones((128, 1), dtype=ml_dtypes.bfloat16)
    c["ones_row"] = np.ones((1, 128), dtype=ml_dtypes.bfloat16)
    c["ones_row_f32"] = np.ones((1, 128), dtype=np.float32)
    c["ones_col_f32"] = np.ones((128, 1), dtype=np.float32)
    i32 = np.arange(32, dtype=np.float64)
    rate = np.exp(-2.0 * i32 / HD * math.log(10000.0))
    pos = np.arange(SEQ, dtype=np.float64)
    th = rate[:, None] * pos[None, :]
    c["ropecos"] = np.tile(np.cos(th), (4, 1)).astype(np.float32)
    c["ropesin"] = np.tile(np.sin(th), (4, 1)).astype(np.float32)
    hs = np.zeros((128, NCH, HEADS), dtype=ml_dtypes.bfloat16)
    for ch in range(NCH):
        for p in range(128):
            hs[p, ch, (ch * 128 + p) // HD] = 1
    c["headsel"] = hs
    return c


GROUPS = [
    ["lnx_a", "lnx_b", "lnx_half", "cosb", "sinb", "outv", "diag", "xres",
     "xnew", "h1b", "h2b", "h3b", "gc2t"],
    ["P", "PT", "xb", "fb16", "sq128", "lnx_c", "gc1t"],
    ["x", "wfcq", "wslotA", "wslotB", "xgrp", "wo_t"],
    ["fout", "q", "k", "wprq"],
    ["v_tm", "cn1", "cn2", "cn3", "cn4", "h", "xtile", "ytile"],
    ["pQ0"], ["pQ1"], ["pQ2"], ["pQ3"],
    ["tmpa"], ["tmpb"], ["wstage0"], ["wstage1"],
]
GROUP_OF = {}
for _g in GROUPS:
    for _n in _g:
        GROUP_OF[_n] = _g


class Builder:
    def __init__(self, phases=None):
        self.phases = list(phases if phases is not None else ALL_PHASES)
        nc = bass.Bass("TRN2", target_bir_lowering=False)
        self.nc = nc
        self.p = Prog(nc)
        self.bufs = {}
        self.xloc = "x"
        self.wsvc = []
        self._declare_io()
        self._alloc_sbuf()
        self._alloc_psum()

    def B(self, name):
        if name not in self.bufs:
            self.bufs[name] = Buf(name)
        return self.bufs[name]

    def take(self, name):
        b = Buf(name)
        for n in GROUP_OF.get(name, [name]):
            o = self.bufs.get(n)
            if o is not None:
                b.w = b.w + o.w
                b.r = b.r + o.r
        self.bufs[name] = b
        return b

    def _declare_io(self):
        nc = self.nc
        self.d = {}
        d = self.d

        def T(n, s):
            d[n] = nc.dram_tensor(n, s, F32, kind="ExternalInput")
        T("xT", [D, SEQ])
        d["y"] = nc.dram_tensor("y", [D, SEQ], F32, kind="ExternalOutput")
        for n, s in [("l0wq", [D, D]), ("l0wk", [D, D]), ("l0wv", [D, D]),
                     ("l0wo", [D, D]), ("cn_w1", [D, D]), ("cn_w2", [D, 512]),
                     ("cn_w3", [512, 256]), ("cn_w4", [256, 1]),
                     ("gc_w1", [D, 512]), ("gc_w2", [512, 256]),
                     ("gc_w3", [256, 1]), ("cn_b1", [128, 8]),
                     ("cn_b2", [128, 4]), ("cn_b3", [128, 2]), ("cn_b4", [1, 1]),
                     ("gc_b1", [128, 4]), ("gc_b2", [128, 2]), ("gc_b3", [1, 1]),
                     ("arange", [1, 1]),
                     ("l1wq", [D, D]), ("l1wk", [D, D]), ("l1wv", [D, D]),
                     ("l1wo", [D, D]), ("l1bq", [128, 8]), ("l1bk", [128, 8]),
                     ("l1bv", [1, D]), ("l1bo", [128, 8]),
                     ("wfcT", [D, 4096]), ("wprT", [4096, D]),
                     ("lnw", [128, 32]), ("lnb", [128, 32])]:
            T(n, s)
        I = lambda n, s, dt: nc.dram_tensor(n, s, dt, kind="Internal")
        d["xs_a"] = I("xs_a", [D, SEQ], F32)
        d["xs_b"] = I("xs_b", [D, SEQ], F32)
        d["lnxs"] = I("lnxs", [D, SEQ], BF16)
        d["wfcq_d"] = I("wfcq_d", [D, 4096], BF16)
        d["wprq_d"] = I("wprq_d", [4096, D], BF16)
        for k, v in make_consts().items():
            d[k] = nc.inline_tensor(v, k)

    def _alloc_sbuf(self):
        nc = self.nc
        self.off = (int(nc.sbuf_base) + 31) // 32 * 32
        self.sb = {}
        self.sboff = {}

        def at(name, shape, dtype, off=None):
            nbytes = int(np.prod(shape[1:])) * (2 if dtype == BF16 else 4)
            if off is None:
                off = self.off
                self.off = off + ((nbytes + 31) // 32) * 32
            h = nc.alloc_sbuf_tensor_at(name, list(shape), dtype, offset=off)
            self.sb[name] = h
            self.sboff[name] = off
            return h

        at("ident", [128, 128], BF16)
        at("pswap", [128, 128], BF16)
        at("maskdiag", [128, 128], F32)
        at("ones_col", [128, 1], BF16)
        at("ones_row", [1, 128], BF16)
        at("ones_row_f32", [1, 128], F32)
        at("headsel", [128, NCH, HEADS], BF16)
        at("bias_cn1", [128, 8], F32)
        at("bias_cn2", [128, 4], F32)
        at("bias_cn3", [128, 2], F32)
        at("bias_gc1", [128, 4], F32)
        at("bias_gc2", [128, 2], F32)
        at("bias_l1q", [128, 8], F32)
        at("bias_l1k", [128, 8], F32)
        at("bias_l1o", [128, 8], F32)
        at("lnw", [128, 32], F32)
        at("lnb", [128, 32], F32)
        at("bvb", [128, D], BF16)
        at("scal", [128, 16], F32)
        at("srow", [1, 1024], F32)
        at("grow", [1, 64], F32)
        at("stats", [128, 8], F32)
        at("rinv", [128, 1], F32)
        at("negB", [128, 24], F32)
        at("invb", [128, 512], BF16)
        at("m2b", [128, 512], BF16)
        at("tmpa", [128, 512], F32)
        at("tmpb", [128, 512], F32)
        at("gc3t", [128, 2, 1], BF16)
        at("ones_col_f32", [128, 1], F32)
        at("gacc", [128, 64], F32)
        at("lnx_a", [128, NCH, GRP], BF16)
        at("lnx_b", [128, NCH, GRP], BF16)
        r1 = self.sboff["lnx_a"]
        at("lnx_half", [128, NCH, 2 * GRP], BF16, off=r1)
        at("cosb", [128, SEQ], BF16, off=r1)
        at("sinb", [128, SEQ], BF16, off=r1 + 8192)
        at("outv", [128, NCH, QT], BF16, off=r1)
        at("diag", [128, 128], BF16, off=r1 + 2048)
        at("xres", [128, NCH, QT], F32, off=r1 + 2432)
        at("xnew", [128, NCH, QT], F32, off=r1 + 2432 + 4096)
        at("h2b", [128, 4, FT1], BF16, off=r1)
        at("h3b", [128, 2, FT1], BF16, off=r1 + 4096)
        at("h1b", [128, NCH, FT1], BF16, off=r1 + 8192)
        at("P", [128, SEQ], BF16)
        at("PT", [128, SEQ], BF16)
        pP = self.sboff["P"]
        at("xb", [128, NCH, GRP], BF16, off=pP)
        at("fb16", [128, NCH, FT1], BF16, off=pP)
        at("sq128", [128, SEQ], BF16, off=pP)
        at("lnx_c", [128, NCH, FT2], BF16, off=pP)
        at("gc1t", [128, NCH, 512], BF16, off=pP)
        at("gc2t", [128, 4, 256], BF16, off=self.sboff["lnx_a"])

        r2 = self.off
        at("x", [128, NCH, SEQ], F32)
        at("wfcq", [128, NCH, 4096], BF16, off=r2)
        at("wslotA", [128, NCH, D], BF16, off=r2)
        at("wslotB", [128, NCH, D], BF16, off=r2 + 16 * 1024)
        at("xgrp", [128, NCH, GRP], F32, off=r2 + 32 * 1024)
        at("wo_t", [128, NCH, D], BF16, off=r2 + 48 * 1024)
        r3 = self.off
        at("fout", [128, NCH, SEQ], F32)
        at("q", [128, NCH, SEQ], BF16, off=r3)
        at("k", [128, NCH, SEQ], BF16, off=r3 + 32 * 1024)
        at("wprq", [128, 32, D], BF16, off=r3)
        r4 = self.off
        at("v_tm", [128, NQT, D], BF16)
        at("cn1", [128, NCH, D], BF16, off=r4)
        at("cn2", [128, NCH, 512], BF16, off=r4 + 16 * 1024)
        at("cn3", [128, 4, 256], BF16, off=r4 + 24 * 1024)
        at("cn4", [128, 2, 1], BF16, off=r4 + 26 * 1024)
        at("h", [128, 32, FT2], BF16, off=r4)
        at("xtile", [128, NCH, FT2], F32, off=r4 + 16 * 1024)
        at("ytile", [128, NCH, FT2], F32, off=r4 + 24 * 1024)
        at("wstage", [128, 2, 1024], F32)

        limit = int(nc.sbuf_top)
        assert self.off <= limit, f"SBUF over budget: {self.off} > {limit}"

    def _alloc_psum(self):
        nc = self.nc
        self.pbig = nc.alloc_psum_tensor("pbig", [128, 2048], F32)
        self.pb = {i: nc.alloc_psum_tensor(f"pb{i}", [128, 512], F32)
                   for i in (4, 5, 6, 7)}

    def pQ(self, i):
        return self.B(f"pQ{i}")

    def takeQ(self, i):
        return self.take(f"pQ{i}")

    def pump(self, n):
        for _ in range(min(n, len(self.wsvc))):
            self.wsvc.pop(0)()

    def pump_all(self):
        while self.wsvc:
            self.wsvc.pop(0)()

    def _wpieces(self, K, M, cap=1024):
        kch = K // 128
        if M >= cap:
            for c in range(kch):
                for m0 in range(0, M, cap):
                    yield (c, c + 1, m0, min(M, m0 + cap))
        else:
            per = max(1, cap // M)
            for c0 in range(0, kch, per):
                yield (c0, min(kch, c0 + per), 0, M)

    def load_weight_cast(self, src, dst, K, M, ceng="S", tag=""):
        p, d, sb = self.p, self.d, self.sb
        dstB = self.B(dst)
        dap = sb[dst].ap()
        sap = d[src].rearrange("(c p) m -> p c m", p=128)
        for i, (c0, c1, m0, m1) in enumerate(self._wpieces(K, M)):
            n = (c1 - c0) * (m1 - m0)
            half = i % 2
            stB = self.take(f"wstage{half}")
            st3 = sb["wstage"].ap()[:, half, 0:n].rearrange(
                "p (a b) -> p a b", b=m1 - m0)
            p.op("Q", lambda e, a=st3, s=sap[:, c0:c1, m0:m1]: e.dma_start(
                a, s, allow_slow_non_contiguous=True),
                writes=[stB], dma=True, tag=f"ldw_{src}_{i}{tag}")
            cp = (lambda e, o, a: e.copy(o, a)) if ceng == "S" else \
                 (lambda e, o, a: e.tensor_copy(o, a))
            p.op(ceng, lambda e, o=dap[:, c0:c1, m0:m1], a=st3, cp=cp: cp(e, o, a),
                 reads=[stB], writes=[dstB], tag=f"cast_{src}_{i}{tag}")

    def emit_ln(self, src_ap, src_buf, ntok, out_ap, out_buf, tag):
        p, sb = self.p, self.sb
        n = ntok
        xb = sb["xb"].ap()[:, :, 0:n]
        xbB = self.take("xb")
        p.op("V", lambda e, o=xb, s=src_ap: e.tensor_copy(o, s), reads=[src_buf],
             writes=[xbB], tag=f"{tag}cast")
        pm = self.pb[5].ap()
        pmB = self.take("pb5")
        oc = sb["ones_col"].ap()
        ocB = self.B("c_ones_col")
        for c in range(NCH):
            p.op("T", lambda e, c=c, n=n: e.matmul(pm[0:1, 0:n], oc, xb[:, c, :],
                                                   start=(c == 0), stop=(c == NCH - 1)),
                 reads=[xbB, ocB], writes=[pmB], tag=f"{tag}mmu")
        pv = self.pb[6].ap()
        pvB = self.take("pb6")
        for c in range(NCH):
            tnm = "tmpa" if c % 2 == 0 else "tmpb"
            sq = self.sb[tnm].ap().bitcast(BF16)[:, 0:n]
            sqB = self.take(tnm)
            p.op("S", lambda e, sq=sq, c=c: e.activation(sq, xb[:, c, :], AF.Square),
                 reads=[xbB], writes=[sqB], tag=f"{tag}sq{c}")
            p.op("T", lambda e, sq=sq, c=c, n=n: e.matmul(
                pv[0:1, 0:n], oc, sq, start=(c == 0), stop=(c == NCH - 1)),
                reads=[sqB, ocB], writes=[pvB], tag=f"{tag}mms")
        sr = sb["srow"].ap()
        mu = sr[:, 0:n]
        var = sr[:, 512:512 + n]
        srB = self.take("srow")
        p.op("V", lambda e, mu=mu, n=n: e.tensor_scalar(
            mu, pm[0:1, 0:n], 1.0 / D, None, ALU.mult),
            reads=[pmB], writes=[srB], tag=f"{tag}mu")
        p.op("V", lambda e, var=var, n=n: e.tensor_scalar(
            var, pv[0:1, 0:n], 1.0 / D, None, ALU.mult),
            reads=[pvB], writes=[srB], tag=f"{tag}ex2")
        ta = sb["tmpa"].ap()[0:1, 0:n]
        taB = self.take("tmpa")
        p.op("V", lambda e, ta=ta, mu=mu: e.tensor_tensor(ta, mu, mu, ALU.mult),
             reads=[srB], writes=[taB], tag=f"{tag}musq")
        p.op("V", lambda e, var=var, ta=ta: e.tensor_tensor(var, var, ta,
                                                            ALU.subtract),
             reads=[srB, taB], writes=[srB], tag=f"{tag}var")
        eps = sb["scal"].ap()[0:1, 0:1]
        p.op("S", lambda e, var=var: e.activation(var, var, AF.Ln, bias=eps),
             reads=[srB, self.B("scal")], writes=[srB], tag=f"{tag}lnv")
        p.op("S", lambda e, var=var: e.activation(var, var, AF.Exp, scale=-0.5),
             reads=[srB], writes=[srB], tag=f"{tag}inv")
        p.op("V", lambda e, mu=mu, var=var: e.tensor_tensor(mu, mu, var, ALU.mult),
             reads=[srB], writes=[srB], tag=f"{tag}m2")
        orf = sb["ones_row_f32"].ap()
        orfB = self.B("c_ones_row_f32")
        pb4 = self.pb[4].ap()
        for rowap, dst in ((var, "invb"), (mu, "m2b")):
            pb4B = self.take("pb4")
            p.op("T", lambda e, r=rowap, n=n: e.matmul(pb4[:, 0:n], orf, r,
                                                       start=True, stop=True),
                 reads=[srB, orfB], writes=[pb4B], tag=f"{tag}bc_{dst}")
            dB = self.take(dst)
            p.op("V", lambda e, o=sb[dst].ap()[:, 0:n], n=n: e.tensor_copy(
                o, pb4[:, 0:n]),
                reads=[pb4B], writes=[dB], tag=f"{tag}bcc_{dst}")
        invb = sb["invb"].ap()[:, 0:n]
        m2bv = sb["m2b"].ap()[:, 0:n]
        tb16 = sb["tmpa"].ap().bitcast(BF16)[:, 0:n]
        for c in range(NCH):
            tB = self.take("tmpa")
            p.op("V", lambda e, c=c, t=tb16, s=src_ap[:, c, 0:n], i=invb:
                 e.tensor_tensor(t, s, i, ALU.mult),
                 reads=[src_buf, self.B("invb")], writes=[tB], tag=f"{tag}ap1_{c}")
            p.op("V", lambda e, c=c, o=out_ap[:, c, 0:n], t=tb16, m=m2bv:
                 e.tensor_tensor(o, t, m, ALU.subtract),
                 reads=[tB, self.B("m2b")], writes=[out_buf], tag=f"{tag}ap2_{c}")

    def phase_init(self):
        p, d, sb = self.p, self.d, self.sb
        loads = [("ident", "ident"), ("pswap", "pswap"), ("maskdiag", "maskdiag"),
                 ("headsel", "headsel"),
                 ("cn_b1", "bias_cn1"), ("cn_b2", "bias_cn2"),
                 ("cn_b3", "bias_cn3"), ("gc_b1", "bias_gc1"),
                 ("gc_b2", "bias_gc2"), ("l1bq", "bias_l1q"),
                 ("l1bk", "bias_l1k"), ("l1bo", "bias_l1o"),
                 ("lnw", "lnw"), ("lnb", "lnb")]
        for src, t in loads:
            p.op("Q", lambda e, s=src, t=t: e.dma_start(sb[t].ap(), d[s].ap()),
                 writes=[self.B(t)], dma=True, tag=f"ld_{t}")
        for src, t, bn in (("ones_col", "ones_col", "c_ones_col"),
                           ("ones_row", "ones_row", "c_ones_row"),
                           ("ones_row_f32", "ones_row_f32", "c_ones_row_f32"),
                           ("ones_col_f32", "ones_col_f32", "c_ones_col_f32")):
            p.op("Q", lambda e, s=src, t=t: e.dma_start(sb[t].ap(), d[s].ap()),
                 writes=[self.B(bn)], dma=True, tag=f"ld_{t}")
        p.op("Q", lambda e: e.dma_start(
            sb["x"].ap(), d["xT"].rearrange("(c p) t -> p c t", p=128)),
            writes=[self.take("x")], dma=True, tag="ld_x")
        p.op("V", lambda e: e.memset(sb["scal"].ap()[:, 0:1], 1e-5),
             writes=[self.B("scal")], tag="eps")
        for i, src in ((0, "arange"), (1, "cn_b4"), (2, "gc_b3")):
            p.op("Q", lambda e, i=i, s=src: e.dma_start(
                sb["grow"].ap()[0:1, i:i + 1], d[s].ap()),
                writes=[self.B("grow")], dma=True, tag=f"ld_{src}")
        self.load_cn()
        self.enqueue_quant()

    def load_cn(self):
        for nm in ("cn1", "cn2", "cn3", "cn4"):
            self.take(nm)
        self.load_weight_cast("cn_w1", "cn1", D, D)
        self.load_weight_cast("cn_w2", "cn2", D, 512)
        self.load_weight_cast("cn_w3", "cn3", 512, 256)
        self.load_weight_cast("cn_w4", "cn4", 256, 1)

    def load_gc(self):
        for nm in ("gc1t", "gc2t", "gc3t"):
            self.take(nm)
        self.load_weight_cast("gc_w1", "gc1t", D, 512)
        self.load_weight_cast("gc_w2", "gc2t", 512, 256)
        self.load_weight_cast("gc_w3", "gc3t", 256, 1)

    def enqueue_quant(self):
        p, d, sb = self.p, self.d, self.sb
        pb7 = self.pb[7].ap()
        tA = sb["tmpa"].ap().bitcast(BF16)
        tB_ = sb["tmpb"].ap().bitcast(BF16)

        def g1piece(src, gcol, idx, c, m0):
            def go():
                half = idx % 2
                stB = self.take(f"wstage{half}")
                ap_st = sb["wstage"].ap()[:, half, :]
                st3 = sb["wstage"].ap()[:, half, :].rearrange(
                    "p (a b) -> p a b", b=128)
                sap = d[src].rearrange("(c p) m -> p c m", p=128)
                p.op("Q", lambda e, a=ap_st, s=sap[:, c:c + 1, m0:m0 + 1024]:
                     e.dma_start(a, s.rearrange("p c m -> p (c m)")),
                     writes=[stB], dma=True, tag=f"g1_{src}_{idx}")
                t8 = sb["tmpa"].ap()[:, 0:8]
                aB = self.take("tmpa")
                p.op("V", lambda e, s3=st3, t8=t8: e.reduce_sum(
                    t8, s3, AX.X, apply_absolute_value=True),
                    reads=[stB], writes=[aB], tag=f"g1r1_{src}_{idx}")
                gaB = self.B("gacc")
                p.op("V", lambda e, t8=t8, gcol=gcol: e.reduce_sum(
                    sb["gacc"].ap()[:, gcol:gcol + 1], t8, AX.X),
                    reads=[aB], writes=[gaB], tag=f"g1r2_{src}_{idx}")
            self.wsvc.append(go)

        for gbase, src, K, M in ((0, "wfcT", D, 4096), (32, "wprT", 4096, D)):
            pieces = list(self._wpieces(K, M))
            assert len(pieces) == 32
            for idx, (c0, c1, m0, m1) in enumerate(pieces):
                g1piece(src, gbase + idx, idx, c0, m0)

        def finalize():
            gr = sb["grow"].ap()
            grB = self.B("grow")
            gaB = self.B("gacc")
            ga3 = sb["gacc"].ap().rearrange("p (a b) -> p a b", b=8)
            for gbase, col, pcol in ((0, 8, 0), (32, 16, 1)):
                taB = self.take("tmpa")
                p.op("V", lambda e, gbase=gbase: e.reduce_sum(
                    sb["tmpa"].ap()[:, 0:4], ga3[:, gbase // 8:gbase // 8 + 4, :],
                    AX.X),
                    reads=[gaB], writes=[taB], tag=f"gf1_{col}")
                p.op("V", lambda e: e.reduce_sum(
                    sb["tmpa"].ap()[:, 8:9], sb["tmpa"].ap()[:, 0:4], AX.X),
                    reads=[taB], writes=[taB], tag=f"gf2_{col}")
                pb7B = self.take("pb7")
                p.op("T", lambda e, pcol=pcol: e.matmul(
                    pb7[0:1, pcol:pcol + 1], sb["tmpa"].ap()[:, 8:9],
                    sb["ones_col_f32"].ap(), start=True, stop=True),
                    reads=[taB, self.B("c_ones_col_f32")], writes=[pb7B],
                    tag=f"gf3_{col}")
                p.op("V", lambda e, col=col, pcol=pcol: e.tensor_scalar(
                    gr[:, col + 1:col + 2], pb7[0:1, pcol:pcol + 1],
                    1.0 / (4096.0 * 1024.0), 1e-6, ALU.mult, ALU.add),
                    reads=[pb7B], writes=[grB], tag=f"gam{col}")
                p.op("V", lambda e, col=col: e.tensor_scalar(
                    gr[:, col + 2:col + 3], gr[:, col + 1:col + 2],
                    0.5, None, ALU.mult),
                    reads=[grB], writes=[grB], tag=f"gthr{col}")
            pb4 = self.pb[4].ap()
            for src_col, dst in ((10, 1), (18, 3)):
                pb4B = self.take("pb4")
                p.op("T", lambda e, sc_=src_col: e.matmul(
                    pb4[:, 0:1], sb["ones_row_f32"].ap(), gr[:, sc_:sc_ + 1],
                    start=True, stop=True),
                    reads=[grB, self.B("c_ones_row_f32")], writes=[pb4B],
                    tag=f"gbc{src_col}")
                sclB = self.B("scal")
                p.op("V", lambda e, dst=dst: e.tensor_copy(
                    sb["scal"].ap()[:, dst:dst + 1], pb4[:, 0:1]),
                    reads=[pb4B], writes=[sclB], tag=f"gbcc{dst}")
                p.op("V", lambda e, dst=dst: e.tensor_scalar(
                    sb["scal"].ap()[:, dst + 1:dst + 2],
                    sb["scal"].ap()[:, dst:dst + 1], -1.0, None, ALU.mult),
                    reads=[sclB], writes=[sclB], tag=f"gneg{dst}")
        self.wsvc.append(finalize)

        def qpiece(src, dst, thr_col, idx, c, m0):
            def go():
                half = idx % 2
                stB = self.take(f"wstage{half}")
                ap_st = sb["wstage"].ap()[:, half, :]
                sap = d[src].rearrange("(c p) m -> p c m", p=128)
                dap = d[dst].rearrange("(c p) m -> p c m", p=128)
                p.op("Q", lambda e, a=ap_st, s=sap[:, c:c + 1, m0:m0 + 1024]:
                     e.dma_start(a, s.rearrange("p c m -> p (c m)")),
                     writes=[stB], dma=True, tag=f"q2_{src}_{idx}")
                thr = sb["scal"].ap()[:, thr_col:thr_col + 1]
                nthr = sb["scal"].ap()[:, thr_col + 1:thr_col + 2]
                aB = self.take("tmpa")
                p.op("V", lambda e, a=ap_st, nthr=nthr: e.tensor_scalar(
                    tA, a, nthr, None, ALU.is_lt),
                    reads=[stB, self.B("scal")], writes=[aB],
                    tag=f"q2lt_{src}_{idx}")
                bB = self.take("tmpb")
                p.op("V", lambda e, a=ap_st, thr=thr: e.scalar_tensor_tensor(
                    tB_, a, thr, tA, ALU.is_gt, ALU.subtract),
                    reads=[stB, aB, self.B("scal")], writes=[bB],
                    tag=f"q2q_{src}_{idx}")
                p.op("G", lambda e, o=dap[:, c:c + 1, m0:m0 + 1024]: e.dma_start(
                    o.rearrange("p c m -> p (c m)"), tB_),
                    reads=[bB], writes=[self.B(dst + "_dram")], dma=True,
                    tag=f"q2st_{src}_{idx}")
            self.wsvc.append(go)

        for src, dst, thr_col, K, M in (("wfcT", "wfcq_d", 1, D, 4096),
                                        ("wprT", "wprq_d", 3, 4096, D)):
            for idx, (c0, c1, m0, m1) in enumerate(self._wpieces(K, M)):
                qpiece(src, dst, thr_col, idx, c0, m0)

    def phase_ln1(self):
        sb = self.sb
        self.take("fout")
        for g in range(NGRP):
            self.emit_ln(sb["x"].ap()[:, :, g * GRP:(g + 1) * GRP], self.B("x"),
                         GRP, sb["fout"].ap()[:, :, g * GRP:(g + 1) * GRP],
                         self.B("fout"), f"ln1g{g}")
            self.pump(10)

    def phase_ln3(self):
        p, d, sb = self.p, self.d, self.sb
        xB = self.take("x")
        p.op("Q", lambda e: e.dma_start(
            sb["x"].ap(), d["xs_b"].rearrange("(c p) t -> p c t", p=128)),
            reads=[self.B("xs_b")], writes=[xB], dma=True, tag="ld_x3")
        self.xloc = "x"
        self.load_cn()
        self.take("fout")
        for g in range(NGRP):
            self.emit_ln(sb["x"].ap()[:, :, g * GRP:(g + 1) * GRP], self.B("x"),
                         GRP, sb["fout"].ap()[:, :, g * GRP:(g + 1) * GRP],
                         self.B("fout"), f"ln3g{g}")
            self.pump(6)

    def phase_ffn0a(self):
        self.emit_ffn0("a", ncyc=_FFN0_NCYC)

    def phase_ffn0b(self):
        self.emit_ffn0("b")

    def emit_ffn0(self, call, ncyc=3, ntile=None):
        p, sb = self.p, self.sb
        self.load_gc()
        foB = self.B("fout")
        gr = sb["grow"].ap()
        grB = self.B("grow")
        gcx = sb["scal"].ap()[:, 8:16]
        sclB = self.B("scal")
        p.op("V", lambda e: e.reduce_sum(gcx, sb["fout"].ap(), AX.X),
             reads=[foB], writes=[sclB], tag=f"f0{call}_gcx")
        gxb = sb["tmpa"].ap().bitcast(BF16)[:, 0:8]
        gxbB = self.take("tmpa")
        p.op("V", lambda e: e.tensor_scalar(gxb, gcx, 1.0 / SEQ, None, ALU.mult),
             reads=[sclB], writes=[gxbB], tag=f"f0{call}_gcxb")
        pb6 = self.pb[6].ap()
        pb6B = self.take("pb6")
        for m in range(4):
            for c in range(NCH):
                p.op("T", lambda e, m=m, c=c: e.matmul(
                    pb6[:, m:m + 1], sb["gc1t"].ap()[:, c, m * 128:(m + 1) * 128],
                    gxb[:, c:c + 1], start=(c == 0), stop=(c == NCH - 1)),
                    reads=[gxbB, self.B("gc1t")], writes=[pb6B],
                    tag=f"f0{call}_g1mm")
        gh1 = sb["tmpb"].ap().bitcast(BF16)[:, 0:4]
        gh1B = self.take("tmpb")
        for m in range(4):
            p.op("S", lambda e, m=m: e.activation(
                gh1[:, m:m + 1], pb6[:, m:m + 1], AF.Gelu,
                bias=sb["bias_gc1"].ap()[:, m:m + 1]),
                reads=[pb6B, self.B("bias_gc1")], writes=[gh1B],
                tag=f"f0{call}_g1a")
        pb6B = self.take("pb6")
        for m in range(2):
            for c in range(4):
                p.op("T", lambda e, m=m, c=c: e.matmul(
                    pb6[:, 8 + m:9 + m],
                    sb["gc2t"].ap()[:, c, m * 128:(m + 1) * 128],
                    gh1[:, c:c + 1], start=(c == 0), stop=(c == 3)),
                    reads=[gh1B, self.B("gc2t")], writes=[pb6B],
                    tag=f"f0{call}_g2mm")
        gh2 = sb["tmpb"].ap().bitcast(BF16)[:, 8:10]
        gh2B = self.take("tmpb")
        for m in range(2):
            p.op("S", lambda e, m=m: e.activation(
                gh2[:, m:m + 1], pb6[:, 8 + m:9 + m], AF.Gelu,
                bias=sb["bias_gc2"].ap()[:, m:m + 1]),
                reads=[pb6B, self.B("bias_gc2")], writes=[gh2B],
                tag=f"f0{call}_g2a")
        pb6B = self.take("pb6")
        for c in range(2):
            p.op("T", lambda e, c=c: e.matmul(
                pb6[0:1, 16:17], sb["gc3t"].ap()[:, c, 0:1], gh2[:, c:c + 1],
                start=(c == 0), stop=(c == 1)),
                reads=[gh2B, self.B("gc3t")], writes=[pb6B], tag=f"f0{call}_g3mm")
        p.op("V", lambda e: e.tensor_scalar(gr[:, 3:4], gr[:, 2:3], 0.5, None,
                                            ALU.mult),
             reads=[grB], writes=[grB], tag=f"f0{call}_hb3")
        p.op("V", lambda e: e.tensor_scalar(gr[:, 4:5], gr[:, 1:2], 0.5, None,
                                            ALU.mult),
             reads=[grB], writes=[grB], tag=f"f0{call}_hb4")
        p.op("S", lambda e: e.activation(gr[:, 5:6], pb6[0:1, 16:17], AF.Tanh,
                                         bias=gr[:, 3:4], scale=0.5),
             reads=[pb6B, grB], writes=[grB], tag=f"f0{call}_gtanh")
        p.op("V", lambda e: e.tensor_scalar(gr[:, 6:7], gr[:, 5:6], 0.5, 0.5,
                                            ALU.mult, ALU.add),
             reads=[grB], writes=[grB], tag=f"f0{call}_gf01")
        p.op("V", lambda e: e.tensor_scalar(gr[:, 7:8], gr[:, 6:7], gr[:, 0:1],
                                            1.4, ALU.mult, ALU.mult),
             reads=[grB], writes=[grB], tag=f"f0{call}_u1")
        p.op("V", lambda e: e.tensor_scalar(gr[:, 26:27], gr[:, 0:1], -0.7, 0.1,
                                            ALU.mult, ALU.add),
             reads=[grB], writes=[grB], tag=f"f0{call}_u2")
        p.op("V", lambda e: e.tensor_tensor(gr[:, 24:25], gr[:, 7:8],
                                            gr[:, 26:27], ALU.add),
             reads=[grB], writes=[grB], tag=f"f0{call}_A")
        p.op("V", lambda e: e.tensor_scalar(gr[:, 25:26], gr[:, 0:1], 0.3, None,
                                            ALU.mult),
             reads=[grB], writes=[grB], tag=f"f0{call}_C")
        if ncyc == 0:
            return
        pbig = self.pbig.ap()
        signs = [(ALU.add, ALU.subtract), (ALU.add, ALU.subtract),
                 (ALU.subtract, ALU.add), (ALU.subtract, ALU.add)]
        fb = sb["fb16"].ap()
        h1 = sb["h1b"].ap()
        h2 = sb["h2b"].ap()
        h3 = sb["h3b"].ap()
        for cyc in range(ncyc):
            for t in range(ntile or (SEQ // FT1)):
                ts_ = slice(t * FT1, (t + 1) * FT1)
                fbB = self.take("fb16")
                p.op("S", lambda e, ts_=ts_: e.activation(
                    fb, sb["fout"].ap()[:, :, ts_], AF.Copy),
                    reads=[foB], writes=[fbB], tag=f"f0{call}cast{cyc}_{t}")
                h1B = self.take("h1b")
                for m in range(NCH):
                    qr = m % 2
                    pqB = self.takeQ(qr)
                    for c in range(NCH):
                        p.op("T", lambda e, m=m, c=c, qr=qr: e.matmul(
                            pbig[:, qr * 512:qr * 512 + 512],
                            sb["cn1"].ap()[:, c, m * 128:(m + 1) * 128],
                            fb[:, c, :], start=(c == 0), stop=(c == NCH - 1)),
                            reads=[fbB, self.B("cn1")], writes=[pqB],
                            tag=f"f0{call}h1mm{cyc}_{t}_{m}")
                    p.op("S", lambda e, m=m, qr=qr: e.activation(
                        h1[:, m, :], pbig[:, qr * 512:qr * 512 + 512], AF.Gelu,
                        bias=sb["bias_cn1"].ap()[:, m:m + 1]),
                        reads=[pqB, self.B("bias_cn1")], writes=[h1B],
                        tag=f"f0{call}h1a{cyc}_{t}_{m}")
                h2B = self.take("h2b")
                for m in range(4):
                    pqB = self.takeQ(2)
                    for c in range(NCH):
                        p.op("T", lambda e, m=m, c=c: e.matmul(
                            pbig[:, 1024:1536],
                            sb["cn2"].ap()[:, c, m * 128:(m + 1) * 128],
                            h1[:, c, :], start=(c == 0), stop=(c == NCH - 1)),
                            reads=[h1B, self.B("cn2")], writes=[pqB],
                            tag=f"f0{call}h2mm{cyc}_{t}_{m}")
                    p.op("S", lambda e, m=m: e.activation(
                        h2[:, m, :], pbig[:, 1024:1536], AF.Gelu,
                        bias=sb["bias_cn2"].ap()[:, m:m + 1]),
                        reads=[pqB, self.B("bias_cn2")], writes=[h2B],
                        tag=f"f0{call}h2a{cyc}_{t}_{m}")
                h3B = self.take("h3b")
                for m in range(2):
                    pqB = self.takeQ(3)
                    for c in range(4):
                        p.op("T", lambda e, m=m, c=c: e.matmul(
                            pbig[:, 1536:2048],
                            sb["cn3"].ap()[:, c, m * 128:(m + 1) * 128],
                            h2[:, c, :], start=(c == 0), stop=(c == 3)),
                            reads=[h2B, self.B("cn3")], writes=[pqB],
                            tag=f"f0{call}h3mm{cyc}_{t}_{m}")
                    p.op("S", lambda e, m=m: e.activation(
                        h3[:, m, :], pbig[:, 1536:2048], AF.Gelu,
                        bias=sb["bias_cn3"].ap()[:, m:m + 1]),
                        reads=[pqB, self.B("bias_cn3")], writes=[h3B],
                        tag=f"f0{call}h3a{cyc}_{t}_{m}")
                pb5 = self.pb[5].ap()
                pzB = self.take("pb5")
                for c in range(2):
                    p.op("T", lambda e, c=c: e.matmul(
                        pb5[0:1, 0:FT1], sb["cn4"].ap()[:, c, 0:1], h3[:, c, :],
                        start=(c == 0), stop=(c == 1)),
                        reads=[h3B, self.B("cn4")], writes=[pzB],
                        tag=f"f0{call}zmm{cyc}_{t}")
                srB = self.take("srow")
                tr = sb["srow"].ap()[:, 0:FT1]
                p.op("S", lambda e: e.activation(tr, pb5[0:1, 0:FT1], AF.Tanh,
                                                 bias=gr[:, 4:5], scale=0.5),
                     reads=[pzB, grB], writes=[srB], tag=f"f0{call}ctanh{cyc}_{t}")
                crow = sb["srow"].ap().bitcast(BF16)[0:1, 1024:1024 + FT1]
                p.op("V", lambda e: e.tensor_scalar(
                    crow, tr, gr[:, 25:26], gr[:, 24:25], ALU.mult, ALU.add),
                    reads=[srB, grB], writes=[srB], tag=f"f0{call}crow{cyc}_{t}")
                pb6 = self.pb[6].ap()
                pcB = self.take("pb6")
                p.op("T", lambda e: e.matmul(pb6[:, 0:FT1], sb["ones_row"].ap(),
                                             crow, start=True, stop=True),
                     reads=[srB, self.B("c_ones_row")], writes=[pcB],
                     tag=f"f0{call}cbc{cyc}_{t}")
                for pr in range(4):
                    sA, sB_ = signs[pr]
                    fA = sb["fout"].ap()[:, pr, ts_]
                    fB_ = sb["fout"].ap()[:, pr + 4, ts_]
                    m1B = self.take("tmpa")
                    p.op("V", lambda e, fB_=fB_: e.tensor_tensor(
                        sb["tmpa"].ap(), pb6[:, 0:FT1], fB_, ALU.mult),
                        reads=[pcB, foB], writes=[m1B],
                        tag=f"f0{call}m1_{cyc}_{t}_{pr}")
                    m2B = self.take("tmpb")
                    p.op("V", lambda e, fA=fA: e.tensor_tensor(
                        sb["tmpb"].ap(), pb6[:, 0:FT1], fA, ALU.mult),
                        reads=[pcB, foB], writes=[m2B],
                        tag=f"f0{call}m2_{cyc}_{t}_{pr}")
                    p.op("V", lambda e, fA=fA, sA=sA: e.tensor_tensor(
                        fA, fA, sb["tmpa"].ap(), sA),
                        reads=[m1B, foB], writes=[foB],
                        tag=f"f0{call}uA_{cyc}_{t}_{pr}")
                    p.op("V", lambda e, fB_=fB_, sB_=sB_: e.tensor_tensor(
                        fB_, fB_, sb["tmpb"].ap(), sB_),
                        reads=[m2B, foB], writes=[foB],
                        tag=f"f0{call}uB_{cyc}_{t}_{pr}")
                self.pump(4)
        xB = self.B("x")
        for t in range(SEQ // FT1):
            ts_ = slice(t * FT1, (t + 1) * FT1)
            p.op("V", lambda e, ts_=ts_: e.scalar_tensor_tensor(
                sb["x"].ap()[:, :, ts_], sb["fout"].ap()[:, :, ts_],
                0.5, sb["x"].ap()[:, :, ts_], ALU.mult, ALU.add),
                reads=[foB, xB], writes=[xB], tag=f"f0{call}res{t}")

    def phase_qkv0(self):
        self.emit_qkv(0)

    def phase_qkv1(self):
        self.emit_qkv(1)

    def emit_qkv(self, layer):
        p, d, sb = self.p, self.d, self.sb
        for g in range(NGRP):
            gs = slice(g * GRP, (g + 1) * GRP)
            if layer == 0:
                src_ap = sb["x"].ap()[:, :, gs]
                srcB = self.B("x")
            else:
                xgB = self.take("xgrp")
                p.op("Q", lambda e, gs=gs: e.dma_start(
                    sb["xgrp"].ap(),
                    d["xs_b"].rearrange("(c p) t -> p c t", p=128)[:, :, gs]),
                    reads=[self.B("xs_b")], writes=[xgB], dma=True,
                    tag=f"q{layer}ldx{g}")
                src_ap = sb["xgrp"].ap()
                srcB = xgB
            nm = "lnx_a" if g % 2 == 0 else "lnx_b"
            outB = self.take(nm)
            self.emit_ln(src_ap, srcB, GRP, sb[nm].ap(), outB, f"q{layer}ln{g}")
            p.op("Q", lambda e, gs=gs, nm=nm: e.dma_start(
                d["lnxs"].rearrange("(c p) t -> p c t", p=128)[:, :, gs],
                sb[nm].ap()),
                reads=[outB], writes=[self.B("lnxs")], dma=True,
                tag=f"q{layer}stln{g}")
            if layer == 0:
                p.op("Q", lambda e, gs=gs: e.dma_start(
                    d["xs_a"].rearrange("(c p) t -> p c t", p=128)[:, :, gs],
                    sb["x"].ap()[:, :, gs]),
                    reads=[self.B("x")], writes=[self.B("xs_a")], dma=True,
                    tag=f"q{layer}dumpx{g}")
            self.pump(8)
        self.xloc = "xs_a" if layer == 0 else "xs_b"
        wsrc = {0: {"q": "l0wq", "k": "l0wk", "v": "l0wv", "o": "l0wo"},
                1: {"q": "l1wq", "k": "l1wk", "v": "l1wv", "o": "l1wo"}}[layer]
        qB = self.take("q")
        kB = self.take("k")
        vB = self.take("v_tm")
        pbig = self.pbig.ap()
        for half in range(2):
            lhB = self.take("lnx_half")
            p.op("Q", lambda e, half=half: e.dma_start(
                sb["lnx_half"].ap(),
                d["lnxs"].rearrange("(c p) t -> p c t", p=128)
                [:, :, half * 1024:(half + 1) * 1024]),
                reads=[self.B("lnxs")], writes=[lhB], dma=True,
                tag=f"q{layer}ldln{half}")
            lh = sb["lnx_half"].ap()
            for wi, w in enumerate(("q", "k", "v")):
                slot = ("wslotA", "wslotB")[wi % 2]
                slotB = self.take(slot)
                self.load_weight_cast(wsrc[w], slot, D, D, ceng="S",
                                      tag=f"_{layer}{half}")
                wap = sb[slot].ap()
                if w in ("q", "k"):
                    dst = sb[w].ap()
                    dstB = qB if w == "q" else kB
                    for m in range(NCH):
                        for ng in range(2):
                            qr = (m * 2 + ng) % 2
                            pqB = self.takeQ(qr)
                            for c in range(NCH):
                                p.op("T", lambda e, m=m, ng=ng, c=c, qr=qr,
                                     wap=wap, lh=lh: e.matmul(
                                         pbig[:, qr * 512:qr * 512 + 512],
                                         wap[:, c, m * 128:(m + 1) * 128],
                                         lh[:, c, ng * 512:(ng + 1) * 512],
                                         start=(c == 0), stop=(c == NCH - 1)),
                                     reads=[lhB, slotB], writes=[pqB],
                                     tag=f"q{layer}{w}mm{half}_{m}_{ng}")
                            osl = dst[:, m, half * 1024 + ng * 512:
                                      half * 1024 + (ng + 1) * 512]
                            if layer == 0:
                                p.op("V", lambda e, o=osl, qr=qr: e.tensor_copy(
                                    o, pbig[:, qr * 512:qr * 512 + 512]),
                                    reads=[pqB], writes=[dstB],
                                    tag=f"q{layer}{w}ev{half}_{m}_{ng}")
                            else:
                                bn = "bias_l1q" if w == "q" else "bias_l1k"
                                p.op("S", lambda e, o=osl, qr=qr, m=m, bn=bn:
                                     e.activation(
                                         o, pbig[:, qr * 512:qr * 512 + 512],
                                         AF.Identity,
                                         bias=sb[bn].ap()[:, m:m + 1]),
                                     reads=[pqB, self.B(bn)], writes=[dstB],
                                     tag=f"q{layer}{w}ev{half}_{m}_{ng}")
                else:
                    for kc in range(8):
                        for ng in range(2):
                            qr = (kc * 2 + ng) % 2
                            pqB = self.takeQ(qr)
                            for c in range(NCH):
                                p.op("T", lambda e, kc=kc, ng=ng, c=c, qr=qr,
                                     wap=wap, lh=lh: e.matmul(
                                         pbig[:, qr * 512:qr * 512 + 512],
                                         lh[:, c, kc * 128:(kc + 1) * 128],
                                         wap[:, c, ng * 512:(ng + 1) * 512],
                                         start=(c == 0), stop=(c == NCH - 1)),
                                     reads=[lhB, slotB], writes=[pqB],
                                     tag=f"q{layer}vmm{half}_{kc}_{ng}")
                            ovs = sb["v_tm"].ap()[:, half * 8 + kc,
                                                  ng * 512:(ng + 1) * 512]
                            if layer == 0:
                                p.op("V", lambda e, o=ovs, qr=qr: e.tensor_copy(
                                    o, pbig[:, qr * 512:qr * 512 + 512]),
                                    reads=[pqB], writes=[vB],
                                    tag=f"q{layer}vev{half}_{kc}_{ng}")
                            else:
                                p.op("V", lambda e, o=ovs, qr=qr, ng=ng:
                                     e.tensor_tensor(
                                         o, pbig[:, qr * 512:qr * 512 + 512],
                                         sb["bvb"].ap()[:, ng * 512:(ng + 1) * 512],
                                         ALU.add),
                                     reads=[pqB, self.B("bvb")], writes=[vB],
                                     tag=f"q{layer}vev{half}_{kc}_{ng}")
                self.pump(4)
        woB = self.take("wo_t")
        self.load_weight_cast(wsrc["o"], "wo_t", D, D, ceng="S", tag=f"_o{layer}")
        if layer == 1:
            self.emit_rope()
        self.emit_negB(layer)

    def emit_bvb(self):
        p, d, sb = self.p, self.d, self.sb
        stB = self.take("wstage0")
        p.op("Q", lambda e: e.dma_start(sb["wstage"].ap()[0:1, 0, 0:D],
                                        d["l1bv"].ap()),
             writes=[stB], dma=True, tag="ld_bv")
        srB = self.take("srow")
        bv = sb["srow"].ap().bitcast(BF16)[0:1, 0:D]
        p.op("V", lambda e: e.tensor_copy(bv, sb["wstage"].ap()[0:1, 0, 0:D]),
             reads=[stB], writes=[srB], tag="bv_cast")
        pbig = self.pbig.ap()
        bvbB = self.take("bvb")
        for ng in range(2):
            pqB = self.takeQ(ng)
            p.op("T", lambda e, ng=ng: e.matmul(
                pbig[:, ng * 512:(ng + 1) * 512], sb["ones_row"].ap(),
                bv[:, ng * 512:(ng + 1) * 512], start=True, stop=True),
                reads=[srB, self.B("c_ones_row")], writes=[pqB], tag=f"bvbc{ng}")
            p.op("V", lambda e, ng=ng: e.tensor_copy(
                sb["bvb"].ap()[:, ng * 512:(ng + 1) * 512],
                pbig[:, ng * 512:(ng + 1) * 512]),
                reads=[pqB], writes=[bvbB], tag=f"bvcp{ng}")

    def emit_rope(self):
        p, d, sb = self.p, self.d, self.sb
        for src, dst in (("ropecos", "cosb"), ("ropesin", "sinb")):
            dB = self.take(dst)
            for piece in range(2):
                stB = self.take(f"wstage{piece}")
                ap_st = sb["wstage"].ap()[:, piece, :]
                p.op("Q", lambda e, a=ap_st, s=src, piece=piece: e.dma_start(
                    a, d[s].ap()[:, piece * 1024:(piece + 1) * 1024]),
                    writes=[stB], dma=True, tag=f"ld_{dst}{piece}")
                p.op("S", lambda e, a=ap_st, dst=dst, piece=piece: e.copy(
                    sb[dst].ap()[:, piece * 1024:(piece + 1) * 1024], a),
                    reads=[stB], writes=[dB], tag=f"cast_{dst}{piece}")
        pbig = self.pbig.ap()
        cosB, sinB = self.B("cosb"), self.B("sinb")
        tAb = sb["tmpa"].ap().bitcast(BF16)[:, 0:512]
        for w in ("q", "k"):
            tgt = sb[w].ap()
            tgtB = self.B(w)
            for ch in range(NCH):
                for kg in range(4):
                    ks = slice(kg * 512, (kg + 1) * 512)
                    qr = kg % 2
                    pqB = self.takeQ(qr)
                    p.op("T", lambda e, ch=ch, ks=ks, qr=qr, tgt=tgt: e.matmul(
                        pbig[:, qr * 512:qr * 512 + 512], sb["pswap"].ap(),
                        tgt[:, ch, ks], start=True, stop=True),
                        reads=[tgtB, self.B("pswap")], writes=[pqB],
                        tag=f"rsw_{w}{ch}_{kg}")
                    tB = self.take("tmpa")
                    p.op("G", lambda e, ch=ch, ks=ks, tgt=tgt: e.tensor_tensor(
                        tAb, tgt[:, ch, ks], sb["cosb"].ap()[:, ks], ALU.mult),
                        reads=[tgtB, cosB], writes=[tB], tag=f"rt_{w}{ch}_{kg}")
                    mB = self.take("tmpb")
                    p.op("V", lambda e, ks=ks, qr=qr: e.tensor_tensor(
                        sb["tmpb"].ap(), pbig[:, qr * 512:qr * 512 + 512],
                        sb["sinb"].ap()[:, ks], ALU.mult),
                        reads=[pqB, sinB], writes=[mB], tag=f"rm_{w}{ch}_{kg}")
                    p.op("V", lambda e, ch=ch, ks=ks, tgt=tgt: e.tensor_tensor(
                        tgt[:, ch, ks], tAb, sb["tmpb"].ap(), ALU.add),
                        reads=[tB, mB], writes=[tgtB], tag=f"ro_{w}{ch}_{kg}")
            self.pump(2)

    def emit_negB(self, layer):
        p, sb = self.p, self.sb
        pbig = self.pbig.ap()
        scale = 0.125 if layer == 1 else 1.0 / 32.0
        nh = 16 if layer == 1 else 1
        st = sb["stats"].ap()
        for iw, w in enumerate(("q", "k")):
            tgt = sb[w].ap()
            tgtB = self.B(w)
            pqall = [self.takeQ(i) for i in range(4)]
            sqB = self.take("sq128")
            for ch in range(NCH):
                p.op("S", lambda e, ch=ch, tgt=tgt: e.activation(
                    sb["sq128"].ap(), tgt[:, ch, :], AF.Square),
                    reads=[tgtB], writes=[sqB], tag=f"nBsq_{w}{ch}")
                for kg in range(4):
                    if layer == 1:
                        p.op("T", lambda e, ch=ch, kg=kg: e.matmul(
                            pbig[0:16, kg * 512:(kg + 1) * 512],
                            sb["headsel"].ap()[:, ch, :],
                            sb["sq128"].ap()[:, kg * 512:(kg + 1) * 512],
                            start=(ch == 0), stop=(ch == NCH - 1)),
                            reads=[sqB, self.B("headsel")], writes=[pqall[kg]],
                            tag=f"nBmm_{w}{ch}_{kg}")
                    else:
                        p.op("T", lambda e, ch=ch, kg=kg: e.matmul(
                            pbig[0:1, kg * 512:(kg + 1) * 512],
                            sb["ones_col"].ap(),
                            sb["sq128"].ap()[:, kg * 512:(kg + 1) * 512],
                            start=(ch == 0), stop=(ch == NCH - 1)),
                            reads=[sqB, self.B("c_ones_col")], writes=[pqall[kg]],
                            tag=f"nBmm_{w}{ch}_{kg}")
            stB = self.take("stats")
            p.op("V", lambda e, nh=nh: e.reduce_max(
                st[0:nh, 0:1], pbig[0:nh, :], AX.X),
                reads=pqall, writes=[stB], tag=f"nBmax_{w}")
            p.op("S", lambda e, nh=nh, iw=iw: e.activation(
                st[0:nh, 1 + iw:2 + iw], st[0:nh, 0:1], AF.Ln),
                reads=[stB], writes=[stB], tag=f"nBln_{w}")
        stB = self.B("stats")
        p.op("V", lambda e, nh=nh: e.tensor_tensor(
            st[0:nh, 3:4], st[0:nh, 1:2], st[0:nh, 2:3], ALU.add),
            reads=[stB], writes=[stB], tag="nBadd")
        p.op("S", lambda e, nh=nh: e.activation(
            st[0:nh, 3:4], st[0:nh, 3:4], AF.Exp, scale=0.5),
            reads=[stB], writes=[stB], tag="nBexp")
        p.op("V", lambda e, nh=nh, scale=scale: e.tensor_scalar(
            st[0:nh, 3:4], st[0:nh, 3:4], -scale, None, ALU.mult),
            reads=[stB], writes=[stB], tag="nBscale")
        bvec = sb["stats"].ap().bitcast(BF16)[0:nh, 8:9]
        p.op("V", lambda e, nh=nh, bvec=bvec: e.tensor_copy(
            bvec, st[0:nh, 3:4]),
            reads=[stB], writes=[stB], tag="nBb16")
        pb4 = self.pb[4].ap()
        pb4b16 = self.pb[4].ap().bitcast(BF16)
        pb4B = self.take("pb4")
        p.op("T", lambda e, nh=nh, bvec=bvec: e.matmul(
            pb4[0:1, 0:nh], bvec, sb["ident"].ap()[0:nh, 0:nh],
            start=True, stop=True),
            reads=[stB, self.B("ident")], writes=[pb4B], tag="nBtr")
        srB = self.take("srow")
        p.op("V", lambda e, nh=nh: e.tensor_copy(sb["srow"].ap()[0:1, 0:nh],
                                                 pb4[0:1, 0:nh]),
             reads=[pb4B], writes=[srB], tag="nBtrc")
        pb4B = self.take("pb4")
        p.op("T", lambda e, nh=nh: e.matmul(
            pb4[:, 0:nh], sb["ones_row_f32"].ap(), sb["srow"].ap()[0:1, 0:nh],
            start=True, stop=True),
            reads=[srB, self.B("c_ones_row_f32")], writes=[pb4B], tag="nBbc")
        nBB = self.take("negB")
        col0 = 0 if layer == 1 else 16
        p.op("V", lambda e, nh=nh, col0=col0: e.tensor_copy(
            sb["negB"].ap()[:, col0:col0 + nh], pb4[:, 0:nh]),
            reads=[pb4B], writes=[nBB], tag="nBcp")

    def phase_attn0(self):
        self.emit_attn(0)

    def phase_attn1(self):
        self.emit_attn(1)

    def emit_attn(self, layer):
        p, d, sb = self.p, self.d, self.sb
        pbig = self.pbig.ap()
        xs_src = d["xs_a"] if layer == 0 else d["xs_b"]
        xs_dst = d["xs_b"] if layer == 0 else d["xs_a"]
        xsrcB = self.B("xs_a" if layer == 0 else "xs_b")
        xdstB = self.B("xs_b" if layer == 0 else "xs_a")
        sc = 1.0 / 32.0 if layer == 0 else 0.125
        heads = 1 if layer == 0 else HEADS
        qB, kB, vB = self.B("q"), self.B("k"), self.B("v_tm")
        woB = self.B("wo_t")
        Pap = sb["P"].ap()
        sums = sb["stats"].ap()
        pvv = pbig.rearrange("p (a b) -> p a b", b=128)
        for qt in range(NQT):
            L = (qt + 1) * 128
            kgs = (L + 511) // 512
            nkc = L // 128
            qs = slice(qt * 128, qt * 128 + 128)
            xrB = self.take("xres")
            p.op("Q", lambda e, qs=qs: e.dma_start(
                sb["xres"].ap(),
                xs_src.rearrange("(c p) t -> p c t", p=128)[:, :, qs]),
                reads=[xsrcB], writes=[xrB], dma=True, tag=f"a{layer}ldx{qt}")
            ovB = self.take("outv")
            for h in range(heads):
                if layer == 0:
                    lq = [(c, 0, 128) for c in range(NCH)]
                else:
                    lq = [(h // 2, (h % 2) * 64, 64)]
                PB = self.take("P")
                smB = self.take("stats")
                colB = 16 if layer == 0 else h
                for kg in range(kgs):
                    n = min(512, L - kg * 512)
                    ks = slice(kg * 512, kg * 512 + n)
                    qr = kg % 2
                    pqB = self.takeQ(qr)
                    for i, (c, poff, kw) in enumerate(lq):
                        p.op("T", lambda e, c=c, poff=poff, kw=kw, ks=ks, qs=qs,
                             qr=qr, i=i, n=n, nl=len(lq): e.matmul(
                                 pbig[:, qr * 512:qr * 512 + n],
                                 sb["q"].ap()[poff:poff + kw, c, qs],
                                 sb["k"].ap()[poff:poff + kw, c, ks],
                                 start=(i == 0), stop=(i == nl - 1)),
                             reads=[qB, kB], writes=[pqB],
                             tag=f"a{layer}s{qt}_{h}_{kg}")
                    if kg == kgs - 1:
                        p.op("V", lambda e, qr=qr, n=n: e.tensor_tensor(
                            pbig[:, qr * 512 + n - 128:qr * 512 + n],
                            pbig[:, qr * 512 + n - 128:qr * 512 + n],
                            sb["maskdiag"].ap(), ALU.add),
                            reads=[self.B("maskdiag"), pqB], writes=[pqB],
                            tag=f"a{layer}mask{qt}_{h}")
                    p.op("S", lambda e, qr=qr, n=n, ks=ks, kg=kg, colB=colB:
                         e.activation(Pap[:, ks], pbig[:, qr * 512:qr * 512 + n],
                                      AF.Exp,
                                      bias=sb["negB"].ap()[:, colB:colB + 1],
                                      scale=sc, accum_out=sums[:, kg:kg + 1]),
                         reads=[pqB, self.B("negB")], writes=[PB, smB],
                         tag=f"a{layer}exp{qt}_{h}_{kg}")
                riB = self.take("rinv")
                p.op("V", lambda e, kgs=kgs: e.reduce_sum(
                    sb["rinv"].ap(), sums[:, 0:kgs], AX.X),
                    reads=[smB], writes=[riB], tag=f"a{layer}rs{qt}_{h}")
                p.op("V", lambda e: e.reciprocal(sb["rinv"].ap(), sb["rinv"].ap()),
                     reads=[riB], writes=[riB], tag=f"a{layer}ri{qt}_{h}")
                dgB = self.take("diag")
                p.op("V", lambda e: e.tensor_scalar(
                    sb["diag"].ap(), sb["ident"].ap(), sb["rinv"].ap(), None,
                    ALU.mult),
                    reads=[riB, self.B("ident")], writes=[dgB],
                    tag=f"a{layer}dg{qt}_{h}")
                PTB = self.take("PT")
                pb4f = self.pb[4].ap()
                for kc0 in range(0, nkc, 4):
                    nb = min(4, nkc - kc0)
                    p4B = self.take("pb4")
                    for j in range(nb):
                        p.op("T", lambda e, kc0=kc0, j=j: e.matmul(
                            pb4f[:, j * 128:(j + 1) * 128],
                            Pap[:, (kc0 + j) * 128:(kc0 + j + 1) * 128],
                            sb["diag"].ap(), start=True, stop=True),
                            reads=[PB, dgB], writes=[p4B],
                            tag=f"a{layer}tr{qt}_{h}_{kc0 + j}")
                    p.op("V", lambda e, kc0=kc0, nb=nb: e.tensor_copy(
                        sb["PT"].ap()[:, kc0 * 128:(kc0 + nb) * 128],
                        pb4f[:, 0:nb * 128]),
                        reads=[p4B], writes=[PTB], tag=f"a{layer}tre{qt}_{h}_{kc0}")
                if layer == 0:
                    pvB = [self.takeQ(2), self.takeQ(3)]
                    for m in range(NCH):
                        for kc in range(nkc):
                            p.op("T", lambda e, kc=kc, m=m, nkc=nkc: e.matmul(
                                pvv[:, 8 + m, :],
                                sb["v_tm"].ap()[:, kc, m * 128:(m + 1) * 128],
                                sb["PT"].ap()[:, kc * 128:(kc + 1) * 128],
                                start=(kc == 0), stop=(kc == nkc - 1)),
                                reads=[PTB, vB], writes=pvB,
                                tag=f"a{layer}pv{qt}_{kc}_{m}")
                    p.op("V", lambda e: e.tensor_copy(
                        sb["outv"].ap().rearrange("p c t -> p (c t)"),
                        pbig[:, 1024:2048]),
                        reads=pvB, writes=[ovB], tag=f"a{layer}pve{qt}")
                else:
                    pb5 = self.pb[5].ap()
                    sl = (h % 2) * 128
                    pvB = self.take("pb5")
                    for kc in range(nkc):
                        p.op("T", lambda e, kc=kc, h=h, sl=sl, nkc=nkc: e.matmul(
                            pb5[0:64, sl:sl + 128],
                            sb["v_tm"].ap()[:, kc, h * 64:(h + 1) * 64],
                            sb["PT"].ap()[:, kc * 128:(kc + 1) * 128],
                            start=(kc == 0), stop=(kc == nkc - 1)),
                            reads=[PTB, vB], writes=[pvB],
                            tag=f"a{layer}pv{qt}_{h}_{kc}")
                    p.op("V", lambda e, h=h, sl=sl: e.tensor_copy(
                        sb["outv"].ap()[(h % 2) * 64:(h % 2) * 64 + 64, h // 2, :],
                        pb5[0:64, sl:sl + 128]),
                        reads=[pvB], writes=[ovB], tag=f"a{layer}pve{qt}_{h}")
            pyB = [self.takeQ(2), self.takeQ(3)]
            for m in range(NCH):
                for c in range(NCH):
                    p.op("T", lambda e, m=m, c=c: e.matmul(
                        pvv[:, 8 + m, :],
                        sb["wo_t"].ap()[:, c, m * 128:(m + 1) * 128],
                        sb["outv"].ap()[:, c, :], start=(c == 0),
                        stop=(c == NCH - 1)),
                        reads=[ovB, woB], writes=pyB, tag=f"a{layer}o{qt}_{m}")
            xnB = self.take("xnew")
            for m in range(NCH):
                if layer == 0:
                    p.op("V", lambda e, m=m: e.tensor_tensor(
                        sb["xnew"].ap()[:, m, :], pvv[:, 8 + m, :],
                        sb["xres"].ap()[:, m, :], ALU.add),
                        reads=pyB + [xrB], writes=[xnB], tag=f"a{layer}r{qt}_{m}")
                else:
                    p.op("V", lambda e, m=m: e.scalar_tensor_tensor(
                        sb["xnew"].ap()[:, m, :], pvv[:, 8 + m, :],
                        sb["bias_l1o"].ap()[:, m:m + 1], sb["xres"].ap()[:, m, :],
                        ALU.add, ALU.add),
                        reads=pyB + [xrB, self.B("bias_l1o")], writes=[xnB],
                        tag=f"a{layer}r{qt}_{m}")
            p.op("Q", lambda e, qs=qs: e.dma_start(
                xs_dst.rearrange("(c p) t -> p c t", p=128)[:, :, qs],
                sb["xnew"].ap()),
                reads=[xnB], writes=[xdstB], dma=True, tag=f"a{layer}st{qt}")
            self.pump(2)
        self.xloc = "xs_b" if layer == 0 else "xs_a"

    def phase_ffn1a(self):
        self.emit_ffn1("a")

    def phase_ffn1b(self):
        self.emit_ffn1("b")

    def emit_ffn1(self, call):
        p, d, sb = self.p, self.d, self.sb
        pbig = self.pbig.ap()
        if call == "a":
            p.op("Q", lambda e: e.dma_start(
                d["xs_a"].rearrange("(c p) t -> p c t", p=128), sb["x"].ap()),
                reads=[self.B("x")], writes=[self.B("xs_a")], dma=True,
                tag="f1adump")
            xs_src, xsB = d["xs_a"], self.B("xs_a")
            xs_dst, xdB = d["xs_b"], self.B("xs_b")
        else:
            xs_src, xsB = d["xs_a"], self.B("xs_a")
            xs_dst, xdB = d["y"], self.B("y")
        fcB = self.take("wfcq")
        p.op("Q", lambda e: e.dma_start(
            sb["wfcq"].ap(), d["wfcq_d"].rearrange("(c p) m -> p c m", p=128)),
            reads=[self.B("wfcq_d_dram")], writes=[fcB], dma=True,
            tag=f"f1{call}ldfc")
        prB = self.take("wprq")
        p.op("Q", lambda e: e.dma_start(
            sb["wprq"].ap(), d["wprq_d"].rearrange("(c p) m -> p c m", p=128)),
            reads=[self.B("wprq_d_dram")], writes=[prB], dma=True,
            tag=f"f1{call}ldpr")
        sq2 = sb["sq128"].ap()
        for t in range(NT2):
            ts_ = slice(t * FT2, (t + 1) * FT2)
            xtB = self.take("xtile")
            p.op("Q", lambda e, ts_=ts_: e.dma_start(
                sb["xtile"].ap(),
                xs_src.rearrange("(c p) t -> p c t", p=128)[:, :, ts_]),
                reads=[xsB], writes=[xtB], dma=True, tag=f"f1{call}ldx{t}")
            lcB = self.take("lnx_c")
            self.emit_ln(sb["xtile"].ap(), xtB, FT2, sb["lnx_c"].ap(), lcB,
                         f"f1{call}ln{t}")
            hB = self.take("h")
            for m in range(32):
                q4 = m % 4
                pqB = self.takeQ(q4 // 2)
                for c in range(NCH):
                    p.op("T", lambda e, m=m, c=c, q4=q4: e.matmul(
                        pbig[:, q4 * 256:q4 * 256 + 256],
                        sb["wfcq"].ap()[:, c, m * 128:(m + 1) * 128],
                        sb["lnx_c"].ap()[:, c, :], start=(c == 0),
                        stop=(c == NCH - 1)),
                        reads=[lcB, fcB], writes=[pqB],
                        tag=f"f1{call}fc{t}_{m}")
                p.op("S", lambda e, m=m, q4=q4: e.activation(
                    sb["h"].ap()[:, m, :], pbig[:, q4 * 256:q4 * 256 + 256],
                    AF.Gelu),
                    reads=[pqB], writes=[hB], tag=f"f1{call}fca{t}_{m}")
            pm = self.pb[5].ap()
            pmB = self.take("pb5")
            oc = sb["ones_col"].ap()
            for m in range(32):
                p.op("T", lambda e, m=m: e.matmul(
                    pm[0:1, 0:FT2], oc, sb["h"].ap()[:, m, :],
                    start=(m == 0), stop=(m == 31)),
                    reads=[hB, self.B("c_ones_col")], writes=[pmB],
                    tag=f"f1{call}hmu{t}")
            pv_ = self.pb[6].ap()
            pvB = self.take("pb6")
            for m in range(32):
                so = (m % 2) * FT2
                sqB = self.take("sq128")
                p.op("S", lambda e, m=m, so=so: e.activation(
                    sq2[:, so:so + FT2], sb["h"].ap()[:, m, :], AF.Square),
                    reads=[hB], writes=[sqB], tag=f"f1{call}hsq{t}_{m}")
                p.op("T", lambda e, m=m, so=so: e.matmul(
                    pv_[0:1, 0:FT2], oc, sq2[:, so:so + FT2],
                    start=(m == 0), stop=(m == 31)),
                    reads=[sqB, self.B("c_ones_col")], writes=[pvB],
                    tag=f"f1{call}hvm{t}_{m}")
            sr = sb["srow"].ap()
            hD = 4096.0
            mu = sr[:, 0:FT2]
            var = sr[:, 512:512 + FT2]
            srB = self.take("srow")
            p.op("V", lambda e, mu=mu: e.tensor_scalar(
                mu, pm[0:1, 0:FT2], 1.0 / hD, None, ALU.mult),
                reads=[pmB], writes=[srB], tag=f"f1{call}mu{t}")
            p.op("V", lambda e, var=var: e.tensor_scalar(
                var, pv_[0:1, 0:FT2], 1.0 / hD, None, ALU.mult),
                reads=[pvB], writes=[srB], tag=f"f1{call}ex2{t}")
            ta = sb["tmpa"].ap()[0:1, 0:FT2]
            taB = self.take("tmpa")
            p.op("V", lambda e, ta=ta, mu=mu: e.tensor_tensor(ta, mu, mu, ALU.mult),
                 reads=[srB], writes=[taB], tag=f"f1{call}musq{t}")
            p.op("V", lambda e, var=var, ta=ta: e.tensor_tensor(
                var, var, ta, ALU.subtract),
                reads=[srB, taB], writes=[srB], tag=f"f1{call}var{t}")
            eps = sb["scal"].ap()[0:1, 0:1]
            p.op("S", lambda e, var=var: e.activation(var, var, AF.Ln, bias=eps),
                 reads=[srB, self.B("scal")], writes=[srB], tag=f"f1{call}lnv{t}")
            p.op("S", lambda e, var=var: e.activation(var, var, AF.Exp, scale=-0.5),
                 reads=[srB], writes=[srB], tag=f"f1{call}inv{t}")
            p.op("V", lambda e, mu=mu, var=var: e.tensor_tensor(
                mu, mu, var, ALU.mult),
                reads=[srB], writes=[srB], tag=f"f1{call}m2{t}")
            orf = sb["ones_row_f32"].ap()
            pb4 = self.pb[4].ap()
            for rowap, dst in ((var, "invb"), (mu, "m2b")):
                pb4B = self.take("pb4")
                p.op("T", lambda e, r=rowap: e.matmul(
                    pb4[:, 0:FT2], orf, r, start=True, stop=True),
                    reads=[srB, self.B("c_ones_row_f32")], writes=[pb4B],
                    tag=f"f1{call}bc{t}_{dst}")
                dB = self.take(dst)
                p.op("V", lambda e, dst=dst: e.tensor_copy(
                    sb[dst].ap()[:, 0:FT2], pb4[:, 0:FT2]),
                    reads=[pb4B], writes=[dB], tag=f"f1{call}bcc{t}_{dst}")
            invb = sb["invb"].ap()[:, 0:FT2]
            m2bv = sb["m2b"].ap()[:, 0:FT2]
            tb16 = sb["tmpa"].ap().bitcast(BF16)[:, 0:FT2]
            for m in range(32):
                taB = self.take("tmpa")
                p.op("V", lambda e, m=m: e.tensor_tensor(
                    tb16, sb["h"].ap()[:, m, :], invb, ALU.mult),
                    reads=[hB, self.B("invb")], writes=[taB],
                    tag=f"f1{call}hap1{t}_{m}")
                p.op("V", lambda e: e.tensor_tensor(tb16, tb16, m2bv,
                                                    ALU.subtract),
                     reads=[taB, self.B("m2b")], writes=[taB],
                     tag=f"f1{call}hap2{t}_{m}")
                p.op("S", lambda e, m=m: e.activation(
                    sb["h"].ap()[:, m, :], tb16, AF.Identity,
                    bias=sb["lnb"].ap()[:, m:m + 1],
                    scale=sb["lnw"].ap()[:, m:m + 1]),
                    reads=[taB, self.B("lnw"), self.B("lnb")], writes=[hB],
                    tag=f"f1{call}hap3{t}_{m}")
            ytB = self.take("ytile")
            for m in range(NCH):
                q4 = 4 + m % 4
                pqB = self.takeQ(q4 // 2)
                for c in range(32):
                    p.op("T", lambda e, m=m, c=c, q4=q4: e.matmul(
                        pbig[:, q4 * 256:q4 * 256 + 256],
                        sb["wprq"].ap()[:, c, m * 128:(m + 1) * 128],
                        sb["h"].ap()[:, c, :], start=(c == 0), stop=(c == 31)),
                        reads=[hB, prB], writes=[pqB],
                        tag=f"f1{call}pr{t}_{m}")
                p.op("V", lambda e, m=m, q4=q4: e.scalar_tensor_tensor(
                    sb["ytile"].ap()[:, m, :], pbig[:, q4 * 256:q4 * 256 + 256],
                    0.5, sb["xtile"].ap()[:, m, :], ALU.mult, ALU.add),
                    reads=[pqB, xtB], writes=[ytB], tag=f"f1{call}ev{t}_{m}")
            p.op("Q", lambda e, ts_=ts_: e.dma_start(
                xs_dst.rearrange("(c p) t -> p c t", p=128)[:, :, ts_],
                sb["ytile"].ap()),
                reads=[ytB], writes=[xdB], dma=True, tag=f"f1{call}st{t}")
            self.pump(2)
        self.xloc = "xs_b" if call == "a" else "y"

    def phase_finish(self):
        p, d, sb = self.p, self.d, self.sb
        if self.xloc == "y":
            p.wait_event("Q", self.B("y").w[0])
            return
        if self.xloc == "x":
            ev = p.op("Q", lambda e: e.dma_start(
                d["y"].rearrange("(c p) t -> p c t", p=128), sb["x"].ap()),
                reads=[self.B("x")], writes=[self.B("y")], dma=True, tag="finx")
        elif self.xloc == "fout":
            ev = p.op("Q", lambda e: e.dma_start(
                d["y"].rearrange("(c p) t -> p c t", p=128), sb["fout"].ap()),
                reads=[self.B("fout")], writes=[self.B("y")], dma=True,
                tag="finf")
        else:
            ev = p.op("Q", lambda e, s=self.xloc: e.dma_start(
                d["y"].ap(), d[s].ap()),
                reads=[self.B(self.xloc)], writes=[self.B("y")], dma=True,
                tag="find")
        p.wait_event("Q", ev)

    def build(self):
        self.phase_init()
        if "qkv1" in self.phases:
            self.emit_bvb()
        for ph in self.phases:
            getattr(self, "phase_" + ph)()
        self.pump_all()
        if self.phases and self.phases[-1] in ("ln1", "ln3"):
            self.xloc = "fout"
        self.phase_finish()
        return self.nc


def kernel(**inputs):
    cores = prep_inputs(inputs)
    b = Builder(_PHASES)
    nc = b.build()
    b.p.emit()
    res = run_bass_kernel_spmd(nc, cores, core_ids=list(range(NCORES)))
    out = np.stack([np.ascontiguousarray(np.asarray(r["y"]).T)
                    for r in res.results])
    return out.astype(np.float32)


# version 02
# speedup vs baseline: 1.7315x; 1.7315x over previous
import math
import numpy as np
import ml_dtypes

import concourse.bass as bass
import concourse.mybir as mybir
from concourse.alu_op_type import AluOpType as ALU
from concourse.bass_utils import run_bass_kernel_spmd

F32 = mybir.dt.float32
BF16 = mybir.dt.bfloat16
AF = mybir.ActivationFunctionType
AX = mybir.AxisListType

NCORES = 8
SEQ = 2048
D = 1024
NCH = 8
HEADS = 16
HD = 64
GRP = 512
NGRP = 4
FT1 = 512
FT2 = 256
NT2 = SEQ // FT2
QT = 128
NQT = 16

ALL_PHASES = ["ln1", "ffn0a", "qkv0", "attn0", "ln3", "ffn0b",
              "ffn1a", "qkv1", "attn1", "ffn1b"]
_PHASES = None
_SAME_ENGINE_SEM = True
_FFN0_NCYC = 3


class Buf:
    __slots__ = ("name", "w", "r")

    def __init__(self, name):
        self.name = name
        self.w = []
        self.r = []


class Op:
    __slots__ = ("fn", "waits", "stream", "seq", "eng", "tag")

    def __init__(self, fn, waits, stream, seq, eng, tag):
        self.fn = fn
        self.waits = waits
        self.stream = stream
        self.seq = seq
        self.eng = eng
        self.tag = tag


class Prog:
    ENGS = ("T", "V", "S", "G", "Q")
    DMA_SLOTS = {"Q": 12, "G": 6, "S": 4, "V": 2, "T": 2}

    def __init__(self, nc):
        self.nc = nc
        self.eng = {"T": nc.tensor, "V": nc.vector, "S": nc.scalar,
                    "G": nc.gpsimd, "Q": nc.sync}
        self.ops = []
        self.seq = {}
        self.waited = {e: {} for e in self.ENGS}
        self.needed = {}
        self.dmacnt = {}

    def op(self, eng, fn, reads=(), writes=(), dma=False, tag=""):
        if dma:
            i = self.dmacnt.get(eng, 0)
            self.dmacnt[eng] = i + 1
            nslots = self.DMA_SLOTS[eng]
            stream = f"{eng}d{i % nslots}"
            s = i // nslots
        else:
            stream = eng
            s = self.seq.get(stream, 0)
        deps = []
        for b in reads:
            deps += b.w
        for b in writes:
            deps += b.w
            deps += b.r
        if dma and s > 0:
            deps.append((stream, s - 1))
        waits = []
        wmap = self.waited[eng]
        for (ps, pq) in deps:
            if ps == "T" and eng == "T" and not dma:
                continue
            if (not _SAME_ENGINE_SEM) and ps == eng and not dma:
                continue
            if wmap.get(ps, -1) < pq:
                wmap[ps] = pq
                self.needed.setdefault(ps, set()).add(pq)
                waits.append((ps, pq))
        self.seq[stream] = s + 1
        self.ops.append(Op(fn, waits, stream, s, eng, tag))
        if dma:
            self.needed.setdefault(stream, set()).add(s)
        ev = (stream, s)
        for b in writes:
            b.w = [ev]
            b.r = []
        for b in reads:
            b.r.append(ev)
        return ev

    def wait_event(self, eng, ev):
        (ps, pq) = ev
        self.needed.setdefault(ps, set()).add(pq)
        self.ops.append(Op(None, [(ps, pq)], None, None, eng, "finwait"))

    def emit(self):
        nc = self.nc
        rank = {}
        sems = {}
        for stream, seqs in self.needed.items():
            order = sorted(seqs)
            step = 16 if len(stream) > 1 else 1
            rank[stream] = {q: (i + 1) * step for i, q in enumerate(order)}
            sems[stream] = nc.alloc_semaphore(f"sem_{stream}")
        for o in self.ops:
            eng = self.eng[o.eng]
            for (ps, pq) in o.waits:
                eng.wait_ge(sems[ps], rank[ps][pq])
            if o.fn is None:
                continue
            inst = o.fn(eng)
            if o.stream in rank and o.seq in rank[o.stream]:
                step = 16 if len(o.stream) > 1 else 1
                inst.then_inc(sems[o.stream], step)
        return len(self.ops)


def _rope_perm():
    pi = np.zeros(D, dtype=np.int64)
    for h in range(HEADS):
        for i in range(32):
            pi[h * 64 + i] = h * 64 + 2 * i
            pi[h * 64 + 32 + i] = h * 64 + 2 * i + 1
    return pi


def _vec_tile(v):
    return np.ascontiguousarray(np.asarray(v, dtype=np.float32).reshape(-1, 128).T)


def prep_inputs(inp):
    f32 = np.float32
    pi = _rope_perm()
    g = {}
    g["l0wq"] = np.ascontiguousarray(inp["L0_attn_Wq"], dtype=f32)
    g["l0wk"] = np.ascontiguousarray(inp["L0_attn_Wk"], dtype=f32)
    g["l0wv"] = np.ascontiguousarray(inp["L0_attn_Wv"], dtype=f32)
    g["l0wo"] = np.ascontiguousarray(inp["L0_attn_Wo"], dtype=f32)
    for k in ("cn_w1", "cn_w2", "cn_w3", "cn_w4", "gc_w1", "gc_w2", "gc_w3"):
        g[k] = np.ascontiguousarray(inp["L0_ffn_" + k], dtype=f32)
    g["cn_b1"] = _vec_tile(inp["L0_ffn_cn_b1"])
    g["cn_b2"] = _vec_tile(inp["L0_ffn_cn_b2"])
    g["cn_b3"] = _vec_tile(inp["L0_ffn_cn_b3"])
    g["cn_b4"] = np.asarray(inp["L0_ffn_cn_b4"], dtype=f32).reshape(1, 1)
    g["gc_b1"] = _vec_tile(inp["L0_ffn_gc_b1"])
    g["gc_b2"] = _vec_tile(inp["L0_ffn_gc_b2"])
    g["gc_b3"] = np.asarray(inp["L0_ffn_gc_b3"], dtype=f32).reshape(1, 1)
    g["arange"] = np.asarray(inp["L0_ffn_adaptive_range"], dtype=f32).reshape(1, 1)
    g["l1wq"] = np.ascontiguousarray(np.asarray(inp["L1_attn_wq"]).T[:, pi], dtype=f32)
    g["l1wk"] = np.ascontiguousarray(np.asarray(inp["L1_attn_wk"]).T[:, pi], dtype=f32)
    g["l1wv"] = np.ascontiguousarray(np.asarray(inp["L1_attn_wv"]).T, dtype=f32)
    g["l1wo"] = np.ascontiguousarray(np.asarray(inp["L1_attn_wo"]).T, dtype=f32)
    g["l1bq"] = _vec_tile(np.asarray(inp["L1_attn_bq"])[pi])
    g["l1bk"] = _vec_tile(np.asarray(inp["L1_attn_bk"])[pi])
    g["l1bv"] = np.asarray(inp["L1_attn_bv"], dtype=f32).reshape(1, D)
    g["l1bo"] = _vec_tile(inp["L1_attn_bo"])
    g["wfcT"] = np.ascontiguousarray(np.asarray(inp["L1_ffn_W_fc"]).T, dtype=f32)
    g["wprT"] = np.ascontiguousarray(np.asarray(inp["L1_ffn_W_proj"]).T, dtype=f32)
    g["lnw"] = _vec_tile(inp["L1_ffn_ln_w"])
    g["lnb"] = _vec_tile(inp["L1_ffn_ln_b"])
    x = np.asarray(inp["L0_attn_x"], dtype=f32)
    cores = []
    for i in range(NCORES):
        m = dict(g)
        m["xT"] = np.ascontiguousarray(x[i].T)
        cores.append(m)
    return cores


def make_consts():
    c = {}
    c["ident"] = np.eye(128, dtype=ml_dtypes.bfloat16)
    ps = np.zeros((128, 128), dtype=np.float64)
    for h in range(2):
        for i in range(32):
            e = h * 64 + i
            o = h * 64 + 32 + i
            ps[o, e] = -1.0
            ps[e, o] = 1.0
    c["pswap"] = ps.astype(ml_dtypes.bfloat16)
    m = np.where(np.arange(128)[None, :] > np.arange(128)[:, None], -1e30, 0.0)
    c["maskdiag"] = m.astype(np.float32)
    c["ones_col"] = np.ones((128, 1), dtype=ml_dtypes.bfloat16)
    c["ones_row"] = np.ones((1, 128), dtype=ml_dtypes.bfloat16)
    c["ones_row_f32"] = np.ones((1, 128), dtype=np.float32)
    c["ones_col_f32"] = np.ones((128, 1), dtype=np.float32)
    i32 = np.arange(32, dtype=np.float64)
    rate = np.exp(-2.0 * i32 / HD * math.log(10000.0))
    pos = np.arange(SEQ, dtype=np.float64)
    th = rate[:, None] * pos[None, :]
    c["ropecos"] = np.tile(np.cos(th), (4, 1)).astype(np.float32)
    c["ropesin"] = np.tile(np.sin(th), (4, 1)).astype(np.float32)
    hs = np.zeros((128, NCH, HEADS), dtype=ml_dtypes.bfloat16)
    for ch in range(NCH):
        for p in range(128):
            hs[p, ch, (ch * 128 + p) // HD] = 1
    c["headsel"] = hs
    return c


GROUPS = [
    ["lnx_a", "lnx_b", "lnx_half", "cosb", "sinb", "outv", "diag", "xres",
     "xnew", "h1b", "h2b", "h3b", "gc2t"],
    ["P", "PT", "xb", "fb16", "sq128", "lnx_c", "gc1t"],
    ["x", "wfcq", "wslotA", "wslotB", "xgrp", "wo_t"],
    ["fout", "q", "k", "wprq"],
    ["v_tm", "cn1", "cn2", "cn3", "cn4", "h", "xtile", "ytile"],
    ["pQ0"], ["pQ1"], ["pQ2"], ["pQ3"],
    ["tmpa"], ["tmpb"], ["wstage0"], ["wstage1"],
]
GROUP_OF = {}
for _g in GROUPS:
    for _n in _g:
        GROUP_OF[_n] = _g


class Builder:
    def __init__(self, phases=None):
        self.phases = list(phases if phases is not None else ALL_PHASES)
        nc = bass.Bass("TRN2", target_bir_lowering=False)
        self.nc = nc
        self.p = Prog(nc)
        self.bufs = {}
        self.xloc = "x"
        self.wsvc = []
        self._declare_io()
        self._alloc_sbuf()
        self._alloc_psum()

    def B(self, name):
        if name not in self.bufs:
            self.bufs[name] = Buf(name)
        return self.bufs[name]

    def take(self, name):
        b = Buf(name)
        for n in GROUP_OF.get(name, [name]):
            o = self.bufs.get(n)
            if o is not None:
                b.w = b.w + o.w
                b.r = b.r + o.r
        self.bufs[name] = b
        return b

    def _declare_io(self):
        nc = self.nc
        self.d = {}
        d = self.d

        def T(n, s):
            d[n] = nc.dram_tensor(n, s, F32, kind="ExternalInput")
        T("xT", [D, SEQ])
        d["y"] = nc.dram_tensor("y", [D, SEQ], F32, kind="ExternalOutput")
        for n, s in [("l0wq", [D, D]), ("l0wk", [D, D]), ("l0wv", [D, D]),
                     ("l0wo", [D, D]), ("cn_w1", [D, D]), ("cn_w2", [D, 512]),
                     ("cn_w3", [512, 256]), ("cn_w4", [256, 1]),
                     ("gc_w1", [D, 512]), ("gc_w2", [512, 256]),
                     ("gc_w3", [256, 1]), ("cn_b1", [128, 8]),
                     ("cn_b2", [128, 4]), ("cn_b3", [128, 2]), ("cn_b4", [1, 1]),
                     ("gc_b1", [128, 4]), ("gc_b2", [128, 2]), ("gc_b3", [1, 1]),
                     ("arange", [1, 1]),
                     ("l1wq", [D, D]), ("l1wk", [D, D]), ("l1wv", [D, D]),
                     ("l1wo", [D, D]), ("l1bq", [128, 8]), ("l1bk", [128, 8]),
                     ("l1bv", [1, D]), ("l1bo", [128, 8]),
                     ("wfcT", [D, 4096]), ("wprT", [4096, D]),
                     ("lnw", [128, 32]), ("lnb", [128, 32])]:
            T(n, s)
        I = lambda n, s, dt: nc.dram_tensor(n, s, dt, kind="Internal")
        d["xs_a"] = I("xs_a", [D, SEQ], F32)
        d["xs_b"] = I("xs_b", [D, SEQ], F32)
        d["lnxs"] = I("lnxs", [D, SEQ], BF16)
        d["wfcq_d"] = I("wfcq_d", [D, 4096], BF16)
        d["wprq_d"] = I("wprq_d", [4096, D], BF16)
        for k, v in make_consts().items():
            d[k] = nc.inline_tensor(v, k)

    def _alloc_sbuf(self):
        nc = self.nc
        self.off = (int(nc.sbuf_base) + 31) // 32 * 32
        self.sb = {}
        self.sboff = {}

        def at(name, shape, dtype, off=None):
            nbytes = int(np.prod(shape[1:])) * (2 if dtype == BF16 else 4)
            if off is None:
                off = self.off
                self.off = off + ((nbytes + 31) // 32) * 32
            h = nc.alloc_sbuf_tensor_at(name, list(shape), dtype, offset=off)
            self.sb[name] = h
            self.sboff[name] = off
            return h

        at("ident", [128, 128], BF16)
        at("pswap", [128, 128], BF16)
        at("maskdiag", [128, 128], F32)
        at("ones_col", [128, 1], BF16)
        at("ones_row", [1, 128], BF16)
        at("ones_row_f32", [1, 128], F32)
        at("headsel", [128, NCH, HEADS], BF16)
        at("bias_cn1", [128, 8], F32)
        at("bias_cn2", [128, 4], F32)
        at("bias_cn3", [128, 2], F32)
        at("bias_gc1", [128, 4], F32)
        at("bias_gc2", [128, 2], F32)
        at("bias_l1q", [128, 8], F32)
        at("bias_l1k", [128, 8], F32)
        at("bias_l1o", [128, 8], F32)
        at("lnw", [128, 32], F32)
        at("lnb", [128, 32], F32)
        at("bvb", [128, D], BF16)
        at("scal", [128, 16], F32)
        at("srow", [1, 1024], F32)
        at("grow", [1, 64], F32)
        at("stats", [128, 8], F32)
        at("rinv", [128, 1], F32)
        at("negB", [128, 24], F32)
        at("invb", [128, 512], BF16)
        at("m2b", [128, 512], BF16)
        at("tmpa", [128, 512], F32)
        at("tmpb", [128, 512], F32)
        at("gc3t", [128, 2, 1], BF16)
        at("ones_col_f32", [128, 1], F32)
        at("gacc", [128, 64], F32)
        at("lnx_a", [128, NCH, GRP], BF16)
        at("lnx_b", [128, NCH, GRP], BF16)
        r1 = self.sboff["lnx_a"]
        at("lnx_half", [128, NCH, 2 * GRP], BF16, off=r1)
        at("cosb", [128, SEQ], BF16, off=r1)
        at("sinb", [128, SEQ], BF16, off=r1 + 8192)
        at("outv", [128, NCH, QT], BF16, off=r1)
        at("diag", [128, 128], BF16, off=r1 + 2048)
        at("xres", [128, NCH, QT], F32, off=r1 + 2432)
        at("xnew", [128, NCH, QT], F32, off=r1 + 2432 + 4096)
        at("h2b", [128, 4, FT1], BF16, off=r1)
        at("h3b", [128, 2, FT1], BF16, off=r1 + 4096)
        at("h1b", [128, NCH, FT1], BF16, off=r1 + 8192)
        at("P", [128, SEQ], BF16)
        at("PT", [128, SEQ], BF16)
        pP = self.sboff["P"]
        at("xb", [128, NCH, GRP], BF16, off=pP)
        at("fb16", [128, NCH, FT1], BF16, off=pP)
        at("sq128", [128, SEQ], BF16, off=pP)
        at("lnx_c", [128, NCH, FT2], BF16, off=pP)
        at("gc1t", [128, NCH, 512], BF16, off=pP)
        at("gc2t", [128, 4, 256], BF16, off=self.sboff["lnx_a"])

        r2 = self.off
        at("x", [128, NCH, SEQ], F32)
        at("wfcq", [128, NCH, 4096], BF16, off=r2)
        at("wslotA", [128, NCH, D], BF16, off=r2)
        at("wslotB", [128, NCH, D], BF16, off=r2 + 16 * 1024)
        at("xgrp", [128, NCH, GRP], F32, off=r2 + 32 * 1024)
        at("wo_t", [128, NCH, D], BF16, off=r2 + 48 * 1024)
        r3 = self.off
        at("fout", [128, NCH, SEQ], F32)
        at("q", [128, NCH, SEQ], BF16, off=r3)
        at("k", [128, NCH, SEQ], BF16, off=r3 + 32 * 1024)
        at("wprq", [128, 32, D], BF16, off=r3)
        r4 = self.off
        at("v_tm", [128, NQT, D], BF16)
        at("cn1", [128, NCH, D], BF16, off=r4)
        at("cn2", [128, NCH, 512], BF16, off=r4 + 16 * 1024)
        at("cn3", [128, 4, 256], BF16, off=r4 + 24 * 1024)
        at("cn4", [128, 2, 1], BF16, off=r4 + 26 * 1024)
        at("h", [128, 32, FT2], BF16, off=r4)
        at("xtile", [128, NCH, FT2], F32, off=r4 + 16 * 1024)
        at("ytile", [128, NCH, FT2], F32, off=r4 + 24 * 1024)
        at("wstage", [128, 2, 1024], F32)

        limit = int(nc.sbuf_top)
        assert self.off <= limit, f"SBUF over budget: {self.off} > {limit}"

    def _alloc_psum(self):
        nc = self.nc
        self.pbig = nc.alloc_psum_tensor("pbig", [128, 2048], F32)
        self.pb = {i: nc.alloc_psum_tensor(f"pb{i}", [128, 512], F32)
                   for i in (4, 5, 6, 7)}

    def pQ(self, i):
        return self.B(f"pQ{i}")

    def takeQ(self, i):
        return self.take(f"pQ{i}")

    def pump(self, n):
        for _ in range(min(n, len(self.wsvc))):
            self.wsvc.pop(0)()

    def pump_all(self):
        while self.wsvc:
            self.wsvc.pop(0)()

    def _wpieces(self, K, M, cap=1024):
        kch = K // 128
        if M >= cap:
            for c in range(kch):
                for m0 in range(0, M, cap):
                    yield (c, c + 1, m0, min(M, m0 + cap))
        else:
            per = max(1, cap // M)
            for c0 in range(0, kch, per):
                yield (c0, min(kch, c0 + per), 0, M)

    def load_weight_cast(self, src, dst, K, M, ceng="S", tag=""):
        p, d, sb = self.p, self.d, self.sb
        dstB = self.B(dst)
        dap = sb[dst].ap()
        sap = d[src].rearrange("(c p) m -> p c m", p=128)
        for i, (c0, c1, m0, m1) in enumerate(self._wpieces(K, M)):
            n = (c1 - c0) * (m1 - m0)
            half = i % 2
            stB = self.take(f"wstage{half}")
            st3 = sb["wstage"].ap()[:, half, 0:n].rearrange(
                "p (a b) -> p a b", b=m1 - m0)
            p.op("Q", lambda e, a=st3, s=sap[:, c0:c1, m0:m1]: e.dma_start(
                a, s, allow_slow_non_contiguous=True),
                writes=[stB], dma=True, tag=f"ldw_{src}_{i}{tag}")
            cp = (lambda e, o, a: e.copy(o, a)) if ceng == "S" else \
                 (lambda e, o, a: e.tensor_copy(o, a))
            p.op(ceng, lambda e, o=dap[:, c0:c1, m0:m1], a=st3, cp=cp: cp(e, o, a),
                 reads=[stB], writes=[dstB], tag=f"cast_{src}_{i}{tag}")

    def emit_ln(self, src_ap, src_buf, ntok, out_ap, out_buf, tag):
        p, sb = self.p, self.sb
        n = ntok
        xb = sb["xb"].ap()[:, :, 0:n]
        xbB = self.take("xb")
        p.op("V", lambda e, o=xb, s=src_ap: e.tensor_copy(o, s), reads=[src_buf],
             writes=[xbB], tag=f"{tag}cast")
        pm = self.pb[5].ap()
        pmB = self.take("pb5")
        oc = sb["ones_col"].ap()
        ocB = self.B("c_ones_col")
        for c in range(NCH):
            p.op("T", lambda e, c=c, n=n: e.matmul(pm[0:1, 0:n], oc, xb[:, c, :],
                                                   start=(c == 0), stop=(c == NCH - 1)),
                 reads=[xbB, ocB], writes=[pmB], tag=f"{tag}mmu")
        pv = self.pb[6].ap()
        pvB = self.take("pb6")
        for c in range(NCH):
            tnm = "tmpa" if c % 2 == 0 else "tmpb"
            sq = self.sb[tnm].ap().bitcast(BF16)[:, 0:n]
            sqB = self.take(tnm)
            p.op("S", lambda e, sq=sq, c=c: e.activation(sq, xb[:, c, :], AF.Square),
                 reads=[xbB], writes=[sqB], tag=f"{tag}sq{c}")
            p.op("T", lambda e, sq=sq, c=c, n=n: e.matmul(
                pv[0:1, 0:n], oc, sq, start=(c == 0), stop=(c == NCH - 1)),
                reads=[sqB, ocB], writes=[pvB], tag=f"{tag}mms")
        sr = sb["srow"].ap()
        mu = sr[:, 0:n]
        var = sr[:, 512:512 + n]
        srB = self.take("srow")
        p.op("V", lambda e, mu=mu, n=n: e.tensor_scalar(
            mu, pm[0:1, 0:n], 1.0 / D, None, ALU.mult),
            reads=[pmB], writes=[srB], tag=f"{tag}mu")
        p.op("V", lambda e, var=var, n=n: e.tensor_scalar(
            var, pv[0:1, 0:n], 1.0 / D, None, ALU.mult),
            reads=[pvB], writes=[srB], tag=f"{tag}ex2")
        ta = sb["tmpa"].ap()[0:1, 0:n]
        taB = self.take("tmpa")
        p.op("V", lambda e, ta=ta, mu=mu: e.tensor_tensor(ta, mu, mu, ALU.mult),
             reads=[srB], writes=[taB], tag=f"{tag}musq")
        p.op("V", lambda e, var=var, ta=ta: e.tensor_tensor(var, var, ta,
                                                            ALU.subtract),
             reads=[srB, taB], writes=[srB], tag=f"{tag}var")
        eps = sb["scal"].ap()[0:1, 0:1]
        p.op("S", lambda e, var=var: e.activation(var, var, AF.Ln, bias=eps),
             reads=[srB, self.B("scal")], writes=[srB], tag=f"{tag}lnv")
        p.op("S", lambda e, var=var: e.activation(var, var, AF.Exp, scale=-0.5),
             reads=[srB], writes=[srB], tag=f"{tag}inv")
        p.op("V", lambda e, mu=mu, var=var: e.tensor_tensor(mu, mu, var, ALU.mult),
             reads=[srB], writes=[srB], tag=f"{tag}m2")
        orf = sb["ones_row_f32"].ap()
        orfB = self.B("c_ones_row_f32")
        pb4 = self.pb[4].ap()
        for rowap, dst in ((var, "invb"), (mu, "m2b")):
            pb4B = self.take("pb4")
            p.op("T", lambda e, r=rowap, n=n: e.matmul(pb4[:, 0:n], orf, r,
                                                       start=True, stop=True),
                 reads=[srB, orfB], writes=[pb4B], tag=f"{tag}bc_{dst}")
            dB = self.take(dst)
            p.op("V", lambda e, o=sb[dst].ap()[:, 0:n], n=n: e.tensor_copy(
                o, pb4[:, 0:n]),
                reads=[pb4B], writes=[dB], tag=f"{tag}bcc_{dst}")
        invb = sb["invb"].ap()[:, 0:n]
        m2bv = sb["m2b"].ap()[:, 0:n]
        tb16 = sb["tmpa"].ap().bitcast(BF16)[:, 0:n]
        for c in range(NCH):
            tB = self.take("tmpa")
            p.op("V", lambda e, c=c, t=tb16, s=src_ap[:, c, 0:n], i=invb:
                 e.tensor_tensor(t, s, i, ALU.mult),
                 reads=[src_buf, self.B("invb")], writes=[tB], tag=f"{tag}ap1_{c}")
            p.op("V", lambda e, c=c, o=out_ap[:, c, 0:n], t=tb16, m=m2bv:
                 e.tensor_tensor(o, t, m, ALU.subtract),
                 reads=[tB, self.B("m2b")], writes=[out_buf], tag=f"{tag}ap2_{c}")

    def phase_init(self):
        p, d, sb = self.p, self.d, self.sb
        loads = [("ident", "ident"), ("pswap", "pswap"), ("maskdiag", "maskdiag"),
                 ("headsel", "headsel"),
                 ("cn_b1", "bias_cn1"), ("cn_b2", "bias_cn2"),
                 ("cn_b3", "bias_cn3"), ("gc_b1", "bias_gc1"),
                 ("gc_b2", "bias_gc2"), ("l1bq", "bias_l1q"),
                 ("l1bk", "bias_l1k"), ("l1bo", "bias_l1o"),
                 ("lnw", "lnw"), ("lnb", "lnb")]
        for src, t in loads:
            p.op("Q", lambda e, s=src, t=t: e.dma_start(sb[t].ap(), d[s].ap()),
                 writes=[self.B(t)], dma=True, tag=f"ld_{t}")
        for src, t, bn in (("ones_col", "ones_col", "c_ones_col"),
                           ("ones_row", "ones_row", "c_ones_row"),
                           ("ones_row_f32", "ones_row_f32", "c_ones_row_f32"),
                           ("ones_col_f32", "ones_col_f32", "c_ones_col_f32")):
            p.op("Q", lambda e, s=src, t=t: e.dma_start(sb[t].ap(), d[s].ap()),
                 writes=[self.B(bn)], dma=True, tag=f"ld_{t}")
        p.op("Q", lambda e: e.dma_start(
            sb["x"].ap(), d["xT"].rearrange("(c p) t -> p c t", p=128)),
            writes=[self.take("x")], dma=True, tag="ld_x")
        p.op("V", lambda e: e.memset(sb["scal"].ap()[:, 0:1], 1e-5),
             writes=[self.B("scal")], tag="eps")
        for i, src in ((0, "arange"), (1, "cn_b4"), (2, "gc_b3")):
            p.op("Q", lambda e, i=i, s=src: e.dma_start(
                sb["grow"].ap()[0:1, i:i + 1], d[s].ap()),
                writes=[self.B("grow")], dma=True, tag=f"ld_{src}")
        self.load_cn()
        self.enqueue_quant()

    def load_cn(self):
        for nm in ("cn1", "cn2", "cn3", "cn4"):
            self.take(nm)
        self.load_weight_cast("cn_w1", "cn1", D, D)
        self.load_weight_cast("cn_w2", "cn2", D, 512)
        self.load_weight_cast("cn_w3", "cn3", 512, 256)
        self.load_weight_cast("cn_w4", "cn4", 256, 1)

    def load_gc(self):
        for nm in ("gc1t", "gc2t", "gc3t"):
            self.take(nm)
        self.load_weight_cast("gc_w1", "gc1t", D, 512)
        self.load_weight_cast("gc_w2", "gc2t", 512, 256)
        self.load_weight_cast("gc_w3", "gc3t", 256, 1)

    def enqueue_quant(self):
        p, d, sb = self.p, self.d, self.sb
        pb7 = self.pb[7].ap()
        tA = sb["tmpa"].ap().bitcast(BF16)
        tB_ = sb["tmpb"].ap().bitcast(BF16)

        def g1piece(src, gcol, idx, c, m0):
            def go():
                half = idx % 2
                stB = self.take(f"wstage{half}")
                ap_st = sb["wstage"].ap()[:, half, :]
                st3 = sb["wstage"].ap()[:, half, :].rearrange(
                    "p (a b) -> p a b", b=128)
                sap = d[src].rearrange("(c p) m -> p c m", p=128)
                p.op("Q", lambda e, a=ap_st, s=sap[:, c:c + 1, m0:m0 + 1024]:
                     e.dma_start(a, s.rearrange("p c m -> p (c m)")),
                     writes=[stB], dma=True, tag=f"g1_{src}_{idx}")
                t8 = sb["tmpa"].ap()[:, 0:8]
                aB = self.take("tmpa")
                p.op("V", lambda e, s3=st3, t8=t8: e.reduce_sum(
                    t8, s3, AX.X, apply_absolute_value=True),
                    reads=[stB], writes=[aB], tag=f"g1r1_{src}_{idx}")
                gaB = self.B("gacc")
                p.op("V", lambda e, t8=t8, gcol=gcol: e.reduce_sum(
                    sb["gacc"].ap()[:, gcol:gcol + 1], t8, AX.X),
                    reads=[aB], writes=[gaB], tag=f"g1r2_{src}_{idx}")
            self.wsvc.append(go)

        for gbase, src, K, M in ((0, "wfcT", D, 4096), (32, "wprT", 4096, D)):
            pieces = list(self._wpieces(K, M))
            assert len(pieces) == 32
            for idx, (c0, c1, m0, m1) in enumerate(pieces):
                g1piece(src, gbase + idx, idx, c0, m0)

        def finalize():
            gr = sb["grow"].ap()
            grB = self.B("grow")
            gaB = self.B("gacc")
            ga3 = sb["gacc"].ap().rearrange("p (a b) -> p a b", b=8)
            for gbase, col, pcol in ((0, 8, 0), (32, 16, 1)):
                taB = self.take("tmpa")
                p.op("V", lambda e, gbase=gbase: e.reduce_sum(
                    sb["tmpa"].ap()[:, 0:4], ga3[:, gbase // 8:gbase // 8 + 4, :],
                    AX.X),
                    reads=[gaB], writes=[taB], tag=f"gf1_{col}")
                p.op("V", lambda e: e.reduce_sum(
                    sb["tmpa"].ap()[:, 8:9], sb["tmpa"].ap()[:, 0:4], AX.X),
                    reads=[taB], writes=[taB], tag=f"gf2_{col}")
                pb7B = self.take("pb7")
                p.op("T", lambda e, pcol=pcol: e.matmul(
                    pb7[0:1, pcol:pcol + 1], sb["tmpa"].ap()[:, 8:9],
                    sb["ones_col_f32"].ap(), start=True, stop=True),
                    reads=[taB, self.B("c_ones_col_f32")], writes=[pb7B],
                    tag=f"gf3_{col}")
                p.op("V", lambda e, col=col, pcol=pcol: e.tensor_scalar(
                    gr[:, col + 1:col + 2], pb7[0:1, pcol:pcol + 1],
                    1.0 / (4096.0 * 1024.0), 1e-6, ALU.mult, ALU.add),
                    reads=[pb7B], writes=[grB], tag=f"gam{col}")
                p.op("V", lambda e, col=col: e.tensor_scalar(
                    gr[:, col + 2:col + 3], gr[:, col + 1:col + 2],
                    0.5, None, ALU.mult),
                    reads=[grB], writes=[grB], tag=f"gthr{col}")
            pb4 = self.pb[4].ap()
            for src_col, dst in ((10, 1), (18, 3)):
                pb4B = self.take("pb4")
                p.op("T", lambda e, sc_=src_col: e.matmul(
                    pb4[:, 0:1], sb["ones_row_f32"].ap(), gr[:, sc_:sc_ + 1],
                    start=True, stop=True),
                    reads=[grB, self.B("c_ones_row_f32")], writes=[pb4B],
                    tag=f"gbc{src_col}")
                sclB = self.B("scal")
                p.op("V", lambda e, dst=dst: e.tensor_copy(
                    sb["scal"].ap()[:, dst:dst + 1], pb4[:, 0:1]),
                    reads=[pb4B], writes=[sclB], tag=f"gbcc{dst}")
                p.op("V", lambda e, dst=dst: e.tensor_scalar(
                    sb["scal"].ap()[:, dst + 1:dst + 2],
                    sb["scal"].ap()[:, dst:dst + 1], -1.0, None, ALU.mult),
                    reads=[sclB], writes=[sclB], tag=f"gneg{dst}")
        self.wsvc.append(finalize)

        def qpiece(src, dst, thr_col, idx, c, m0):
            def go():
                half = idx % 2
                stB = self.take(f"wstage{half}")
                ap_st = sb["wstage"].ap()[:, half, :]
                sap = d[src].rearrange("(c p) m -> p c m", p=128)
                dap = d[dst].rearrange("(c p) m -> p c m", p=128)
                p.op("Q", lambda e, a=ap_st, s=sap[:, c:c + 1, m0:m0 + 1024]:
                     e.dma_start(a, s.rearrange("p c m -> p (c m)")),
                     writes=[stB], dma=True, tag=f"q2_{src}_{idx}")
                thr = sb["scal"].ap()[:, thr_col:thr_col + 1]
                nthr = sb["scal"].ap()[:, thr_col + 1:thr_col + 2]
                aB = self.take("tmpa")
                p.op("V", lambda e, a=ap_st, nthr=nthr: e.tensor_scalar(
                    tA, a, nthr, None, ALU.is_lt),
                    reads=[stB, self.B("scal")], writes=[aB],
                    tag=f"q2lt_{src}_{idx}")
                bB = self.take("tmpb")
                p.op("V", lambda e, a=ap_st, thr=thr: e.scalar_tensor_tensor(
                    tB_, a, thr, tA, ALU.is_gt, ALU.subtract),
                    reads=[stB, aB, self.B("scal")], writes=[bB],
                    tag=f"q2q_{src}_{idx}")
                p.op("G", lambda e, o=dap[:, c:c + 1, m0:m0 + 1024]: e.dma_start(
                    o.rearrange("p c m -> p (c m)"), tB_),
                    reads=[bB], writes=[self.B(dst + "_dram")], dma=True,
                    tag=f"q2st_{src}_{idx}")
            self.wsvc.append(go)

        for src, dst, thr_col, K, M in (("wfcT", "wfcq_d", 1, D, 4096),
                                        ("wprT", "wprq_d", 3, 4096, D)):
            for idx, (c0, c1, m0, m1) in enumerate(self._wpieces(K, M)):
                qpiece(src, dst, thr_col, idx, c0, m0)

    def phase_ln1(self):
        sb = self.sb
        self.take("fout")
        for g in range(NGRP):
            self.emit_ln(sb["x"].ap()[:, :, g * GRP:(g + 1) * GRP], self.B("x"),
                         GRP, sb["fout"].ap()[:, :, g * GRP:(g + 1) * GRP],
                         self.B("fout"), f"ln1g{g}")
            self.pump(10)

    def phase_ln3(self):
        p, d, sb = self.p, self.d, self.sb
        xB = self.take("x")
        p.op("Q", lambda e: e.dma_start(
            sb["x"].ap(), d["xs_b"].rearrange("(c p) t -> p c t", p=128)),
            reads=[self.B("xs_b")], writes=[xB], dma=True, tag="ld_x3")
        self.xloc = "x"
        self.load_cn()
        self.take("fout")
        for g in range(NGRP):
            self.emit_ln(sb["x"].ap()[:, :, g * GRP:(g + 1) * GRP], self.B("x"),
                         GRP, sb["fout"].ap()[:, :, g * GRP:(g + 1) * GRP],
                         self.B("fout"), f"ln3g{g}")
            self.pump(6)

    def phase_ffn0a(self):
        self.emit_ffn0("a", ncyc=_FFN0_NCYC)

    def phase_ffn0b(self):
        self.emit_ffn0("b")

    def emit_ffn0(self, call, ncyc=3, ntile=None):
        p, sb = self.p, self.sb
        self.load_gc()
        foB = self.B("fout")
        gr = sb["grow"].ap()
        grB = self.B("grow")
        gcx = sb["scal"].ap()[:, 8:16]
        sclB = self.B("scal")
        p.op("V", lambda e: e.reduce_sum(gcx, sb["fout"].ap(), AX.X),
             reads=[foB], writes=[sclB], tag=f"f0{call}_gcx")
        gxb = sb["tmpa"].ap().bitcast(BF16)[:, 0:8]
        gxbB = self.take("tmpa")
        p.op("V", lambda e: e.tensor_scalar(gxb, gcx, 1.0 / SEQ, None, ALU.mult),
             reads=[sclB], writes=[gxbB], tag=f"f0{call}_gcxb")
        pb6 = self.pb[6].ap()
        pb6B = self.take("pb6")
        for m in range(4):
            for c in range(NCH):
                p.op("T", lambda e, m=m, c=c: e.matmul(
                    pb6[:, m:m + 1], sb["gc1t"].ap()[:, c, m * 128:(m + 1) * 128],
                    gxb[:, c:c + 1], start=(c == 0), stop=(c == NCH - 1)),
                    reads=[gxbB, self.B("gc1t")], writes=[pb6B],
                    tag=f"f0{call}_g1mm")
        gh1 = sb["tmpb"].ap().bitcast(BF16)[:, 0:4]
        gh1B = self.take("tmpb")
        for m in range(4):
            p.op("S", lambda e, m=m: e.activation(
                gh1[:, m:m + 1], pb6[:, m:m + 1], AF.Gelu,
                bias=sb["bias_gc1"].ap()[:, m:m + 1]),
                reads=[pb6B, self.B("bias_gc1")], writes=[gh1B],
                tag=f"f0{call}_g1a")
        pb6B = self.take("pb6")
        for m in range(2):
            for c in range(4):
                p.op("T", lambda e, m=m, c=c: e.matmul(
                    pb6[:, 8 + m:9 + m],
                    sb["gc2t"].ap()[:, c, m * 128:(m + 1) * 128],
                    gh1[:, c:c + 1], start=(c == 0), stop=(c == 3)),
                    reads=[gh1B, self.B("gc2t")], writes=[pb6B],
                    tag=f"f0{call}_g2mm")
        gh2 = sb["tmpb"].ap().bitcast(BF16)[:, 8:10]
        gh2B = self.take("tmpb")
        for m in range(2):
            p.op("S", lambda e, m=m: e.activation(
                gh2[:, m:m + 1], pb6[:, 8 + m:9 + m], AF.Gelu,
                bias=sb["bias_gc2"].ap()[:, m:m + 1]),
                reads=[pb6B, self.B("bias_gc2")], writes=[gh2B],
                tag=f"f0{call}_g2a")
        pb6B = self.take("pb6")
        for c in range(2):
            p.op("T", lambda e, c=c: e.matmul(
                pb6[0:1, 16:17], sb["gc3t"].ap()[:, c, 0:1], gh2[:, c:c + 1],
                start=(c == 0), stop=(c == 1)),
                reads=[gh2B, self.B("gc3t")], writes=[pb6B], tag=f"f0{call}_g3mm")
        p.op("V", lambda e: e.tensor_scalar(gr[:, 3:4], gr[:, 2:3], 0.5, None,
                                            ALU.mult),
             reads=[grB], writes=[grB], tag=f"f0{call}_hb3")
        p.op("V", lambda e: e.tensor_scalar(gr[:, 4:5], gr[:, 1:2], 0.5, None,
                                            ALU.mult),
             reads=[grB], writes=[grB], tag=f"f0{call}_hb4")
        p.op("S", lambda e: e.activation(gr[:, 5:6], pb6[0:1, 16:17], AF.Tanh,
                                         bias=gr[:, 3:4], scale=0.5),
             reads=[pb6B, grB], writes=[grB], tag=f"f0{call}_gtanh")
        p.op("V", lambda e: e.tensor_scalar(gr[:, 6:7], gr[:, 5:6], 0.5, 0.5,
                                            ALU.mult, ALU.add),
             reads=[grB], writes=[grB], tag=f"f0{call}_gf01")
        p.op("V", lambda e: e.tensor_scalar(gr[:, 7:8], gr[:, 6:7], gr[:, 0:1],
                                            1.4, ALU.mult, ALU.mult),
             reads=[grB], writes=[grB], tag=f"f0{call}_u1")
        p.op("V", lambda e: e.tensor_scalar(gr[:, 26:27], gr[:, 0:1], -0.7, 0.1,
                                            ALU.mult, ALU.add),
             reads=[grB], writes=[grB], tag=f"f0{call}_u2")
        p.op("V", lambda e: e.tensor_tensor(gr[:, 24:25], gr[:, 7:8],
                                            gr[:, 26:27], ALU.add),
             reads=[grB], writes=[grB], tag=f"f0{call}_A")
        p.op("V", lambda e: e.tensor_scalar(gr[:, 25:26], gr[:, 0:1], 0.3, None,
                                            ALU.mult),
             reads=[grB], writes=[grB], tag=f"f0{call}_C")
        if ncyc == 0:
            return
        pbig = self.pbig.ap()
        signs = [(ALU.add, ALU.subtract), (ALU.add, ALU.subtract),
                 (ALU.subtract, ALU.add), (ALU.subtract, ALU.add)]
        fb = sb["fb16"].ap()
        h1 = sb["h1b"].ap()
        h2 = sb["h2b"].ap()
        h3 = sb["h3b"].ap()
        for cyc in range(ncyc):
            for t in range(ntile or (SEQ // FT1)):
                ts_ = slice(t * FT1, (t + 1) * FT1)
                fbB = self.take("fb16")
                p.op("S", lambda e, ts_=ts_: e.activation(
                    fb, sb["fout"].ap()[:, :, ts_], AF.Copy),
                    reads=[foB], writes=[fbB], tag=f"f0{call}cast{cyc}_{t}")
                h1B = self.take("h1b")
                for m in range(NCH):
                    qr = m % 2
                    pqB = self.takeQ(qr)
                    for c in range(NCH):
                        p.op("T", lambda e, m=m, c=c, qr=qr: e.matmul(
                            pbig[:, qr * 512:qr * 512 + 512],
                            sb["cn1"].ap()[:, c, m * 128:(m + 1) * 128],
                            fb[:, c, :], start=(c == 0), stop=(c == NCH - 1)),
                            reads=[fbB, self.B("cn1")], writes=[pqB],
                            tag=f"f0{call}h1mm{cyc}_{t}_{m}")
                    p.op("S", lambda e, m=m, qr=qr: e.activation(
                        h1[:, m, :], pbig[:, qr * 512:qr * 512 + 512], AF.Gelu,
                        bias=sb["bias_cn1"].ap()[:, m:m + 1]),
                        reads=[pqB, self.B("bias_cn1")], writes=[h1B],
                        tag=f"f0{call}h1a{cyc}_{t}_{m}")
                h2B = self.take("h2b")
                for m in range(4):
                    pqB = self.takeQ(2)
                    for c in range(NCH):
                        p.op("T", lambda e, m=m, c=c: e.matmul(
                            pbig[:, 1024:1536],
                            sb["cn2"].ap()[:, c, m * 128:(m + 1) * 128],
                            h1[:, c, :], start=(c == 0), stop=(c == NCH - 1)),
                            reads=[h1B, self.B("cn2")], writes=[pqB],
                            tag=f"f0{call}h2mm{cyc}_{t}_{m}")
                    p.op("S", lambda e, m=m: e.activation(
                        h2[:, m, :], pbig[:, 1024:1536], AF.Gelu,
                        bias=sb["bias_cn2"].ap()[:, m:m + 1]),
                        reads=[pqB, self.B("bias_cn2")], writes=[h2B],
                        tag=f"f0{call}h2a{cyc}_{t}_{m}")
                h3B = self.take("h3b")
                for m in range(2):
                    pqB = self.takeQ(3)
                    for c in range(4):
                        p.op("T", lambda e, m=m, c=c: e.matmul(
                            pbig[:, 1536:2048],
                            sb["cn3"].ap()[:, c, m * 128:(m + 1) * 128],
                            h2[:, c, :], start=(c == 0), stop=(c == 3)),
                            reads=[h2B, self.B("cn3")], writes=[pqB],
                            tag=f"f0{call}h3mm{cyc}_{t}_{m}")
                    p.op("S", lambda e, m=m: e.activation(
                        h3[:, m, :], pbig[:, 1536:2048], AF.Gelu,
                        bias=sb["bias_cn3"].ap()[:, m:m + 1]),
                        reads=[pqB, self.B("bias_cn3")], writes=[h3B],
                        tag=f"f0{call}h3a{cyc}_{t}_{m}")
                pb5 = self.pb[5].ap()
                pzB = self.take("pb5")
                for c in range(2):
                    p.op("T", lambda e, c=c: e.matmul(
                        pb5[0:1, 0:FT1], sb["cn4"].ap()[:, c, 0:1], h3[:, c, :],
                        start=(c == 0), stop=(c == 1)),
                        reads=[h3B, self.B("cn4")], writes=[pzB],
                        tag=f"f0{call}zmm{cyc}_{t}")
                srB = self.take("srow")
                tr = sb["srow"].ap()[:, 0:FT1]
                p.op("S", lambda e: e.activation(tr, pb5[0:1, 0:FT1], AF.Tanh,
                                                 bias=gr[:, 4:5], scale=0.5),
                     reads=[pzB, grB], writes=[srB], tag=f"f0{call}ctanh{cyc}_{t}")
                crow = sb["srow"].ap().bitcast(BF16)[0:1, 1024:1024 + FT1]
                p.op("V", lambda e: e.tensor_scalar(
                    crow, tr, gr[:, 25:26], gr[:, 24:25], ALU.mult, ALU.add),
                    reads=[srB, grB], writes=[srB], tag=f"f0{call}crow{cyc}_{t}")
                pb6 = self.pb[6].ap()
                pcB = self.take("pb6")
                p.op("T", lambda e: e.matmul(pb6[:, 0:FT1], sb["ones_row"].ap(),
                                             crow, start=True, stop=True),
                     reads=[srB, self.B("c_ones_row")], writes=[pcB],
                     tag=f"f0{call}cbc{cyc}_{t}")
                for pr in range(4):
                    sA, sB_ = signs[pr]
                    fA = sb["fout"].ap()[:, pr, ts_]
                    fB_ = sb["fout"].ap()[:, pr + 4, ts_]
                    m1B = self.take("tmpa")
                    p.op("V", lambda e, fB_=fB_: e.tensor_tensor(
                        sb["tmpa"].ap(), pb6[:, 0:FT1], fB_, ALU.mult),
                        reads=[pcB, foB], writes=[m1B],
                        tag=f"f0{call}m1_{cyc}_{t}_{pr}")
                    m2B = self.take("tmpb")
                    p.op("V", lambda e, fA=fA: e.tensor_tensor(
                        sb["tmpb"].ap(), pb6[:, 0:FT1], fA, ALU.mult),
                        reads=[pcB, foB], writes=[m2B],
                        tag=f"f0{call}m2_{cyc}_{t}_{pr}")
                    p.op("V", lambda e, fA=fA, sA=sA: e.tensor_tensor(
                        fA, fA, sb["tmpa"].ap(), sA),
                        reads=[m1B, foB], writes=[foB],
                        tag=f"f0{call}uA_{cyc}_{t}_{pr}")
                    p.op("V", lambda e, fB_=fB_, sB_=sB_: e.tensor_tensor(
                        fB_, fB_, sb["tmpb"].ap(), sB_),
                        reads=[m2B, foB], writes=[foB],
                        tag=f"f0{call}uB_{cyc}_{t}_{pr}")
                self.pump(4)
        xB = self.B("x")
        for t in range(SEQ // FT1):
            ts_ = slice(t * FT1, (t + 1) * FT1)
            p.op("V", lambda e, ts_=ts_: e.scalar_tensor_tensor(
                sb["x"].ap()[:, :, ts_], sb["fout"].ap()[:, :, ts_],
                0.5, sb["x"].ap()[:, :, ts_], ALU.mult, ALU.add),
                reads=[foB, xB], writes=[xB], tag=f"f0{call}res{t}")

    def phase_qkv0(self):
        self.emit_qkv(0)

    def phase_qkv1(self):
        self.emit_qkv(1)

    def emit_qkv(self, layer):
        p, d, sb = self.p, self.d, self.sb
        for g in range(NGRP):
            gs = slice(g * GRP, (g + 1) * GRP)
            if layer == 0:
                src_ap = sb["x"].ap()[:, :, gs]
                srcB = self.B("x")
            else:
                xgB = self.take("xgrp")
                p.op("Q", lambda e, gs=gs: e.dma_start(
                    sb["xgrp"].ap(),
                    d["xs_b"].rearrange("(c p) t -> p c t", p=128)[:, :, gs]),
                    reads=[self.B("xs_b")], writes=[xgB], dma=True,
                    tag=f"q{layer}ldx{g}")
                src_ap = sb["xgrp"].ap()
                srcB = xgB
            nm = "lnx_a" if g % 2 == 0 else "lnx_b"
            outB = self.take(nm)
            self.emit_ln(src_ap, srcB, GRP, sb[nm].ap(), outB, f"q{layer}ln{g}")
            p.op("Q", lambda e, gs=gs, nm=nm: e.dma_start(
                d["lnxs"].rearrange("(c p) t -> p c t", p=128)[:, :, gs],
                sb[nm].ap()),
                reads=[outB], writes=[self.B("lnxs")], dma=True,
                tag=f"q{layer}stln{g}")
            if layer == 0:
                p.op("Q", lambda e, gs=gs: e.dma_start(
                    d["xs_a"].rearrange("(c p) t -> p c t", p=128)[:, :, gs],
                    sb["x"].ap()[:, :, gs]),
                    reads=[self.B("x")], writes=[self.B("xs_a")], dma=True,
                    tag=f"q{layer}dumpx{g}")
            self.pump(8)
        self.xloc = "xs_a" if layer == 0 else "xs_b"
        wsrc = {0: {"q": "l0wq", "k": "l0wk", "v": "l0wv", "o": "l0wo"},
                1: {"q": "l1wq", "k": "l1wk", "v": "l1wv", "o": "l1wo"}}[layer]
        qB = self.take("q")
        kB = self.take("k")
        vB = self.take("v_tm")
        pbig = self.pbig.ap()
        for half in range(2):
            lhB = self.take("lnx_half")
            p.op("Q", lambda e, half=half: e.dma_start(
                sb["lnx_half"].ap(),
                d["lnxs"].rearrange("(c p) t -> p c t", p=128)
                [:, :, half * 1024:(half + 1) * 1024]),
                reads=[self.B("lnxs")], writes=[lhB], dma=True,
                tag=f"q{layer}ldln{half}")
            lh = sb["lnx_half"].ap()
            for wi, w in enumerate(("q", "k", "v")):
                slot = ("wslotA", "wslotB")[wi % 2]
                slotB = self.take(slot)
                self.load_weight_cast(wsrc[w], slot, D, D, ceng="S",
                                      tag=f"_{layer}{half}")
                wap = sb[slot].ap()
                if w in ("q", "k"):
                    dst = sb[w].ap()
                    dstB = qB if w == "q" else kB
                    for m in range(NCH):
                        for ng in range(2):
                            qr = (m * 2 + ng) % 2
                            pqB = self.takeQ(qr)
                            for c in range(NCH):
                                p.op("T", lambda e, m=m, ng=ng, c=c, qr=qr,
                                     wap=wap, lh=lh: e.matmul(
                                         pbig[:, qr * 512:qr * 512 + 512],
                                         wap[:, c, m * 128:(m + 1) * 128],
                                         lh[:, c, ng * 512:(ng + 1) * 512],
                                         start=(c == 0), stop=(c == NCH - 1)),
                                     reads=[lhB, slotB], writes=[pqB],
                                     tag=f"q{layer}{w}mm{half}_{m}_{ng}")
                            osl = dst[:, m, half * 1024 + ng * 512:
                                      half * 1024 + (ng + 1) * 512]
                            if layer == 0:
                                p.op("V", lambda e, o=osl, qr=qr: e.tensor_copy(
                                    o, pbig[:, qr * 512:qr * 512 + 512]),
                                    reads=[pqB], writes=[dstB],
                                    tag=f"q{layer}{w}ev{half}_{m}_{ng}")
                            else:
                                bn = "bias_l1q" if w == "q" else "bias_l1k"
                                p.op("S", lambda e, o=osl, qr=qr, m=m, bn=bn:
                                     e.activation(
                                         o, pbig[:, qr * 512:qr * 512 + 512],
                                         AF.Identity,
                                         bias=sb[bn].ap()[:, m:m + 1]),
                                     reads=[pqB, self.B(bn)], writes=[dstB],
                                     tag=f"q{layer}{w}ev{half}_{m}_{ng}")
                else:
                    for kc in range(8):
                        for ng in range(2):
                            qr = (kc * 2 + ng) % 2
                            pqB = self.takeQ(qr)
                            for c in range(NCH):
                                p.op("T", lambda e, kc=kc, ng=ng, c=c, qr=qr,
                                     wap=wap, lh=lh: e.matmul(
                                         pbig[:, qr * 512:qr * 512 + 512],
                                         lh[:, c, kc * 128:(kc + 1) * 128],
                                         wap[:, c, ng * 512:(ng + 1) * 512],
                                         start=(c == 0), stop=(c == NCH - 1)),
                                     reads=[lhB, slotB], writes=[pqB],
                                     tag=f"q{layer}vmm{half}_{kc}_{ng}")
                            ovs = sb["v_tm"].ap()[:, half * 8 + kc,
                                                  ng * 512:(ng + 1) * 512]
                            if layer == 0:
                                p.op("V", lambda e, o=ovs, qr=qr: e.tensor_copy(
                                    o, pbig[:, qr * 512:qr * 512 + 512]),
                                    reads=[pqB], writes=[vB],
                                    tag=f"q{layer}vev{half}_{kc}_{ng}")
                            else:
                                p.op("V", lambda e, o=ovs, qr=qr, ng=ng:
                                     e.tensor_tensor(
                                         o, pbig[:, qr * 512:qr * 512 + 512],
                                         sb["bvb"].ap()[:, ng * 512:(ng + 1) * 512],
                                         ALU.add),
                                     reads=[pqB, self.B("bvb")], writes=[vB],
                                     tag=f"q{layer}vev{half}_{kc}_{ng}")
                self.pump(4)
        woB = self.take("wo_t")
        self.load_weight_cast(wsrc["o"], "wo_t", D, D, ceng="S", tag=f"_o{layer}")
        if layer == 1:
            self.emit_rope()
        self.emit_negB(layer)

    def emit_bvb(self):
        p, d, sb = self.p, self.d, self.sb
        stB = self.take("wstage0")
        p.op("Q", lambda e: e.dma_start(sb["wstage"].ap()[0:1, 0, 0:D],
                                        d["l1bv"].ap()),
             writes=[stB], dma=True, tag="ld_bv")
        srB = self.take("srow")
        bv = sb["srow"].ap().bitcast(BF16)[0:1, 0:D]
        p.op("V", lambda e: e.tensor_copy(bv, sb["wstage"].ap()[0:1, 0, 0:D]),
             reads=[stB], writes=[srB], tag="bv_cast")
        pbig = self.pbig.ap()
        bvbB = self.take("bvb")
        for ng in range(2):
            pqB = self.takeQ(ng)
            p.op("T", lambda e, ng=ng: e.matmul(
                pbig[:, ng * 512:(ng + 1) * 512], sb["ones_row"].ap(),
                bv[:, ng * 512:(ng + 1) * 512], start=True, stop=True),
                reads=[srB, self.B("c_ones_row")], writes=[pqB], tag=f"bvbc{ng}")
            p.op("V", lambda e, ng=ng: e.tensor_copy(
                sb["bvb"].ap()[:, ng * 512:(ng + 1) * 512],
                pbig[:, ng * 512:(ng + 1) * 512]),
                reads=[pqB], writes=[bvbB], tag=f"bvcp{ng}")

    def emit_rope(self):
        p, d, sb = self.p, self.d, self.sb
        for src, dst in (("ropecos", "cosb"), ("ropesin", "sinb")):
            dB = self.take(dst)
            for piece in range(2):
                stB = self.take(f"wstage{piece}")
                ap_st = sb["wstage"].ap()[:, piece, :]
                p.op("Q", lambda e, a=ap_st, s=src, piece=piece: e.dma_start(
                    a, d[s].ap()[:, piece * 1024:(piece + 1) * 1024]),
                    writes=[stB], dma=True, tag=f"ld_{dst}{piece}")
                p.op("S", lambda e, a=ap_st, dst=dst, piece=piece: e.copy(
                    sb[dst].ap()[:, piece * 1024:(piece + 1) * 1024], a),
                    reads=[stB], writes=[dB], tag=f"cast_{dst}{piece}")
        pbig = self.pbig.ap()
        cosB, sinB = self.B("cosb"), self.B("sinb")
        tAb = sb["tmpa"].ap().bitcast(BF16)[:, 0:512]
        for w in ("q", "k"):
            tgt = sb[w].ap()
            tgtB = self.B(w)
            for ch in range(NCH):
                for kg in range(4):
                    ks = slice(kg * 512, (kg + 1) * 512)
                    qr = kg % 2
                    pqB = self.takeQ(qr)
                    p.op("T", lambda e, ch=ch, ks=ks, qr=qr, tgt=tgt: e.matmul(
                        pbig[:, qr * 512:qr * 512 + 512], sb["pswap"].ap(),
                        tgt[:, ch, ks], start=True, stop=True),
                        reads=[tgtB, self.B("pswap")], writes=[pqB],
                        tag=f"rsw_{w}{ch}_{kg}")
                    tB = self.take("tmpa")
                    p.op("G", lambda e, ch=ch, ks=ks, tgt=tgt: e.tensor_tensor(
                        tAb, tgt[:, ch, ks], sb["cosb"].ap()[:, ks], ALU.mult),
                        reads=[tgtB, cosB], writes=[tB], tag=f"rt_{w}{ch}_{kg}")
                    mB = self.take("tmpb")
                    p.op("V", lambda e, ks=ks, qr=qr: e.tensor_tensor(
                        sb["tmpb"].ap(), pbig[:, qr * 512:qr * 512 + 512],
                        sb["sinb"].ap()[:, ks], ALU.mult),
                        reads=[pqB, sinB], writes=[mB], tag=f"rm_{w}{ch}_{kg}")
                    p.op("V", lambda e, ch=ch, ks=ks, tgt=tgt: e.tensor_tensor(
                        tgt[:, ch, ks], tAb, sb["tmpb"].ap(), ALU.add),
                        reads=[tB, mB], writes=[tgtB], tag=f"ro_{w}{ch}_{kg}")
            self.pump(2)

    def emit_negB(self, layer):
        p, sb = self.p, self.sb
        pbig = self.pbig.ap()
        scale = 0.125 if layer == 1 else 1.0 / 32.0
        nh = 16 if layer == 1 else 1
        st = sb["stats"].ap()
        for iw, w in enumerate(("q", "k")):
            tgt = sb[w].ap()
            tgtB = self.B(w)
            pqall = [self.takeQ(i) for i in range(4)]
            sqB = self.take("sq128")
            for ch in range(NCH):
                p.op("S", lambda e, ch=ch, tgt=tgt: e.activation(
                    sb["sq128"].ap(), tgt[:, ch, :], AF.Square),
                    reads=[tgtB], writes=[sqB], tag=f"nBsq_{w}{ch}")
                for kg in range(4):
                    if layer == 1:
                        p.op("T", lambda e, ch=ch, kg=kg: e.matmul(
                            pbig[0:16, kg * 512:(kg + 1) * 512],
                            sb["headsel"].ap()[:, ch, :],
                            sb["sq128"].ap()[:, kg * 512:(kg + 1) * 512],
                            start=(ch == 0), stop=(ch == NCH - 1)),
                            reads=[sqB, self.B("headsel")], writes=[pqall[kg]],
                            tag=f"nBmm_{w}{ch}_{kg}")
                    else:
                        p.op("T", lambda e, ch=ch, kg=kg: e.matmul(
                            pbig[0:1, kg * 512:(kg + 1) * 512],
                            sb["ones_col"].ap(),
                            sb["sq128"].ap()[:, kg * 512:(kg + 1) * 512],
                            start=(ch == 0), stop=(ch == NCH - 1)),
                            reads=[sqB, self.B("c_ones_col")], writes=[pqall[kg]],
                            tag=f"nBmm_{w}{ch}_{kg}")
            stB = self.take("stats")
            p.op("V", lambda e, nh=nh: e.reduce_max(
                st[0:nh, 0:1], pbig[0:nh, :], AX.X),
                reads=pqall, writes=[stB], tag=f"nBmax_{w}")
            p.op("S", lambda e, nh=nh, iw=iw: e.activation(
                st[0:nh, 1 + iw:2 + iw], st[0:nh, 0:1], AF.Ln),
                reads=[stB], writes=[stB], tag=f"nBln_{w}")
        stB = self.B("stats")
        p.op("V", lambda e, nh=nh: e.tensor_tensor(
            st[0:nh, 3:4], st[0:nh, 1:2], st[0:nh, 2:3], ALU.add),
            reads=[stB], writes=[stB], tag="nBadd")
        p.op("S", lambda e, nh=nh: e.activation(
            st[0:nh, 3:4], st[0:nh, 3:4], AF.Exp, scale=0.5),
            reads=[stB], writes=[stB], tag="nBexp")
        p.op("V", lambda e, nh=nh, scale=scale: e.tensor_scalar(
            st[0:nh, 3:4], st[0:nh, 3:4], -scale, None, ALU.mult),
            reads=[stB], writes=[stB], tag="nBscale")
        bvec = sb["stats"].ap().bitcast(BF16)[0:nh, 8:9]
        p.op("V", lambda e, nh=nh, bvec=bvec: e.tensor_copy(
            bvec, st[0:nh, 3:4]),
            reads=[stB], writes=[stB], tag="nBb16")
        pb4 = self.pb[4].ap()
        pb4b16 = self.pb[4].ap().bitcast(BF16)
        pb4B = self.take("pb4")
        p.op("T", lambda e, nh=nh, bvec=bvec: e.matmul(
            pb4[0:1, 0:nh], bvec, sb["ident"].ap()[0:nh, 0:nh],
            start=True, stop=True),
            reads=[stB, self.B("ident")], writes=[pb4B], tag="nBtr")
        srB = self.take("srow")
        p.op("V", lambda e, nh=nh: e.tensor_copy(sb["srow"].ap()[0:1, 0:nh],
                                                 pb4[0:1, 0:nh]),
             reads=[pb4B], writes=[srB], tag="nBtrc")
        pb4B = self.take("pb4")
        p.op("T", lambda e, nh=nh: e.matmul(
            pb4[:, 0:nh], sb["ones_row_f32"].ap(), sb["srow"].ap()[0:1, 0:nh],
            start=True, stop=True),
            reads=[srB, self.B("c_ones_row_f32")], writes=[pb4B], tag="nBbc")
        nBB = self.take("negB")
        col0 = 0 if layer == 1 else 16
        p.op("V", lambda e, nh=nh, col0=col0: e.tensor_copy(
            sb["negB"].ap()[:, col0:col0 + nh], pb4[:, 0:nh]),
            reads=[pb4B], writes=[nBB], tag="nBcp")

    def phase_attn0(self):
        self.emit_attn(0)

    def phase_attn1(self):
        self.emit_attn(1)

    def emit_attn(self, layer):
        p, d, sb = self.p, self.d, self.sb
        pbig = self.pbig.ap()
        xs_src = d["xs_a"] if layer == 0 else d["xs_b"]
        xs_dst = d["xs_b"] if layer == 0 else d["xs_a"]
        xsrcB = self.B("xs_a" if layer == 0 else "xs_b")
        xdstB = self.B("xs_b" if layer == 0 else "xs_a")
        sc = 1.0 / 32.0 if layer == 0 else 0.125
        heads = 1 if layer == 0 else HEADS
        qB, kB, vB = self.B("q"), self.B("k"), self.B("v_tm")
        woB = self.B("wo_t")
        Pap = sb["P"].ap()
        sums = sb["stats"].ap()
        pvv = pbig.rearrange("p (a b) -> p a b", b=128)
        for qt in range(NQT):
            L = (qt + 1) * 128
            kgs = (L + 511) // 512
            nkc = L // 128
            qs = slice(qt * 128, qt * 128 + 128)
            xrB = self.take("xres")
            p.op("Q", lambda e, qs=qs: e.dma_start(
                sb["xres"].ap(),
                xs_src.rearrange("(c p) t -> p c t", p=128)[:, :, qs]),
                reads=[xsrcB], writes=[xrB], dma=True, tag=f"a{layer}ldx{qt}")
            ovB = self.take("outv")
            for h in range(heads):
                if layer == 0:
                    lq = [(c, 0, 128) for c in range(NCH)]
                else:
                    lq = [(h // 2, (h % 2) * 64, 64)]
                PB = self.take("P")
                smB = self.take("stats")
                colB = 16 if layer == 0 else h
                for kg in range(kgs):
                    n = min(512, L - kg * 512)
                    ks = slice(kg * 512, kg * 512 + n)
                    qr = kg % 2
                    pqB = self.takeQ(qr)
                    for i, (c, poff, kw) in enumerate(lq):
                        p.op("T", lambda e, c=c, poff=poff, kw=kw, ks=ks, qs=qs,
                             qr=qr, i=i, n=n, nl=len(lq): e.matmul(
                                 pbig[:, qr * 512:qr * 512 + n],
                                 sb["q"].ap()[poff:poff + kw, c, qs],
                                 sb["k"].ap()[poff:poff + kw, c, ks],
                                 start=(i == 0), stop=(i == nl - 1)),
                             reads=[qB, kB], writes=[pqB],
                             tag=f"a{layer}s{qt}_{h}_{kg}")
                    if kg == kgs - 1:
                        p.op("V", lambda e, qr=qr, n=n: e.tensor_tensor(
                            pbig[:, qr * 512 + n - 128:qr * 512 + n],
                            pbig[:, qr * 512 + n - 128:qr * 512 + n],
                            sb["maskdiag"].ap(), ALU.add),
                            reads=[self.B("maskdiag"), pqB], writes=[pqB],
                            tag=f"a{layer}mask{qt}_{h}")
                    p.op("S", lambda e, qr=qr, n=n, ks=ks, kg=kg, colB=colB:
                         e.activation(Pap[:, ks], pbig[:, qr * 512:qr * 512 + n],
                                      AF.Exp,
                                      bias=sb["negB"].ap()[:, colB:colB + 1],
                                      scale=sc, accum_out=sums[:, kg:kg + 1]),
                         reads=[pqB, self.B("negB")], writes=[PB, smB],
                         tag=f"a{layer}exp{qt}_{h}_{kg}")
                riB = self.take("rinv")
                p.op("V", lambda e, kgs=kgs: e.reduce_sum(
                    sb["rinv"].ap(), sums[:, 0:kgs], AX.X),
                    reads=[smB], writes=[riB], tag=f"a{layer}rs{qt}_{h}")
                p.op("V", lambda e: e.reciprocal(sb["rinv"].ap(), sb["rinv"].ap()),
                     reads=[riB], writes=[riB], tag=f"a{layer}ri{qt}_{h}")
                dgB = self.take("diag")
                p.op("V", lambda e: e.tensor_scalar(
                    sb["diag"].ap(), sb["ident"].ap(), sb["rinv"].ap(), None,
                    ALU.mult),
                    reads=[riB, self.B("ident")], writes=[dgB],
                    tag=f"a{layer}dg{qt}_{h}")
                PTB = self.take("PT")
                pb4f = self.pb[4].ap()
                for kc0 in range(0, nkc, 4):
                    nb = min(4, nkc - kc0)
                    p4B = self.take("pb4")
                    for j in range(nb):
                        p.op("T", lambda e, kc0=kc0, j=j: e.matmul(
                            pb4f[:, j * 128:(j + 1) * 128],
                            Pap[:, (kc0 + j) * 128:(kc0 + j + 1) * 128],
                            sb["diag"].ap(), start=True, stop=True),
                            reads=[PB, dgB], writes=[p4B],
                            tag=f"a{layer}tr{qt}_{h}_{kc0 + j}")
                    p.op("V", lambda e, kc0=kc0, nb=nb: e.tensor_copy(
                        sb["PT"].ap()[:, kc0 * 128:(kc0 + nb) * 128],
                        pb4f[:, 0:nb * 128]),
                        reads=[p4B], writes=[PTB], tag=f"a{layer}tre{qt}_{h}_{kc0}")
                if layer == 0:
                    pvB = [self.takeQ(2), self.takeQ(3)]
                    for m in range(NCH):
                        for kc in range(nkc):
                            p.op("T", lambda e, kc=kc, m=m, nkc=nkc: e.matmul(
                                pvv[:, 8 + m, :],
                                sb["v_tm"].ap()[:, kc, m * 128:(m + 1) * 128],
                                sb["PT"].ap()[:, kc * 128:(kc + 1) * 128],
                                start=(kc == 0), stop=(kc == nkc - 1)),
                                reads=[PTB, vB], writes=pvB,
                                tag=f"a{layer}pv{qt}_{kc}_{m}")
                    p.op("V", lambda e: e.tensor_copy(
                        sb["outv"].ap().rearrange("p c t -> p (c t)"),
                        pbig[:, 1024:2048]),
                        reads=pvB, writes=[ovB], tag=f"a{layer}pve{qt}")
                else:
                    pb5 = self.pb[5].ap()
                    sl = (h % 2) * 128
                    pvB = self.take("pb5")
                    for kc in range(nkc):
                        p.op("T", lambda e, kc=kc, h=h, sl=sl, nkc=nkc: e.matmul(
                            pb5[0:64, sl:sl + 128],
                            sb["v_tm"].ap()[:, kc, h * 64:(h + 1) * 64],
                            sb["PT"].ap()[:, kc * 128:(kc + 1) * 128],
                            start=(kc == 0), stop=(kc == nkc - 1)),
                            reads=[PTB, vB], writes=[pvB],
                            tag=f"a{layer}pv{qt}_{h}_{kc}")
                    p.op("V", lambda e, h=h, sl=sl: e.tensor_copy(
                        sb["outv"].ap()[(h % 2) * 64:(h % 2) * 64 + 64, h // 2, :],
                        pb5[0:64, sl:sl + 128]),
                        reads=[pvB], writes=[ovB], tag=f"a{layer}pve{qt}_{h}")
            pyB = [self.takeQ(2), self.takeQ(3)]
            for m in range(NCH):
                for c in range(NCH):
                    p.op("T", lambda e, m=m, c=c: e.matmul(
                        pvv[:, 8 + m, :],
                        sb["wo_t"].ap()[:, c, m * 128:(m + 1) * 128],
                        sb["outv"].ap()[:, c, :], start=(c == 0),
                        stop=(c == NCH - 1)),
                        reads=[ovB, woB], writes=pyB, tag=f"a{layer}o{qt}_{m}")
            xnB = self.take("xnew")
            for m in range(NCH):
                if layer == 0:
                    p.op("V", lambda e, m=m: e.tensor_tensor(
                        sb["xnew"].ap()[:, m, :], pvv[:, 8 + m, :],
                        sb["xres"].ap()[:, m, :], ALU.add),
                        reads=pyB + [xrB], writes=[xnB], tag=f"a{layer}r{qt}_{m}")
                else:
                    p.op("V", lambda e, m=m: e.scalar_tensor_tensor(
                        sb["xnew"].ap()[:, m, :], pvv[:, 8 + m, :],
                        sb["bias_l1o"].ap()[:, m:m + 1], sb["xres"].ap()[:, m, :],
                        ALU.add, ALU.add),
                        reads=pyB + [xrB, self.B("bias_l1o")], writes=[xnB],
                        tag=f"a{layer}r{qt}_{m}")
            p.op("Q", lambda e, qs=qs: e.dma_start(
                xs_dst.rearrange("(c p) t -> p c t", p=128)[:, :, qs],
                sb["xnew"].ap()),
                reads=[xnB], writes=[xdstB], dma=True, tag=f"a{layer}st{qt}")
            self.pump(2)
        self.xloc = "xs_b" if layer == 0 else "xs_a"

    def phase_ffn1a(self):
        self.emit_ffn1("a")

    def phase_ffn1b(self):
        self.emit_ffn1("b")

    def emit_ffn1(self, call):
        p, d, sb = self.p, self.d, self.sb
        pbig = self.pbig.ap()
        if call == "a":
            p.op("Q", lambda e: e.dma_start(
                d["xs_a"].rearrange("(c p) t -> p c t", p=128), sb["x"].ap()),
                reads=[self.B("x")], writes=[self.B("xs_a")], dma=True,
                tag="f1adump")
            xs_src, xsB = d["xs_a"], self.B("xs_a")
            xs_dst, xdB = d["xs_b"], self.B("xs_b")
        else:
            xs_src, xsB = d["xs_a"], self.B("xs_a")
            xs_dst, xdB = d["y"], self.B("y")
        fcB = self.take("wfcq")
        p.op("Q", lambda e: e.dma_start(
            sb["wfcq"].ap(), d["wfcq_d"].rearrange("(c p) m -> p c m", p=128)),
            reads=[self.B("wfcq_d_dram")], writes=[fcB], dma=True,
            tag=f"f1{call}ldfc")
        prB = self.take("wprq")
        p.op("Q", lambda e: e.dma_start(
            sb["wprq"].ap(), d["wprq_d"].rearrange("(c p) m -> p c m", p=128)),
            reads=[self.B("wprq_d_dram")], writes=[prB], dma=True,
            tag=f"f1{call}ldpr")
        sq2 = sb["sq128"].ap()
        for t in range(NT2):
            ts_ = slice(t * FT2, (t + 1) * FT2)
            xtB = self.take("xtile")
            p.op("Q", lambda e, ts_=ts_: e.dma_start(
                sb["xtile"].ap(),
                xs_src.rearrange("(c p) t -> p c t", p=128)[:, :, ts_]),
                reads=[xsB], writes=[xtB], dma=True, tag=f"f1{call}ldx{t}")
            lcB = self.take("lnx_c")
            self.emit_ln(sb["xtile"].ap(), xtB, FT2, sb["lnx_c"].ap(), lcB,
                         f"f1{call}ln{t}")
            hB = self.take("h")
            for m in range(32):
                q4 = m % 4
                pqB = self.takeQ(q4 // 2)
                for c in range(NCH):
                    p.op("T", lambda e, m=m, c=c, q4=q4: e.matmul(
                        pbig[:, q4 * 256:q4 * 256 + 256],
                        sb["wfcq"].ap()[:, c, m * 128:(m + 1) * 128],
                        sb["lnx_c"].ap()[:, c, :], start=(c == 0),
                        stop=(c == NCH - 1)),
                        reads=[lcB, fcB], writes=[pqB],
                        tag=f"f1{call}fc{t}_{m}")
                p.op("S", lambda e, m=m, q4=q4: e.activation(
                    sb["h"].ap()[:, m, :], pbig[:, q4 * 256:q4 * 256 + 256],
                    AF.Gelu),
                    reads=[pqB], writes=[hB], tag=f"f1{call}fca{t}_{m}")
            pm = self.pb[5].ap()
            pmB = self.take("pb5")
            oc = sb["ones_col"].ap()
            for m in range(32):
                p.op("T", lambda e, m=m: e.matmul(
                    pm[0:1, 0:FT2], oc, sb["h"].ap()[:, m, :],
                    start=(m == 0), stop=(m == 31)),
                    reads=[hB, self.B("c_ones_col")], writes=[pmB],
                    tag=f"f1{call}hmu{t}")
            pv_ = self.pb[6].ap()
            pvB = self.take("pb6")
            for m in range(32):
                so = (m % 2) * FT2
                sqB = self.take("sq128")
                p.op("S", lambda e, m=m, so=so: e.activation(
                    sq2[:, so:so + FT2], sb["h"].ap()[:, m, :], AF.Square),
                    reads=[hB], writes=[sqB], tag=f"f1{call}hsq{t}_{m}")
                p.op("T", lambda e, m=m, so=so: e.matmul(
                    pv_[0:1, 0:FT2], oc, sq2[:, so:so + FT2],
                    start=(m == 0), stop=(m == 31)),
                    reads=[sqB, self.B("c_ones_col")], writes=[pvB],
                    tag=f"f1{call}hvm{t}_{m}")
            sr = sb["srow"].ap()
            hD = 4096.0
            mu = sr[:, 0:FT2]
            var = sr[:, 512:512 + FT2]
            srB = self.take("srow")
            p.op("V", lambda e, mu=mu: e.tensor_scalar(
                mu, pm[0:1, 0:FT2], 1.0 / hD, None, ALU.mult),
                reads=[pmB], writes=[srB], tag=f"f1{call}mu{t}")
            p.op("V", lambda e, var=var: e.tensor_scalar(
                var, pv_[0:1, 0:FT2], 1.0 / hD, None, ALU.mult),
                reads=[pvB], writes=[srB], tag=f"f1{call}ex2{t}")
            ta = sb["tmpa"].ap()[0:1, 0:FT2]
            taB = self.take("tmpa")
            p.op("V", lambda e, ta=ta, mu=mu: e.tensor_tensor(ta, mu, mu, ALU.mult),
                 reads=[srB], writes=[taB], tag=f"f1{call}musq{t}")
            p.op("V", lambda e, var=var, ta=ta: e.tensor_tensor(
                var, var, ta, ALU.subtract),
                reads=[srB, taB], writes=[srB], tag=f"f1{call}var{t}")
            eps = sb["scal"].ap()[0:1, 0:1]
            p.op("S", lambda e, var=var: e.activation(var, var, AF.Ln, bias=eps),
                 reads=[srB, self.B("scal")], writes=[srB], tag=f"f1{call}lnv{t}")
            p.op("S", lambda e, var=var: e.activation(var, var, AF.Exp, scale=-0.5),
                 reads=[srB], writes=[srB], tag=f"f1{call}inv{t}")
            p.op("V", lambda e, mu=mu, var=var: e.tensor_tensor(
                mu, mu, var, ALU.mult),
                reads=[srB], writes=[srB], tag=f"f1{call}m2{t}")
            orf = sb["ones_row_f32"].ap()
            pb4 = self.pb[4].ap()
            for rowap, dst in ((var, "invb"), (mu, "m2b")):
                pb4B = self.take("pb4")
                p.op("T", lambda e, r=rowap: e.matmul(
                    pb4[:, 0:FT2], orf, r, start=True, stop=True),
                    reads=[srB, self.B("c_ones_row_f32")], writes=[pb4B],
                    tag=f"f1{call}bc{t}_{dst}")
                dB = self.take(dst)
                p.op("V", lambda e, dst=dst: e.tensor_copy(
                    sb[dst].ap()[:, 0:FT2], pb4[:, 0:FT2]),
                    reads=[pb4B], writes=[dB], tag=f"f1{call}bcc{t}_{dst}")
            invb = sb["invb"].ap()[:, 0:FT2]
            m2bv = sb["m2b"].ap()[:, 0:FT2]
            tb16 = sb["tmpa"].ap().bitcast(BF16)[:, 0:FT2]
            for m in range(32):
                taB = self.take("tmpa")
                p.op("V", lambda e, m=m: e.tensor_tensor(
                    tb16, sb["h"].ap()[:, m, :], invb, ALU.mult),
                    reads=[hB, self.B("invb")], writes=[taB],
                    tag=f"f1{call}hap1{t}_{m}")
                p.op("V", lambda e: e.tensor_tensor(tb16, tb16, m2bv,
                                                    ALU.subtract),
                     reads=[taB, self.B("m2b")], writes=[taB],
                     tag=f"f1{call}hap2{t}_{m}")
                p.op("S", lambda e, m=m: e.activation(
                    sb["h"].ap()[:, m, :], tb16, AF.Identity,
                    bias=sb["lnb"].ap()[:, m:m + 1],
                    scale=sb["lnw"].ap()[:, m:m + 1]),
                    reads=[taB, self.B("lnw"), self.B("lnb")], writes=[hB],
                    tag=f"f1{call}hap3{t}_{m}")
            ytB = self.take("ytile")
            for m in range(NCH):
                q4 = 4 + m % 4
                pqB = self.takeQ(q4 // 2)
                for c in range(32):
                    p.op("T", lambda e, m=m, c=c, q4=q4: e.matmul(
                        pbig[:, q4 * 256:q4 * 256 + 256],
                        sb["wprq"].ap()[:, c, m * 128:(m + 1) * 128],
                        sb["h"].ap()[:, c, :], start=(c == 0), stop=(c == 31)),
                        reads=[hB, prB], writes=[pqB],
                        tag=f"f1{call}pr{t}_{m}")
                p.op("V", lambda e, m=m, q4=q4: e.scalar_tensor_tensor(
                    sb["ytile"].ap()[:, m, :], pbig[:, q4 * 256:q4 * 256 + 256],
                    0.5, sb["xtile"].ap()[:, m, :], ALU.mult, ALU.add),
                    reads=[pqB, xtB], writes=[ytB], tag=f"f1{call}ev{t}_{m}")
            p.op("Q", lambda e, ts_=ts_: e.dma_start(
                xs_dst.rearrange("(c p) t -> p c t", p=128)[:, :, ts_],
                sb["ytile"].ap()),
                reads=[ytB], writes=[xdB], dma=True, tag=f"f1{call}st{t}")
            self.pump(2)
        self.xloc = "xs_b" if call == "a" else "y"

    def phase_finish(self):
        p, d, sb = self.p, self.d, self.sb
        if self.xloc == "y":
            p.wait_event("Q", self.B("y").w[0])
            return
        if self.xloc == "x":
            ev = p.op("Q", lambda e: e.dma_start(
                d["y"].rearrange("(c p) t -> p c t", p=128), sb["x"].ap()),
                reads=[self.B("x")], writes=[self.B("y")], dma=True, tag="finx")
        elif self.xloc == "fout":
            ev = p.op("Q", lambda e: e.dma_start(
                d["y"].rearrange("(c p) t -> p c t", p=128), sb["fout"].ap()),
                reads=[self.B("fout")], writes=[self.B("y")], dma=True,
                tag="finf")
        else:
            ev = p.op("Q", lambda e, s=self.xloc: e.dma_start(
                d["y"].ap(), d[s].ap()),
                reads=[self.B(self.xloc)], writes=[self.B("y")], dma=True,
                tag="find")
        p.wait_event("Q", ev)

    def build(self):
        self.phase_init()
        if "qkv1" in self.phases:
            self.emit_bvb()
        for ph in self.phases:
            getattr(self, "phase_" + ph)()
        self.pump_all()
        if self.phases and self.phases[-1] in ("ln1", "ln3"):
            self.xloc = "fout"
        self.phase_finish()
        return self.nc


def kernel(**inputs):
    cores = prep_inputs(inputs)
    b = Builder(_PHASES)
    nc = b.build()
    b.p.emit()
    res = run_bass_kernel_spmd(nc, cores, core_ids=list(range(NCORES)))
    out = np.stack([np.ascontiguousarray(np.asarray(r["y"]).T)
                    for r in res.results])
    return out.astype(np.float32)


# version 03
# speedup vs baseline: 1.7363x; 1.0028x over previous
import math
import numpy as np
import ml_dtypes

import concourse.bass as bass
import concourse.mybir as mybir
from concourse.alu_op_type import AluOpType as ALU
from concourse.bass_utils import run_bass_kernel_spmd

F32 = mybir.dt.float32
BF16 = mybir.dt.bfloat16
AF = mybir.ActivationFunctionType
AX = mybir.AxisListType

NCORES = 8
SEQ = 2048
D = 1024
NCH = 8
HEADS = 16
HD = 64
GRP = 512
NGRP = 4
FT1 = 512
FT2 = 256
NT2 = SEQ // FT2
QT = 128
NQT = 16

ALL_PHASES = ["ln1", "ffn0a", "qkv0", "attn0", "ln3", "ffn0b",
              "ffn1a", "qkv1", "attn1", "ffn1b"]
_PHASES = None
_SAME_ENGINE_SEM = True
_FFN0_NCYC = 3


class Buf:
    __slots__ = ("name", "w", "r")

    def __init__(self, name):
        self.name = name
        self.w = []
        self.r = []


class Op:
    __slots__ = ("fn", "waits", "stream", "seq", "eng", "tag")

    def __init__(self, fn, waits, stream, seq, eng, tag):
        self.fn = fn
        self.waits = waits
        self.stream = stream
        self.seq = seq
        self.eng = eng
        self.tag = tag


class Prog:
    ENGS = ("T", "V", "S", "G", "Q")
    DMA_SLOTS = {"Q": 12, "G": 6, "S": 4, "V": 2, "T": 2}

    def __init__(self, nc):
        self.nc = nc
        self.eng = {"T": nc.tensor, "V": nc.vector, "S": nc.scalar,
                    "G": nc.gpsimd, "Q": nc.sync}
        self.ops = []
        self.seq = {}
        self.waited = {e: {} for e in self.ENGS}
        self.needed = {}
        self.dmacnt = {}

    def op(self, eng, fn, reads=(), writes=(), dma=False, tag=""):
        if dma:
            i = self.dmacnt.get(eng, 0)
            self.dmacnt[eng] = i + 1
            nslots = self.DMA_SLOTS[eng]
            stream = f"{eng}d{i % nslots}"
            s = i // nslots
        else:
            stream = eng
            s = self.seq.get(stream, 0)
        deps = []
        for b in reads:
            deps += b.w
        for b in writes:
            deps += b.w
            deps += b.r
        if dma and s > 0:
            deps.append((stream, s - 1))
        waits = []
        wmap = self.waited[eng]
        for (ps, pq) in deps:
            if ps == "T" and eng == "T" and not dma:
                continue
            if (not _SAME_ENGINE_SEM) and ps == eng and not dma:
                continue
            if wmap.get(ps, -1) < pq:
                wmap[ps] = pq
                self.needed.setdefault(ps, set()).add(pq)
                waits.append((ps, pq))
        self.seq[stream] = s + 1
        self.ops.append(Op(fn, waits, stream, s, eng, tag))
        if dma:
            self.needed.setdefault(stream, set()).add(s)
        ev = (stream, s)
        for b in writes:
            b.w = [ev]
            b.r = []
        for b in reads:
            b.r.append(ev)
        return ev

    def wait_event(self, eng, ev):
        (ps, pq) = ev
        self.needed.setdefault(ps, set()).add(pq)
        self.ops.append(Op(None, [(ps, pq)], None, None, eng, "finwait"))

    def emit(self):
        nc = self.nc
        rank = {}
        sems = {}
        for stream, seqs in self.needed.items():
            order = sorted(seqs)
            step = 16 if len(stream) > 1 else 1
            rank[stream] = {q: (i + 1) * step for i, q in enumerate(order)}
            sems[stream] = nc.alloc_semaphore(f"sem_{stream}")
        for o in self.ops:
            eng = self.eng[o.eng]
            for (ps, pq) in o.waits:
                eng.wait_ge(sems[ps], rank[ps][pq])
            if o.fn is None:
                continue
            inst = o.fn(eng)
            if o.tag:
                try:
                    inst.annotate(o.tag)
                except Exception:
                    pass
            if o.stream in rank and o.seq in rank[o.stream]:
                step = 16 if len(o.stream) > 1 else 1
                inst.then_inc(sems[o.stream], step)
        return len(self.ops)


def _rope_perm():
    pi = np.zeros(D, dtype=np.int64)
    for h in range(HEADS):
        for i in range(32):
            pi[h * 64 + i] = h * 64 + 2 * i
            pi[h * 64 + 32 + i] = h * 64 + 2 * i + 1
    return pi


def _vec_tile(v):
    return np.ascontiguousarray(np.asarray(v, dtype=np.float32).reshape(-1, 128).T)


def prep_inputs(inp):
    f32 = np.float32
    pi = _rope_perm()
    g = {}
    g["l0wq"] = np.ascontiguousarray(inp["L0_attn_Wq"], dtype=f32)
    g["l0wk"] = np.ascontiguousarray(inp["L0_attn_Wk"], dtype=f32)
    g["l0wv"] = np.ascontiguousarray(inp["L0_attn_Wv"], dtype=f32)
    g["l0wo"] = np.ascontiguousarray(inp["L0_attn_Wo"], dtype=f32)
    for k in ("cn_w1", "cn_w2", "cn_w3", "cn_w4", "gc_w1", "gc_w2", "gc_w3"):
        g[k] = np.ascontiguousarray(inp["L0_ffn_" + k], dtype=f32)
    g["cn_b1"] = _vec_tile(inp["L0_ffn_cn_b1"])
    g["cn_b2"] = _vec_tile(inp["L0_ffn_cn_b2"])
    g["cn_b3"] = _vec_tile(inp["L0_ffn_cn_b3"])
    g["cn_b4"] = np.asarray(inp["L0_ffn_cn_b4"], dtype=f32).reshape(1, 1)
    g["gc_b1"] = _vec_tile(inp["L0_ffn_gc_b1"])
    g["gc_b2"] = _vec_tile(inp["L0_ffn_gc_b2"])
    g["gc_b3"] = np.asarray(inp["L0_ffn_gc_b3"], dtype=f32).reshape(1, 1)
    g["arange"] = np.asarray(inp["L0_ffn_adaptive_range"], dtype=f32).reshape(1, 1)
    g["l1wq"] = np.ascontiguousarray(np.asarray(inp["L1_attn_wq"]).T[:, pi], dtype=f32)
    g["l1wk"] = np.ascontiguousarray(np.asarray(inp["L1_attn_wk"]).T[:, pi], dtype=f32)
    g["l1wv"] = np.ascontiguousarray(np.asarray(inp["L1_attn_wv"]).T, dtype=f32)
    g["l1wo"] = np.ascontiguousarray(np.asarray(inp["L1_attn_wo"]).T, dtype=f32)
    g["l1bq"] = _vec_tile(np.asarray(inp["L1_attn_bq"])[pi])
    g["l1bk"] = _vec_tile(np.asarray(inp["L1_attn_bk"])[pi])
    g["l1bv"] = np.asarray(inp["L1_attn_bv"], dtype=f32).reshape(1, D)
    g["l1bo"] = _vec_tile(inp["L1_attn_bo"])
    g["wfcT"] = np.ascontiguousarray(np.asarray(inp["L1_ffn_W_fc"]).T, dtype=f32)
    g["wprT"] = np.ascontiguousarray(np.asarray(inp["L1_ffn_W_proj"]).T, dtype=f32)
    g["lnw"] = _vec_tile(inp["L1_ffn_ln_w"])
    g["lnb"] = _vec_tile(inp["L1_ffn_ln_b"])
    x = np.asarray(inp["L0_attn_x"], dtype=f32)
    cores = []
    for i in range(NCORES):
        m = dict(g)
        m["xT"] = np.ascontiguousarray(x[i].T)
        cores.append(m)
    return cores


def make_consts():
    c = {}
    c["ident"] = np.eye(128, dtype=ml_dtypes.bfloat16)
    ps = np.zeros((128, 128), dtype=np.float64)
    for h in range(2):
        for i in range(32):
            e = h * 64 + i
            o = h * 64 + 32 + i
            ps[o, e] = -1.0
            ps[e, o] = 1.0
    c["pswap"] = ps.astype(ml_dtypes.bfloat16)
    m = np.where(np.arange(128)[None, :] > np.arange(128)[:, None], -1e30, 0.0)
    c["maskdiag"] = m.astype(np.float32)
    c["ones_col"] = np.ones((128, 1), dtype=ml_dtypes.bfloat16)
    c["ones_row"] = np.ones((1, 128), dtype=ml_dtypes.bfloat16)
    c["ones_row_f32"] = np.ones((1, 128), dtype=np.float32)
    c["ones_col_f32"] = np.ones((128, 1), dtype=np.float32)
    i32 = np.arange(32, dtype=np.float64)
    rate = np.exp(-2.0 * i32 / HD * math.log(10000.0))
    pos = np.arange(SEQ, dtype=np.float64)
    th = rate[:, None] * pos[None, :]
    c["ropecos"] = np.tile(np.cos(th), (4, 1)).astype(np.float32)
    c["ropesin"] = np.tile(np.sin(th), (4, 1)).astype(np.float32)
    hs = np.zeros((128, NCH, HEADS), dtype=ml_dtypes.bfloat16)
    for ch in range(NCH):
        for p in range(128):
            hs[p, ch, (ch * 128 + p) // HD] = 1
    c["headsel"] = hs
    return c


GROUPS = [
    ["lnx_a", "lnx_b", "lnx_half", "cosb", "sinb", "outv", "diag", "xres",
     "xnew", "h1b", "h2b", "h3b", "gc2t", "lnx_c", "xb2"],
    ["P", "PT", "xb", "fb16", "sq128", "sqA", "sqB", "gc1t"],
    ["x", "wfcq", "wslotA", "wslotB", "xgrp", "wo_t"],
    ["fout", "q", "k", "wprq"],
    ["v_tm", "cn1", "cn2", "cn3", "cn4", "h", "xtile", "ytile"],
    ["pQ0"], ["pQ1"], ["pQ2"], ["pQ3"],
    ["tmpa"], ["tmpb"], ["wstage0"], ["wstage1"],
]
GROUP_OF = {}
for _g in GROUPS:
    for _n in _g:
        GROUP_OF[_n] = _g


class Builder:
    def __init__(self, phases=None):
        self.phases = list(phases if phases is not None else ALL_PHASES)
        nc = bass.Bass("TRN2", target_bir_lowering=False)
        self.nc = nc
        self.p = Prog(nc)
        self.bufs = {}
        self.xloc = "x"
        self.wsvc = []
        self._declare_io()
        self._alloc_sbuf()
        self._alloc_psum()

    def B(self, name):
        if name not in self.bufs:
            self.bufs[name] = Buf(name)
        return self.bufs[name]

    def take(self, name):
        b = Buf(name)
        for n in GROUP_OF.get(name, [name]):
            o = self.bufs.get(n)
            if o is not None:
                b.w = b.w + o.w
                b.r = b.r + o.r
        self.bufs[name] = b
        return b

    def _declare_io(self):
        nc = self.nc
        self.d = {}
        d = self.d

        def T(n, s):
            d[n] = nc.dram_tensor(n, s, F32, kind="ExternalInput")
        T("xT", [D, SEQ])
        d["y"] = nc.dram_tensor("y", [D, SEQ], F32, kind="ExternalOutput")
        for n, s in [("l0wq", [D, D]), ("l0wk", [D, D]), ("l0wv", [D, D]),
                     ("l0wo", [D, D]), ("cn_w1", [D, D]), ("cn_w2", [D, 512]),
                     ("cn_w3", [512, 256]), ("cn_w4", [256, 1]),
                     ("gc_w1", [D, 512]), ("gc_w2", [512, 256]),
                     ("gc_w3", [256, 1]), ("cn_b1", [128, 8]),
                     ("cn_b2", [128, 4]), ("cn_b3", [128, 2]), ("cn_b4", [1, 1]),
                     ("gc_b1", [128, 4]), ("gc_b2", [128, 2]), ("gc_b3", [1, 1]),
                     ("arange", [1, 1]),
                     ("l1wq", [D, D]), ("l1wk", [D, D]), ("l1wv", [D, D]),
                     ("l1wo", [D, D]), ("l1bq", [128, 8]), ("l1bk", [128, 8]),
                     ("l1bv", [1, D]), ("l1bo", [128, 8]),
                     ("wfcT", [D, 4096]), ("wprT", [4096, D]),
                     ("lnw", [128, 32]), ("lnb", [128, 32])]:
            T(n, s)
        I = lambda n, s, dt: nc.dram_tensor(n, s, dt, kind="Internal")
        d["xs_a"] = I("xs_a", [D, SEQ], F32)
        d["xs_b"] = I("xs_b", [D, SEQ], F32)
        d["lnxs"] = I("lnxs", [D, SEQ], BF16)
        d["wfcq_d"] = I("wfcq_d", [D, 4096], BF16)
        d["wprq_d"] = I("wprq_d", [4096, D], BF16)
        for k, v in make_consts().items():
            d[k] = nc.inline_tensor(v, k)

    def _alloc_sbuf(self):
        nc = self.nc
        self.off = (int(nc.sbuf_base) + 31) // 32 * 32
        self.sb = {}
        self.sboff = {}

        def at(name, shape, dtype, off=None):
            nbytes = int(np.prod(shape[1:])) * (2 if dtype == BF16 else 4)
            if off is None:
                off = self.off
                self.off = off + ((nbytes + 31) // 32) * 32
            h = nc.alloc_sbuf_tensor_at(name, list(shape), dtype, offset=off)
            self.sb[name] = h
            self.sboff[name] = off
            return h

        at("ident", [128, 128], BF16)
        at("pswap", [128, 128], BF16)
        at("maskdiag", [128, 128], F32)
        at("ones_col", [128, 1], BF16)
        at("ones_row", [1, 128], BF16)
        at("ones_row_f32", [1, 128], F32)
        at("headsel", [128, NCH, HEADS], BF16)
        at("bias_cn1", [128, 8], F32)
        at("bias_cn2", [128, 4], F32)
        at("bias_cn3", [128, 2], F32)
        at("bias_gc1", [128, 4], F32)
        at("bias_gc2", [128, 2], F32)
        at("bias_l1q", [128, 8], F32)
        at("bias_l1k", [128, 8], F32)
        at("bias_l1o", [128, 8], F32)
        at("lnw", [128, 32], F32)
        at("lnb", [128, 32], F32)
        at("bvb", [128, D], BF16)
        at("scal", [128, 16], F32)
        at("srow", [1, 1024], F32)
        at("grow", [1, 64], F32)
        at("stats", [128, 8], F32)
        at("rinv", [128, 1], F32)
        at("negB", [128, 24], F32)
        at("invb", [128, 512], BF16)
        at("m2b", [128, 512], BF16)
        at("tmpa", [128, 512], F32)
        at("tmpb", [128, 512], F32)
        at("gc3t", [128, 2, 1], BF16)
        at("ones_col_f32", [128, 1], F32)
        at("gacc", [128, 64], F32)
        at("lnx_a", [128, NCH, GRP], BF16)
        at("lnx_b", [128, NCH, GRP], BF16)
        r1 = self.sboff["lnx_a"]
        at("lnx_half", [128, NCH, 2 * GRP], BF16, off=r1)
        at("cosb", [128, SEQ], BF16, off=r1)
        at("sinb", [128, SEQ], BF16, off=r1 + 8192)
        at("outv", [128, NCH, QT], BF16, off=r1)
        at("diag", [128, 128], BF16, off=r1 + 2048)
        at("xres", [128, NCH, QT], F32, off=r1 + 2432)
        at("xnew", [128, NCH, QT], F32, off=r1 + 2432 + 4096)
        at("h2b", [128, 4, FT1], BF16, off=r1)
        at("h3b", [128, 2, FT1], BF16, off=r1 + 4096)
        at("h1b", [128, NCH, FT1], BF16, off=r1 + 8192)
        at("P", [128, SEQ], BF16)
        at("PT", [128, SEQ], BF16)
        pP = self.sboff["P"]
        at("xb", [128, NCH, GRP], BF16, off=pP)
        at("fb16", [128, NCH, FT1], BF16, off=pP)
        at("sq128", [128, SEQ], BF16, off=pP)
        at("lnx_c", [128, NCH, FT2], BF16, off=r1)
        at("xb2", [128, NCH, FT2], BF16, off=r1 + 8192)
        at("gc1t", [128, NCH, 512], BF16, off=pP)
        at("gc2t", [128, 4, 256], BF16, off=self.sboff["lnx_a"])

        r2 = self.off
        at("x", [128, NCH, SEQ], F32)
        at("wfcq", [128, NCH, 4096], BF16, off=r2)
        at("wslotA", [128, NCH, D], BF16, off=r2)
        at("wslotB", [128, NCH, D], BF16, off=r2 + 16 * 1024)
        at("xgrp", [128, NCH, GRP], F32, off=r2 + 32 * 1024)
        at("wo_t", [128, NCH, D], BF16, off=r2 + 48 * 1024)
        r3 = self.off
        at("fout", [128, NCH, SEQ], F32)
        at("q", [128, NCH, SEQ], BF16, off=r3)
        at("k", [128, NCH, SEQ], BF16, off=r3 + 32 * 1024)
        at("wprq", [128, 32, D], BF16, off=r3)
        r4 = self.off
        at("v_tm", [128, NQT, D], BF16)
        at("cn1", [128, NCH, D], BF16, off=r4)
        at("cn2", [128, NCH, 512], BF16, off=r4 + 16 * 1024)
        at("cn3", [128, 4, 256], BF16, off=r4 + 24 * 1024)
        at("cn4", [128, 2, 1], BF16, off=r4 + 26 * 1024)
        at("h", [128, 32, FT2], BF16, off=r4)
        at("xtile", [128, NCH, FT2], F32, off=r4 + 16 * 1024)
        at("ytile", [128, NCH, FT2], F32, off=r4 + 24 * 1024)
        at("wstage", [128, 2, 1024], F32)

        limit = int(nc.sbuf_top)
        assert self.off <= limit, f"SBUF over budget: {self.off} > {limit}"

    def _alloc_psum(self):
        nc = self.nc
        self.pbig = nc.alloc_psum_tensor("pbig", [128, 2048], F32)
        self.pb = {i: nc.alloc_psum_tensor(f"pb{i}", [128, 512], F32)
                   for i in (4, 5, 6, 7)}

    def pQ(self, i):
        return self.B(f"pQ{i}")

    def takeQ(self, i):
        return self.take(f"pQ{i}")

    def pump(self, n):
        for _ in range(min(n, len(self.wsvc))):
            self.wsvc.pop(0)()

    def pump_all(self):
        while self.wsvc:
            self.wsvc.pop(0)()

    def _wpieces(self, K, M, cap=1024):
        kch = K // 128
        if M >= cap:
            for c in range(kch):
                for m0 in range(0, M, cap):
                    yield (c, c + 1, m0, min(M, m0 + cap))
        else:
            per = max(1, cap // M)
            for c0 in range(0, kch, per):
                yield (c0, min(kch, c0 + per), 0, M)

    def load_weight_cast(self, src, dst, K, M, ceng="S", tag=""):
        p, d, sb = self.p, self.d, self.sb
        dstB = self.B(dst)
        dap = sb[dst].ap()
        sap = d[src].rearrange("(c p) m -> p c m", p=128)
        for i, (c0, c1, m0, m1) in enumerate(self._wpieces(K, M)):
            n = (c1 - c0) * (m1 - m0)
            half = i % 2
            stB = self.take(f"wstage{half}")
            st3 = sb["wstage"].ap()[:, half, 0:n].rearrange(
                "p (a b) -> p a b", b=m1 - m0)
            p.op("Q", lambda e, a=st3, s=sap[:, c0:c1, m0:m1]: e.dma_start(
                a, s, allow_slow_non_contiguous=True),
                writes=[stB], dma=True, tag=f"ldw_{src}_{i}{tag}")
            cp = (lambda e, o, a: e.copy(o, a)) if ceng == "S" else \
                 (lambda e, o, a: e.tensor_copy(o, a))
            p.op(ceng, lambda e, o=dap[:, c0:c1, m0:m1], a=st3, cp=cp: cp(e, o, a),
                 reads=[stB], writes=[dstB], tag=f"cast_{src}_{i}{tag}")

    def emit_ln(self, src_ap, src_buf, ntok, out_ap, out_buf, tag,
                xb_name="xb", sq_eng="S"):
        p, sb = self.p, self.sb
        n = ntok
        xb = sb[xb_name].ap()[:, :, 0:n]
        xbB = self.take(xb_name)
        p.op("V", lambda e, o=xb, s=src_ap: e.tensor_copy(o, s), reads=[src_buf],
             writes=[xbB], tag=f"{tag}cast")
        pm = self.pb[5].ap()
        pmB = self.take("pb5")
        oc = sb["ones_col"].ap()
        ocB = self.B("c_ones_col")
        for c in range(NCH):
            p.op("T", lambda e, c=c, n=n: e.matmul(pm[0:1, 0:n], oc, xb[:, c, :],
                                                   start=(c == 0), stop=(c == NCH - 1)),
                 reads=[xbB, ocB], writes=[pmB], tag=f"{tag}mmu")
        pv = self.pb[6].ap()
        pvB = self.take("pb6")
        for c in range(NCH):
            tnm = "tmpa" if c % 2 == 0 else "tmpb"
            sq = self.sb[tnm].ap().bitcast(BF16)[:, 0:n]
            sqB = self.take(tnm)
            if sq_eng == "S":
                p.op("S", lambda e, sq=sq, c=c: e.activation(
                    sq, xb[:, c, :], AF.Square),
                    reads=[xbB], writes=[sqB], tag=f"{tag}sq{c}")
            else:
                p.op("V", lambda e, sq=sq, c=c: e.tensor_tensor(
                    sq, xb[:, c, :], xb[:, c, :], ALU.mult),
                    reads=[xbB], writes=[sqB], tag=f"{tag}sq{c}")
            p.op("T", lambda e, sq=sq, c=c, n=n: e.matmul(
                pv[0:1, 0:n], oc, sq, start=(c == 0), stop=(c == NCH - 1)),
                reads=[sqB, ocB], writes=[pvB], tag=f"{tag}mms")
        sr = sb["srow"].ap()
        mu = sr[:, 0:n]
        var = sr[:, 512:512 + n]
        srB = self.take("srow")
        p.op("V", lambda e, mu=mu, n=n: e.tensor_scalar(
            mu, pm[0:1, 0:n], 1.0 / D, None, ALU.mult),
            reads=[pmB], writes=[srB], tag=f"{tag}mu")
        p.op("V", lambda e, var=var, n=n: e.tensor_scalar(
            var, pv[0:1, 0:n], 1.0 / D, None, ALU.mult),
            reads=[pvB], writes=[srB], tag=f"{tag}ex2")
        ta = sb["tmpa"].ap()[0:1, 0:n]
        taB = self.take("tmpa")
        p.op("V", lambda e, ta=ta, mu=mu: e.tensor_tensor(ta, mu, mu, ALU.mult),
             reads=[srB], writes=[taB], tag=f"{tag}musq")
        p.op("V", lambda e, var=var, ta=ta: e.tensor_tensor(var, var, ta,
                                                            ALU.subtract),
             reads=[srB, taB], writes=[srB], tag=f"{tag}var")
        eps = sb["scal"].ap()[0:1, 0:1]
        p.op("S", lambda e, var=var: e.activation(var, var, AF.Ln, bias=eps),
             reads=[srB, self.B("scal")], writes=[srB], tag=f"{tag}lnv")
        p.op("S", lambda e, var=var: e.activation(var, var, AF.Exp, scale=-0.5),
             reads=[srB], writes=[srB], tag=f"{tag}inv")
        p.op("V", lambda e, mu=mu, var=var: e.tensor_tensor(mu, mu, var, ALU.mult),
             reads=[srB], writes=[srB], tag=f"{tag}m2")
        orf = sb["ones_row_f32"].ap()
        orfB = self.B("c_ones_row_f32")
        pb4 = self.pb[4].ap()
        for rowap, dst in ((var, "invb"), (mu, "m2b")):
            pb4B = self.take("pb4")
            p.op("T", lambda e, r=rowap, n=n: e.matmul(pb4[:, 0:n], orf, r,
                                                       start=True, stop=True),
                 reads=[srB, orfB], writes=[pb4B], tag=f"{tag}bc_{dst}")
            dB = self.take(dst)
            p.op("V", lambda e, o=sb[dst].ap()[:, 0:n], n=n: e.tensor_copy(
                o, pb4[:, 0:n]),
                reads=[pb4B], writes=[dB], tag=f"{tag}bcc_{dst}")
        invb = sb["invb"].ap()[:, 0:n]
        m2bv = sb["m2b"].ap()[:, 0:n]
        tb16 = sb["tmpa"].ap().bitcast(BF16)[:, 0:n]
        for c in range(NCH):
            tB = self.take("tmpa")
            p.op("V", lambda e, c=c, t=tb16, s=src_ap[:, c, 0:n], i=invb:
                 e.tensor_tensor(t, s, i, ALU.mult),
                 reads=[src_buf, self.B("invb")], writes=[tB], tag=f"{tag}ap1_{c}")
            p.op("V", lambda e, c=c, o=out_ap[:, c, 0:n], t=tb16, m=m2bv:
                 e.tensor_tensor(o, t, m, ALU.subtract),
                 reads=[tB, self.B("m2b")], writes=[out_buf], tag=f"{tag}ap2_{c}")

    def phase_init(self):
        p, d, sb = self.p, self.d, self.sb
        loads = [("ident", "ident"), ("pswap", "pswap"), ("maskdiag", "maskdiag"),
                 ("headsel", "headsel"),
                 ("cn_b1", "bias_cn1"), ("cn_b2", "bias_cn2"),
                 ("cn_b3", "bias_cn3"), ("gc_b1", "bias_gc1"),
                 ("gc_b2", "bias_gc2"), ("l1bq", "bias_l1q"),
                 ("l1bk", "bias_l1k"), ("l1bo", "bias_l1o"),
                 ("lnw", "lnw"), ("lnb", "lnb")]
        for src, t in loads:
            p.op("Q", lambda e, s=src, t=t: e.dma_start(sb[t].ap(), d[s].ap()),
                 writes=[self.B(t)], dma=True, tag=f"ld_{t}")
        for src, t, bn in (("ones_col", "ones_col", "c_ones_col"),
                           ("ones_row", "ones_row", "c_ones_row"),
                           ("ones_row_f32", "ones_row_f32", "c_ones_row_f32"),
                           ("ones_col_f32", "ones_col_f32", "c_ones_col_f32")):
            p.op("Q", lambda e, s=src, t=t: e.dma_start(sb[t].ap(), d[s].ap()),
                 writes=[self.B(bn)], dma=True, tag=f"ld_{t}")
        p.op("Q", lambda e: e.dma_start(
            sb["x"].ap(), d["xT"].rearrange("(c p) t -> p c t", p=128)),
            writes=[self.take("x")], dma=True, tag="ld_x")
        p.op("V", lambda e: e.memset(sb["scal"].ap()[:, 0:1], 1e-5),
             writes=[self.B("scal")], tag="eps")
        for i, src in ((0, "arange"), (1, "cn_b4"), (2, "gc_b3")):
            p.op("Q", lambda e, i=i, s=src: e.dma_start(
                sb["grow"].ap()[0:1, i:i + 1], d[s].ap()),
                writes=[self.B("grow")], dma=True, tag=f"ld_{src}")
        self.load_cn()
        self.enqueue_quant()

    def load_cn(self):
        for nm in ("cn1", "cn2", "cn3", "cn4"):
            self.take(nm)
        self.load_weight_cast("cn_w1", "cn1", D, D)
        self.load_weight_cast("cn_w2", "cn2", D, 512)
        self.load_weight_cast("cn_w3", "cn3", 512, 256)
        self.load_weight_cast("cn_w4", "cn4", 256, 1)

    def load_gc(self):
        for nm in ("gc1t", "gc2t", "gc3t"):
            self.take(nm)
        self.load_weight_cast("gc_w1", "gc1t", D, 512)
        self.load_weight_cast("gc_w2", "gc2t", 512, 256)
        self.load_weight_cast("gc_w3", "gc3t", 256, 1)

    def enqueue_quant(self):
        p, d, sb = self.p, self.d, self.sb
        pb7 = self.pb[7].ap()
        tA = sb["tmpa"].ap().bitcast(BF16)
        tB_ = sb["tmpb"].ap().bitcast(BF16)

        def g1piece(src, gcol, idx, c, m0):
            def go():
                half = idx % 2
                stB = self.take(f"wstage{half}")
                ap_st = sb["wstage"].ap()[:, half, :]
                st3 = sb["wstage"].ap()[:, half, :].rearrange(
                    "p (a b) -> p a b", b=128)
                sap = d[src].rearrange("(c p) m -> p c m", p=128)
                p.op("Q", lambda e, a=ap_st, s=sap[:, c:c + 1, m0:m0 + 1024]:
                     e.dma_start(a, s.rearrange("p c m -> p (c m)")),
                     writes=[stB], dma=True, tag=f"g1_{src}_{idx}")
                t8 = sb["tmpa"].ap()[:, 0:8]
                aB = self.take("tmpa")
                p.op("V", lambda e, s3=st3, t8=t8: e.reduce_sum(
                    t8, s3, AX.X, apply_absolute_value=True),
                    reads=[stB], writes=[aB], tag=f"g1r1_{src}_{idx}")
                gaB = self.B("gacc")
                p.op("V", lambda e, t8=t8, gcol=gcol: e.reduce_sum(
                    sb["gacc"].ap()[:, gcol:gcol + 1], t8, AX.X),
                    reads=[aB], writes=[gaB], tag=f"g1r2_{src}_{idx}")
            self.wsvc.append(go)

        for gbase, src, K, M in ((0, "wfcT", D, 4096), (32, "wprT", 4096, D)):
            pieces = list(self._wpieces(K, M))
            assert len(pieces) == 32
            for idx, (c0, c1, m0, m1) in enumerate(pieces):
                g1piece(src, gbase + idx, idx, c0, m0)

        def finalize():
            gr = sb["grow"].ap()
            grB = self.B("grow")
            gaB = self.B("gacc")
            ga3 = sb["gacc"].ap().rearrange("p (a b) -> p a b", b=8)
            for gbase, col, pcol in ((0, 8, 0), (32, 16, 1)):
                taB = self.take("tmpa")
                p.op("V", lambda e, gbase=gbase: e.reduce_sum(
                    sb["tmpa"].ap()[:, 0:4], ga3[:, gbase // 8:gbase // 8 + 4, :],
                    AX.X),
                    reads=[gaB], writes=[taB], tag=f"gf1_{col}")
                p.op("V", lambda e: e.reduce_sum(
                    sb["tmpa"].ap()[:, 8:9], sb["tmpa"].ap()[:, 0:4], AX.X),
                    reads=[taB], writes=[taB], tag=f"gf2_{col}")
                pb7B = self.take("pb7")
                p.op("T", lambda e, pcol=pcol: e.matmul(
                    pb7[0:1, pcol:pcol + 1], sb["tmpa"].ap()[:, 8:9],
                    sb["ones_col_f32"].ap(), start=True, stop=True),
                    reads=[taB, self.B("c_ones_col_f32")], writes=[pb7B],
                    tag=f"gf3_{col}")
                p.op("V", lambda e, col=col, pcol=pcol: e.tensor_scalar(
                    gr[:, col + 1:col + 2], pb7[0:1, pcol:pcol + 1],
                    1.0 / (4096.0 * 1024.0), 1e-6, ALU.mult, ALU.add),
                    reads=[pb7B], writes=[grB], tag=f"gam{col}")
                p.op("V", lambda e, col=col: e.tensor_scalar(
                    gr[:, col + 2:col + 3], gr[:, col + 1:col + 2],
                    0.5, None, ALU.mult),
                    reads=[grB], writes=[grB], tag=f"gthr{col}")
            pb4 = self.pb[4].ap()
            for src_col, dst in ((10, 1), (18, 3)):
                pb4B = self.take("pb4")
                p.op("T", lambda e, sc_=src_col: e.matmul(
                    pb4[:, 0:1], sb["ones_row_f32"].ap(), gr[:, sc_:sc_ + 1],
                    start=True, stop=True),
                    reads=[grB, self.B("c_ones_row_f32")], writes=[pb4B],
                    tag=f"gbc{src_col}")
                sclB = self.B("scal")
                p.op("V", lambda e, dst=dst: e.tensor_copy(
                    sb["scal"].ap()[:, dst:dst + 1], pb4[:, 0:1]),
                    reads=[pb4B], writes=[sclB], tag=f"gbcc{dst}")
                p.op("V", lambda e, dst=dst: e.tensor_scalar(
                    sb["scal"].ap()[:, dst + 1:dst + 2],
                    sb["scal"].ap()[:, dst:dst + 1], -1.0, None, ALU.mult),
                    reads=[sclB], writes=[sclB], tag=f"gneg{dst}")
        self.wsvc.append(finalize)

        def qpiece(src, dst, thr_col, idx, c, m0):
            def go():
                half = idx % 2
                stB = self.take(f"wstage{half}")
                ap_st = sb["wstage"].ap()[:, half, :]
                sap = d[src].rearrange("(c p) m -> p c m", p=128)
                dap = d[dst].rearrange("(c p) m -> p c m", p=128)
                p.op("Q", lambda e, a=ap_st, s=sap[:, c:c + 1, m0:m0 + 1024]:
                     e.dma_start(a, s.rearrange("p c m -> p (c m)")),
                     writes=[stB], dma=True, tag=f"q2_{src}_{idx}")
                thr = sb["scal"].ap()[:, thr_col:thr_col + 1]
                nthr = sb["scal"].ap()[:, thr_col + 1:thr_col + 2]
                aB = self.take("tmpa")
                p.op("V", lambda e, a=ap_st, nthr=nthr: e.tensor_scalar(
                    tA, a, nthr, None, ALU.is_lt),
                    reads=[stB, self.B("scal")], writes=[aB],
                    tag=f"q2lt_{src}_{idx}")
                bB = self.take("tmpb")
                p.op("V", lambda e, a=ap_st, thr=thr: e.scalar_tensor_tensor(
                    tB_, a, thr, tA, ALU.is_gt, ALU.subtract),
                    reads=[stB, aB, self.B("scal")], writes=[bB],
                    tag=f"q2q_{src}_{idx}")
                p.op("G", lambda e, o=dap[:, c:c + 1, m0:m0 + 1024]: e.dma_start(
                    o.rearrange("p c m -> p (c m)"), tB_),
                    reads=[bB], writes=[self.B(dst + "_dram")], dma=True,
                    tag=f"q2st_{src}_{idx}")
            self.wsvc.append(go)

        for src, dst, thr_col, K, M in (("wfcT", "wfcq_d", 1, D, 4096),
                                        ("wprT", "wprq_d", 3, 4096, D)):
            for idx, (c0, c1, m0, m1) in enumerate(self._wpieces(K, M)):
                qpiece(src, dst, thr_col, idx, c0, m0)

    def phase_ln1(self):
        sb = self.sb
        self.take("fout")
        for g in range(NGRP):
            self.emit_ln(sb["x"].ap()[:, :, g * GRP:(g + 1) * GRP], self.B("x"),
                         GRP, sb["fout"].ap()[:, :, g * GRP:(g + 1) * GRP],
                         self.B("fout"), f"ln1g{g}")
            self.pump(10)

    def phase_ln3(self):
        p, d, sb = self.p, self.d, self.sb
        xB = self.take("x")
        p.op("Q", lambda e: e.dma_start(
            sb["x"].ap(), d["xs_b"].rearrange("(c p) t -> p c t", p=128)),
            reads=[self.B("xs_b")], writes=[xB], dma=True, tag="ld_x3")
        self.xloc = "x"
        self.load_cn()
        self.take("fout")
        for g in range(NGRP):
            self.emit_ln(sb["x"].ap()[:, :, g * GRP:(g + 1) * GRP], self.B("x"),
                         GRP, sb["fout"].ap()[:, :, g * GRP:(g + 1) * GRP],
                         self.B("fout"), f"ln3g{g}")
            self.pump(6)

    def phase_ffn0a(self):
        self.emit_ffn0("a", ncyc=_FFN0_NCYC)

    def phase_ffn0b(self):
        self.emit_ffn0("b")

    def emit_ffn0(self, call, ncyc=3, ntile=None):
        p, sb = self.p, self.sb
        self.load_gc()
        foB = self.B("fout")
        gr = sb["grow"].ap()
        grB = self.B("grow")
        gcx = sb["scal"].ap()[:, 8:16]
        sclB = self.B("scal")
        p.op("V", lambda e: e.reduce_sum(gcx, sb["fout"].ap(), AX.X),
             reads=[foB], writes=[sclB], tag=f"f0{call}_gcx")
        gxb = sb["tmpa"].ap().bitcast(BF16)[:, 0:8]
        gxbB = self.take("tmpa")
        p.op("V", lambda e: e.tensor_scalar(gxb, gcx, 1.0 / SEQ, None, ALU.mult),
             reads=[sclB], writes=[gxbB], tag=f"f0{call}_gcxb")
        pb6 = self.pb[6].ap()
        pb6B = self.take("pb6")
        for m in range(4):
            for c in range(NCH):
                p.op("T", lambda e, m=m, c=c: e.matmul(
                    pb6[:, m:m + 1], sb["gc1t"].ap()[:, c, m * 128:(m + 1) * 128],
                    gxb[:, c:c + 1], start=(c == 0), stop=(c == NCH - 1)),
                    reads=[gxbB, self.B("gc1t")], writes=[pb6B],
                    tag=f"f0{call}_g1mm")
        gh1 = sb["tmpb"].ap().bitcast(BF16)[:, 0:4]
        gh1B = self.take("tmpb")
        for m in range(4):
            p.op("S", lambda e, m=m: e.activation(
                gh1[:, m:m + 1], pb6[:, m:m + 1], AF.Gelu,
                bias=sb["bias_gc1"].ap()[:, m:m + 1]),
                reads=[pb6B, self.B("bias_gc1")], writes=[gh1B],
                tag=f"f0{call}_g1a")
        pb6B = self.take("pb6")
        for m in range(2):
            for c in range(4):
                p.op("T", lambda e, m=m, c=c: e.matmul(
                    pb6[:, 8 + m:9 + m],
                    sb["gc2t"].ap()[:, c, m * 128:(m + 1) * 128],
                    gh1[:, c:c + 1], start=(c == 0), stop=(c == 3)),
                    reads=[gh1B, self.B("gc2t")], writes=[pb6B],
                    tag=f"f0{call}_g2mm")
        gh2 = sb["tmpb"].ap().bitcast(BF16)[:, 8:10]
        gh2B = self.take("tmpb")
        for m in range(2):
            p.op("S", lambda e, m=m: e.activation(
                gh2[:, m:m + 1], pb6[:, 8 + m:9 + m], AF.Gelu,
                bias=sb["bias_gc2"].ap()[:, m:m + 1]),
                reads=[pb6B, self.B("bias_gc2")], writes=[gh2B],
                tag=f"f0{call}_g2a")
        pb6B = self.take("pb6")
        for c in range(2):
            p.op("T", lambda e, c=c: e.matmul(
                pb6[0:1, 16:17], sb["gc3t"].ap()[:, c, 0:1], gh2[:, c:c + 1],
                start=(c == 0), stop=(c == 1)),
                reads=[gh2B, self.B("gc3t")], writes=[pb6B], tag=f"f0{call}_g3mm")
        p.op("V", lambda e: e.tensor_scalar(gr[:, 3:4], gr[:, 2:3], 0.5, None,
                                            ALU.mult),
             reads=[grB], writes=[grB], tag=f"f0{call}_hb3")
        p.op("V", lambda e: e.tensor_scalar(gr[:, 4:5], gr[:, 1:2], 0.5, None,
                                            ALU.mult),
             reads=[grB], writes=[grB], tag=f"f0{call}_hb4")
        p.op("S", lambda e: e.activation(gr[:, 5:6], pb6[0:1, 16:17], AF.Tanh,
                                         bias=gr[:, 3:4], scale=0.5),
             reads=[pb6B, grB], writes=[grB], tag=f"f0{call}_gtanh")
        p.op("V", lambda e: e.tensor_scalar(gr[:, 6:7], gr[:, 5:6], 0.5, 0.5,
                                            ALU.mult, ALU.add),
             reads=[grB], writes=[grB], tag=f"f0{call}_gf01")
        p.op("V", lambda e: e.tensor_scalar(gr[:, 7:8], gr[:, 6:7], gr[:, 0:1],
                                            1.4, ALU.mult, ALU.mult),
             reads=[grB], writes=[grB], tag=f"f0{call}_u1")
        p.op("V", lambda e: e.tensor_scalar(gr[:, 26:27], gr[:, 0:1], -0.7, 0.1,
                                            ALU.mult, ALU.add),
             reads=[grB], writes=[grB], tag=f"f0{call}_u2")
        p.op("V", lambda e: e.tensor_tensor(gr[:, 24:25], gr[:, 7:8],
                                            gr[:, 26:27], ALU.add),
             reads=[grB], writes=[grB], tag=f"f0{call}_A")
        p.op("V", lambda e: e.tensor_scalar(gr[:, 25:26], gr[:, 0:1], 0.3, None,
                                            ALU.mult),
             reads=[grB], writes=[grB], tag=f"f0{call}_C")
        if ncyc == 0:
            return
        pbig = self.pbig.ap()
        signs = [(ALU.add, ALU.subtract), (ALU.add, ALU.subtract),
                 (ALU.subtract, ALU.add), (ALU.subtract, ALU.add)]
        fb = sb["fb16"].ap()
        h1 = sb["h1b"].ap()
        h2 = sb["h2b"].ap()
        h3 = sb["h3b"].ap()
        for cyc in range(ncyc):
            for t in range(ntile or (SEQ // FT1)):
                ts_ = slice(t * FT1, (t + 1) * FT1)
                fbB = self.take("fb16")
                p.op("S", lambda e, ts_=ts_: e.activation(
                    fb, sb["fout"].ap()[:, :, ts_], AF.Copy),
                    reads=[foB], writes=[fbB], tag=f"f0{call}cast{cyc}_{t}")
                h1B = self.take("h1b")
                for m in range(NCH):
                    qr = m % 2
                    pqB = self.takeQ(qr)
                    for c in range(NCH):
                        p.op("T", lambda e, m=m, c=c, qr=qr: e.matmul(
                            pbig[:, qr * 512:qr * 512 + 512],
                            sb["cn1"].ap()[:, c, m * 128:(m + 1) * 128],
                            fb[:, c, :], start=(c == 0), stop=(c == NCH - 1)),
                            reads=[fbB, self.B("cn1")], writes=[pqB],
                            tag=f"f0{call}h1mm{cyc}_{t}_{m}")
                    p.op("S", lambda e, m=m, qr=qr: e.activation(
                        h1[:, m, :], pbig[:, qr * 512:qr * 512 + 512], AF.Gelu,
                        bias=sb["bias_cn1"].ap()[:, m:m + 1]),
                        reads=[pqB, self.B("bias_cn1")], writes=[h1B],
                        tag=f"f0{call}h1a{cyc}_{t}_{m}")
                h2B = self.take("h2b")
                for m in range(4):
                    pqB = self.takeQ(2)
                    for c in range(NCH):
                        p.op("T", lambda e, m=m, c=c: e.matmul(
                            pbig[:, 1024:1536],
                            sb["cn2"].ap()[:, c, m * 128:(m + 1) * 128],
                            h1[:, c, :], start=(c == 0), stop=(c == NCH - 1)),
                            reads=[h1B, self.B("cn2")], writes=[pqB],
                            tag=f"f0{call}h2mm{cyc}_{t}_{m}")
                    p.op("S", lambda e, m=m: e.activation(
                        h2[:, m, :], pbig[:, 1024:1536], AF.Gelu,
                        bias=sb["bias_cn2"].ap()[:, m:m + 1]),
                        reads=[pqB, self.B("bias_cn2")], writes=[h2B],
                        tag=f"f0{call}h2a{cyc}_{t}_{m}")
                h3B = self.take("h3b")
                for m in range(2):
                    pqB = self.takeQ(3)
                    for c in range(4):
                        p.op("T", lambda e, m=m, c=c: e.matmul(
                            pbig[:, 1536:2048],
                            sb["cn3"].ap()[:, c, m * 128:(m + 1) * 128],
                            h2[:, c, :], start=(c == 0), stop=(c == 3)),
                            reads=[h2B, self.B("cn3")], writes=[pqB],
                            tag=f"f0{call}h3mm{cyc}_{t}_{m}")
                    p.op("S", lambda e, m=m: e.activation(
                        h3[:, m, :], pbig[:, 1536:2048], AF.Gelu,
                        bias=sb["bias_cn3"].ap()[:, m:m + 1]),
                        reads=[pqB, self.B("bias_cn3")], writes=[h3B],
                        tag=f"f0{call}h3a{cyc}_{t}_{m}")
                pb5 = self.pb[5].ap()
                pzB = self.take("pb5")
                for c in range(2):
                    p.op("T", lambda e, c=c: e.matmul(
                        pb5[0:1, 0:FT1], sb["cn4"].ap()[:, c, 0:1], h3[:, c, :],
                        start=(c == 0), stop=(c == 1)),
                        reads=[h3B, self.B("cn4")], writes=[pzB],
                        tag=f"f0{call}zmm{cyc}_{t}")
                srB = self.take("srow")
                tr = sb["srow"].ap()[:, 0:FT1]
                p.op("S", lambda e: e.activation(tr, pb5[0:1, 0:FT1], AF.Tanh,
                                                 bias=gr[:, 4:5], scale=0.5),
                     reads=[pzB, grB], writes=[srB], tag=f"f0{call}ctanh{cyc}_{t}")
                crow = sb["srow"].ap().bitcast(BF16)[0:1, 1024:1024 + FT1]
                p.op("V", lambda e: e.tensor_scalar(
                    crow, tr, gr[:, 25:26], gr[:, 24:25], ALU.mult, ALU.add),
                    reads=[srB, grB], writes=[srB], tag=f"f0{call}crow{cyc}_{t}")
                pb6 = self.pb[6].ap()
                pcB = self.take("pb6")
                p.op("T", lambda e: e.matmul(pb6[:, 0:FT1], sb["ones_row"].ap(),
                                             crow, start=True, stop=True),
                     reads=[srB, self.B("c_ones_row")], writes=[pcB],
                     tag=f"f0{call}cbc{cyc}_{t}")
                for pr in range(4):
                    sA, sB_ = signs[pr]
                    fA = sb["fout"].ap()[:, pr, ts_]
                    fB_ = sb["fout"].ap()[:, pr + 4, ts_]
                    m1B = self.take("tmpa")
                    p.op("V", lambda e, fB_=fB_: e.tensor_tensor(
                        sb["tmpa"].ap(), pb6[:, 0:FT1], fB_, ALU.mult),
                        reads=[pcB, foB], writes=[m1B],
                        tag=f"f0{call}m1_{cyc}_{t}_{pr}")
                    m2B = self.take("tmpb")
                    p.op("V", lambda e, fA=fA: e.tensor_tensor(
                        sb["tmpb"].ap(), pb6[:, 0:FT1], fA, ALU.mult),
                        reads=[pcB, foB], writes=[m2B],
                        tag=f"f0{call}m2_{cyc}_{t}_{pr}")
                    p.op("V", lambda e, fA=fA, sA=sA: e.tensor_tensor(
                        fA, fA, sb["tmpa"].ap(), sA),
                        reads=[m1B, foB], writes=[foB],
                        tag=f"f0{call}uA_{cyc}_{t}_{pr}")
                    p.op("V", lambda e, fB_=fB_, sB_=sB_: e.tensor_tensor(
                        fB_, fB_, sb["tmpb"].ap(), sB_),
                        reads=[m2B, foB], writes=[foB],
                        tag=f"f0{call}uB_{cyc}_{t}_{pr}")
                self.pump(4)
        xB = self.B("x")
        for t in range(SEQ // FT1):
            ts_ = slice(t * FT1, (t + 1) * FT1)
            p.op("V", lambda e, ts_=ts_: e.scalar_tensor_tensor(
                sb["x"].ap()[:, :, ts_], sb["fout"].ap()[:, :, ts_],
                0.5, sb["x"].ap()[:, :, ts_], ALU.mult, ALU.add),
                reads=[foB, xB], writes=[xB], tag=f"f0{call}res{t}")

    def phase_qkv0(self):
        self.emit_qkv(0)

    def phase_qkv1(self):
        self.emit_qkv(1)

    def emit_qkv(self, layer):
        p, d, sb = self.p, self.d, self.sb
        for g in range(NGRP):
            gs = slice(g * GRP, (g + 1) * GRP)
            if layer == 0:
                src_ap = sb["x"].ap()[:, :, gs]
                srcB = self.B("x")
            else:
                xgB = self.take("xgrp")
                p.op("Q", lambda e, gs=gs: e.dma_start(
                    sb["xgrp"].ap(),
                    d["xs_b"].rearrange("(c p) t -> p c t", p=128)[:, :, gs]),
                    reads=[self.B("xs_b")], writes=[xgB], dma=True,
                    tag=f"q{layer}ldx{g}")
                src_ap = sb["xgrp"].ap()
                srcB = xgB
            nm = "lnx_a" if g % 2 == 0 else "lnx_b"
            outB = self.take(nm)
            self.emit_ln(src_ap, srcB, GRP, sb[nm].ap(), outB, f"q{layer}ln{g}")
            p.op("Q", lambda e, gs=gs, nm=nm: e.dma_start(
                d["lnxs"].rearrange("(c p) t -> p c t", p=128)[:, :, gs],
                sb[nm].ap()),
                reads=[outB], writes=[self.B("lnxs")], dma=True,
                tag=f"q{layer}stln{g}")
            if layer == 0:
                p.op("Q", lambda e, gs=gs: e.dma_start(
                    d["xs_a"].rearrange("(c p) t -> p c t", p=128)[:, :, gs],
                    sb["x"].ap()[:, :, gs]),
                    reads=[self.B("x")], writes=[self.B("xs_a")], dma=True,
                    tag=f"q{layer}dumpx{g}")
            self.pump(8)
        self.xloc = "xs_a" if layer == 0 else "xs_b"
        wsrc = {0: {"q": "l0wq", "k": "l0wk", "v": "l0wv", "o": "l0wo"},
                1: {"q": "l1wq", "k": "l1wk", "v": "l1wv", "o": "l1wo"}}[layer]
        qB = self.take("q")
        kB = self.take("k")
        vB = self.take("v_tm")
        pbig = self.pbig.ap()
        for half in range(2):
            lhB = self.take("lnx_half")
            p.op("Q", lambda e, half=half: e.dma_start(
                sb["lnx_half"].ap(),
                d["lnxs"].rearrange("(c p) t -> p c t", p=128)
                [:, :, half * 1024:(half + 1) * 1024]),
                reads=[self.B("lnxs")], writes=[lhB], dma=True,
                tag=f"q{layer}ldln{half}")
            lh = sb["lnx_half"].ap()
            for wi, w in enumerate(("q", "k", "v")):
                slot = ("wslotA", "wslotB")[wi % 2]
                slotB = self.take(slot)
                self.load_weight_cast(wsrc[w], slot, D, D, ceng="S",
                                      tag=f"_{layer}{half}")
                wap = sb[slot].ap()
                if w in ("q", "k"):
                    dst = sb[w].ap()
                    dstB = qB if w == "q" else kB
                    for m in range(NCH):
                        for ng in range(2):
                            qr = (m * 2 + ng) % 2
                            pqB = self.takeQ(qr)
                            for c in range(NCH):
                                p.op("T", lambda e, m=m, ng=ng, c=c, qr=qr,
                                     wap=wap, lh=lh: e.matmul(
                                         pbig[:, qr * 512:qr * 512 + 512],
                                         wap[:, c, m * 128:(m + 1) * 128],
                                         lh[:, c, ng * 512:(ng + 1) * 512],
                                         start=(c == 0), stop=(c == NCH - 1)),
                                     reads=[lhB, slotB], writes=[pqB],
                                     tag=f"q{layer}{w}mm{half}_{m}_{ng}")
                            osl = dst[:, m, half * 1024 + ng * 512:
                                      half * 1024 + (ng + 1) * 512]
                            if layer == 0:
                                p.op("V", lambda e, o=osl, qr=qr: e.tensor_copy(
                                    o, pbig[:, qr * 512:qr * 512 + 512]),
                                    reads=[pqB], writes=[dstB],
                                    tag=f"q{layer}{w}ev{half}_{m}_{ng}")
                            else:
                                bn = "bias_l1q" if w == "q" else "bias_l1k"
                                p.op("S", lambda e, o=osl, qr=qr, m=m, bn=bn:
                                     e.activation(
                                         o, pbig[:, qr * 512:qr * 512 + 512],
                                         AF.Identity,
                                         bias=sb[bn].ap()[:, m:m + 1]),
                                     reads=[pqB, self.B(bn)], writes=[dstB],
                                     tag=f"q{layer}{w}ev{half}_{m}_{ng}")
                else:
                    for kc in range(8):
                        for ng in range(2):
                            qr = (kc * 2 + ng) % 2
                            pqB = self.takeQ(qr)
                            for c in range(NCH):
                                p.op("T", lambda e, kc=kc, ng=ng, c=c, qr=qr,
                                     wap=wap, lh=lh: e.matmul(
                                         pbig[:, qr * 512:qr * 512 + 512],
                                         lh[:, c, kc * 128:(kc + 1) * 128],
                                         wap[:, c, ng * 512:(ng + 1) * 512],
                                         start=(c == 0), stop=(c == NCH - 1)),
                                     reads=[lhB, slotB], writes=[pqB],
                                     tag=f"q{layer}vmm{half}_{kc}_{ng}")
                            ovs = sb["v_tm"].ap()[:, half * 8 + kc,
                                                  ng * 512:(ng + 1) * 512]
                            if layer == 0:
                                p.op("V", lambda e, o=ovs, qr=qr: e.tensor_copy(
                                    o, pbig[:, qr * 512:qr * 512 + 512]),
                                    reads=[pqB], writes=[vB],
                                    tag=f"q{layer}vev{half}_{kc}_{ng}")
                            else:
                                p.op("V", lambda e, o=ovs, qr=qr, ng=ng:
                                     e.tensor_tensor(
                                         o, pbig[:, qr * 512:qr * 512 + 512],
                                         sb["bvb"].ap()[:, ng * 512:(ng + 1) * 512],
                                         ALU.add),
                                     reads=[pqB, self.B("bvb")], writes=[vB],
                                     tag=f"q{layer}vev{half}_{kc}_{ng}")
                self.pump(4)
        woB = self.take("wo_t")
        self.load_weight_cast(wsrc["o"], "wo_t", D, D, ceng="S", tag=f"_o{layer}")
        if layer == 1:
            self.emit_rope()
        self.emit_negB(layer)

    def emit_bvb(self):
        p, d, sb = self.p, self.d, self.sb
        stB = self.take("wstage0")
        p.op("Q", lambda e: e.dma_start(sb["wstage"].ap()[0:1, 0, 0:D],
                                        d["l1bv"].ap()),
             writes=[stB], dma=True, tag="ld_bv")
        srB = self.take("srow")
        bv = sb["srow"].ap().bitcast(BF16)[0:1, 0:D]
        p.op("V", lambda e: e.tensor_copy(bv, sb["wstage"].ap()[0:1, 0, 0:D]),
             reads=[stB], writes=[srB], tag="bv_cast")
        pbig = self.pbig.ap()
        bvbB = self.take("bvb")
        for ng in range(2):
            pqB = self.takeQ(ng)
            p.op("T", lambda e, ng=ng: e.matmul(
                pbig[:, ng * 512:(ng + 1) * 512], sb["ones_row"].ap(),
                bv[:, ng * 512:(ng + 1) * 512], start=True, stop=True),
                reads=[srB, self.B("c_ones_row")], writes=[pqB], tag=f"bvbc{ng}")
            p.op("V", lambda e, ng=ng: e.tensor_copy(
                sb["bvb"].ap()[:, ng * 512:(ng + 1) * 512],
                pbig[:, ng * 512:(ng + 1) * 512]),
                reads=[pqB], writes=[bvbB], tag=f"bvcp{ng}")

    def emit_rope(self):
        p, d, sb = self.p, self.d, self.sb
        for src, dst in (("ropecos", "cosb"), ("ropesin", "sinb")):
            dB = self.take(dst)
            for piece in range(2):
                stB = self.take(f"wstage{piece}")
                ap_st = sb["wstage"].ap()[:, piece, :]
                p.op("Q", lambda e, a=ap_st, s=src, piece=piece: e.dma_start(
                    a, d[s].ap()[:, piece * 1024:(piece + 1) * 1024]),
                    writes=[stB], dma=True, tag=f"ld_{dst}{piece}")
                p.op("S", lambda e, a=ap_st, dst=dst, piece=piece: e.copy(
                    sb[dst].ap()[:, piece * 1024:(piece + 1) * 1024], a),
                    reads=[stB], writes=[dB], tag=f"cast_{dst}{piece}")
        pbig = self.pbig.ap()
        cosB, sinB = self.B("cosb"), self.B("sinb")
        tAb = sb["tmpa"].ap().bitcast(BF16)[:, 0:512]
        for w in ("q", "k"):
            tgt = sb[w].ap()
            tgtB = self.B(w)
            for ch in range(NCH):
                for kg in range(4):
                    ks = slice(kg * 512, (kg + 1) * 512)
                    qr = kg % 2
                    pqB = self.takeQ(qr)
                    p.op("T", lambda e, ch=ch, ks=ks, qr=qr, tgt=tgt: e.matmul(
                        pbig[:, qr * 512:qr * 512 + 512], sb["pswap"].ap(),
                        tgt[:, ch, ks], start=True, stop=True),
                        reads=[tgtB, self.B("pswap")], writes=[pqB],
                        tag=f"rsw_{w}{ch}_{kg}")
                    tB = self.take("tmpa")
                    p.op("G", lambda e, ch=ch, ks=ks, tgt=tgt: e.tensor_tensor(
                        tAb, tgt[:, ch, ks], sb["cosb"].ap()[:, ks], ALU.mult),
                        reads=[tgtB, cosB], writes=[tB], tag=f"rt_{w}{ch}_{kg}")
                    mB = self.take("tmpb")
                    p.op("V", lambda e, ks=ks, qr=qr: e.tensor_tensor(
                        sb["tmpb"].ap(), pbig[:, qr * 512:qr * 512 + 512],
                        sb["sinb"].ap()[:, ks], ALU.mult),
                        reads=[pqB, sinB], writes=[mB], tag=f"rm_{w}{ch}_{kg}")
                    p.op("V", lambda e, ch=ch, ks=ks, tgt=tgt: e.tensor_tensor(
                        tgt[:, ch, ks], tAb, sb["tmpb"].ap(), ALU.add),
                        reads=[tB, mB], writes=[tgtB], tag=f"ro_{w}{ch}_{kg}")
            self.pump(2)

    def emit_negB(self, layer):
        p, sb = self.p, self.sb
        pbig = self.pbig.ap()
        scale = 0.125 if layer == 1 else 1.0 / 32.0
        nh = 16 if layer == 1 else 1
        st = sb["stats"].ap()
        for iw, w in enumerate(("q", "k")):
            tgt = sb[w].ap()
            tgtB = self.B(w)
            pqall = [self.takeQ(i) for i in range(4)]
            sqB = self.take("sq128")
            for ch in range(NCH):
                p.op("S", lambda e, ch=ch, tgt=tgt: e.activation(
                    sb["sq128"].ap(), tgt[:, ch, :], AF.Square),
                    reads=[tgtB], writes=[sqB], tag=f"nBsq_{w}{ch}")
                for kg in range(4):
                    if layer == 1:
                        p.op("T", lambda e, ch=ch, kg=kg: e.matmul(
                            pbig[0:16, kg * 512:(kg + 1) * 512],
                            sb["headsel"].ap()[:, ch, :],
                            sb["sq128"].ap()[:, kg * 512:(kg + 1) * 512],
                            start=(ch == 0), stop=(ch == NCH - 1)),
                            reads=[sqB, self.B("headsel")], writes=[pqall[kg]],
                            tag=f"nBmm_{w}{ch}_{kg}")
                    else:
                        p.op("T", lambda e, ch=ch, kg=kg: e.matmul(
                            pbig[0:1, kg * 512:(kg + 1) * 512],
                            sb["ones_col"].ap(),
                            sb["sq128"].ap()[:, kg * 512:(kg + 1) * 512],
                            start=(ch == 0), stop=(ch == NCH - 1)),
                            reads=[sqB, self.B("c_ones_col")], writes=[pqall[kg]],
                            tag=f"nBmm_{w}{ch}_{kg}")
            stB = self.take("stats")
            p.op("V", lambda e, nh=nh: e.reduce_max(
                st[0:nh, 0:1], pbig[0:nh, :], AX.X),
                reads=pqall, writes=[stB], tag=f"nBmax_{w}")
            p.op("S", lambda e, nh=nh, iw=iw: e.activation(
                st[0:nh, 1 + iw:2 + iw], st[0:nh, 0:1], AF.Ln),
                reads=[stB], writes=[stB], tag=f"nBln_{w}")
        stB = self.B("stats")
        p.op("V", lambda e, nh=nh: e.tensor_tensor(
            st[0:nh, 3:4], st[0:nh, 1:2], st[0:nh, 2:3], ALU.add),
            reads=[stB], writes=[stB], tag="nBadd")
        p.op("S", lambda e, nh=nh: e.activation(
            st[0:nh, 3:4], st[0:nh, 3:4], AF.Exp, scale=0.5),
            reads=[stB], writes=[stB], tag="nBexp")
        p.op("V", lambda e, nh=nh, scale=scale: e.tensor_scalar(
            st[0:nh, 3:4], st[0:nh, 3:4], -scale, None, ALU.mult),
            reads=[stB], writes=[stB], tag="nBscale")
        bvec = sb["stats"].ap().bitcast(BF16)[0:nh, 8:9]
        p.op("V", lambda e, nh=nh, bvec=bvec: e.tensor_copy(
            bvec, st[0:nh, 3:4]),
            reads=[stB], writes=[stB], tag="nBb16")
        pb4 = self.pb[4].ap()
        pb4b16 = self.pb[4].ap().bitcast(BF16)
        pb4B = self.take("pb4")
        p.op("T", lambda e, nh=nh, bvec=bvec: e.matmul(
            pb4[0:1, 0:nh], bvec, sb["ident"].ap()[0:nh, 0:nh],
            start=True, stop=True),
            reads=[stB, self.B("ident")], writes=[pb4B], tag="nBtr")
        srB = self.take("srow")
        p.op("V", lambda e, nh=nh: e.tensor_copy(sb["srow"].ap()[0:1, 0:nh],
                                                 pb4[0:1, 0:nh]),
             reads=[pb4B], writes=[srB], tag="nBtrc")
        pb4B = self.take("pb4")
        p.op("T", lambda e, nh=nh: e.matmul(
            pb4[:, 0:nh], sb["ones_row_f32"].ap(), sb["srow"].ap()[0:1, 0:nh],
            start=True, stop=True),
            reads=[srB, self.B("c_ones_row_f32")], writes=[pb4B], tag="nBbc")
        nBB = self.take("negB")
        col0 = 0 if layer == 1 else 16
        p.op("V", lambda e, nh=nh, col0=col0: e.tensor_copy(
            sb["negB"].ap()[:, col0:col0 + nh], pb4[:, 0:nh]),
            reads=[pb4B], writes=[nBB], tag="nBcp")

    def phase_attn0(self):
        self.emit_attn(0)

    def phase_attn1(self):
        self.emit_attn(1)

    def emit_attn(self, layer):
        p, d, sb = self.p, self.d, self.sb
        pbig = self.pbig.ap()
        xs_src = d["xs_a"] if layer == 0 else d["xs_b"]
        xs_dst = d["xs_b"] if layer == 0 else d["xs_a"]
        xsrcB = self.B("xs_a" if layer == 0 else "xs_b")
        xdstB = self.B("xs_b" if layer == 0 else "xs_a")
        sc = 1.0 / 32.0 if layer == 0 else 0.125
        heads = 1 if layer == 0 else HEADS
        qB, kB, vB = self.B("q"), self.B("k"), self.B("v_tm")
        woB = self.B("wo_t")
        Pap = sb["P"].ap()
        sums = sb["stats"].ap()
        pvv = pbig.rearrange("p (a b) -> p a b", b=128)
        for qt in range(NQT):
            L = (qt + 1) * 128
            kgs = (L + 511) // 512
            nkc = L // 128
            qs = slice(qt * 128, qt * 128 + 128)
            xrB = self.take("xres")
            p.op("Q", lambda e, qs=qs: e.dma_start(
                sb["xres"].ap(),
                xs_src.rearrange("(c p) t -> p c t", p=128)[:, :, qs]),
                reads=[xsrcB], writes=[xrB], dma=True, tag=f"a{layer}ldx{qt}")
            ovB = self.take("outv")
            for h in range(heads):
                if layer == 0:
                    lq = [(c, 0, 128) for c in range(NCH)]
                else:
                    lq = [(h // 2, (h % 2) * 64, 64)]
                PB = self.take("P")
                smB = self.take("stats")
                colB = 16 if layer == 0 else h
                for kg in range(kgs):
                    n = min(512, L - kg * 512)
                    ks = slice(kg * 512, kg * 512 + n)
                    qr = kg % 2
                    pqB = self.takeQ(qr)
                    for i, (c, poff, kw) in enumerate(lq):
                        p.op("T", lambda e, c=c, poff=poff, kw=kw, ks=ks, qs=qs,
                             qr=qr, i=i, n=n, nl=len(lq): e.matmul(
                                 pbig[:, qr * 512:qr * 512 + n],
                                 sb["q"].ap()[poff:poff + kw, c, qs],
                                 sb["k"].ap()[poff:poff + kw, c, ks],
                                 start=(i == 0), stop=(i == nl - 1)),
                             reads=[qB, kB], writes=[pqB],
                             tag=f"a{layer}s{qt}_{h}_{kg}")
                    if kg == kgs - 1:
                        p.op("V", lambda e, qr=qr, n=n: e.tensor_tensor(
                            pbig[:, qr * 512 + n - 128:qr * 512 + n],
                            pbig[:, qr * 512 + n - 128:qr * 512 + n],
                            sb["maskdiag"].ap(), ALU.add),
                            reads=[self.B("maskdiag"), pqB], writes=[pqB],
                            tag=f"a{layer}mask{qt}_{h}")
                    p.op("S", lambda e, qr=qr, n=n, ks=ks, kg=kg, colB=colB:
                         e.activation(Pap[:, ks], pbig[:, qr * 512:qr * 512 + n],
                                      AF.Exp,
                                      bias=sb["negB"].ap()[:, colB:colB + 1],
                                      scale=sc, accum_out=sums[:, kg:kg + 1]),
                         reads=[pqB, self.B("negB")], writes=[PB, smB],
                         tag=f"a{layer}exp{qt}_{h}_{kg}")
                riB = self.take("rinv")
                p.op("V", lambda e, kgs=kgs: e.reduce_sum(
                    sb["rinv"].ap(), sums[:, 0:kgs], AX.X),
                    reads=[smB], writes=[riB], tag=f"a{layer}rs{qt}_{h}")
                p.op("V", lambda e: e.reciprocal(sb["rinv"].ap(), sb["rinv"].ap()),
                     reads=[riB], writes=[riB], tag=f"a{layer}ri{qt}_{h}")
                dgB = self.take("diag")
                p.op("V", lambda e: e.tensor_scalar(
                    sb["diag"].ap(), sb["ident"].ap(), sb["rinv"].ap(), None,
                    ALU.mult),
                    reads=[riB, self.B("ident")], writes=[dgB],
                    tag=f"a{layer}dg{qt}_{h}")
                PTB = self.take("PT")
                pb4f = self.pb[4].ap()
                for kc0 in range(0, nkc, 4):
                    nb = min(4, nkc - kc0)
                    p4B = self.take("pb4")
                    for j in range(nb):
                        p.op("T", lambda e, kc0=kc0, j=j: e.matmul(
                            pb4f[:, j * 128:(j + 1) * 128],
                            Pap[:, (kc0 + j) * 128:(kc0 + j + 1) * 128],
                            sb["diag"].ap(), start=True, stop=True),
                            reads=[PB, dgB], writes=[p4B],
                            tag=f"a{layer}tr{qt}_{h}_{kc0 + j}")
                    p.op("V", lambda e, kc0=kc0, nb=nb: e.tensor_copy(
                        sb["PT"].ap()[:, kc0 * 128:(kc0 + nb) * 128],
                        pb4f[:, 0:nb * 128]),
                        reads=[p4B], writes=[PTB], tag=f"a{layer}tre{qt}_{h}_{kc0}")
                if layer == 0:
                    pvB = [self.takeQ(2), self.takeQ(3)]
                    for m in range(NCH):
                        for kc in range(nkc):
                            p.op("T", lambda e, kc=kc, m=m, nkc=nkc: e.matmul(
                                pvv[:, 8 + m, :],
                                sb["v_tm"].ap()[:, kc, m * 128:(m + 1) * 128],
                                sb["PT"].ap()[:, kc * 128:(kc + 1) * 128],
                                start=(kc == 0), stop=(kc == nkc - 1)),
                                reads=[PTB, vB], writes=pvB,
                                tag=f"a{layer}pv{qt}_{kc}_{m}")
                    p.op("V", lambda e: e.tensor_copy(
                        sb["outv"].ap().rearrange("p c t -> p (c t)"),
                        pbig[:, 1024:2048]),
                        reads=pvB, writes=[ovB], tag=f"a{layer}pve{qt}")
                else:
                    pb5 = self.pb[5].ap()
                    sl = (h % 2) * 128
                    pvB = self.take("pb5")
                    for kc in range(nkc):
                        p.op("T", lambda e, kc=kc, h=h, sl=sl, nkc=nkc: e.matmul(
                            pb5[0:64, sl:sl + 128],
                            sb["v_tm"].ap()[:, kc, h * 64:(h + 1) * 64],
                            sb["PT"].ap()[:, kc * 128:(kc + 1) * 128],
                            start=(kc == 0), stop=(kc == nkc - 1)),
                            reads=[PTB, vB], writes=[pvB],
                            tag=f"a{layer}pv{qt}_{h}_{kc}")
                    p.op("V", lambda e, h=h, sl=sl: e.tensor_copy(
                        sb["outv"].ap()[(h % 2) * 64:(h % 2) * 64 + 64, h // 2, :],
                        pb5[0:64, sl:sl + 128]),
                        reads=[pvB], writes=[ovB], tag=f"a{layer}pve{qt}_{h}")
            pyB = [self.takeQ(2), self.takeQ(3)]
            for m in range(NCH):
                for c in range(NCH):
                    p.op("T", lambda e, m=m, c=c: e.matmul(
                        pvv[:, 8 + m, :],
                        sb["wo_t"].ap()[:, c, m * 128:(m + 1) * 128],
                        sb["outv"].ap()[:, c, :], start=(c == 0),
                        stop=(c == NCH - 1)),
                        reads=[ovB, woB], writes=pyB, tag=f"a{layer}o{qt}_{m}")
            xnB = self.take("xnew")
            for m in range(NCH):
                if layer == 0:
                    p.op("V", lambda e, m=m: e.tensor_tensor(
                        sb["xnew"].ap()[:, m, :], pvv[:, 8 + m, :],
                        sb["xres"].ap()[:, m, :], ALU.add),
                        reads=pyB + [xrB], writes=[xnB], tag=f"a{layer}r{qt}_{m}")
                else:
                    p.op("V", lambda e, m=m: e.scalar_tensor_tensor(
                        sb["xnew"].ap()[:, m, :], pvv[:, 8 + m, :],
                        sb["bias_l1o"].ap()[:, m:m + 1], sb["xres"].ap()[:, m, :],
                        ALU.add, ALU.add),
                        reads=pyB + [xrB, self.B("bias_l1o")], writes=[xnB],
                        tag=f"a{layer}r{qt}_{m}")
            p.op("Q", lambda e, qs=qs: e.dma_start(
                xs_dst.rearrange("(c p) t -> p c t", p=128)[:, :, qs],
                sb["xnew"].ap()),
                reads=[xnB], writes=[xdstB], dma=True, tag=f"a{layer}st{qt}")
            self.pump(2)
        self.xloc = "xs_b" if layer == 0 else "xs_a"

    def phase_ffn1a(self):
        self.emit_ffn1("a")

    def phase_ffn1b(self):
        self.emit_ffn1("b")

    def emit_ffn1(self, call):
        p, d, sb = self.p, self.d, self.sb
        pbig = self.pbig.ap()
        if call == "a":
            p.op("Q", lambda e: e.dma_start(
                d["xs_a"].rearrange("(c p) t -> p c t", p=128), sb["x"].ap()),
                reads=[self.B("x")], writes=[self.B("xs_a")], dma=True,
                tag="f1adump")
            xs_src, xsB = d["xs_a"], self.B("xs_a")
            xs_dst, xdB = d["xs_b"], self.B("xs_b")
        else:
            xs_src, xsB = d["xs_a"], self.B("xs_a")
            xs_dst, xdB = d["y"], self.B("y")
        fcB = self.take("wfcq")
        p.op("Q", lambda e: e.dma_start(
            sb["wfcq"].ap(), d["wfcq_d"].rearrange("(c p) m -> p c m", p=128)),
            reads=[self.B("wfcq_d_dram")], writes=[fcB], dma=True,
            tag=f"f1{call}ldfc")
        prB = self.take("wprq")
        p.op("Q", lambda e: e.dma_start(
            sb["wprq"].ap(), d["wprq_d"].rearrange("(c p) m -> p c m", p=128)),
            reads=[self.B("wprq_d_dram")], writes=[prB], dma=True,
            tag=f"f1{call}ldpr")
        sq2 = sb["sq128"].ap()
        for t in range(NT2):
            ts_ = slice(t * FT2, (t + 1) * FT2)
            xtB = self.take("xtile")
            p.op("Q", lambda e, ts_=ts_: e.dma_start(
                sb["xtile"].ap(),
                xs_src.rearrange("(c p) t -> p c t", p=128)[:, :, ts_]),
                reads=[xsB], writes=[xtB], dma=True, tag=f"f1{call}ldx{t}")
            lcB = self.take("lnx_c")
            self.emit_ln(sb["xtile"].ap(), xtB, FT2, sb["lnx_c"].ap(), lcB,
                         f"f1{call}ln{t}", xb_name="xb2")
            hB = self.take("h")
            for m in range(32):
                q4 = m % 4
                pqB = self.takeQ(q4 // 2)
                for c in range(NCH):
                    p.op("T", lambda e, m=m, c=c, q4=q4: e.matmul(
                        pbig[:, q4 * 256:q4 * 256 + 256],
                        sb["wfcq"].ap()[:, c, m * 128:(m + 1) * 128],
                        sb["lnx_c"].ap()[:, c, :], start=(c == 0),
                        stop=(c == NCH - 1)),
                        reads=[lcB, fcB], writes=[pqB],
                        tag=f"f1{call}fc{t}_{m}")
                p.op("S", lambda e, m=m, q4=q4: e.activation(
                    sb["h"].ap()[:, m, :], pbig[:, q4 * 256:q4 * 256 + 256],
                    AF.Gelu),
                    reads=[pqB], writes=[hB], tag=f"f1{call}fca{t}_{m}")
            pm = self.pb[5].ap()
            pmB = self.take("pb5")
            oc = sb["ones_col"].ap()
            for m in range(32):
                p.op("T", lambda e, m=m: e.matmul(
                    pm[0:1, 0:FT2], oc, sb["h"].ap()[:, m, :],
                    start=(m == 0), stop=(m == 31)),
                    reads=[hB, self.B("c_ones_col")], writes=[pmB],
                    tag=f"f1{call}hmu{t}")
            pv_ = self.pb[6].ap()
            pvB = self.take("pb6")
            for m in range(32):
                so = (m % 2) * FT2
                sqB = self.take("sqA" if m % 2 == 0 else "sqB")
                p.op("V", lambda e, m=m, so=so: e.tensor_tensor(
                    sq2[:, so:so + FT2], sb["h"].ap()[:, m, :],
                    sb["h"].ap()[:, m, :], ALU.mult),
                    reads=[hB], writes=[sqB], tag=f"f1{call}hsq{t}_{m}")
                p.op("T", lambda e, m=m, so=so: e.matmul(
                    pv_[0:1, 0:FT2], oc, sq2[:, so:so + FT2],
                    start=(m == 0), stop=(m == 31)),
                    reads=[sqB, self.B("c_ones_col")], writes=[pvB],
                    tag=f"f1{call}hvm{t}_{m}")
            sr = sb["srow"].ap()
            hD = 4096.0
            mu = sr[:, 0:FT2]
            var = sr[:, 512:512 + FT2]
            srB = self.take("srow")
            p.op("V", lambda e, mu=mu: e.tensor_scalar(
                mu, pm[0:1, 0:FT2], 1.0 / hD, None, ALU.mult),
                reads=[pmB], writes=[srB], tag=f"f1{call}mu{t}")
            p.op("V", lambda e, var=var: e.tensor_scalar(
                var, pv_[0:1, 0:FT2], 1.0 / hD, None, ALU.mult),
                reads=[pvB], writes=[srB], tag=f"f1{call}ex2{t}")
            ta = sb["tmpa"].ap()[0:1, 0:FT2]
            taB = self.take("tmpa")
            p.op("V", lambda e, ta=ta, mu=mu: e.tensor_tensor(ta, mu, mu, ALU.mult),
                 reads=[srB], writes=[taB], tag=f"f1{call}musq{t}")
            p.op("V", lambda e, var=var, ta=ta: e.tensor_tensor(
                var, var, ta, ALU.subtract),
                reads=[srB, taB], writes=[srB], tag=f"f1{call}var{t}")
            eps = sb["scal"].ap()[0:1, 0:1]
            p.op("S", lambda e, var=var: e.activation(var, var, AF.Ln, bias=eps),
                 reads=[srB, self.B("scal")], writes=[srB], tag=f"f1{call}lnv{t}")
            p.op("S", lambda e, var=var: e.activation(var, var, AF.Exp, scale=-0.5),
                 reads=[srB], writes=[srB], tag=f"f1{call}inv{t}")
            p.op("V", lambda e, mu=mu, var=var: e.tensor_tensor(
                mu, mu, var, ALU.mult),
                reads=[srB], writes=[srB], tag=f"f1{call}m2{t}")
            orf = sb["ones_row_f32"].ap()
            pb4 = self.pb[4].ap()
            for rowap, dst in ((var, "invb"), (mu, "m2b")):
                pb4B = self.take("pb4")
                p.op("T", lambda e, r=rowap: e.matmul(
                    pb4[:, 0:FT2], orf, r, start=True, stop=True),
                    reads=[srB, self.B("c_ones_row_f32")], writes=[pb4B],
                    tag=f"f1{call}bc{t}_{dst}")
                dB = self.take(dst)
                p.op("V", lambda e, dst=dst: e.tensor_copy(
                    sb[dst].ap()[:, 0:FT2], pb4[:, 0:FT2]),
                    reads=[pb4B], writes=[dB], tag=f"f1{call}bcc{t}_{dst}")
            invb = sb["invb"].ap()[:, 0:FT2]
            m2bv = sb["m2b"].ap()[:, 0:FT2]
            tb16 = sb["tmpa"].ap().bitcast(BF16)[:, 0:FT2]
            for m in range(32):
                taB = self.take("tmpa")
                p.op("V", lambda e, m=m: e.tensor_tensor(
                    tb16, sb["h"].ap()[:, m, :], invb, ALU.mult),
                    reads=[hB, self.B("invb")], writes=[taB],
                    tag=f"f1{call}hap1{t}_{m}")
                p.op("V", lambda e: e.tensor_tensor(tb16, tb16, m2bv,
                                                    ALU.subtract),
                     reads=[taB, self.B("m2b")], writes=[taB],
                     tag=f"f1{call}hap2{t}_{m}")
                p.op("S", lambda e, m=m: e.activation(
                    sb["h"].ap()[:, m, :], tb16, AF.Identity,
                    bias=sb["lnb"].ap()[:, m:m + 1],
                    scale=sb["lnw"].ap()[:, m:m + 1]),
                    reads=[taB, self.B("lnw"), self.B("lnb")], writes=[hB],
                    tag=f"f1{call}hap3{t}_{m}")
            ytB = self.take("ytile")
            for m in range(NCH):
                q4 = 4 + m % 4
                pqB = self.takeQ(q4 // 2)
                for c in range(32):
                    p.op("T", lambda e, m=m, c=c, q4=q4: e.matmul(
                        pbig[:, q4 * 256:q4 * 256 + 256],
                        sb["wprq"].ap()[:, c, m * 128:(m + 1) * 128],
                        sb["h"].ap()[:, c, :], start=(c == 0), stop=(c == 31)),
                        reads=[hB, prB], writes=[pqB],
                        tag=f"f1{call}pr{t}_{m}")
                p.op("V", lambda e, m=m, q4=q4: e.scalar_tensor_tensor(
                    sb["ytile"].ap()[:, m, :], pbig[:, q4 * 256:q4 * 256 + 256],
                    0.5, sb["xtile"].ap()[:, m, :], ALU.mult, ALU.add),
                    reads=[pqB, xtB], writes=[ytB], tag=f"f1{call}ev{t}_{m}")
            p.op("Q", lambda e, ts_=ts_: e.dma_start(
                xs_dst.rearrange("(c p) t -> p c t", p=128)[:, :, ts_],
                sb["ytile"].ap()),
                reads=[ytB], writes=[xdB], dma=True, tag=f"f1{call}st{t}")
            self.pump(2)
        self.xloc = "xs_b" if call == "a" else "y"

    def phase_finish(self):
        p, d, sb = self.p, self.d, self.sb
        if self.xloc == "y":
            p.wait_event("Q", self.B("y").w[0])
            return
        if self.xloc == "x":
            ev = p.op("Q", lambda e: e.dma_start(
                d["y"].rearrange("(c p) t -> p c t", p=128), sb["x"].ap()),
                reads=[self.B("x")], writes=[self.B("y")], dma=True, tag="finx")
        elif self.xloc == "fout":
            ev = p.op("Q", lambda e: e.dma_start(
                d["y"].rearrange("(c p) t -> p c t", p=128), sb["fout"].ap()),
                reads=[self.B("fout")], writes=[self.B("y")], dma=True,
                tag="finf")
        else:
            ev = p.op("Q", lambda e, s=self.xloc: e.dma_start(
                d["y"].ap(), d[s].ap()),
                reads=[self.B(self.xloc)], writes=[self.B("y")], dma=True,
                tag="find")
        p.wait_event("Q", ev)

    def build(self):
        self.phase_init()
        if "qkv1" in self.phases:
            self.emit_bvb()
        for ph in self.phases:
            getattr(self, "phase_" + ph)()
        self.pump_all()
        if self.phases and self.phases[-1] in ("ln1", "ln3"):
            self.xloc = "fout"
        self.phase_finish()
        return self.nc


def kernel(**inputs):
    cores = prep_inputs(inputs)
    b = Builder(_PHASES)
    nc = b.build()
    b.p.emit()
    res = run_bass_kernel_spmd(nc, cores, core_ids=list(range(NCORES)))
    out = np.stack([np.ascontiguousarray(np.asarray(r["y"]).T)
                    for r in res.results])
    return out.astype(np.float32)


# version 04
# speedup vs baseline: 1.7382x; 1.0011x over previous
import math
import numpy as np
import ml_dtypes

import concourse.bass as bass
import concourse.mybir as mybir
from concourse.alu_op_type import AluOpType as ALU
from concourse.bass_utils import run_bass_kernel_spmd

F32 = mybir.dt.float32
BF16 = mybir.dt.bfloat16
AF = mybir.ActivationFunctionType
AX = mybir.AxisListType

NCORES = 8
SEQ = 2048
D = 1024
NCH = 8
HEADS = 16
HD = 64
GRP = 512
NGRP = 4
FT1 = 512
FT2 = 256
NT2 = SEQ // FT2
QT = 128
NQT = 16

ALL_PHASES = ["ln1", "ffn0a", "qkv0", "attn0", "ln3", "ffn0b",
              "ffn1a", "qkv1", "attn1", "ffn1b"]
_PHASES = None
_SAME_ENGINE_SEM = True
_FFN0_NCYC = 3


class Buf:
    __slots__ = ("name", "w", "r")

    def __init__(self, name):
        self.name = name
        self.w = []
        self.r = []


class Op:
    __slots__ = ("fn", "waits", "stream", "seq", "eng", "tag")

    def __init__(self, fn, waits, stream, seq, eng, tag):
        self.fn = fn
        self.waits = waits
        self.stream = stream
        self.seq = seq
        self.eng = eng
        self.tag = tag


class Prog:
    ENGS = ("T", "V", "S", "G", "Q")
    DMA_SLOTS = {"Q": 12, "G": 6, "S": 4, "V": 2, "T": 2}

    def __init__(self, nc):
        self.nc = nc
        self.eng = {"T": nc.tensor, "V": nc.vector, "S": nc.scalar,
                    "G": nc.gpsimd, "Q": nc.sync}
        self.ops = []
        self.seq = {}
        self.waited = {e: {} for e in self.ENGS}
        self.needed = {}
        self.dmacnt = {}

    def op(self, eng, fn, reads=(), writes=(), dma=False, tag=""):
        if dma:
            i = self.dmacnt.get(eng, 0)
            self.dmacnt[eng] = i + 1
            nslots = self.DMA_SLOTS[eng]
            stream = f"{eng}d{i % nslots}"
            s = i // nslots
        else:
            stream = eng
            s = self.seq.get(stream, 0)
        deps = []
        for b in reads:
            deps += b.w
        for b in writes:
            deps += b.w
            deps += b.r
        if dma and s > 0:
            deps.append((stream, s - 1))
        waits = []
        wmap = self.waited[eng]
        for (ps, pq) in deps:
            if ps == "T" and eng == "T" and not dma:
                continue
            if (not _SAME_ENGINE_SEM) and ps == eng and not dma:
                continue
            if wmap.get(ps, -1) < pq:
                wmap[ps] = pq
                self.needed.setdefault(ps, set()).add(pq)
                waits.append((ps, pq))
        self.seq[stream] = s + 1
        self.ops.append(Op(fn, waits, stream, s, eng, tag))
        if dma:
            self.needed.setdefault(stream, set()).add(s)
        ev = (stream, s)
        for b in writes:
            b.w = [ev]
            b.r = []
        for b in reads:
            b.r.append(ev)
        return ev

    def wait_event(self, eng, ev):
        (ps, pq) = ev
        self.needed.setdefault(ps, set()).add(pq)
        self.ops.append(Op(None, [(ps, pq)], None, None, eng, "finwait"))

    def emit(self):
        nc = self.nc
        rank = {}
        sems = {}
        for stream, seqs in self.needed.items():
            order = sorted(seqs)
            step = 16 if len(stream) > 1 else 1
            rank[stream] = {q: (i + 1) * step for i, q in enumerate(order)}
            sems[stream] = nc.alloc_semaphore(f"sem_{stream}")
        for o in self.ops:
            eng = self.eng[o.eng]
            for (ps, pq) in o.waits:
                eng.wait_ge(sems[ps], rank[ps][pq])
            if o.fn is None:
                continue
            inst = o.fn(eng)
            if o.tag:
                try:
                    inst.annotate(o.tag)
                except Exception:
                    pass
            if o.stream in rank and o.seq in rank[o.stream]:
                step = 16 if len(o.stream) > 1 else 1
                inst.then_inc(sems[o.stream], step)
        return len(self.ops)


def _rope_perm():
    pi = np.zeros(D, dtype=np.int64)
    for h in range(HEADS):
        for i in range(32):
            pi[h * 64 + i] = h * 64 + 2 * i
            pi[h * 64 + 32 + i] = h * 64 + 2 * i + 1
    return pi


def _vec_tile(v):
    return np.ascontiguousarray(np.asarray(v, dtype=np.float32).reshape(-1, 128).T)


def prep_inputs(inp):
    f32 = np.float32
    pi = _rope_perm()
    g = {}
    g["l0wq"] = np.ascontiguousarray(inp["L0_attn_Wq"], dtype=f32)
    g["l0wk"] = np.ascontiguousarray(inp["L0_attn_Wk"], dtype=f32)
    g["l0wv"] = np.ascontiguousarray(inp["L0_attn_Wv"], dtype=f32)
    g["l0wo"] = np.ascontiguousarray(inp["L0_attn_Wo"], dtype=f32)
    for k in ("cn_w1", "cn_w2", "cn_w3", "cn_w4", "gc_w1", "gc_w2", "gc_w3"):
        g[k] = np.ascontiguousarray(inp["L0_ffn_" + k], dtype=f32)
    g["cn_b1"] = _vec_tile(inp["L0_ffn_cn_b1"])
    g["cn_b2"] = _vec_tile(inp["L0_ffn_cn_b2"])
    g["cn_b3"] = _vec_tile(inp["L0_ffn_cn_b3"])
    g["cn_b4"] = np.asarray(inp["L0_ffn_cn_b4"], dtype=f32).reshape(1, 1)
    g["gc_b1"] = _vec_tile(inp["L0_ffn_gc_b1"])
    g["gc_b2"] = _vec_tile(inp["L0_ffn_gc_b2"])
    g["gc_b3"] = np.asarray(inp["L0_ffn_gc_b3"], dtype=f32).reshape(1, 1)
    g["arange"] = np.asarray(inp["L0_ffn_adaptive_range"], dtype=f32).reshape(1, 1)
    g["l1wq"] = np.ascontiguousarray(np.asarray(inp["L1_attn_wq"]).T[:, pi], dtype=f32)
    g["l1wk"] = np.ascontiguousarray(np.asarray(inp["L1_attn_wk"]).T[:, pi], dtype=f32)
    g["l1wv"] = np.ascontiguousarray(np.asarray(inp["L1_attn_wv"]).T, dtype=f32)
    g["l1wo"] = np.ascontiguousarray(np.asarray(inp["L1_attn_wo"]).T, dtype=f32)
    g["l1bq"] = _vec_tile(np.asarray(inp["L1_attn_bq"])[pi])
    g["l1bk"] = _vec_tile(np.asarray(inp["L1_attn_bk"])[pi])
    g["l1bv"] = np.asarray(inp["L1_attn_bv"], dtype=f32).reshape(1, D)
    g["l1bo"] = _vec_tile(inp["L1_attn_bo"])
    g["wfcT"] = np.ascontiguousarray(np.asarray(inp["L1_ffn_W_fc"]).T, dtype=f32)
    g["wprT"] = np.ascontiguousarray(np.asarray(inp["L1_ffn_W_proj"]).T, dtype=f32)
    g["lnw"] = _vec_tile(inp["L1_ffn_ln_w"])
    g["lnb"] = _vec_tile(inp["L1_ffn_ln_b"])
    x = np.asarray(inp["L0_attn_x"], dtype=f32)
    cores = []
    for i in range(NCORES):
        m = dict(g)
        m["xT"] = np.ascontiguousarray(x[i].T)
        cores.append(m)
    return cores


def make_consts():
    c = {}
    c["ident"] = np.eye(128, dtype=ml_dtypes.bfloat16)
    ps = np.zeros((128, 128), dtype=np.float64)
    for h in range(2):
        for i in range(32):
            e = h * 64 + i
            o = h * 64 + 32 + i
            ps[o, e] = -1.0
            ps[e, o] = 1.0
    c["pswap"] = ps.astype(ml_dtypes.bfloat16)
    m = np.where(np.arange(128)[None, :] > np.arange(128)[:, None], -1e30, 0.0)
    c["maskdiag"] = m.astype(np.float32)
    c["ones_col"] = np.ones((128, 1), dtype=ml_dtypes.bfloat16)
    c["ones_row"] = np.ones((1, 128), dtype=ml_dtypes.bfloat16)
    c["ones_row_f32"] = np.ones((1, 128), dtype=np.float32)
    c["ones_col_f32"] = np.ones((128, 1), dtype=np.float32)
    i32 = np.arange(32, dtype=np.float64)
    rate = np.exp(-2.0 * i32 / HD * math.log(10000.0))
    pos = np.arange(SEQ, dtype=np.float64)
    th = rate[:, None] * pos[None, :]
    c["ropecos"] = np.tile(np.cos(th), (4, 1)).astype(np.float32)
    c["ropesin"] = np.tile(np.sin(th), (4, 1)).astype(np.float32)
    hs = np.zeros((128, NCH, HEADS), dtype=ml_dtypes.bfloat16)
    for ch in range(NCH):
        for p in range(128):
            hs[p, ch, (ch * 128 + p) // HD] = 1
    c["headsel"] = hs
    return c


GROUPS = [
    ["lnx_a", "lnx_b", "lnx_half", "cosb", "sinb", "outv", "diag", "xres",
     "xnew", "h1b", "h2b", "h3b", "gc2t", "lnx_c", "xb2"],
    ["P", "PT", "xb", "fb16", "sq128", "sqA", "sqB", "gc1t"],
    ["x", "wprq", "wslotA", "wslotB", "xgrp", "wo_t"],
    ["fout", "q", "k", "wfcq"],
    ["v_tm", "cn1", "cn2", "cn3", "cn4", "h", "xtile", "ytile"],
    ["pQ0"], ["pQ1"], ["pQ2"], ["pQ3"],
    ["tmpa"], ["tmpb"], ["wstage0"], ["wstage1"],
]
GROUP_OF = {}
for _g in GROUPS:
    for _n in _g:
        GROUP_OF[_n] = _g


class Builder:
    def __init__(self, phases=None):
        self.phases = list(phases if phases is not None else ALL_PHASES)
        nc = bass.Bass("TRN2", target_bir_lowering=False)
        self.nc = nc
        self.p = Prog(nc)
        self.bufs = {}
        self.xloc = "x"
        self.wsvc = []
        self._declare_io()
        self._alloc_sbuf()
        self._alloc_psum()

    def B(self, name):
        if name not in self.bufs:
            self.bufs[name] = Buf(name)
        return self.bufs[name]

    def take(self, name):
        b = Buf(name)
        for n in GROUP_OF.get(name, [name]):
            o = self.bufs.get(n)
            if o is not None:
                b.w = b.w + o.w
                b.r = b.r + o.r
        self.bufs[name] = b
        return b

    def _declare_io(self):
        nc = self.nc
        self.d = {}
        d = self.d

        def T(n, s):
            d[n] = nc.dram_tensor(n, s, F32, kind="ExternalInput")
        T("xT", [D, SEQ])
        d["y"] = nc.dram_tensor("y", [D, SEQ], F32, kind="ExternalOutput")
        for n, s in [("l0wq", [D, D]), ("l0wk", [D, D]), ("l0wv", [D, D]),
                     ("l0wo", [D, D]), ("cn_w1", [D, D]), ("cn_w2", [D, 512]),
                     ("cn_w3", [512, 256]), ("cn_w4", [256, 1]),
                     ("gc_w1", [D, 512]), ("gc_w2", [512, 256]),
                     ("gc_w3", [256, 1]), ("cn_b1", [128, 8]),
                     ("cn_b2", [128, 4]), ("cn_b3", [128, 2]), ("cn_b4", [1, 1]),
                     ("gc_b1", [128, 4]), ("gc_b2", [128, 2]), ("gc_b3", [1, 1]),
                     ("arange", [1, 1]),
                     ("l1wq", [D, D]), ("l1wk", [D, D]), ("l1wv", [D, D]),
                     ("l1wo", [D, D]), ("l1bq", [128, 8]), ("l1bk", [128, 8]),
                     ("l1bv", [1, D]), ("l1bo", [128, 8]),
                     ("wfcT", [D, 4096]), ("wprT", [4096, D]),
                     ("lnw", [128, 32]), ("lnb", [128, 32])]:
            T(n, s)
        I = lambda n, s, dt: nc.dram_tensor(n, s, dt, kind="Internal")
        d["xs_a"] = I("xs_a", [D, SEQ], F32)
        d["xs_b"] = I("xs_b", [D, SEQ], F32)
        d["lnxs"] = I("lnxs", [D, SEQ], BF16)
        d["wfcq_d"] = I("wfcq_d", [D, 4096], BF16)
        d["wprq_d"] = I("wprq_d", [4096, D], BF16)
        for k, v in make_consts().items():
            d[k] = nc.inline_tensor(v, k)

    def _alloc_sbuf(self):
        nc = self.nc
        self.off = (int(nc.sbuf_base) + 31) // 32 * 32
        self.sb = {}
        self.sboff = {}

        def at(name, shape, dtype, off=None):
            nbytes = int(np.prod(shape[1:])) * (2 if dtype == BF16 else 4)
            if off is None:
                off = self.off
                self.off = off + ((nbytes + 31) // 32) * 32
            h = nc.alloc_sbuf_tensor_at(name, list(shape), dtype, offset=off)
            self.sb[name] = h
            self.sboff[name] = off
            return h

        at("ident", [128, 128], BF16)
        at("pswap", [128, 128], BF16)
        at("maskdiag", [128, 128], F32)
        at("ones_col", [128, 1], BF16)
        at("ones_row", [1, 128], BF16)
        at("ones_row_f32", [1, 128], F32)
        at("headsel", [128, NCH, HEADS], BF16)
        at("bias_cn1", [128, 8], F32)
        at("bias_cn2", [128, 4], F32)
        at("bias_cn3", [128, 2], F32)
        at("bias_gc1", [128, 4], F32)
        at("bias_gc2", [128, 2], F32)
        at("bias_l1q", [128, 8], F32)
        at("bias_l1k", [128, 8], F32)
        at("bias_l1o", [128, 8], F32)
        at("lnw", [128, 32], F32)
        at("lnb", [128, 32], F32)
        at("bvb", [128, D], BF16)
        at("scal", [128, 16], F32)
        at("srow", [1, 1024], F32)
        at("grow", [1, 64], F32)
        at("stats", [128, 8], F32)
        at("rinv", [128, 1], F32)
        at("negB", [128, 24], F32)
        at("invb", [128, 512], BF16)
        at("m2b", [128, 512], BF16)
        at("tmpa", [128, 512], F32)
        at("tmpb", [128, 512], F32)
        at("gc3t", [128, 2, 1], BF16)
        at("ones_col_f32", [128, 1], F32)
        at("gacc", [128, 64], F32)
        at("lnx_a", [128, NCH, GRP], BF16)
        at("lnx_b", [128, NCH, GRP], BF16)
        r1 = self.sboff["lnx_a"]
        at("lnx_half", [128, NCH, 2 * GRP], BF16, off=r1)
        at("cosb", [128, SEQ], BF16, off=r1)
        at("sinb", [128, SEQ], BF16, off=r1 + 8192)
        at("outv", [128, NCH, QT], BF16, off=r1)
        at("diag", [128, 128], BF16, off=r1 + 2048)
        at("xres", [128, NCH, QT], F32, off=r1 + 2432)
        at("xnew", [128, NCH, QT], F32, off=r1 + 2432 + 4096)
        at("h2b", [128, 4, FT1], BF16, off=r1)
        at("h3b", [128, 2, FT1], BF16, off=r1 + 4096)
        at("h1b", [128, NCH, FT1], BF16, off=r1 + 8192)
        at("P", [128, SEQ], BF16)
        at("PT", [128, SEQ], BF16)
        pP = self.sboff["P"]
        at("xb", [128, NCH, GRP], BF16, off=pP)
        at("fb16", [128, NCH, FT1], BF16, off=pP)
        at("sq128", [128, SEQ], BF16, off=pP)
        at("lnx_c", [128, NCH, FT2], BF16, off=r1)
        at("xb2", [128, NCH, FT2], BF16, off=r1 + 8192)
        at("gc1t", [128, NCH, 512], BF16, off=pP)
        at("gc2t", [128, 4, 256], BF16, off=self.sboff["lnx_a"])

        r2 = self.off
        at("x", [128, NCH, SEQ], F32)
        at("wprq", [128, 32, D], BF16, off=r2)
        at("wslotA", [128, NCH, D], BF16, off=r2)
        at("wslotB", [128, NCH, D], BF16, off=r2 + 16 * 1024)
        at("xgrp", [128, NCH, GRP], F32, off=r2 + 32 * 1024)
        at("wo_t", [128, NCH, D], BF16, off=r2 + 48 * 1024)
        r3 = self.off
        at("fout", [128, NCH, SEQ], F32)
        at("q", [128, NCH, SEQ], BF16, off=r3)
        at("k", [128, NCH, SEQ], BF16, off=r3 + 32 * 1024)
        at("wfcq", [128, NCH, 4096], BF16, off=r3)
        r4 = self.off
        at("v_tm", [128, NQT, D], BF16)
        at("cn1", [128, NCH, D], BF16, off=r4)
        at("cn2", [128, NCH, 512], BF16, off=r4 + 16 * 1024)
        at("cn3", [128, 4, 256], BF16, off=r4 + 24 * 1024)
        at("cn4", [128, 2, 1], BF16, off=r4 + 26 * 1024)
        at("h", [128, 32, FT2], BF16, off=r4)
        at("xtile", [128, NCH, FT2], F32, off=r4 + 16 * 1024)
        at("ytile", [128, NCH, FT2], F32, off=r4 + 24 * 1024)
        at("wstage", [128, 2, 1024], F32)

        limit = int(nc.sbuf_top)
        assert self.off <= limit, f"SBUF over budget: {self.off} > {limit}"

    def _alloc_psum(self):
        nc = self.nc
        self.pbig = nc.alloc_psum_tensor("pbig", [128, 2048], F32)
        self.pb = {i: nc.alloc_psum_tensor(f"pb{i}", [128, 512], F32)
                   for i in (4, 5, 6, 7)}

    def pQ(self, i):
        return self.B(f"pQ{i}")

    def takeQ(self, i):
        return self.take(f"pQ{i}")

    def pump(self, n):
        for _ in range(min(n, len(self.wsvc))):
            self.wsvc.pop(0)()

    def pump_all(self):
        while self.wsvc:
            self.wsvc.pop(0)()

    def _wpieces(self, K, M, cap=1024):
        kch = K // 128
        if M >= cap:
            for c in range(kch):
                for m0 in range(0, M, cap):
                    yield (c, c + 1, m0, min(M, m0 + cap))
        else:
            per = max(1, cap // M)
            for c0 in range(0, kch, per):
                yield (c0, min(kch, c0 + per), 0, M)

    def load_weight_cast(self, src, dst, K, M, ceng="S", tag=""):
        p, d, sb = self.p, self.d, self.sb
        dstB = self.B(dst)
        dap = sb[dst].ap()
        sap = d[src].rearrange("(c p) m -> p c m", p=128)
        for i, (c0, c1, m0, m1) in enumerate(self._wpieces(K, M)):
            n = (c1 - c0) * (m1 - m0)
            half = i % 2
            stB = self.take(f"wstage{half}")
            st3 = sb["wstage"].ap()[:, half, 0:n].rearrange(
                "p (a b) -> p a b", b=m1 - m0)
            p.op("Q", lambda e, a=st3, s=sap[:, c0:c1, m0:m1]: e.dma_start(
                a, s, allow_slow_non_contiguous=True),
                writes=[stB], dma=True, tag=f"ldw_{src}_{i}{tag}")
            cp = (lambda e, o, a: e.copy(o, a)) if ceng == "S" else \
                 (lambda e, o, a: e.tensor_copy(o, a))
            p.op(ceng, lambda e, o=dap[:, c0:c1, m0:m1], a=st3, cp=cp: cp(e, o, a),
                 reads=[stB], writes=[dstB], tag=f"cast_{src}_{i}{tag}")

    def emit_ln(self, src_ap, src_buf, ntok, out_ap, out_buf, tag,
                xb_name="xb", sq_eng="S"):
        p, sb = self.p, self.sb
        n = ntok
        xb = sb[xb_name].ap()[:, :, 0:n]
        xbB = self.take(xb_name)
        p.op("V", lambda e, o=xb, s=src_ap: e.tensor_copy(o, s), reads=[src_buf],
             writes=[xbB], tag=f"{tag}cast")
        pm = self.pb[5].ap()
        pmB = self.take("pb5")
        oc = sb["ones_col"].ap()
        ocB = self.B("c_ones_col")
        for c in range(NCH):
            p.op("T", lambda e, c=c, n=n: e.matmul(pm[0:1, 0:n], oc, xb[:, c, :],
                                                   start=(c == 0), stop=(c == NCH - 1)),
                 reads=[xbB, ocB], writes=[pmB], tag=f"{tag}mmu")
        pv = self.pb[6].ap()
        pvB = self.take("pb6")
        for c in range(NCH):
            tnm = "tmpa" if c % 2 == 0 else "tmpb"
            sq = self.sb[tnm].ap().bitcast(BF16)[:, 0:n]
            sqB = self.take(tnm)
            if sq_eng == "S":
                p.op("S", lambda e, sq=sq, c=c: e.activation(
                    sq, xb[:, c, :], AF.Square),
                    reads=[xbB], writes=[sqB], tag=f"{tag}sq{c}")
            else:
                p.op("V", lambda e, sq=sq, c=c: e.tensor_tensor(
                    sq, xb[:, c, :], xb[:, c, :], ALU.mult),
                    reads=[xbB], writes=[sqB], tag=f"{tag}sq{c}")
            p.op("T", lambda e, sq=sq, c=c, n=n: e.matmul(
                pv[0:1, 0:n], oc, sq, start=(c == 0), stop=(c == NCH - 1)),
                reads=[sqB, ocB], writes=[pvB], tag=f"{tag}mms")
        sr = sb["srow"].ap()
        mu = sr[:, 0:n]
        var = sr[:, 512:512 + n]
        srB = self.take("srow")
        p.op("V", lambda e, mu=mu, n=n: e.tensor_scalar(
            mu, pm[0:1, 0:n], 1.0 / D, None, ALU.mult),
            reads=[pmB], writes=[srB], tag=f"{tag}mu")
        p.op("V", lambda e, var=var, n=n: e.tensor_scalar(
            var, pv[0:1, 0:n], 1.0 / D, None, ALU.mult),
            reads=[pvB], writes=[srB], tag=f"{tag}ex2")
        ta = sb["tmpa"].ap()[0:1, 0:n]
        taB = self.take("tmpa")
        p.op("V", lambda e, ta=ta, mu=mu: e.tensor_tensor(ta, mu, mu, ALU.mult),
             reads=[srB], writes=[taB], tag=f"{tag}musq")
        p.op("V", lambda e, var=var, ta=ta: e.tensor_tensor(var, var, ta,
                                                            ALU.subtract),
             reads=[srB, taB], writes=[srB], tag=f"{tag}var")
        eps = sb["scal"].ap()[0:1, 0:1]
        p.op("S", lambda e, var=var: e.activation(var, var, AF.Ln, bias=eps),
             reads=[srB, self.B("scal")], writes=[srB], tag=f"{tag}lnv")
        p.op("S", lambda e, var=var: e.activation(var, var, AF.Exp, scale=-0.5),
             reads=[srB], writes=[srB], tag=f"{tag}inv")
        p.op("V", lambda e, mu=mu, var=var: e.tensor_tensor(mu, mu, var, ALU.mult),
             reads=[srB], writes=[srB], tag=f"{tag}m2")
        orf = sb["ones_row_f32"].ap()
        orfB = self.B("c_ones_row_f32")
        pb4 = self.pb[4].ap()
        for rowap, dst in ((var, "invb"), (mu, "m2b")):
            pb4B = self.take("pb4")
            p.op("T", lambda e, r=rowap, n=n: e.matmul(pb4[:, 0:n], orf, r,
                                                       start=True, stop=True),
                 reads=[srB, orfB], writes=[pb4B], tag=f"{tag}bc_{dst}")
            dB = self.take(dst)
            p.op("V", lambda e, o=sb[dst].ap()[:, 0:n], n=n: e.tensor_copy(
                o, pb4[:, 0:n]),
                reads=[pb4B], writes=[dB], tag=f"{tag}bcc_{dst}")
        invb = sb["invb"].ap()[:, 0:n]
        m2bv = sb["m2b"].ap()[:, 0:n]
        tb16 = sb["tmpa"].ap().bitcast(BF16)[:, 0:n]
        for c in range(NCH):
            tB = self.take("tmpa")
            p.op("V", lambda e, c=c, t=tb16, s=src_ap[:, c, 0:n], i=invb:
                 e.tensor_tensor(t, s, i, ALU.mult),
                 reads=[src_buf, self.B("invb")], writes=[tB], tag=f"{tag}ap1_{c}")
            p.op("V", lambda e, c=c, o=out_ap[:, c, 0:n], t=tb16, m=m2bv:
                 e.tensor_tensor(o, t, m, ALU.subtract),
                 reads=[tB, self.B("m2b")], writes=[out_buf], tag=f"{tag}ap2_{c}")

    def phase_init(self):
        p, d, sb = self.p, self.d, self.sb
        loads = [("ident", "ident"), ("pswap", "pswap"), ("maskdiag", "maskdiag"),
                 ("headsel", "headsel"),
                 ("cn_b1", "bias_cn1"), ("cn_b2", "bias_cn2"),
                 ("cn_b3", "bias_cn3"), ("gc_b1", "bias_gc1"),
                 ("gc_b2", "bias_gc2"), ("l1bq", "bias_l1q"),
                 ("l1bk", "bias_l1k"), ("l1bo", "bias_l1o"),
                 ("lnw", "lnw"), ("lnb", "lnb")]
        for src, t in loads:
            p.op("Q", lambda e, s=src, t=t: e.dma_start(sb[t].ap(), d[s].ap()),
                 writes=[self.B(t)], dma=True, tag=f"ld_{t}")
        for src, t, bn in (("ones_col", "ones_col", "c_ones_col"),
                           ("ones_row", "ones_row", "c_ones_row"),
                           ("ones_row_f32", "ones_row_f32", "c_ones_row_f32"),
                           ("ones_col_f32", "ones_col_f32", "c_ones_col_f32")):
            p.op("Q", lambda e, s=src, t=t: e.dma_start(sb[t].ap(), d[s].ap()),
                 writes=[self.B(bn)], dma=True, tag=f"ld_{t}")
        p.op("Q", lambda e: e.dma_start(
            sb["x"].ap(), d["xT"].rearrange("(c p) t -> p c t", p=128)),
            writes=[self.take("x")], dma=True, tag="ld_x")
        p.op("V", lambda e: e.memset(sb["scal"].ap()[:, 0:1], 1e-5),
             writes=[self.B("scal")], tag="eps")
        for i, src in ((0, "arange"), (1, "cn_b4"), (2, "gc_b3")):
            p.op("Q", lambda e, i=i, s=src: e.dma_start(
                sb["grow"].ap()[0:1, i:i + 1], d[s].ap()),
                writes=[self.B("grow")], dma=True, tag=f"ld_{src}")
        self.load_cn()
        self.enqueue_quant()

    def load_cn(self):
        for nm in ("cn1", "cn2", "cn3", "cn4"):
            self.take(nm)
        self.load_weight_cast("cn_w1", "cn1", D, D)
        self.load_weight_cast("cn_w2", "cn2", D, 512)
        self.load_weight_cast("cn_w3", "cn3", 512, 256)
        self.load_weight_cast("cn_w4", "cn4", 256, 1)

    def load_gc(self):
        for nm in ("gc1t", "gc2t", "gc3t"):
            self.take(nm)
        self.load_weight_cast("gc_w1", "gc1t", D, 512)
        self.load_weight_cast("gc_w2", "gc2t", 512, 256)
        self.load_weight_cast("gc_w3", "gc3t", 256, 1)

    def enqueue_quant(self):
        p, d, sb = self.p, self.d, self.sb
        pb7 = self.pb[7].ap()
        tA = sb["tmpa"].ap().bitcast(BF16)
        tB_ = sb["tmpb"].ap().bitcast(BF16)

        def g1piece(src, gcol, idx, c, m0):
            def go():
                half = idx % 2
                stB = self.take(f"wstage{half}")
                ap_st = sb["wstage"].ap()[:, half, :]
                st3 = sb["wstage"].ap()[:, half, :].rearrange(
                    "p (a b) -> p a b", b=128)
                sap = d[src].rearrange("(c p) m -> p c m", p=128)
                p.op("Q", lambda e, a=ap_st, s=sap[:, c:c + 1, m0:m0 + 1024]:
                     e.dma_start(a, s.rearrange("p c m -> p (c m)")),
                     writes=[stB], dma=True, tag=f"g1_{src}_{idx}")
                t8 = sb["tmpa"].ap()[:, 0:8]
                aB = self.take("tmpa")
                p.op("V", lambda e, s3=st3, t8=t8: e.reduce_sum(
                    t8, s3, AX.X, apply_absolute_value=True),
                    reads=[stB], writes=[aB], tag=f"g1r1_{src}_{idx}")
                gaB = self.B("gacc")
                p.op("V", lambda e, t8=t8, gcol=gcol: e.reduce_sum(
                    sb["gacc"].ap()[:, gcol:gcol + 1], t8, AX.X),
                    reads=[aB], writes=[gaB], tag=f"g1r2_{src}_{idx}")
            self.wsvc.append(go)

        for gbase, src, K, M in ((0, "wfcT", D, 4096), (32, "wprT", 4096, D)):
            pieces = list(self._wpieces(K, M))
            assert len(pieces) == 32
            for idx, (c0, c1, m0, m1) in enumerate(pieces):
                g1piece(src, gbase + idx, idx, c0, m0)

        def finalize():
            gr = sb["grow"].ap()
            grB = self.B("grow")
            gaB = self.B("gacc")
            ga3 = sb["gacc"].ap().rearrange("p (a b) -> p a b", b=8)
            for gbase, col, pcol in ((0, 8, 0), (32, 16, 1)):
                taB = self.take("tmpa")
                p.op("V", lambda e, gbase=gbase: e.reduce_sum(
                    sb["tmpa"].ap()[:, 0:4], ga3[:, gbase // 8:gbase // 8 + 4, :],
                    AX.X),
                    reads=[gaB], writes=[taB], tag=f"gf1_{col}")
                p.op("V", lambda e: e.reduce_sum(
                    sb["tmpa"].ap()[:, 8:9], sb["tmpa"].ap()[:, 0:4], AX.X),
                    reads=[taB], writes=[taB], tag=f"gf2_{col}")
                pb7B = self.take("pb7")
                p.op("T", lambda e, pcol=pcol: e.matmul(
                    pb7[0:1, pcol:pcol + 1], sb["tmpa"].ap()[:, 8:9],
                    sb["ones_col_f32"].ap(), start=True, stop=True),
                    reads=[taB, self.B("c_ones_col_f32")], writes=[pb7B],
                    tag=f"gf3_{col}")
                p.op("V", lambda e, col=col, pcol=pcol: e.tensor_scalar(
                    gr[:, col + 1:col + 2], pb7[0:1, pcol:pcol + 1],
                    1.0 / (4096.0 * 1024.0), 1e-6, ALU.mult, ALU.add),
                    reads=[pb7B], writes=[grB], tag=f"gam{col}")
                p.op("V", lambda e, col=col: e.tensor_scalar(
                    gr[:, col + 2:col + 3], gr[:, col + 1:col + 2],
                    0.5, None, ALU.mult),
                    reads=[grB], writes=[grB], tag=f"gthr{col}")
            pb4 = self.pb[4].ap()
            for src_col, dst in ((10, 1), (18, 3)):
                pb4B = self.take("pb4")
                p.op("T", lambda e, sc_=src_col: e.matmul(
                    pb4[:, 0:1], sb["ones_row_f32"].ap(), gr[:, sc_:sc_ + 1],
                    start=True, stop=True),
                    reads=[grB, self.B("c_ones_row_f32")], writes=[pb4B],
                    tag=f"gbc{src_col}")
                sclB = self.B("scal")
                p.op("V", lambda e, dst=dst: e.tensor_copy(
                    sb["scal"].ap()[:, dst:dst + 1], pb4[:, 0:1]),
                    reads=[pb4B], writes=[sclB], tag=f"gbcc{dst}")
                p.op("V", lambda e, dst=dst: e.tensor_scalar(
                    sb["scal"].ap()[:, dst + 1:dst + 2],
                    sb["scal"].ap()[:, dst:dst + 1], -1.0, None, ALU.mult),
                    reads=[sclB], writes=[sclB], tag=f"gneg{dst}")
        self.wsvc.append(finalize)

        def qpiece(src, dst, thr_col, idx, c, m0):
            def go():
                half = idx % 2
                stB = self.take(f"wstage{half}")
                ap_st = sb["wstage"].ap()[:, half, :]
                sap = d[src].rearrange("(c p) m -> p c m", p=128)
                dap = d[dst].rearrange("(c p) m -> p c m", p=128)
                p.op("Q", lambda e, a=ap_st, s=sap[:, c:c + 1, m0:m0 + 1024]:
                     e.dma_start(a, s.rearrange("p c m -> p (c m)")),
                     writes=[stB], dma=True, tag=f"q2_{src}_{idx}")
                thr = sb["scal"].ap()[:, thr_col:thr_col + 1]
                nthr = sb["scal"].ap()[:, thr_col + 1:thr_col + 2]
                aB = self.take("tmpa")
                p.op("V", lambda e, a=ap_st, nthr=nthr: e.tensor_scalar(
                    tA, a, nthr, None, ALU.is_lt),
                    reads=[stB, self.B("scal")], writes=[aB],
                    tag=f"q2lt_{src}_{idx}")
                bB = self.take("tmpb")
                p.op("V", lambda e, a=ap_st, thr=thr: e.scalar_tensor_tensor(
                    tB_, a, thr, tA, ALU.is_gt, ALU.subtract),
                    reads=[stB, aB, self.B("scal")], writes=[bB],
                    tag=f"q2q_{src}_{idx}")
                p.op("G", lambda e, o=dap[:, c:c + 1, m0:m0 + 1024]: e.dma_start(
                    o.rearrange("p c m -> p (c m)"), tB_),
                    reads=[bB], writes=[self.B(dst + "_dram")], dma=True,
                    tag=f"q2st_{src}_{idx}")
            self.wsvc.append(go)

        for src, dst, thr_col, K, M in (("wfcT", "wfcq_d", 1, D, 4096),
                                        ("wprT", "wprq_d", 3, 4096, D)):
            for idx, (c0, c1, m0, m1) in enumerate(self._wpieces(K, M)):
                qpiece(src, dst, thr_col, idx, c0, m0)

    def phase_ln1(self):
        sb = self.sb
        self.take("fout")
        for g in range(NGRP):
            self.emit_ln(sb["x"].ap()[:, :, g * GRP:(g + 1) * GRP], self.B("x"),
                         GRP, sb["fout"].ap()[:, :, g * GRP:(g + 1) * GRP],
                         self.B("fout"), f"ln1g{g}")
            self.pump(10)

    def phase_ln3(self):
        p, d, sb = self.p, self.d, self.sb
        xB = self.take("x")
        p.op("Q", lambda e: e.dma_start(
            sb["x"].ap(), d["xs_b"].rearrange("(c p) t -> p c t", p=128)),
            reads=[self.B("xs_b")], writes=[xB], dma=True, tag="ld_x3")
        self.xloc = "x"
        self.load_cn()
        self.take("fout")
        for g in range(NGRP):
            self.emit_ln(sb["x"].ap()[:, :, g * GRP:(g + 1) * GRP], self.B("x"),
                         GRP, sb["fout"].ap()[:, :, g * GRP:(g + 1) * GRP],
                         self.B("fout"), f"ln3g{g}")
            self.pump(6)

    def phase_ffn0a(self):
        self.emit_ffn0("a", ncyc=_FFN0_NCYC)

    def phase_ffn0b(self):
        self.emit_ffn0("b")

    def emit_ffn0(self, call, ncyc=3, ntile=None):
        p, sb = self.p, self.sb
        self.load_gc()
        foB = self.B("fout")
        gr = sb["grow"].ap()
        grB = self.B("grow")
        gcx = sb["scal"].ap()[:, 8:16]
        sclB = self.B("scal")
        p.op("V", lambda e: e.reduce_sum(gcx, sb["fout"].ap(), AX.X),
             reads=[foB], writes=[sclB], tag=f"f0{call}_gcx")
        gxb = sb["tmpa"].ap().bitcast(BF16)[:, 0:8]
        gxbB = self.take("tmpa")
        p.op("V", lambda e: e.tensor_scalar(gxb, gcx, 1.0 / SEQ, None, ALU.mult),
             reads=[sclB], writes=[gxbB], tag=f"f0{call}_gcxb")
        pb6 = self.pb[6].ap()
        pb6B = self.take("pb6")
        for m in range(4):
            for c in range(NCH):
                p.op("T", lambda e, m=m, c=c: e.matmul(
                    pb6[:, m:m + 1], sb["gc1t"].ap()[:, c, m * 128:(m + 1) * 128],
                    gxb[:, c:c + 1], start=(c == 0), stop=(c == NCH - 1)),
                    reads=[gxbB, self.B("gc1t")], writes=[pb6B],
                    tag=f"f0{call}_g1mm")
        gh1 = sb["tmpb"].ap().bitcast(BF16)[:, 0:4]
        gh1B = self.take("tmpb")
        for m in range(4):
            p.op("S", lambda e, m=m: e.activation(
                gh1[:, m:m + 1], pb6[:, m:m + 1], AF.Gelu,
                bias=sb["bias_gc1"].ap()[:, m:m + 1]),
                reads=[pb6B, self.B("bias_gc1")], writes=[gh1B],
                tag=f"f0{call}_g1a")
        pb6B = self.take("pb6")
        for m in range(2):
            for c in range(4):
                p.op("T", lambda e, m=m, c=c: e.matmul(
                    pb6[:, 8 + m:9 + m],
                    sb["gc2t"].ap()[:, c, m * 128:(m + 1) * 128],
                    gh1[:, c:c + 1], start=(c == 0), stop=(c == 3)),
                    reads=[gh1B, self.B("gc2t")], writes=[pb6B],
                    tag=f"f0{call}_g2mm")
        gh2 = sb["tmpb"].ap().bitcast(BF16)[:, 8:10]
        gh2B = self.take("tmpb")
        for m in range(2):
            p.op("S", lambda e, m=m: e.activation(
                gh2[:, m:m + 1], pb6[:, 8 + m:9 + m], AF.Gelu,
                bias=sb["bias_gc2"].ap()[:, m:m + 1]),
                reads=[pb6B, self.B("bias_gc2")], writes=[gh2B],
                tag=f"f0{call}_g2a")
        pb6B = self.take("pb6")
        for c in range(2):
            p.op("T", lambda e, c=c: e.matmul(
                pb6[0:1, 16:17], sb["gc3t"].ap()[:, c, 0:1], gh2[:, c:c + 1],
                start=(c == 0), stop=(c == 1)),
                reads=[gh2B, self.B("gc3t")], writes=[pb6B], tag=f"f0{call}_g3mm")
        p.op("V", lambda e: e.tensor_scalar(gr[:, 3:4], gr[:, 2:3], 0.5, None,
                                            ALU.mult),
             reads=[grB], writes=[grB], tag=f"f0{call}_hb3")
        p.op("V", lambda e: e.tensor_scalar(gr[:, 4:5], gr[:, 1:2], 0.5, None,
                                            ALU.mult),
             reads=[grB], writes=[grB], tag=f"f0{call}_hb4")
        p.op("S", lambda e: e.activation(gr[:, 5:6], pb6[0:1, 16:17], AF.Tanh,
                                         bias=gr[:, 3:4], scale=0.5),
             reads=[pb6B, grB], writes=[grB], tag=f"f0{call}_gtanh")
        p.op("V", lambda e: e.tensor_scalar(gr[:, 6:7], gr[:, 5:6], 0.5, 0.5,
                                            ALU.mult, ALU.add),
             reads=[grB], writes=[grB], tag=f"f0{call}_gf01")
        p.op("V", lambda e: e.tensor_scalar(gr[:, 7:8], gr[:, 6:7], gr[:, 0:1],
                                            1.4, ALU.mult, ALU.mult),
             reads=[grB], writes=[grB], tag=f"f0{call}_u1")
        p.op("V", lambda e: e.tensor_scalar(gr[:, 26:27], gr[:, 0:1], -0.7, 0.1,
                                            ALU.mult, ALU.add),
             reads=[grB], writes=[grB], tag=f"f0{call}_u2")
        p.op("V", lambda e: e.tensor_tensor(gr[:, 24:25], gr[:, 7:8],
                                            gr[:, 26:27], ALU.add),
             reads=[grB], writes=[grB], tag=f"f0{call}_A")
        p.op("V", lambda e: e.tensor_scalar(gr[:, 25:26], gr[:, 0:1], 0.3, None,
                                            ALU.mult),
             reads=[grB], writes=[grB], tag=f"f0{call}_C")
        if ncyc == 0:
            return
        pbig = self.pbig.ap()
        signs = [(ALU.add, ALU.subtract), (ALU.add, ALU.subtract),
                 (ALU.subtract, ALU.add), (ALU.subtract, ALU.add)]
        fb = sb["fb16"].ap()
        h1 = sb["h1b"].ap()
        h2 = sb["h2b"].ap()
        h3 = sb["h3b"].ap()
        for cyc in range(ncyc):
            for t in range(ntile or (SEQ // FT1)):
                ts_ = slice(t * FT1, (t + 1) * FT1)
                fbB = self.take("fb16")
                p.op("S", lambda e, ts_=ts_: e.activation(
                    fb, sb["fout"].ap()[:, :, ts_], AF.Copy),
                    reads=[foB], writes=[fbB], tag=f"f0{call}cast{cyc}_{t}")
                h1B = self.take("h1b")
                for m in range(NCH):
                    qr = m % 2
                    pqB = self.takeQ(qr)
                    for c in range(NCH):
                        p.op("T", lambda e, m=m, c=c, qr=qr: e.matmul(
                            pbig[:, qr * 512:qr * 512 + 512],
                            sb["cn1"].ap()[:, c, m * 128:(m + 1) * 128],
                            fb[:, c, :], start=(c == 0), stop=(c == NCH - 1)),
                            reads=[fbB, self.B("cn1")], writes=[pqB],
                            tag=f"f0{call}h1mm{cyc}_{t}_{m}")
                    p.op("S", lambda e, m=m, qr=qr: e.activation(
                        h1[:, m, :], pbig[:, qr * 512:qr * 512 + 512], AF.Gelu,
                        bias=sb["bias_cn1"].ap()[:, m:m + 1]),
                        reads=[pqB, self.B("bias_cn1")], writes=[h1B],
                        tag=f"f0{call}h1a{cyc}_{t}_{m}")
                h2B = self.take("h2b")
                for m in range(4):
                    pqB = self.takeQ(2)
                    for c in range(NCH):
                        p.op("T", lambda e, m=m, c=c: e.matmul(
                            pbig[:, 1024:1536],
                            sb["cn2"].ap()[:, c, m * 128:(m + 1) * 128],
                            h1[:, c, :], start=(c == 0), stop=(c == NCH - 1)),
                            reads=[h1B, self.B("cn2")], writes=[pqB],
                            tag=f"f0{call}h2mm{cyc}_{t}_{m}")
                    p.op("S", lambda e, m=m: e.activation(
                        h2[:, m, :], pbig[:, 1024:1536], AF.Gelu,
                        bias=sb["bias_cn2"].ap()[:, m:m + 1]),
                        reads=[pqB, self.B("bias_cn2")], writes=[h2B],
                        tag=f"f0{call}h2a{cyc}_{t}_{m}")
                h3B = self.take("h3b")
                for m in range(2):
                    pqB = self.takeQ(3)
                    for c in range(4):
                        p.op("T", lambda e, m=m, c=c: e.matmul(
                            pbig[:, 1536:2048],
                            sb["cn3"].ap()[:, c, m * 128:(m + 1) * 128],
                            h2[:, c, :], start=(c == 0), stop=(c == 3)),
                            reads=[h2B, self.B("cn3")], writes=[pqB],
                            tag=f"f0{call}h3mm{cyc}_{t}_{m}")
                    p.op("S", lambda e, m=m: e.activation(
                        h3[:, m, :], pbig[:, 1536:2048], AF.Gelu,
                        bias=sb["bias_cn3"].ap()[:, m:m + 1]),
                        reads=[pqB, self.B("bias_cn3")], writes=[h3B],
                        tag=f"f0{call}h3a{cyc}_{t}_{m}")
                pb5 = self.pb[5].ap()
                pzB = self.take("pb5")
                for c in range(2):
                    p.op("T", lambda e, c=c: e.matmul(
                        pb5[0:1, 0:FT1], sb["cn4"].ap()[:, c, 0:1], h3[:, c, :],
                        start=(c == 0), stop=(c == 1)),
                        reads=[h3B, self.B("cn4")], writes=[pzB],
                        tag=f"f0{call}zmm{cyc}_{t}")
                srB = self.take("srow")
                tr = sb["srow"].ap()[:, 0:FT1]
                p.op("S", lambda e: e.activation(tr, pb5[0:1, 0:FT1], AF.Tanh,
                                                 bias=gr[:, 4:5], scale=0.5),
                     reads=[pzB, grB], writes=[srB], tag=f"f0{call}ctanh{cyc}_{t}")
                crow = sb["srow"].ap().bitcast(BF16)[0:1, 1024:1024 + FT1]
                p.op("V", lambda e: e.tensor_scalar(
                    crow, tr, gr[:, 25:26], gr[:, 24:25], ALU.mult, ALU.add),
                    reads=[srB, grB], writes=[srB], tag=f"f0{call}crow{cyc}_{t}")
                pb6 = self.pb[6].ap()
                pcB = self.take("pb6")
                p.op("T", lambda e: e.matmul(pb6[:, 0:FT1], sb["ones_row"].ap(),
                                             crow, start=True, stop=True),
                     reads=[srB, self.B("c_ones_row")], writes=[pcB],
                     tag=f"f0{call}cbc{cyc}_{t}")
                for pr in range(4):
                    sA, sB_ = signs[pr]
                    fA = sb["fout"].ap()[:, pr, ts_]
                    fB_ = sb["fout"].ap()[:, pr + 4, ts_]
                    m1B = self.take("tmpa")
                    p.op("V", lambda e, fB_=fB_: e.tensor_tensor(
                        sb["tmpa"].ap(), pb6[:, 0:FT1], fB_, ALU.mult),
                        reads=[pcB, foB], writes=[m1B],
                        tag=f"f0{call}m1_{cyc}_{t}_{pr}")
                    m2B = self.take("tmpb")
                    p.op("V", lambda e, fA=fA: e.tensor_tensor(
                        sb["tmpb"].ap(), pb6[:, 0:FT1], fA, ALU.mult),
                        reads=[pcB, foB], writes=[m2B],
                        tag=f"f0{call}m2_{cyc}_{t}_{pr}")
                    p.op("V", lambda e, fA=fA, sA=sA: e.tensor_tensor(
                        fA, fA, sb["tmpa"].ap(), sA),
                        reads=[m1B, foB], writes=[foB],
                        tag=f"f0{call}uA_{cyc}_{t}_{pr}")
                    p.op("V", lambda e, fB_=fB_, sB_=sB_: e.tensor_tensor(
                        fB_, fB_, sb["tmpb"].ap(), sB_),
                        reads=[m2B, foB], writes=[foB],
                        tag=f"f0{call}uB_{cyc}_{t}_{pr}")
                self.pump(4)
        xB = self.B("x")
        for t in range(SEQ // FT1):
            ts_ = slice(t * FT1, (t + 1) * FT1)
            p.op("V", lambda e, ts_=ts_: e.scalar_tensor_tensor(
                sb["x"].ap()[:, :, ts_], sb["fout"].ap()[:, :, ts_],
                0.5, sb["x"].ap()[:, :, ts_], ALU.mult, ALU.add),
                reads=[foB, xB], writes=[xB], tag=f"f0{call}res{t}")

    def phase_qkv0(self):
        self.emit_qkv(0)

    def phase_qkv1(self):
        self.emit_qkv(1)

    def emit_qkv(self, layer):
        p, d, sb = self.p, self.d, self.sb
        for g in range(NGRP):
            gs = slice(g * GRP, (g + 1) * GRP)
            if layer == 0:
                src_ap = sb["x"].ap()[:, :, gs]
                srcB = self.B("x")
            else:
                xgB = self.take("xgrp")
                p.op("Q", lambda e, gs=gs: e.dma_start(
                    sb["xgrp"].ap(),
                    d["xs_b"].rearrange("(c p) t -> p c t", p=128)[:, :, gs]),
                    reads=[self.B("xs_b")], writes=[xgB], dma=True,
                    tag=f"q{layer}ldx{g}")
                src_ap = sb["xgrp"].ap()
                srcB = xgB
            nm = "lnx_a" if g % 2 == 0 else "lnx_b"
            outB = self.take(nm)
            self.emit_ln(src_ap, srcB, GRP, sb[nm].ap(), outB, f"q{layer}ln{g}")
            p.op("Q", lambda e, gs=gs, nm=nm: e.dma_start(
                d["lnxs"].rearrange("(c p) t -> p c t", p=128)[:, :, gs],
                sb[nm].ap()),
                reads=[outB], writes=[self.B("lnxs")], dma=True,
                tag=f"q{layer}stln{g}")
            if layer == 0:
                p.op("Q", lambda e, gs=gs: e.dma_start(
                    d["xs_a"].rearrange("(c p) t -> p c t", p=128)[:, :, gs],
                    sb["x"].ap()[:, :, gs]),
                    reads=[self.B("x")], writes=[self.B("xs_a")], dma=True,
                    tag=f"q{layer}dumpx{g}")
            self.pump(8)
        self.xloc = "xs_a" if layer == 0 else "xs_b"
        wsrc = {0: {"q": "l0wq", "k": "l0wk", "v": "l0wv", "o": "l0wo"},
                1: {"q": "l1wq", "k": "l1wk", "v": "l1wv", "o": "l1wo"}}[layer]
        qB = self.take("q")
        kB = self.take("k")
        vB = self.take("v_tm")
        pbig = self.pbig.ap()
        for half in range(2):
            lhB = self.take("lnx_half")
            p.op("Q", lambda e, half=half: e.dma_start(
                sb["lnx_half"].ap(),
                d["lnxs"].rearrange("(c p) t -> p c t", p=128)
                [:, :, half * 1024:(half + 1) * 1024]),
                reads=[self.B("lnxs")], writes=[lhB], dma=True,
                tag=f"q{layer}ldln{half}")
            lh = sb["lnx_half"].ap()
            for wi, w in enumerate(("q", "k", "v")):
                slot = ("wslotA", "wslotB")[wi % 2]
                slotB = self.take(slot)
                self.load_weight_cast(wsrc[w], slot, D, D, ceng="S",
                                      tag=f"_{layer}{half}")
                wap = sb[slot].ap()
                if w in ("q", "k"):
                    dst = sb[w].ap()
                    dstB = qB if w == "q" else kB
                    for m in range(NCH):
                        for ng in range(2):
                            qr = (m * 2 + ng) % 2
                            pqB = self.takeQ(qr)
                            for c in range(NCH):
                                p.op("T", lambda e, m=m, ng=ng, c=c, qr=qr,
                                     wap=wap, lh=lh: e.matmul(
                                         pbig[:, qr * 512:qr * 512 + 512],
                                         wap[:, c, m * 128:(m + 1) * 128],
                                         lh[:, c, ng * 512:(ng + 1) * 512],
                                         start=(c == 0), stop=(c == NCH - 1)),
                                     reads=[lhB, slotB], writes=[pqB],
                                     tag=f"q{layer}{w}mm{half}_{m}_{ng}")
                            osl = dst[:, m, half * 1024 + ng * 512:
                                      half * 1024 + (ng + 1) * 512]
                            if layer == 0:
                                p.op("V", lambda e, o=osl, qr=qr: e.tensor_copy(
                                    o, pbig[:, qr * 512:qr * 512 + 512]),
                                    reads=[pqB], writes=[dstB],
                                    tag=f"q{layer}{w}ev{half}_{m}_{ng}")
                            else:
                                bn = "bias_l1q" if w == "q" else "bias_l1k"
                                p.op("S", lambda e, o=osl, qr=qr, m=m, bn=bn:
                                     e.activation(
                                         o, pbig[:, qr * 512:qr * 512 + 512],
                                         AF.Identity,
                                         bias=sb[bn].ap()[:, m:m + 1]),
                                     reads=[pqB, self.B(bn)], writes=[dstB],
                                     tag=f"q{layer}{w}ev{half}_{m}_{ng}")
                else:
                    for kc in range(8):
                        for ng in range(2):
                            qr = (kc * 2 + ng) % 2
                            pqB = self.takeQ(qr)
                            for c in range(NCH):
                                p.op("T", lambda e, kc=kc, ng=ng, c=c, qr=qr,
                                     wap=wap, lh=lh: e.matmul(
                                         pbig[:, qr * 512:qr * 512 + 512],
                                         lh[:, c, kc * 128:(kc + 1) * 128],
                                         wap[:, c, ng * 512:(ng + 1) * 512],
                                         start=(c == 0), stop=(c == NCH - 1)),
                                     reads=[lhB, slotB], writes=[pqB],
                                     tag=f"q{layer}vmm{half}_{kc}_{ng}")
                            ovs = sb["v_tm"].ap()[:, half * 8 + kc,
                                                  ng * 512:(ng + 1) * 512]
                            if layer == 0:
                                p.op("V", lambda e, o=ovs, qr=qr: e.tensor_copy(
                                    o, pbig[:, qr * 512:qr * 512 + 512]),
                                    reads=[pqB], writes=[vB],
                                    tag=f"q{layer}vev{half}_{kc}_{ng}")
                            else:
                                p.op("V", lambda e, o=ovs, qr=qr, ng=ng:
                                     e.tensor_tensor(
                                         o, pbig[:, qr * 512:qr * 512 + 512],
                                         sb["bvb"].ap()[:, ng * 512:(ng + 1) * 512],
                                         ALU.add),
                                     reads=[pqB, self.B("bvb")], writes=[vB],
                                     tag=f"q{layer}vev{half}_{kc}_{ng}")
                self.pump(4)
        woB = self.take("wo_t")
        self.load_weight_cast(wsrc["o"], "wo_t", D, D, ceng="S", tag=f"_o{layer}")
        if layer == 1:
            self.emit_rope()
        self.emit_negB(layer)

    def emit_bvb(self):
        p, d, sb = self.p, self.d, self.sb
        stB = self.take("wstage0")
        p.op("Q", lambda e: e.dma_start(sb["wstage"].ap()[0:1, 0, 0:D],
                                        d["l1bv"].ap()),
             writes=[stB], dma=True, tag="ld_bv")
        srB = self.take("srow")
        bv = sb["srow"].ap().bitcast(BF16)[0:1, 0:D]
        p.op("V", lambda e: e.tensor_copy(bv, sb["wstage"].ap()[0:1, 0, 0:D]),
             reads=[stB], writes=[srB], tag="bv_cast")
        pbig = self.pbig.ap()
        bvbB = self.take("bvb")
        for ng in range(2):
            pqB = self.takeQ(ng)
            p.op("T", lambda e, ng=ng: e.matmul(
                pbig[:, ng * 512:(ng + 1) * 512], sb["ones_row"].ap(),
                bv[:, ng * 512:(ng + 1) * 512], start=True, stop=True),
                reads=[srB, self.B("c_ones_row")], writes=[pqB], tag=f"bvbc{ng}")
            p.op("V", lambda e, ng=ng: e.tensor_copy(
                sb["bvb"].ap()[:, ng * 512:(ng + 1) * 512],
                pbig[:, ng * 512:(ng + 1) * 512]),
                reads=[pqB], writes=[bvbB], tag=f"bvcp{ng}")

    def emit_rope(self):
        p, d, sb = self.p, self.d, self.sb
        for src, dst in (("ropecos", "cosb"), ("ropesin", "sinb")):
            dB = self.take(dst)
            for piece in range(2):
                stB = self.take(f"wstage{piece}")
                ap_st = sb["wstage"].ap()[:, piece, :]
                p.op("Q", lambda e, a=ap_st, s=src, piece=piece: e.dma_start(
                    a, d[s].ap()[:, piece * 1024:(piece + 1) * 1024]),
                    writes=[stB], dma=True, tag=f"ld_{dst}{piece}")
                p.op("S", lambda e, a=ap_st, dst=dst, piece=piece: e.copy(
                    sb[dst].ap()[:, piece * 1024:(piece + 1) * 1024], a),
                    reads=[stB], writes=[dB], tag=f"cast_{dst}{piece}")
        pbig = self.pbig.ap()
        cosB, sinB = self.B("cosb"), self.B("sinb")
        tAb = sb["tmpa"].ap().bitcast(BF16)[:, 0:512]
        for w in ("q", "k"):
            tgt = sb[w].ap()
            tgtB = self.B(w)
            for ch in range(NCH):
                for kg in range(4):
                    ks = slice(kg * 512, (kg + 1) * 512)
                    qr = kg % 2
                    pqB = self.takeQ(qr)
                    p.op("T", lambda e, ch=ch, ks=ks, qr=qr, tgt=tgt: e.matmul(
                        pbig[:, qr * 512:qr * 512 + 512], sb["pswap"].ap(),
                        tgt[:, ch, ks], start=True, stop=True),
                        reads=[tgtB, self.B("pswap")], writes=[pqB],
                        tag=f"rsw_{w}{ch}_{kg}")
                    tB = self.take("tmpa")
                    p.op("G", lambda e, ch=ch, ks=ks, tgt=tgt: e.tensor_tensor(
                        tAb, tgt[:, ch, ks], sb["cosb"].ap()[:, ks], ALU.mult),
                        reads=[tgtB, cosB], writes=[tB], tag=f"rt_{w}{ch}_{kg}")
                    mB = self.take("tmpb")
                    p.op("V", lambda e, ks=ks, qr=qr: e.tensor_tensor(
                        sb["tmpb"].ap(), pbig[:, qr * 512:qr * 512 + 512],
                        sb["sinb"].ap()[:, ks], ALU.mult),
                        reads=[pqB, sinB], writes=[mB], tag=f"rm_{w}{ch}_{kg}")
                    p.op("V", lambda e, ch=ch, ks=ks, tgt=tgt: e.tensor_tensor(
                        tgt[:, ch, ks], tAb, sb["tmpb"].ap(), ALU.add),
                        reads=[tB, mB], writes=[tgtB], tag=f"ro_{w}{ch}_{kg}")
            self.pump(2)

    def emit_negB(self, layer):
        p, sb = self.p, self.sb
        pbig = self.pbig.ap()
        scale = 0.125 if layer == 1 else 1.0 / 32.0
        nh = 16 if layer == 1 else 1
        st = sb["stats"].ap()
        for iw, w in enumerate(("q", "k")):
            tgt = sb[w].ap()
            tgtB = self.B(w)
            pqall = [self.takeQ(i) for i in range(4)]
            sqB = self.take("sq128")
            for ch in range(NCH):
                p.op("S", lambda e, ch=ch, tgt=tgt: e.activation(
                    sb["sq128"].ap(), tgt[:, ch, :], AF.Square),
                    reads=[tgtB], writes=[sqB], tag=f"nBsq_{w}{ch}")
                for kg in range(4):
                    if layer == 1:
                        p.op("T", lambda e, ch=ch, kg=kg: e.matmul(
                            pbig[0:16, kg * 512:(kg + 1) * 512],
                            sb["headsel"].ap()[:, ch, :],
                            sb["sq128"].ap()[:, kg * 512:(kg + 1) * 512],
                            start=(ch == 0), stop=(ch == NCH - 1)),
                            reads=[sqB, self.B("headsel")], writes=[pqall[kg]],
                            tag=f"nBmm_{w}{ch}_{kg}")
                    else:
                        p.op("T", lambda e, ch=ch, kg=kg: e.matmul(
                            pbig[0:1, kg * 512:(kg + 1) * 512],
                            sb["ones_col"].ap(),
                            sb["sq128"].ap()[:, kg * 512:(kg + 1) * 512],
                            start=(ch == 0), stop=(ch == NCH - 1)),
                            reads=[sqB, self.B("c_ones_col")], writes=[pqall[kg]],
                            tag=f"nBmm_{w}{ch}_{kg}")
            stB = self.take("stats")
            p.op("V", lambda e, nh=nh: e.reduce_max(
                st[0:nh, 0:1], pbig[0:nh, :], AX.X),
                reads=pqall, writes=[stB], tag=f"nBmax_{w}")
            p.op("S", lambda e, nh=nh, iw=iw: e.activation(
                st[0:nh, 1 + iw:2 + iw], st[0:nh, 0:1], AF.Ln),
                reads=[stB], writes=[stB], tag=f"nBln_{w}")
        stB = self.B("stats")
        p.op("V", lambda e, nh=nh: e.tensor_tensor(
            st[0:nh, 3:4], st[0:nh, 1:2], st[0:nh, 2:3], ALU.add),
            reads=[stB], writes=[stB], tag="nBadd")
        p.op("S", lambda e, nh=nh: e.activation(
            st[0:nh, 3:4], st[0:nh, 3:4], AF.Exp, scale=0.5),
            reads=[stB], writes=[stB], tag="nBexp")
        p.op("V", lambda e, nh=nh, scale=scale: e.tensor_scalar(
            st[0:nh, 3:4], st[0:nh, 3:4], -scale, None, ALU.mult),
            reads=[stB], writes=[stB], tag="nBscale")
        bvec = sb["stats"].ap().bitcast(BF16)[0:nh, 8:9]
        p.op("V", lambda e, nh=nh, bvec=bvec: e.tensor_copy(
            bvec, st[0:nh, 3:4]),
            reads=[stB], writes=[stB], tag="nBb16")
        pb4 = self.pb[4].ap()
        pb4b16 = self.pb[4].ap().bitcast(BF16)
        pb4B = self.take("pb4")
        p.op("T", lambda e, nh=nh, bvec=bvec: e.matmul(
            pb4[0:1, 0:nh], bvec, sb["ident"].ap()[0:nh, 0:nh],
            start=True, stop=True),
            reads=[stB, self.B("ident")], writes=[pb4B], tag="nBtr")
        srB = self.take("srow")
        p.op("V", lambda e, nh=nh: e.tensor_copy(sb["srow"].ap()[0:1, 0:nh],
                                                 pb4[0:1, 0:nh]),
             reads=[pb4B], writes=[srB], tag="nBtrc")
        pb4B = self.take("pb4")
        p.op("T", lambda e, nh=nh: e.matmul(
            pb4[:, 0:nh], sb["ones_row_f32"].ap(), sb["srow"].ap()[0:1, 0:nh],
            start=True, stop=True),
            reads=[srB, self.B("c_ones_row_f32")], writes=[pb4B], tag="nBbc")
        nBB = self.take("negB")
        col0 = 0 if layer == 1 else 16
        p.op("V", lambda e, nh=nh, col0=col0: e.tensor_copy(
            sb["negB"].ap()[:, col0:col0 + nh], pb4[:, 0:nh]),
            reads=[pb4B], writes=[nBB], tag="nBcp")

    def phase_attn0(self):
        self.emit_attn(0)

    def phase_attn1(self):
        self.emit_attn(1)

    def emit_attn(self, layer):
        p, d, sb = self.p, self.d, self.sb
        pbig = self.pbig.ap()
        xs_src = d["xs_a"] if layer == 0 else d["xs_b"]
        xs_dst = d["xs_b"] if layer == 0 else d["xs_a"]
        xsrcB = self.B("xs_a" if layer == 0 else "xs_b")
        xdstB = self.B("xs_b" if layer == 0 else "xs_a")
        sc = 1.0 / 32.0 if layer == 0 else 0.125
        heads = 1 if layer == 0 else HEADS
        qB, kB, vB = self.B("q"), self.B("k"), self.B("v_tm")
        woB = self.B("wo_t")
        Pap = sb["P"].ap()
        sums = sb["stats"].ap()
        pvv = pbig.rearrange("p (a b) -> p a b", b=128)
        for qt in range(NQT):
            L = (qt + 1) * 128
            kgs = (L + 511) // 512
            nkc = L // 128
            qs = slice(qt * 128, qt * 128 + 128)
            xrB = self.take("xres")
            p.op("Q", lambda e, qs=qs: e.dma_start(
                sb["xres"].ap(),
                xs_src.rearrange("(c p) t -> p c t", p=128)[:, :, qs]),
                reads=[xsrcB], writes=[xrB], dma=True, tag=f"a{layer}ldx{qt}")
            ovB = self.take("outv")
            for h in range(heads):
                if layer == 0:
                    lq = [(c, 0, 128) for c in range(NCH)]
                else:
                    lq = [(h // 2, (h % 2) * 64, 64)]
                PB = self.take("P")
                smB = self.take("stats")
                colB = 16 if layer == 0 else h
                for kg in range(kgs):
                    n = min(512, L - kg * 512)
                    ks = slice(kg * 512, kg * 512 + n)
                    qr = kg % 2
                    pqB = self.takeQ(qr)
                    for i, (c, poff, kw) in enumerate(lq):
                        p.op("T", lambda e, c=c, poff=poff, kw=kw, ks=ks, qs=qs,
                             qr=qr, i=i, n=n, nl=len(lq): e.matmul(
                                 pbig[:, qr * 512:qr * 512 + n],
                                 sb["q"].ap()[poff:poff + kw, c, qs],
                                 sb["k"].ap()[poff:poff + kw, c, ks],
                                 start=(i == 0), stop=(i == nl - 1)),
                             reads=[qB, kB], writes=[pqB],
                             tag=f"a{layer}s{qt}_{h}_{kg}")
                    if kg == kgs - 1:
                        p.op("V", lambda e, qr=qr, n=n: e.tensor_tensor(
                            pbig[:, qr * 512 + n - 128:qr * 512 + n],
                            pbig[:, qr * 512 + n - 128:qr * 512 + n],
                            sb["maskdiag"].ap(), ALU.add),
                            reads=[self.B("maskdiag"), pqB], writes=[pqB],
                            tag=f"a{layer}mask{qt}_{h}")
                    p.op("S", lambda e, qr=qr, n=n, ks=ks, kg=kg, colB=colB:
                         e.activation(Pap[:, ks], pbig[:, qr * 512:qr * 512 + n],
                                      AF.Exp,
                                      bias=sb["negB"].ap()[:, colB:colB + 1],
                                      scale=sc, accum_out=sums[:, kg:kg + 1]),
                         reads=[pqB, self.B("negB")], writes=[PB, smB],
                         tag=f"a{layer}exp{qt}_{h}_{kg}")
                riB = self.take("rinv")
                p.op("V", lambda e, kgs=kgs: e.reduce_sum(
                    sb["rinv"].ap(), sums[:, 0:kgs], AX.X),
                    reads=[smB], writes=[riB], tag=f"a{layer}rs{qt}_{h}")
                p.op("V", lambda e: e.reciprocal(sb["rinv"].ap(), sb["rinv"].ap()),
                     reads=[riB], writes=[riB], tag=f"a{layer}ri{qt}_{h}")
                dgB = self.take("diag")
                p.op("V", lambda e: e.tensor_scalar(
                    sb["diag"].ap(), sb["ident"].ap(), sb["rinv"].ap(), None,
                    ALU.mult),
                    reads=[riB, self.B("ident")], writes=[dgB],
                    tag=f"a{layer}dg{qt}_{h}")
                PTB = self.take("PT")
                pb4f = self.pb[4].ap()
                for kc0 in range(0, nkc, 4):
                    nb = min(4, nkc - kc0)
                    p4B = self.take("pb4")
                    for j in range(nb):
                        p.op("T", lambda e, kc0=kc0, j=j: e.matmul(
                            pb4f[:, j * 128:(j + 1) * 128],
                            Pap[:, (kc0 + j) * 128:(kc0 + j + 1) * 128],
                            sb["diag"].ap(), start=True, stop=True),
                            reads=[PB, dgB], writes=[p4B],
                            tag=f"a{layer}tr{qt}_{h}_{kc0 + j}")
                    p.op("V", lambda e, kc0=kc0, nb=nb: e.tensor_copy(
                        sb["PT"].ap()[:, kc0 * 128:(kc0 + nb) * 128],
                        pb4f[:, 0:nb * 128]),
                        reads=[p4B], writes=[PTB], tag=f"a{layer}tre{qt}_{h}_{kc0}")
                if layer == 0:
                    pvB = [self.takeQ(2), self.takeQ(3)]
                    for m in range(NCH):
                        for kc in range(nkc):
                            p.op("T", lambda e, kc=kc, m=m, nkc=nkc: e.matmul(
                                pvv[:, 8 + m, :],
                                sb["v_tm"].ap()[:, kc, m * 128:(m + 1) * 128],
                                sb["PT"].ap()[:, kc * 128:(kc + 1) * 128],
                                start=(kc == 0), stop=(kc == nkc - 1)),
                                reads=[PTB, vB], writes=pvB,
                                tag=f"a{layer}pv{qt}_{kc}_{m}")
                    p.op("V", lambda e: e.tensor_copy(
                        sb["outv"].ap().rearrange("p c t -> p (c t)"),
                        pbig[:, 1024:2048]),
                        reads=pvB, writes=[ovB], tag=f"a{layer}pve{qt}")
                else:
                    pb5 = self.pb[5].ap()
                    sl = (h % 2) * 128
                    pvB = self.take("pb5")
                    for kc in range(nkc):
                        p.op("T", lambda e, kc=kc, h=h, sl=sl, nkc=nkc: e.matmul(
                            pb5[0:64, sl:sl + 128],
                            sb["v_tm"].ap()[:, kc, h * 64:(h + 1) * 64],
                            sb["PT"].ap()[:, kc * 128:(kc + 1) * 128],
                            start=(kc == 0), stop=(kc == nkc - 1)),
                            reads=[PTB, vB], writes=[pvB],
                            tag=f"a{layer}pv{qt}_{h}_{kc}")
                    p.op("V", lambda e, h=h, sl=sl: e.tensor_copy(
                        sb["outv"].ap()[(h % 2) * 64:(h % 2) * 64 + 64, h // 2, :],
                        pb5[0:64, sl:sl + 128]),
                        reads=[pvB], writes=[ovB], tag=f"a{layer}pve{qt}_{h}")
            pyB = [self.takeQ(2), self.takeQ(3)]
            for m in range(NCH):
                for c in range(NCH):
                    p.op("T", lambda e, m=m, c=c: e.matmul(
                        pvv[:, 8 + m, :],
                        sb["wo_t"].ap()[:, c, m * 128:(m + 1) * 128],
                        sb["outv"].ap()[:, c, :], start=(c == 0),
                        stop=(c == NCH - 1)),
                        reads=[ovB, woB], writes=pyB, tag=f"a{layer}o{qt}_{m}")
            xnB = self.take("xnew")
            for m in range(NCH):
                if layer == 0:
                    p.op("V", lambda e, m=m: e.tensor_tensor(
                        sb["xnew"].ap()[:, m, :], pvv[:, 8 + m, :],
                        sb["xres"].ap()[:, m, :], ALU.add),
                        reads=pyB + [xrB], writes=[xnB], tag=f"a{layer}r{qt}_{m}")
                else:
                    p.op("V", lambda e, m=m: e.scalar_tensor_tensor(
                        sb["xnew"].ap()[:, m, :], pvv[:, 8 + m, :],
                        sb["bias_l1o"].ap()[:, m:m + 1], sb["xres"].ap()[:, m, :],
                        ALU.add, ALU.add),
                        reads=pyB + [xrB, self.B("bias_l1o")], writes=[xnB],
                        tag=f"a{layer}r{qt}_{m}")
            p.op("Q", lambda e, qs=qs: e.dma_start(
                xs_dst.rearrange("(c p) t -> p c t", p=128)[:, :, qs],
                sb["xnew"].ap()),
                reads=[xnB], writes=[xdstB], dma=True, tag=f"a{layer}st{qt}")
            self.pump(2)
        self.xloc = "xs_b" if layer == 0 else "xs_a"

    def phase_ffn1a(self):
        self.emit_ffn1("a")

    def phase_ffn1b(self):
        self.emit_ffn1("b")

    def emit_ffn1(self, call):
        p, d, sb = self.p, self.d, self.sb
        pbig = self.pbig.ap()
        if call == "a":
            p.op("Q", lambda e: e.dma_start(
                d["xs_a"].rearrange("(c p) t -> p c t", p=128), sb["x"].ap()),
                reads=[self.B("x")], writes=[self.B("xs_a")], dma=True,
                tag="f1adump")
            xs_src, xsB = d["xs_a"], self.B("xs_a")
            xs_dst, xdB = d["xs_b"], self.B("xs_b")
        else:
            xs_src, xsB = d["xs_a"], self.B("xs_a")
            xs_dst, xdB = d["y"], self.B("y")
        fcB = self.take("wfcq")
        p.op("Q", lambda e: e.dma_start(
            sb["wfcq"].ap(), d["wfcq_d"].rearrange("(c p) m -> p c m", p=128)),
            reads=[self.B("wfcq_d_dram")], writes=[fcB], dma=True,
            tag=f"f1{call}ldfc")
        prB = self.take("wprq")
        p.op("Q", lambda e: e.dma_start(
            sb["wprq"].ap(), d["wprq_d"].rearrange("(c p) m -> p c m", p=128)),
            reads=[self.B("wprq_d_dram")], writes=[prB], dma=True,
            tag=f"f1{call}ldpr")
        sq2 = sb["sq128"].ap()
        for t in range(NT2):
            ts_ = slice(t * FT2, (t + 1) * FT2)
            xtB = self.take("xtile")
            p.op("G", lambda e, ts_=ts_: e.dma_start(
                sb["xtile"].ap(),
                xs_src.rearrange("(c p) t -> p c t", p=128)[:, :, ts_]),
                reads=[xsB], writes=[xtB], dma=True, tag=f"f1{call}ldx{t}")
            lcB = self.take("lnx_c")
            self.emit_ln(sb["xtile"].ap(), xtB, FT2, sb["lnx_c"].ap(), lcB,
                         f"f1{call}ln{t}", xb_name="xb2")
            hB = self.take("h")
            for m in range(32):
                q4 = m % 4
                ps_ = pbig[:, q4 * 512:q4 * 512 + 256]
                pqB = self.takeQ(q4)
                for c in range(NCH):
                    p.op("T", lambda e, m=m, c=c, ps_=ps_: e.matmul(
                        ps_, sb["wfcq"].ap()[:, c, m * 128:(m + 1) * 128],
                        sb["lnx_c"].ap()[:, c, :], start=(c == 0),
                        stop=(c == NCH - 1)),
                        reads=[lcB, fcB], writes=[pqB],
                        tag=f"f1{call}fc{t}_{m}")
                p.op("S", lambda e, m=m, ps_=ps_: e.activation(
                    sb["h"].ap()[:, m, :], ps_, AF.Gelu),
                    reads=[pqB], writes=[hB], tag=f"f1{call}fca{t}_{m}")
            pm = self.pb[5].ap()
            pmB = self.take("pb5")
            oc = sb["ones_col"].ap()
            for m in range(32):
                p.op("T", lambda e, m=m: e.matmul(
                    pm[0:1, 0:FT2], oc, sb["h"].ap()[:, m, :],
                    start=(m == 0), stop=(m == 31)),
                    reads=[hB, self.B("c_ones_col")], writes=[pmB],
                    tag=f"f1{call}hmu{t}")
            pv_ = self.pb[6].ap()
            pvB = self.take("pb6")
            for m in range(32):
                so = (m % 2) * FT2
                sqB = self.take("sqA" if m % 2 == 0 else "sqB")
                p.op("V", lambda e, m=m, so=so: e.tensor_tensor(
                    sq2[:, so:so + FT2], sb["h"].ap()[:, m, :],
                    sb["h"].ap()[:, m, :], ALU.mult),
                    reads=[hB], writes=[sqB], tag=f"f1{call}hsq{t}_{m}")
                p.op("T", lambda e, m=m, so=so: e.matmul(
                    pv_[0:1, 0:FT2], oc, sq2[:, so:so + FT2],
                    start=(m == 0), stop=(m == 31)),
                    reads=[sqB, self.B("c_ones_col")], writes=[pvB],
                    tag=f"f1{call}hvm{t}_{m}")
            sr = sb["srow"].ap()
            hD = 4096.0
            mu = sr[:, 0:FT2]
            var = sr[:, 512:512 + FT2]
            srB = self.take("srow")
            p.op("V", lambda e, mu=mu: e.tensor_scalar(
                mu, pm[0:1, 0:FT2], 1.0 / hD, None, ALU.mult),
                reads=[pmB], writes=[srB], tag=f"f1{call}mu{t}")
            p.op("V", lambda e, var=var: e.tensor_scalar(
                var, pv_[0:1, 0:FT2], 1.0 / hD, None, ALU.mult),
                reads=[pvB], writes=[srB], tag=f"f1{call}ex2{t}")
            ta = sb["tmpa"].ap()[0:1, 0:FT2]
            taB = self.take("tmpa")
            p.op("V", lambda e, ta=ta, mu=mu: e.tensor_tensor(ta, mu, mu, ALU.mult),
                 reads=[srB], writes=[taB], tag=f"f1{call}musq{t}")
            p.op("V", lambda e, var=var, ta=ta: e.tensor_tensor(
                var, var, ta, ALU.subtract),
                reads=[srB, taB], writes=[srB], tag=f"f1{call}var{t}")
            eps = sb["scal"].ap()[0:1, 0:1]
            p.op("S", lambda e, var=var: e.activation(var, var, AF.Ln, bias=eps),
                 reads=[srB, self.B("scal")], writes=[srB], tag=f"f1{call}lnv{t}")
            p.op("S", lambda e, var=var: e.activation(var, var, AF.Exp, scale=-0.5),
                 reads=[srB], writes=[srB], tag=f"f1{call}inv{t}")
            p.op("V", lambda e, mu=mu, var=var: e.tensor_tensor(
                mu, mu, var, ALU.mult),
                reads=[srB], writes=[srB], tag=f"f1{call}m2{t}")
            orf = sb["ones_row_f32"].ap()
            pb4 = self.pb[4].ap()
            for rowap, dst in ((var, "invb"), (mu, "m2b")):
                pb4B = self.take("pb4")
                p.op("T", lambda e, r=rowap: e.matmul(
                    pb4[:, 0:FT2], orf, r, start=True, stop=True),
                    reads=[srB, self.B("c_ones_row_f32")], writes=[pb4B],
                    tag=f"f1{call}bc{t}_{dst}")
                dB = self.take(dst)
                p.op("V", lambda e, dst=dst: e.tensor_copy(
                    sb[dst].ap()[:, 0:FT2], pb4[:, 0:FT2]),
                    reads=[pb4B], writes=[dB], tag=f"f1{call}bcc{t}_{dst}")
            invb = sb["invb"].ap()[:, 0:FT2]
            m2bv = sb["m2b"].ap()[:, 0:FT2]
            tb16 = sb["tmpa"].ap().bitcast(BF16)[:, 0:FT2]
            for m in range(32):
                taB = self.take("tmpa")
                p.op("V", lambda e, m=m: e.tensor_tensor(
                    tb16, sb["h"].ap()[:, m, :], invb, ALU.mult),
                    reads=[hB, self.B("invb")], writes=[taB],
                    tag=f"f1{call}hap1{t}_{m}")
                p.op("V", lambda e: e.tensor_tensor(tb16, tb16, m2bv,
                                                    ALU.subtract),
                     reads=[taB, self.B("m2b")], writes=[taB],
                     tag=f"f1{call}hap2{t}_{m}")
                p.op("S", lambda e, m=m: e.activation(
                    sb["h"].ap()[:, m, :], tb16, AF.Identity,
                    bias=sb["lnb"].ap()[:, m:m + 1],
                    scale=sb["lnw"].ap()[:, m:m + 1]),
                    reads=[taB, self.B("lnw"), self.B("lnb")], writes=[hB],
                    tag=f"f1{call}hap3{t}_{m}")
            ytB = self.take("ytile")
            for m in range(NCH):
                q4 = m % 4
                ps_ = pbig[:, q4 * 512 + 256:q4 * 512 + 512]
                pqB = self.takeQ(q4)
                for c in range(32):
                    p.op("T", lambda e, m=m, c=c, ps_=ps_: e.matmul(
                        ps_, sb["wprq"].ap()[:, c, m * 128:(m + 1) * 128],
                        sb["h"].ap()[:, c, :], start=(c == 0), stop=(c == 31)),
                        reads=[hB, prB], writes=[pqB],
                        tag=f"f1{call}pr{t}_{m}")
                p.op("V", lambda e, m=m, ps_=ps_: e.scalar_tensor_tensor(
                    sb["ytile"].ap()[:, m, :], ps_,
                    0.5, sb["xtile"].ap()[:, m, :], ALU.mult, ALU.add),
                    reads=[pqB, xtB], writes=[ytB], tag=f"f1{call}ev{t}_{m}")
            p.op("G", lambda e, ts_=ts_: e.dma_start(
                xs_dst.rearrange("(c p) t -> p c t", p=128)[:, :, ts_],
                sb["ytile"].ap()),
                reads=[ytB], writes=[xdB], dma=True, tag=f"f1{call}st{t}")
            self.pump(2)
        self.xloc = "xs_b" if call == "a" else "y"

    def phase_finish(self):
        p, d, sb = self.p, self.d, self.sb
        if self.xloc == "y":
            p.wait_event("Q", self.B("y").w[0])
            return
        if self.xloc == "x":
            ev = p.op("Q", lambda e: e.dma_start(
                d["y"].rearrange("(c p) t -> p c t", p=128), sb["x"].ap()),
                reads=[self.B("x")], writes=[self.B("y")], dma=True, tag="finx")
        elif self.xloc == "fout":
            ev = p.op("Q", lambda e: e.dma_start(
                d["y"].rearrange("(c p) t -> p c t", p=128), sb["fout"].ap()),
                reads=[self.B("fout")], writes=[self.B("y")], dma=True,
                tag="finf")
        else:
            ev = p.op("Q", lambda e, s=self.xloc: e.dma_start(
                d["y"].ap(), d[s].ap()),
                reads=[self.B(self.xloc)], writes=[self.B("y")], dma=True,
                tag="find")
        p.wait_event("Q", ev)

    def build(self):
        self.phase_init()
        if "qkv1" in self.phases:
            self.emit_bvb()
        for ph in self.phases:
            getattr(self, "phase_" + ph)()
        self.pump_all()
        if self.phases and self.phases[-1] in ("ln1", "ln3"):
            self.xloc = "fout"
        self.phase_finish()
        return self.nc


def kernel(**inputs):
    cores = prep_inputs(inputs)
    b = Builder(_PHASES)
    nc = b.build()
    b.p.emit()
    res = run_bass_kernel_spmd(nc, cores, core_ids=list(range(NCORES)))
    out = np.stack([np.ascontiguousarray(np.asarray(r["y"]).T)
                    for r in res.results])
    return out.astype(np.float32)


# version 05
# speedup vs baseline: 1.7656x; 1.0157x over previous
import math
import numpy as np
import ml_dtypes

import concourse.bass as bass
import concourse.mybir as mybir
from concourse.alu_op_type import AluOpType as ALU
from concourse.bass_utils import run_bass_kernel_spmd

F32 = mybir.dt.float32
BF16 = mybir.dt.bfloat16
AF = mybir.ActivationFunctionType
AX = mybir.AxisListType

NCORES = 8
SEQ = 2048
D = 1024
NCH = 8
HEADS = 16
HD = 64
GRP = 512
NGRP = 4
FT1 = 512
FT2 = 256
NT2 = SEQ // FT2
QT = 128
NQT = 16

ALL_PHASES = ["ln1", "ffn0a", "qkv0", "attn0", "ln3", "ffn0b",
              "ffn1a", "qkv1", "attn1", "ffn1b"]
_PHASES = None
_SAME_ENGINE_SEM = True
_FFN0_NCYC = 3


class Buf:
    __slots__ = ("name", "w", "r")

    def __init__(self, name):
        self.name = name
        self.w = []
        self.r = []


class Op:
    __slots__ = ("fn", "waits", "stream", "seq", "eng", "tag")

    def __init__(self, fn, waits, stream, seq, eng, tag):
        self.fn = fn
        self.waits = waits
        self.stream = stream
        self.seq = seq
        self.eng = eng
        self.tag = tag


class Prog:
    ENGS = ("T", "V", "S", "G", "Q")
    DMA_SLOTS = {"Q": 12, "G": 6, "S": 4, "V": 2, "T": 2}

    def __init__(self, nc):
        self.nc = nc
        self.eng = {"T": nc.tensor, "V": nc.vector, "S": nc.scalar,
                    "G": nc.gpsimd, "Q": nc.sync}
        self.ops = []
        self.seq = {}
        self.waited = {e: {} for e in self.ENGS}
        self.needed = {}
        self.dmacnt = {}

    def op(self, eng, fn, reads=(), writes=(), dma=False, tag=""):
        if dma:
            i = self.dmacnt.get(eng, 0)
            self.dmacnt[eng] = i + 1
            nslots = self.DMA_SLOTS[eng]
            stream = f"{eng}d{i % nslots}"
            s = i // nslots
        else:
            stream = eng
            s = self.seq.get(stream, 0)
        deps = []
        for b in reads:
            deps += b.w
        for b in writes:
            deps += b.w
            deps += b.r
        if dma and s > 0:
            deps.append((stream, s - 1))
        waits = []
        wmap = self.waited[eng]
        for (ps, pq) in deps:
            if ps == "T" and eng == "T" and not dma:
                continue
            if (not _SAME_ENGINE_SEM) and ps == eng and not dma:
                continue
            if wmap.get(ps, -1) < pq:
                wmap[ps] = pq
                self.needed.setdefault(ps, set()).add(pq)
                waits.append((ps, pq))
        self.seq[stream] = s + 1
        self.ops.append(Op(fn, waits, stream, s, eng, tag))
        if dma:
            self.needed.setdefault(stream, set()).add(s)
        ev = (stream, s)
        for b in writes:
            b.w = [ev]
            b.r = []
        for b in reads:
            b.r.append(ev)
        return ev

    def wait_event(self, eng, ev):
        (ps, pq) = ev
        self.needed.setdefault(ps, set()).add(pq)
        self.ops.append(Op(None, [(ps, pq)], None, None, eng, "finwait"))

    def emit(self):
        nc = self.nc
        rank = {}
        sems = {}
        for stream, seqs in self.needed.items():
            order = sorted(seqs)
            step = 16 if len(stream) > 1 else 1
            rank[stream] = {q: (i + 1) * step for i, q in enumerate(order)}
            sems[stream] = nc.alloc_semaphore(f"sem_{stream}")
        for o in self.ops:
            eng = self.eng[o.eng]
            for (ps, pq) in o.waits:
                eng.wait_ge(sems[ps], rank[ps][pq])
            if o.fn is None:
                continue
            inst = o.fn(eng)
            if o.tag:
                try:
                    inst.annotate(o.tag)
                except Exception:
                    pass
            if o.stream in rank and o.seq in rank[o.stream]:
                step = 16 if len(o.stream) > 1 else 1
                inst.then_inc(sems[o.stream], step)
        return len(self.ops)


def _rope_perm():
    pi = np.zeros(D, dtype=np.int64)
    for h in range(HEADS):
        for i in range(32):
            pi[h * 64 + i] = h * 64 + 2 * i
            pi[h * 64 + 32 + i] = h * 64 + 2 * i + 1
    return pi


def _vec_tile(v):
    return np.ascontiguousarray(np.asarray(v, dtype=np.float32).reshape(-1, 128).T)


def prep_inputs(inp):
    f32 = np.float32
    pi = _rope_perm()
    g = {}
    g["l0wq"] = np.ascontiguousarray(inp["L0_attn_Wq"], dtype=f32)
    g["l0wk"] = np.ascontiguousarray(inp["L0_attn_Wk"], dtype=f32)
    g["l0wv"] = np.ascontiguousarray(inp["L0_attn_Wv"], dtype=f32)
    g["l0wo"] = np.ascontiguousarray(inp["L0_attn_Wo"], dtype=f32)
    for k in ("cn_w1", "cn_w2", "cn_w3", "cn_w4", "gc_w1", "gc_w2", "gc_w3"):
        g[k] = np.ascontiguousarray(inp["L0_ffn_" + k], dtype=f32)
    g["cn_b1"] = _vec_tile(inp["L0_ffn_cn_b1"])
    g["cn_b2"] = _vec_tile(inp["L0_ffn_cn_b2"])
    g["cn_b3"] = _vec_tile(inp["L0_ffn_cn_b3"])
    g["cn_b4"] = np.asarray(inp["L0_ffn_cn_b4"], dtype=f32).reshape(1, 1)
    g["gc_b1"] = _vec_tile(inp["L0_ffn_gc_b1"])
    g["gc_b2"] = _vec_tile(inp["L0_ffn_gc_b2"])
    g["gc_b3"] = np.asarray(inp["L0_ffn_gc_b3"], dtype=f32).reshape(1, 1)
    g["arange"] = np.asarray(inp["L0_ffn_adaptive_range"], dtype=f32).reshape(1, 1)
    g["l1wq"] = np.ascontiguousarray(np.asarray(inp["L1_attn_wq"]).T[:, pi], dtype=f32)
    g["l1wk"] = np.ascontiguousarray(np.asarray(inp["L1_attn_wk"]).T[:, pi], dtype=f32)
    g["l1wv"] = np.ascontiguousarray(np.asarray(inp["L1_attn_wv"]).T, dtype=f32)
    g["l1wo"] = np.ascontiguousarray(np.asarray(inp["L1_attn_wo"]).T, dtype=f32)
    g["l1bq"] = _vec_tile(np.asarray(inp["L1_attn_bq"])[pi])
    g["l1bk"] = _vec_tile(np.asarray(inp["L1_attn_bk"])[pi])
    g["l1bv"] = np.asarray(inp["L1_attn_bv"], dtype=f32).reshape(1, D)
    g["l1bo"] = _vec_tile(inp["L1_attn_bo"])
    g["wfcT"] = np.ascontiguousarray(np.asarray(inp["L1_ffn_W_fc"]).T, dtype=f32)
    g["wprT"] = np.ascontiguousarray(np.asarray(inp["L1_ffn_W_proj"]).T, dtype=f32)
    g["lnw"] = _vec_tile(inp["L1_ffn_ln_w"])
    g["lnb"] = _vec_tile(inp["L1_ffn_ln_b"])
    x = np.asarray(inp["L0_attn_x"], dtype=f32)
    cores = []
    for i in range(NCORES):
        m = dict(g)
        m["xT"] = np.ascontiguousarray(x[i].T)
        cores.append(m)
    return cores


def make_consts():
    c = {}
    c["ident"] = np.eye(128, dtype=ml_dtypes.bfloat16)
    ps = np.zeros((128, 128), dtype=np.float64)
    for h in range(2):
        for i in range(32):
            e = h * 64 + i
            o = h * 64 + 32 + i
            ps[o, e] = -1.0
            ps[e, o] = 1.0
    c["pswap"] = ps.astype(ml_dtypes.bfloat16)
    m = np.where(np.arange(128)[None, :] > np.arange(128)[:, None], -1e30, 0.0)
    c["maskdiag"] = m.astype(np.float32)
    c["ones_col"] = np.ones((128, 1), dtype=ml_dtypes.bfloat16)
    c["ones_row"] = np.ones((1, 128), dtype=ml_dtypes.bfloat16)
    c["ones_row_f32"] = np.ones((1, 128), dtype=np.float32)
    c["ones_col_f32"] = np.ones((128, 1), dtype=np.float32)
    i32 = np.arange(32, dtype=np.float64)
    rate = np.exp(-2.0 * i32 / HD * math.log(10000.0))
    pos = np.arange(SEQ, dtype=np.float64)
    th = rate[:, None] * pos[None, :]
    c["ropecos"] = np.tile(np.cos(th), (4, 1)).astype(np.float32)
    c["ropesin"] = np.tile(np.sin(th), (4, 1)).astype(np.float32)
    hs = np.zeros((128, NCH, HEADS), dtype=ml_dtypes.bfloat16)
    for ch in range(NCH):
        for p in range(128):
            hs[p, ch, (ch * 128 + p) // HD] = 1
    c["headsel"] = hs
    return c


GROUPS = [
    ["lnx_a", "lnx_b", "lnx_half", "cosb", "sinb", "outv", "diag", "xres",
     "xnew", "h1b", "h2b", "h3b", "gc2t", "lnx_c", "xb2", "diag2", "P2"],
    ["P", "PT", "xb", "fb16", "sq128", "sqA", "sqB", "gc1t"],
    ["x", "wprq", "wslotA", "wslotB", "xgrp", "wo_t"],
    ["fout", "q", "k", "wfcq"],
    ["v_tm", "cn1", "cn2", "cn3", "cn4", "h", "xtile", "ytile"],
    ["pQ0"], ["pQ1"], ["pQ2"], ["pQ3"],
    ["tmpa"], ["tmpb"], ["wstage0"], ["wstage1"],
]
GROUP_OF = {}
for _g in GROUPS:
    for _n in _g:
        GROUP_OF[_n] = _g


class Builder:
    def __init__(self, phases=None):
        self.phases = list(phases if phases is not None else ALL_PHASES)
        nc = bass.Bass("TRN2", target_bir_lowering=False)
        self.nc = nc
        self.p = Prog(nc)
        self.bufs = {}
        self.xloc = "x"
        self.wsvc = []
        self._declare_io()
        self._alloc_sbuf()
        self._alloc_psum()

    def B(self, name):
        if name not in self.bufs:
            self.bufs[name] = Buf(name)
        return self.bufs[name]

    def take(self, name):
        b = Buf(name)
        for n in GROUP_OF.get(name, [name]):
            o = self.bufs.get(n)
            if o is not None:
                b.w = b.w + o.w
                b.r = b.r + o.r
        self.bufs[name] = b
        return b

    def _declare_io(self):
        nc = self.nc
        self.d = {}
        d = self.d

        def T(n, s):
            d[n] = nc.dram_tensor(n, s, F32, kind="ExternalInput")
        T("xT", [D, SEQ])
        d["y"] = nc.dram_tensor("y", [D, SEQ], F32, kind="ExternalOutput")
        for n, s in [("l0wq", [D, D]), ("l0wk", [D, D]), ("l0wv", [D, D]),
                     ("l0wo", [D, D]), ("cn_w1", [D, D]), ("cn_w2", [D, 512]),
                     ("cn_w3", [512, 256]), ("cn_w4", [256, 1]),
                     ("gc_w1", [D, 512]), ("gc_w2", [512, 256]),
                     ("gc_w3", [256, 1]), ("cn_b1", [128, 8]),
                     ("cn_b2", [128, 4]), ("cn_b3", [128, 2]), ("cn_b4", [1, 1]),
                     ("gc_b1", [128, 4]), ("gc_b2", [128, 2]), ("gc_b3", [1, 1]),
                     ("arange", [1, 1]),
                     ("l1wq", [D, D]), ("l1wk", [D, D]), ("l1wv", [D, D]),
                     ("l1wo", [D, D]), ("l1bq", [128, 8]), ("l1bk", [128, 8]),
                     ("l1bv", [1, D]), ("l1bo", [128, 8]),
                     ("wfcT", [D, 4096]), ("wprT", [4096, D]),
                     ("lnw", [128, 32]), ("lnb", [128, 32])]:
            T(n, s)
        I = lambda n, s, dt: nc.dram_tensor(n, s, dt, kind="Internal")
        d["xs_a"] = I("xs_a", [D, SEQ], F32)
        d["xs_b"] = I("xs_b", [D, SEQ], F32)
        d["lnxs"] = I("lnxs", [D, SEQ], BF16)
        d["wfcq_d"] = I("wfcq_d", [D, 4096], BF16)
        d["wprq_d"] = I("wprq_d", [4096, D], BF16)
        for k, v in make_consts().items():
            d[k] = nc.inline_tensor(v, k)

    def _alloc_sbuf(self):
        nc = self.nc
        self.off = (int(nc.sbuf_base) + 31) // 32 * 32
        self.sb = {}
        self.sboff = {}

        def at(name, shape, dtype, off=None):
            nbytes = int(np.prod(shape[1:])) * (2 if dtype == BF16 else 4)
            if off is None:
                off = self.off
                self.off = off + ((nbytes + 31) // 32) * 32
            h = nc.alloc_sbuf_tensor_at(name, list(shape), dtype, offset=off)
            self.sb[name] = h
            self.sboff[name] = off
            return h

        at("ident", [128, 128], BF16)
        at("pswap", [128, 128], BF16)
        at("maskdiag", [128, 128], F32)
        at("ones_col", [128, 1], BF16)
        at("ones_row", [1, 128], BF16)
        at("ones_row_f32", [1, 128], F32)
        at("headsel", [128, NCH, HEADS], BF16)
        at("bias_cn1", [128, 8], F32)
        at("bias_cn2", [128, 4], F32)
        at("bias_cn3", [128, 2], F32)
        at("bias_gc1", [128, 4], F32)
        at("bias_gc2", [128, 2], F32)
        at("bias_l1q", [128, 8], F32)
        at("bias_l1k", [128, 8], F32)
        at("bias_l1o", [128, 8], F32)
        at("lnw", [128, 32], F32)
        at("lnb", [128, 32], F32)
        at("bvb", [128, D], BF16)
        at("scal", [128, 16], F32)
        at("srow", [1, 1024], F32)
        at("grow", [1, 64], F32)
        at("stats", [128, 8], F32)
        at("rinv", [128, 2], F32)
        at("negB", [128, 24], F32)
        at("invb", [128, 512], BF16)
        at("m2b", [128, 512], BF16)
        at("tmpa", [128, 512], F32)
        at("tmpb", [128, 512], F32)
        at("gc3t", [128, 2, 1], BF16)
        at("ones_col_f32", [128, 1], F32)
        at("gacc", [128, 64], F32)
        at("lnx_a", [128, NCH, GRP], BF16)
        at("lnx_b", [128, NCH, GRP], BF16)
        r1 = self.sboff["lnx_a"]
        at("lnx_half", [128, NCH, 2 * GRP], BF16, off=r1)
        at("cosb", [128, SEQ], BF16, off=r1)
        at("sinb", [128, SEQ], BF16, off=r1 + 8192)
        at("outv", [128, NCH, QT], BF16, off=r1)
        at("diag", [128, 128], BF16, off=r1 + 2048)
        at("diag2", [128, 128], BF16, off=r1 + 2304)
        at("xres", [128, NCH, QT], F32, off=r1 + 2560)
        at("xnew", [128, NCH, QT], F32, off=r1 + 2560 + 4096)
        at("P2", [128, SEQ], BF16, off=r1 + 2560 + 8192)
        at("h2b", [128, 4, FT1], BF16, off=r1)
        at("h3b", [128, 2, FT1], BF16, off=r1 + 4096)
        at("h1b", [128, NCH, FT1], BF16, off=r1 + 8192)
        at("P", [128, SEQ], BF16)
        at("PT", [128, SEQ], BF16)
        pP = self.sboff["P"]
        at("xb", [128, NCH, GRP], BF16, off=pP)
        at("fb16", [128, NCH, FT1], BF16, off=pP)
        at("sq128", [128, SEQ], BF16, off=pP)
        at("lnx_c", [128, NCH, FT2], BF16, off=r1)
        at("xb2", [128, NCH, FT2], BF16, off=r1 + 8192)
        at("gc1t", [128, NCH, 512], BF16, off=pP)
        at("gc2t", [128, 4, 256], BF16, off=self.sboff["lnx_a"])

        r2 = self.off
        at("x", [128, NCH, SEQ], F32)
        at("wprq", [128, 32, D], BF16, off=r2)
        at("wslotA", [128, NCH, D], BF16, off=r2)
        at("wslotB", [128, NCH, D], BF16, off=r2 + 16 * 1024)
        at("xgrp", [128, NCH, GRP], F32, off=r2 + 32 * 1024)
        at("wo_t", [128, NCH, D], BF16, off=r2 + 48 * 1024)
        r3 = self.off
        at("fout", [128, NCH, SEQ], F32)
        at("q", [128, NCH, SEQ], BF16, off=r3)
        at("k", [128, NCH, SEQ], BF16, off=r3 + 32 * 1024)
        at("wfcq", [128, NCH, 4096], BF16, off=r3)
        r4 = self.off
        at("v_tm", [128, NQT, D], BF16)
        at("cn1", [128, NCH, D], BF16, off=r4)
        at("cn2", [128, NCH, 512], BF16, off=r4 + 16 * 1024)
        at("cn3", [128, 4, 256], BF16, off=r4 + 24 * 1024)
        at("cn4", [128, 2, 1], BF16, off=r4 + 26 * 1024)
        at("h", [128, 32, FT2], BF16, off=r4)
        at("xtile", [128, NCH, FT2], F32, off=r4 + 16 * 1024)
        at("ytile", [128, NCH, FT2], F32, off=r4 + 24 * 1024)
        at("wstage", [128, 2, 1024], F32)

        limit = int(nc.sbuf_top)
        assert self.off <= limit, f"SBUF over budget: {self.off} > {limit}"

    def _alloc_psum(self):
        nc = self.nc
        self.pbig = nc.alloc_psum_tensor("pbig", [128, 2048], F32)
        self.pb = {i: nc.alloc_psum_tensor(f"pb{i}", [128, 512], F32)
                   for i in (4, 5, 6, 7)}

    def pQ(self, i):
        return self.B(f"pQ{i}")

    def takeQ(self, i):
        return self.take(f"pQ{i}")

    def pump(self, n):
        for _ in range(min(n, len(self.wsvc))):
            self.wsvc.pop(0)()

    def pump_all(self):
        while self.wsvc:
            self.wsvc.pop(0)()

    def _wpieces(self, K, M, cap=1024):
        kch = K // 128
        if M >= cap:
            for c in range(kch):
                for m0 in range(0, M, cap):
                    yield (c, c + 1, m0, min(M, m0 + cap))
        else:
            per = max(1, cap // M)
            for c0 in range(0, kch, per):
                yield (c0, min(kch, c0 + per), 0, M)

    def load_weight_cast(self, src, dst, K, M, ceng="S", tag=""):
        p, d, sb = self.p, self.d, self.sb
        dstB = self.B(dst)
        dap = sb[dst].ap()
        sap = d[src].rearrange("(c p) m -> p c m", p=128)
        for i, (c0, c1, m0, m1) in enumerate(self._wpieces(K, M)):
            n = (c1 - c0) * (m1 - m0)
            half = i % 2
            stB = self.take(f"wstage{half}")
            st3 = sb["wstage"].ap()[:, half, 0:n].rearrange(
                "p (a b) -> p a b", b=m1 - m0)
            p.op("Q", lambda e, a=st3, s=sap[:, c0:c1, m0:m1]: e.dma_start(
                a, s, allow_slow_non_contiguous=True),
                writes=[stB], dma=True, tag=f"ldw_{src}_{i}{tag}")
            cp = (lambda e, o, a: e.copy(o, a)) if ceng == "S" else \
                 (lambda e, o, a: e.tensor_copy(o, a))
            p.op(ceng, lambda e, o=dap[:, c0:c1, m0:m1], a=st3, cp=cp: cp(e, o, a),
                 reads=[stB], writes=[dstB], tag=f"cast_{src}_{i}{tag}")

    def emit_ln(self, src_ap, src_buf, ntok, out_ap, out_buf, tag,
                xb_name="xb", sq_eng="S"):
        p, sb = self.p, self.sb
        n = ntok
        xb = sb[xb_name].ap()[:, :, 0:n]
        xbB = self.take(xb_name)
        p.op("V", lambda e, o=xb, s=src_ap: e.tensor_copy(o, s), reads=[src_buf],
             writes=[xbB], tag=f"{tag}cast")
        pm = self.pb[5].ap()
        pmB = self.take("pb5")
        oc = sb["ones_col"].ap()
        ocB = self.B("c_ones_col")
        for c in range(NCH):
            p.op("T", lambda e, c=c, n=n: e.matmul(pm[0:1, 0:n], oc, xb[:, c, :],
                                                   start=(c == 0), stop=(c == NCH - 1)),
                 reads=[xbB, ocB], writes=[pmB], tag=f"{tag}mmu")
        pv = self.pb[6].ap()
        pvB = self.take("pb6")
        for c in range(NCH):
            tnm = "tmpa" if c % 2 == 0 else "tmpb"
            sq = self.sb[tnm].ap().bitcast(BF16)[:, 0:n]
            sqB = self.take(tnm)
            if sq_eng == "S":
                p.op("S", lambda e, sq=sq, c=c: e.activation(
                    sq, xb[:, c, :], AF.Square),
                    reads=[xbB], writes=[sqB], tag=f"{tag}sq{c}")
            else:
                p.op("V", lambda e, sq=sq, c=c: e.tensor_tensor(
                    sq, xb[:, c, :], xb[:, c, :], ALU.mult),
                    reads=[xbB], writes=[sqB], tag=f"{tag}sq{c}")
            p.op("T", lambda e, sq=sq, c=c, n=n: e.matmul(
                pv[0:1, 0:n], oc, sq, start=(c == 0), stop=(c == NCH - 1)),
                reads=[sqB, ocB], writes=[pvB], tag=f"{tag}mms")
        sr = sb["srow"].ap()
        mu = sr[:, 0:n]
        var = sr[:, 512:512 + n]
        srB = self.take("srow")
        p.op("V", lambda e, mu=mu, n=n: e.tensor_scalar(
            mu, pm[0:1, 0:n], 1.0 / D, None, ALU.mult),
            reads=[pmB], writes=[srB], tag=f"{tag}mu")
        p.op("V", lambda e, var=var, n=n: e.tensor_scalar(
            var, pv[0:1, 0:n], 1.0 / D, None, ALU.mult),
            reads=[pvB], writes=[srB], tag=f"{tag}ex2")
        ta = sb["tmpa"].ap()[0:1, 0:n]
        taB = self.take("tmpa")
        p.op("V", lambda e, ta=ta, mu=mu: e.tensor_tensor(ta, mu, mu, ALU.mult),
             reads=[srB], writes=[taB], tag=f"{tag}musq")
        p.op("V", lambda e, var=var, ta=ta: e.tensor_tensor(var, var, ta,
                                                            ALU.subtract),
             reads=[srB, taB], writes=[srB], tag=f"{tag}var")
        eps = sb["scal"].ap()[0:1, 0:1]
        p.op("S", lambda e, var=var: e.activation(var, var, AF.Ln, bias=eps),
             reads=[srB, self.B("scal")], writes=[srB], tag=f"{tag}lnv")
        p.op("S", lambda e, var=var: e.activation(var, var, AF.Exp, scale=-0.5),
             reads=[srB], writes=[srB], tag=f"{tag}inv")
        p.op("V", lambda e, mu=mu, var=var: e.tensor_tensor(mu, mu, var, ALU.mult),
             reads=[srB], writes=[srB], tag=f"{tag}m2")
        orf = sb["ones_row_f32"].ap()
        orfB = self.B("c_ones_row_f32")
        pb4 = self.pb[4].ap()
        for rowap, dst in ((var, "invb"), (mu, "m2b")):
            pb4B = self.take("pb4")
            p.op("T", lambda e, r=rowap, n=n: e.matmul(pb4[:, 0:n], orf, r,
                                                       start=True, stop=True),
                 reads=[srB, orfB], writes=[pb4B], tag=f"{tag}bc_{dst}")
            dB = self.take(dst)
            p.op("V", lambda e, o=sb[dst].ap()[:, 0:n], n=n: e.tensor_copy(
                o, pb4[:, 0:n]),
                reads=[pb4B], writes=[dB], tag=f"{tag}bcc_{dst}")
        invb = sb["invb"].ap()[:, 0:n]
        m2bv = sb["m2b"].ap()[:, 0:n]
        tb16 = sb["tmpa"].ap().bitcast(BF16)[:, 0:n]
        for c in range(NCH):
            tB = self.take("tmpa")
            p.op("V", lambda e, c=c, t=tb16, s=src_ap[:, c, 0:n], i=invb:
                 e.tensor_tensor(t, s, i, ALU.mult),
                 reads=[src_buf, self.B("invb")], writes=[tB], tag=f"{tag}ap1_{c}")
            p.op("V", lambda e, c=c, o=out_ap[:, c, 0:n], t=tb16, m=m2bv:
                 e.tensor_tensor(o, t, m, ALU.subtract),
                 reads=[tB, self.B("m2b")], writes=[out_buf], tag=f"{tag}ap2_{c}")

    def phase_init(self):
        p, d, sb = self.p, self.d, self.sb
        loads = [("ident", "ident"), ("pswap", "pswap"), ("maskdiag", "maskdiag"),
                 ("headsel", "headsel"),
                 ("cn_b1", "bias_cn1"), ("cn_b2", "bias_cn2"),
                 ("cn_b3", "bias_cn3"), ("gc_b1", "bias_gc1"),
                 ("gc_b2", "bias_gc2"), ("l1bq", "bias_l1q"),
                 ("l1bk", "bias_l1k"), ("l1bo", "bias_l1o"),
                 ("lnw", "lnw"), ("lnb", "lnb")]
        for src, t in loads:
            p.op("Q", lambda e, s=src, t=t: e.dma_start(sb[t].ap(), d[s].ap()),
                 writes=[self.B(t)], dma=True, tag=f"ld_{t}")
        for src, t, bn in (("ones_col", "ones_col", "c_ones_col"),
                           ("ones_row", "ones_row", "c_ones_row"),
                           ("ones_row_f32", "ones_row_f32", "c_ones_row_f32"),
                           ("ones_col_f32", "ones_col_f32", "c_ones_col_f32")):
            p.op("Q", lambda e, s=src, t=t: e.dma_start(sb[t].ap(), d[s].ap()),
                 writes=[self.B(bn)], dma=True, tag=f"ld_{t}")
        p.op("Q", lambda e: e.dma_start(
            sb["x"].ap(), d["xT"].rearrange("(c p) t -> p c t", p=128)),
            writes=[self.take("x")], dma=True, tag="ld_x")
        p.op("V", lambda e: e.memset(sb["scal"].ap()[:, 0:1], 1e-5),
             writes=[self.B("scal")], tag="eps")
        for i, src in ((0, "arange"), (1, "cn_b4"), (2, "gc_b3")):
            p.op("Q", lambda e, i=i, s=src: e.dma_start(
                sb["grow"].ap()[0:1, i:i + 1], d[s].ap()),
                writes=[self.B("grow")], dma=True, tag=f"ld_{src}")
        self.load_cn()
        self.enqueue_quant()

    def load_cn(self):
        for nm in ("cn1", "cn2", "cn3", "cn4"):
            self.take(nm)
        self.load_weight_cast("cn_w1", "cn1", D, D)
        self.load_weight_cast("cn_w2", "cn2", D, 512)
        self.load_weight_cast("cn_w3", "cn3", 512, 256)
        self.load_weight_cast("cn_w4", "cn4", 256, 1)

    def load_gc(self):
        for nm in ("gc1t", "gc2t", "gc3t"):
            self.take(nm)
        self.load_weight_cast("gc_w1", "gc1t", D, 512)
        self.load_weight_cast("gc_w2", "gc2t", 512, 256)
        self.load_weight_cast("gc_w3", "gc3t", 256, 1)

    def enqueue_quant(self):
        p, d, sb = self.p, self.d, self.sb
        pb7 = self.pb[7].ap()
        tA = sb["tmpa"].ap().bitcast(BF16)
        tB_ = sb["tmpb"].ap().bitcast(BF16)

        def g1piece(src, gcol, idx, c, m0):
            def go():
                half = idx % 2
                stB = self.take(f"wstage{half}")
                ap_st = sb["wstage"].ap()[:, half, :]
                st3 = sb["wstage"].ap()[:, half, :].rearrange(
                    "p (a b) -> p a b", b=128)
                sap = d[src].rearrange("(c p) m -> p c m", p=128)
                p.op("Q", lambda e, a=ap_st, s=sap[:, c:c + 1, m0:m0 + 1024]:
                     e.dma_start(a, s.rearrange("p c m -> p (c m)")),
                     writes=[stB], dma=True, tag=f"g1_{src}_{idx}")
                t8 = sb["tmpa"].ap()[:, 0:8]
                aB = self.take("tmpa")
                p.op("V", lambda e, s3=st3, t8=t8: e.reduce_sum(
                    t8, s3, AX.X, apply_absolute_value=True),
                    reads=[stB], writes=[aB], tag=f"g1r1_{src}_{idx}")
                gaB = self.B("gacc")
                p.op("V", lambda e, t8=t8, gcol=gcol: e.reduce_sum(
                    sb["gacc"].ap()[:, gcol:gcol + 1], t8, AX.X),
                    reads=[aB], writes=[gaB], tag=f"g1r2_{src}_{idx}")
            self.wsvc.append(go)

        for gbase, src, K, M in ((0, "wfcT", D, 4096), (32, "wprT", 4096, D)):
            pieces = list(self._wpieces(K, M))
            assert len(pieces) == 32
            for idx, (c0, c1, m0, m1) in enumerate(pieces):
                g1piece(src, gbase + idx, idx, c0, m0)

        def finalize():
            gr = sb["grow"].ap()
            grB = self.B("grow")
            gaB = self.B("gacc")
            ga3 = sb["gacc"].ap().rearrange("p (a b) -> p a b", b=8)
            for gbase, col, pcol in ((0, 8, 0), (32, 16, 1)):
                taB = self.take("tmpa")
                p.op("V", lambda e, gbase=gbase: e.reduce_sum(
                    sb["tmpa"].ap()[:, 0:4], ga3[:, gbase // 8:gbase // 8 + 4, :],
                    AX.X),
                    reads=[gaB], writes=[taB], tag=f"gf1_{col}")
                p.op("V", lambda e: e.reduce_sum(
                    sb["tmpa"].ap()[:, 8:9], sb["tmpa"].ap()[:, 0:4], AX.X),
                    reads=[taB], writes=[taB], tag=f"gf2_{col}")
                pb7B = self.take("pb7")
                p.op("T", lambda e, pcol=pcol: e.matmul(
                    pb7[0:1, pcol:pcol + 1], sb["tmpa"].ap()[:, 8:9],
                    sb["ones_col_f32"].ap(), start=True, stop=True),
                    reads=[taB, self.B("c_ones_col_f32")], writes=[pb7B],
                    tag=f"gf3_{col}")
                p.op("V", lambda e, col=col, pcol=pcol: e.tensor_scalar(
                    gr[:, col + 1:col + 2], pb7[0:1, pcol:pcol + 1],
                    1.0 / (4096.0 * 1024.0), 1e-6, ALU.mult, ALU.add),
                    reads=[pb7B], writes=[grB], tag=f"gam{col}")
                p.op("V", lambda e, col=col: e.tensor_scalar(
                    gr[:, col + 2:col + 3], gr[:, col + 1:col + 2],
                    0.5, None, ALU.mult),
                    reads=[grB], writes=[grB], tag=f"gthr{col}")
            pb4 = self.pb[4].ap()
            for src_col, dst in ((10, 1), (18, 3)):
                pb4B = self.take("pb4")
                p.op("T", lambda e, sc_=src_col: e.matmul(
                    pb4[:, 0:1], sb["ones_row_f32"].ap(), gr[:, sc_:sc_ + 1],
                    start=True, stop=True),
                    reads=[grB, self.B("c_ones_row_f32")], writes=[pb4B],
                    tag=f"gbc{src_col}")
                sclB = self.B("scal")
                p.op("V", lambda e, dst=dst: e.tensor_copy(
                    sb["scal"].ap()[:, dst:dst + 1], pb4[:, 0:1]),
                    reads=[pb4B], writes=[sclB], tag=f"gbcc{dst}")
                p.op("V", lambda e, dst=dst: e.tensor_scalar(
                    sb["scal"].ap()[:, dst + 1:dst + 2],
                    sb["scal"].ap()[:, dst:dst + 1], -1.0, None, ALU.mult),
                    reads=[sclB], writes=[sclB], tag=f"gneg{dst}")
        self.wsvc.append(finalize)

        def qpiece(src, dst, thr_col, idx, c, m0):
            def go():
                half = idx % 2
                stB = self.take(f"wstage{half}")
                ap_st = sb["wstage"].ap()[:, half, :]
                sap = d[src].rearrange("(c p) m -> p c m", p=128)
                dap = d[dst].rearrange("(c p) m -> p c m", p=128)
                p.op("Q", lambda e, a=ap_st, s=sap[:, c:c + 1, m0:m0 + 1024]:
                     e.dma_start(a, s.rearrange("p c m -> p (c m)")),
                     writes=[stB], dma=True, tag=f"q2_{src}_{idx}")
                thr = sb["scal"].ap()[:, thr_col:thr_col + 1]
                nthr = sb["scal"].ap()[:, thr_col + 1:thr_col + 2]
                aB = self.take("tmpa")
                p.op("V", lambda e, a=ap_st, nthr=nthr: e.tensor_scalar(
                    tA, a, nthr, None, ALU.is_lt),
                    reads=[stB, self.B("scal")], writes=[aB],
                    tag=f"q2lt_{src}_{idx}")
                bB = self.take("tmpb")
                p.op("V", lambda e, a=ap_st, thr=thr: e.scalar_tensor_tensor(
                    tB_, a, thr, tA, ALU.is_gt, ALU.subtract),
                    reads=[stB, aB, self.B("scal")], writes=[bB],
                    tag=f"q2q_{src}_{idx}")
                p.op("G", lambda e, o=dap[:, c:c + 1, m0:m0 + 1024]: e.dma_start(
                    o.rearrange("p c m -> p (c m)"), tB_),
                    reads=[bB], writes=[self.B(dst + "_dram")], dma=True,
                    tag=f"q2st_{src}_{idx}")
            self.wsvc.append(go)

        for src, dst, thr_col, K, M in (("wfcT", "wfcq_d", 1, D, 4096),
                                        ("wprT", "wprq_d", 3, 4096, D)):
            for idx, (c0, c1, m0, m1) in enumerate(self._wpieces(K, M)):
                qpiece(src, dst, thr_col, idx, c0, m0)

    def phase_ln1(self):
        sb = self.sb
        self.take("fout")
        for g in range(NGRP):
            self.emit_ln(sb["x"].ap()[:, :, g * GRP:(g + 1) * GRP], self.B("x"),
                         GRP, sb["fout"].ap()[:, :, g * GRP:(g + 1) * GRP],
                         self.B("fout"), f"ln1g{g}")
            self.pump(10)

    def phase_ln3(self):
        p, d, sb = self.p, self.d, self.sb
        xB = self.take("x")
        p.op("Q", lambda e: e.dma_start(
            sb["x"].ap(), d["xs_b"].rearrange("(c p) t -> p c t", p=128)),
            reads=[self.B("xs_b")], writes=[xB], dma=True, tag="ld_x3")
        self.xloc = "x"
        self.load_cn()
        self.take("fout")
        for g in range(NGRP):
            self.emit_ln(sb["x"].ap()[:, :, g * GRP:(g + 1) * GRP], self.B("x"),
                         GRP, sb["fout"].ap()[:, :, g * GRP:(g + 1) * GRP],
                         self.B("fout"), f"ln3g{g}")
            self.pump(6)

    def phase_ffn0a(self):
        self.emit_ffn0("a", ncyc=_FFN0_NCYC)

    def phase_ffn0b(self):
        self.emit_ffn0("b")

    def emit_ffn0(self, call, ncyc=3, ntile=None):
        p, sb = self.p, self.sb
        self.load_gc()
        foB = self.B("fout")
        gr = sb["grow"].ap()
        grB = self.B("grow")
        gcx = sb["scal"].ap()[:, 8:16]
        sclB = self.B("scal")
        p.op("V", lambda e: e.reduce_sum(gcx, sb["fout"].ap(), AX.X),
             reads=[foB], writes=[sclB], tag=f"f0{call}_gcx")
        gxb = sb["tmpa"].ap().bitcast(BF16)[:, 0:8]
        gxbB = self.take("tmpa")
        p.op("V", lambda e: e.tensor_scalar(gxb, gcx, 1.0 / SEQ, None, ALU.mult),
             reads=[sclB], writes=[gxbB], tag=f"f0{call}_gcxb")
        pb6 = self.pb[6].ap()
        pb6B = self.take("pb6")
        for m in range(4):
            for c in range(NCH):
                p.op("T", lambda e, m=m, c=c: e.matmul(
                    pb6[:, m:m + 1], sb["gc1t"].ap()[:, c, m * 128:(m + 1) * 128],
                    gxb[:, c:c + 1], start=(c == 0), stop=(c == NCH - 1)),
                    reads=[gxbB, self.B("gc1t")], writes=[pb6B],
                    tag=f"f0{call}_g1mm")
        gh1 = sb["tmpb"].ap().bitcast(BF16)[:, 0:4]
        gh1B = self.take("tmpb")
        for m in range(4):
            p.op("S", lambda e, m=m: e.activation(
                gh1[:, m:m + 1], pb6[:, m:m + 1], AF.Gelu,
                bias=sb["bias_gc1"].ap()[:, m:m + 1]),
                reads=[pb6B, self.B("bias_gc1")], writes=[gh1B],
                tag=f"f0{call}_g1a")
        pb6B = self.take("pb6")
        for m in range(2):
            for c in range(4):
                p.op("T", lambda e, m=m, c=c: e.matmul(
                    pb6[:, 8 + m:9 + m],
                    sb["gc2t"].ap()[:, c, m * 128:(m + 1) * 128],
                    gh1[:, c:c + 1], start=(c == 0), stop=(c == 3)),
                    reads=[gh1B, self.B("gc2t")], writes=[pb6B],
                    tag=f"f0{call}_g2mm")
        gh2 = sb["tmpb"].ap().bitcast(BF16)[:, 8:10]
        gh2B = self.take("tmpb")
        for m in range(2):
            p.op("S", lambda e, m=m: e.activation(
                gh2[:, m:m + 1], pb6[:, 8 + m:9 + m], AF.Gelu,
                bias=sb["bias_gc2"].ap()[:, m:m + 1]),
                reads=[pb6B, self.B("bias_gc2")], writes=[gh2B],
                tag=f"f0{call}_g2a")
        pb6B = self.take("pb6")
        for c in range(2):
            p.op("T", lambda e, c=c: e.matmul(
                pb6[0:1, 16:17], sb["gc3t"].ap()[:, c, 0:1], gh2[:, c:c + 1],
                start=(c == 0), stop=(c == 1)),
                reads=[gh2B, self.B("gc3t")], writes=[pb6B], tag=f"f0{call}_g3mm")
        p.op("V", lambda e: e.tensor_scalar(gr[:, 3:4], gr[:, 2:3], 0.5, None,
                                            ALU.mult),
             reads=[grB], writes=[grB], tag=f"f0{call}_hb3")
        p.op("V", lambda e: e.tensor_scalar(gr[:, 4:5], gr[:, 1:2], 0.5, None,
                                            ALU.mult),
             reads=[grB], writes=[grB], tag=f"f0{call}_hb4")
        p.op("S", lambda e: e.activation(gr[:, 5:6], pb6[0:1, 16:17], AF.Tanh,
                                         bias=gr[:, 3:4], scale=0.5),
             reads=[pb6B, grB], writes=[grB], tag=f"f0{call}_gtanh")
        p.op("V", lambda e: e.tensor_scalar(gr[:, 6:7], gr[:, 5:6], 0.5, 0.5,
                                            ALU.mult, ALU.add),
             reads=[grB], writes=[grB], tag=f"f0{call}_gf01")
        p.op("V", lambda e: e.tensor_scalar(gr[:, 7:8], gr[:, 6:7], gr[:, 0:1],
                                            1.4, ALU.mult, ALU.mult),
             reads=[grB], writes=[grB], tag=f"f0{call}_u1")
        p.op("V", lambda e: e.tensor_scalar(gr[:, 26:27], gr[:, 0:1], -0.7, 0.1,
                                            ALU.mult, ALU.add),
             reads=[grB], writes=[grB], tag=f"f0{call}_u2")
        p.op("V", lambda e: e.tensor_tensor(gr[:, 24:25], gr[:, 7:8],
                                            gr[:, 26:27], ALU.add),
             reads=[grB], writes=[grB], tag=f"f0{call}_A")
        p.op("V", lambda e: e.tensor_scalar(gr[:, 25:26], gr[:, 0:1], 0.3, None,
                                            ALU.mult),
             reads=[grB], writes=[grB], tag=f"f0{call}_C")
        if ncyc == 0:
            return
        pbig = self.pbig.ap()
        signs = [(ALU.add, ALU.subtract), (ALU.add, ALU.subtract),
                 (ALU.subtract, ALU.add), (ALU.subtract, ALU.add)]
        fb = sb["fb16"].ap()
        h1 = sb["h1b"].ap()
        h2 = sb["h2b"].ap()
        h3 = sb["h3b"].ap()
        for cyc in range(ncyc):
            for t in range(ntile or (SEQ // FT1)):
                ts_ = slice(t * FT1, (t + 1) * FT1)
                fbB = self.take("fb16")
                p.op("S", lambda e, ts_=ts_: e.activation(
                    fb, sb["fout"].ap()[:, :, ts_], AF.Copy),
                    reads=[foB], writes=[fbB], tag=f"f0{call}cast{cyc}_{t}")
                h1B = self.take("h1b")
                for m in range(NCH):
                    qr = m % 2
                    pqB = self.takeQ(qr)
                    for c in range(NCH):
                        p.op("T", lambda e, m=m, c=c, qr=qr: e.matmul(
                            pbig[:, qr * 512:qr * 512 + 512],
                            sb["cn1"].ap()[:, c, m * 128:(m + 1) * 128],
                            fb[:, c, :], start=(c == 0), stop=(c == NCH - 1)),
                            reads=[fbB, self.B("cn1")], writes=[pqB],
                            tag=f"f0{call}h1mm{cyc}_{t}_{m}")
                    p.op("S", lambda e, m=m, qr=qr: e.activation(
                        h1[:, m, :], pbig[:, qr * 512:qr * 512 + 512], AF.Gelu,
                        bias=sb["bias_cn1"].ap()[:, m:m + 1]),
                        reads=[pqB, self.B("bias_cn1")], writes=[h1B],
                        tag=f"f0{call}h1a{cyc}_{t}_{m}")
                h2B = self.take("h2b")
                for m in range(4):
                    pqB = self.takeQ(2)
                    for c in range(NCH):
                        p.op("T", lambda e, m=m, c=c: e.matmul(
                            pbig[:, 1024:1536],
                            sb["cn2"].ap()[:, c, m * 128:(m + 1) * 128],
                            h1[:, c, :], start=(c == 0), stop=(c == NCH - 1)),
                            reads=[h1B, self.B("cn2")], writes=[pqB],
                            tag=f"f0{call}h2mm{cyc}_{t}_{m}")
                    p.op("S", lambda e, m=m: e.activation(
                        h2[:, m, :], pbig[:, 1024:1536], AF.Gelu,
                        bias=sb["bias_cn2"].ap()[:, m:m + 1]),
                        reads=[pqB, self.B("bias_cn2")], writes=[h2B],
                        tag=f"f0{call}h2a{cyc}_{t}_{m}")
                h3B = self.take("h3b")
                for m in range(2):
                    pqB = self.takeQ(3)
                    for c in range(4):
                        p.op("T", lambda e, m=m, c=c: e.matmul(
                            pbig[:, 1536:2048],
                            sb["cn3"].ap()[:, c, m * 128:(m + 1) * 128],
                            h2[:, c, :], start=(c == 0), stop=(c == 3)),
                            reads=[h2B, self.B("cn3")], writes=[pqB],
                            tag=f"f0{call}h3mm{cyc}_{t}_{m}")
                    p.op("S", lambda e, m=m: e.activation(
                        h3[:, m, :], pbig[:, 1536:2048], AF.Gelu,
                        bias=sb["bias_cn3"].ap()[:, m:m + 1]),
                        reads=[pqB, self.B("bias_cn3")], writes=[h3B],
                        tag=f"f0{call}h3a{cyc}_{t}_{m}")
                pb5 = self.pb[5].ap()
                pzB = self.take("pb5")
                for c in range(2):
                    p.op("T", lambda e, c=c: e.matmul(
                        pb5[0:1, 0:FT1], sb["cn4"].ap()[:, c, 0:1], h3[:, c, :],
                        start=(c == 0), stop=(c == 1)),
                        reads=[h3B, self.B("cn4")], writes=[pzB],
                        tag=f"f0{call}zmm{cyc}_{t}")
                srB = self.take("srow")
                tr = sb["srow"].ap()[:, 0:FT1]
                p.op("S", lambda e: e.activation(tr, pb5[0:1, 0:FT1], AF.Tanh,
                                                 bias=gr[:, 4:5], scale=0.5),
                     reads=[pzB, grB], writes=[srB], tag=f"f0{call}ctanh{cyc}_{t}")
                crow = sb["srow"].ap().bitcast(BF16)[0:1, 1024:1024 + FT1]
                p.op("V", lambda e: e.tensor_scalar(
                    crow, tr, gr[:, 25:26], gr[:, 24:25], ALU.mult, ALU.add),
                    reads=[srB, grB], writes=[srB], tag=f"f0{call}crow{cyc}_{t}")
                pb6 = self.pb[6].ap()
                pcB = self.take("pb6")
                p.op("T", lambda e: e.matmul(pb6[:, 0:FT1], sb["ones_row"].ap(),
                                             crow, start=True, stop=True),
                     reads=[srB, self.B("c_ones_row")], writes=[pcB],
                     tag=f"f0{call}cbc{cyc}_{t}")
                for pr in range(4):
                    sA, sB_ = signs[pr]
                    fA = sb["fout"].ap()[:, pr, ts_]
                    fB_ = sb["fout"].ap()[:, pr + 4, ts_]
                    m1B = self.take("tmpa")
                    p.op("V", lambda e, fB_=fB_: e.tensor_tensor(
                        sb["tmpa"].ap(), pb6[:, 0:FT1], fB_, ALU.mult),
                        reads=[pcB, foB], writes=[m1B],
                        tag=f"f0{call}m1_{cyc}_{t}_{pr}")
                    m2B = self.take("tmpb")
                    p.op("V", lambda e, fA=fA: e.tensor_tensor(
                        sb["tmpb"].ap(), pb6[:, 0:FT1], fA, ALU.mult),
                        reads=[pcB, foB], writes=[m2B],
                        tag=f"f0{call}m2_{cyc}_{t}_{pr}")
                    p.op("V", lambda e, fA=fA, sA=sA: e.tensor_tensor(
                        fA, fA, sb["tmpa"].ap(), sA),
                        reads=[m1B, foB], writes=[foB],
                        tag=f"f0{call}uA_{cyc}_{t}_{pr}")
                    p.op("V", lambda e, fB_=fB_, sB_=sB_: e.tensor_tensor(
                        fB_, fB_, sb["tmpb"].ap(), sB_),
                        reads=[m2B, foB], writes=[foB],
                        tag=f"f0{call}uB_{cyc}_{t}_{pr}")
                self.pump(4)
        xB = self.B("x")
        for t in range(SEQ // FT1):
            ts_ = slice(t * FT1, (t + 1) * FT1)
            p.op("V", lambda e, ts_=ts_: e.scalar_tensor_tensor(
                sb["x"].ap()[:, :, ts_], sb["fout"].ap()[:, :, ts_],
                0.5, sb["x"].ap()[:, :, ts_], ALU.mult, ALU.add),
                reads=[foB, xB], writes=[xB], tag=f"f0{call}res{t}")

    def phase_qkv0(self):
        self.emit_qkv(0)

    def phase_qkv1(self):
        self.emit_qkv(1)

    def emit_qkv(self, layer):
        p, d, sb = self.p, self.d, self.sb
        for g in range(NGRP):
            gs = slice(g * GRP, (g + 1) * GRP)
            if layer == 0:
                src_ap = sb["x"].ap()[:, :, gs]
                srcB = self.B("x")
            else:
                xgB = self.take("xgrp")
                p.op("Q", lambda e, gs=gs: e.dma_start(
                    sb["xgrp"].ap(),
                    d["xs_b"].rearrange("(c p) t -> p c t", p=128)[:, :, gs]),
                    reads=[self.B("xs_b")], writes=[xgB], dma=True,
                    tag=f"q{layer}ldx{g}")
                src_ap = sb["xgrp"].ap()
                srcB = xgB
            nm = "lnx_a" if g % 2 == 0 else "lnx_b"
            outB = self.take(nm)
            self.emit_ln(src_ap, srcB, GRP, sb[nm].ap(), outB, f"q{layer}ln{g}")
            p.op("Q", lambda e, gs=gs, nm=nm: e.dma_start(
                d["lnxs"].rearrange("(c p) t -> p c t", p=128)[:, :, gs],
                sb[nm].ap()),
                reads=[outB], writes=[self.B("lnxs")], dma=True,
                tag=f"q{layer}stln{g}")
            if layer == 0:
                p.op("Q", lambda e, gs=gs: e.dma_start(
                    d["xs_a"].rearrange("(c p) t -> p c t", p=128)[:, :, gs],
                    sb["x"].ap()[:, :, gs]),
                    reads=[self.B("x")], writes=[self.B("xs_a")], dma=True,
                    tag=f"q{layer}dumpx{g}")
            self.pump(8)
        self.xloc = "xs_a" if layer == 0 else "xs_b"
        wsrc = {0: {"q": "l0wq", "k": "l0wk", "v": "l0wv", "o": "l0wo"},
                1: {"q": "l1wq", "k": "l1wk", "v": "l1wv", "o": "l1wo"}}[layer]
        qB = self.take("q")
        kB = self.take("k")
        vB = self.take("v_tm")
        pbig = self.pbig.ap()
        for half in range(2):
            lhB = self.take("lnx_half")
            p.op("Q", lambda e, half=half: e.dma_start(
                sb["lnx_half"].ap(),
                d["lnxs"].rearrange("(c p) t -> p c t", p=128)
                [:, :, half * 1024:(half + 1) * 1024]),
                reads=[self.B("lnxs")], writes=[lhB], dma=True,
                tag=f"q{layer}ldln{half}")
            lh = sb["lnx_half"].ap()
            for wi, w in enumerate(("q", "k", "v")):
                slot = ("wslotA", "wslotB")[wi % 2]
                slotB = self.take(slot)
                self.load_weight_cast(wsrc[w], slot, D, D, ceng="S",
                                      tag=f"_{layer}{half}")
                wap = sb[slot].ap()
                if w in ("q", "k"):
                    dst = sb[w].ap()
                    dstB = qB if w == "q" else kB
                    for m in range(NCH):
                        for ng in range(2):
                            qr = (m * 2 + ng) % 2
                            pqB = self.takeQ(qr)
                            for c in range(NCH):
                                p.op("T", lambda e, m=m, ng=ng, c=c, qr=qr,
                                     wap=wap, lh=lh: e.matmul(
                                         pbig[:, qr * 512:qr * 512 + 512],
                                         wap[:, c, m * 128:(m + 1) * 128],
                                         lh[:, c, ng * 512:(ng + 1) * 512],
                                         start=(c == 0), stop=(c == NCH - 1)),
                                     reads=[lhB, slotB], writes=[pqB],
                                     tag=f"q{layer}{w}mm{half}_{m}_{ng}")
                            osl = dst[:, m, half * 1024 + ng * 512:
                                      half * 1024 + (ng + 1) * 512]
                            if layer == 0:
                                p.op("V", lambda e, o=osl, qr=qr: e.tensor_copy(
                                    o, pbig[:, qr * 512:qr * 512 + 512]),
                                    reads=[pqB], writes=[dstB],
                                    tag=f"q{layer}{w}ev{half}_{m}_{ng}")
                            else:
                                bn = "bias_l1q" if w == "q" else "bias_l1k"
                                p.op("S", lambda e, o=osl, qr=qr, m=m, bn=bn:
                                     e.activation(
                                         o, pbig[:, qr * 512:qr * 512 + 512],
                                         AF.Identity,
                                         bias=sb[bn].ap()[:, m:m + 1]),
                                     reads=[pqB, self.B(bn)], writes=[dstB],
                                     tag=f"q{layer}{w}ev{half}_{m}_{ng}")
                else:
                    for kc in range(8):
                        for ng in range(2):
                            qr = (kc * 2 + ng) % 2
                            pqB = self.takeQ(qr)
                            for c in range(NCH):
                                p.op("T", lambda e, kc=kc, ng=ng, c=c, qr=qr,
                                     wap=wap, lh=lh: e.matmul(
                                         pbig[:, qr * 512:qr * 512 + 512],
                                         lh[:, c, kc * 128:(kc + 1) * 128],
                                         wap[:, c, ng * 512:(ng + 1) * 512],
                                         start=(c == 0), stop=(c == NCH - 1)),
                                     reads=[lhB, slotB], writes=[pqB],
                                     tag=f"q{layer}vmm{half}_{kc}_{ng}")
                            ovs = sb["v_tm"].ap()[:, half * 8 + kc,
                                                  ng * 512:(ng + 1) * 512]
                            if layer == 0:
                                p.op("V", lambda e, o=ovs, qr=qr: e.tensor_copy(
                                    o, pbig[:, qr * 512:qr * 512 + 512]),
                                    reads=[pqB], writes=[vB],
                                    tag=f"q{layer}vev{half}_{kc}_{ng}")
                            else:
                                p.op("V", lambda e, o=ovs, qr=qr, ng=ng:
                                     e.tensor_tensor(
                                         o, pbig[:, qr * 512:qr * 512 + 512],
                                         sb["bvb"].ap()[:, ng * 512:(ng + 1) * 512],
                                         ALU.add),
                                     reads=[pqB, self.B("bvb")], writes=[vB],
                                     tag=f"q{layer}vev{half}_{kc}_{ng}")
                self.pump(4)
        woB = self.take("wo_t")
        self.load_weight_cast(wsrc["o"], "wo_t", D, D, ceng="S", tag=f"_o{layer}")
        if layer == 1:
            self.emit_rope()
        self.emit_negB(layer)

    def emit_bvb(self):
        p, d, sb = self.p, self.d, self.sb
        stB = self.take("wstage0")
        p.op("Q", lambda e: e.dma_start(sb["wstage"].ap()[0:1, 0, 0:D],
                                        d["l1bv"].ap()),
             writes=[stB], dma=True, tag="ld_bv")
        srB = self.take("srow")
        bv = sb["srow"].ap().bitcast(BF16)[0:1, 0:D]
        p.op("V", lambda e: e.tensor_copy(bv, sb["wstage"].ap()[0:1, 0, 0:D]),
             reads=[stB], writes=[srB], tag="bv_cast")
        pbig = self.pbig.ap()
        bvbB = self.take("bvb")
        for ng in range(2):
            pqB = self.takeQ(ng)
            p.op("T", lambda e, ng=ng: e.matmul(
                pbig[:, ng * 512:(ng + 1) * 512], sb["ones_row"].ap(),
                bv[:, ng * 512:(ng + 1) * 512], start=True, stop=True),
                reads=[srB, self.B("c_ones_row")], writes=[pqB], tag=f"bvbc{ng}")
            p.op("V", lambda e, ng=ng: e.tensor_copy(
                sb["bvb"].ap()[:, ng * 512:(ng + 1) * 512],
                pbig[:, ng * 512:(ng + 1) * 512]),
                reads=[pqB], writes=[bvbB], tag=f"bvcp{ng}")

    def emit_rope(self):
        p, d, sb = self.p, self.d, self.sb
        for src, dst in (("ropecos", "cosb"), ("ropesin", "sinb")):
            dB = self.take(dst)
            for piece in range(2):
                stB = self.take(f"wstage{piece}")
                ap_st = sb["wstage"].ap()[:, piece, :]
                p.op("Q", lambda e, a=ap_st, s=src, piece=piece: e.dma_start(
                    a, d[s].ap()[:, piece * 1024:(piece + 1) * 1024]),
                    writes=[stB], dma=True, tag=f"ld_{dst}{piece}")
                p.op("S", lambda e, a=ap_st, dst=dst, piece=piece: e.copy(
                    sb[dst].ap()[:, piece * 1024:(piece + 1) * 1024], a),
                    reads=[stB], writes=[dB], tag=f"cast_{dst}{piece}")
        pbig = self.pbig.ap()
        cosB, sinB = self.B("cosb"), self.B("sinb")
        tAb = sb["tmpa"].ap().bitcast(BF16)[:, 0:512]
        for w in ("q", "k"):
            tgt = sb[w].ap()
            tgtB = self.B(w)
            for ch in range(NCH):
                for kg in range(4):
                    ks = slice(kg * 512, (kg + 1) * 512)
                    qr = kg % 2
                    pqB = self.takeQ(qr)
                    p.op("T", lambda e, ch=ch, ks=ks, qr=qr, tgt=tgt: e.matmul(
                        pbig[:, qr * 512:qr * 512 + 512], sb["pswap"].ap(),
                        tgt[:, ch, ks], start=True, stop=True),
                        reads=[tgtB, self.B("pswap")], writes=[pqB],
                        tag=f"rsw_{w}{ch}_{kg}")
                    tB = self.take("tmpa")
                    p.op("G", lambda e, ch=ch, ks=ks, tgt=tgt: e.tensor_tensor(
                        tAb, tgt[:, ch, ks], sb["cosb"].ap()[:, ks], ALU.mult),
                        reads=[tgtB, cosB], writes=[tB], tag=f"rt_{w}{ch}_{kg}")
                    mB = self.take("tmpb")
                    p.op("V", lambda e, ks=ks, qr=qr: e.tensor_tensor(
                        sb["tmpb"].ap(), pbig[:, qr * 512:qr * 512 + 512],
                        sb["sinb"].ap()[:, ks], ALU.mult),
                        reads=[pqB, sinB], writes=[mB], tag=f"rm_{w}{ch}_{kg}")
                    p.op("V", lambda e, ch=ch, ks=ks, tgt=tgt: e.tensor_tensor(
                        tgt[:, ch, ks], tAb, sb["tmpb"].ap(), ALU.add),
                        reads=[tB, mB], writes=[tgtB], tag=f"ro_{w}{ch}_{kg}")
            self.pump(2)

    def emit_negB(self, layer):
        p, sb = self.p, self.sb
        pbig = self.pbig.ap()
        scale = 0.125 if layer == 1 else 1.0 / 32.0
        nh = 16 if layer == 1 else 1
        st = sb["stats"].ap()
        for iw, w in enumerate(("q", "k")):
            tgt = sb[w].ap()
            tgtB = self.B(w)
            pqall = [self.takeQ(i) for i in range(4)]
            sqB = self.take("sq128")
            for ch in range(NCH):
                p.op("S", lambda e, ch=ch, tgt=tgt: e.activation(
                    sb["sq128"].ap(), tgt[:, ch, :], AF.Square),
                    reads=[tgtB], writes=[sqB], tag=f"nBsq_{w}{ch}")
                for kg in range(4):
                    if layer == 1:
                        p.op("T", lambda e, ch=ch, kg=kg: e.matmul(
                            pbig[0:16, kg * 512:(kg + 1) * 512],
                            sb["headsel"].ap()[:, ch, :],
                            sb["sq128"].ap()[:, kg * 512:(kg + 1) * 512],
                            start=(ch == 0), stop=(ch == NCH - 1)),
                            reads=[sqB, self.B("headsel")], writes=[pqall[kg]],
                            tag=f"nBmm_{w}{ch}_{kg}")
                    else:
                        p.op("T", lambda e, ch=ch, kg=kg: e.matmul(
                            pbig[0:1, kg * 512:(kg + 1) * 512],
                            sb["ones_col"].ap(),
                            sb["sq128"].ap()[:, kg * 512:(kg + 1) * 512],
                            start=(ch == 0), stop=(ch == NCH - 1)),
                            reads=[sqB, self.B("c_ones_col")], writes=[pqall[kg]],
                            tag=f"nBmm_{w}{ch}_{kg}")
            stB = self.take("stats")
            p.op("V", lambda e, nh=nh: e.reduce_max(
                st[0:nh, 0:1], pbig[0:nh, :], AX.X),
                reads=pqall, writes=[stB], tag=f"nBmax_{w}")
            p.op("S", lambda e, nh=nh, iw=iw: e.activation(
                st[0:nh, 1 + iw:2 + iw], st[0:nh, 0:1], AF.Ln),
                reads=[stB], writes=[stB], tag=f"nBln_{w}")
        stB = self.B("stats")
        p.op("V", lambda e, nh=nh: e.tensor_tensor(
            st[0:nh, 3:4], st[0:nh, 1:2], st[0:nh, 2:3], ALU.add),
            reads=[stB], writes=[stB], tag="nBadd")
        p.op("S", lambda e, nh=nh: e.activation(
            st[0:nh, 3:4], st[0:nh, 3:4], AF.Exp, scale=0.5),
            reads=[stB], writes=[stB], tag="nBexp")
        p.op("V", lambda e, nh=nh, scale=scale: e.tensor_scalar(
            st[0:nh, 3:4], st[0:nh, 3:4], -scale, None, ALU.mult),
            reads=[stB], writes=[stB], tag="nBscale")
        bvec = sb["stats"].ap().bitcast(BF16)[0:nh, 8:9]
        p.op("V", lambda e, nh=nh, bvec=bvec: e.tensor_copy(
            bvec, st[0:nh, 3:4]),
            reads=[stB], writes=[stB], tag="nBb16")
        pb4 = self.pb[4].ap()
        pb4b16 = self.pb[4].ap().bitcast(BF16)
        pb4B = self.take("pb4")
        p.op("T", lambda e, nh=nh, bvec=bvec: e.matmul(
            pb4[0:1, 0:nh], bvec, sb["ident"].ap()[0:nh, 0:nh],
            start=True, stop=True),
            reads=[stB, self.B("ident")], writes=[pb4B], tag="nBtr")
        srB = self.take("srow")
        p.op("V", lambda e, nh=nh: e.tensor_copy(sb["srow"].ap()[0:1, 0:nh],
                                                 pb4[0:1, 0:nh]),
             reads=[pb4B], writes=[srB], tag="nBtrc")
        pb4B = self.take("pb4")
        p.op("T", lambda e, nh=nh: e.matmul(
            pb4[:, 0:nh], sb["ones_row_f32"].ap(), sb["srow"].ap()[0:1, 0:nh],
            start=True, stop=True),
            reads=[srB, self.B("c_ones_row_f32")], writes=[pb4B], tag="nBbc")
        nBB = self.take("negB")
        col0 = 0 if layer == 1 else 16
        p.op("V", lambda e, nh=nh, col0=col0: e.tensor_copy(
            sb["negB"].ap()[:, col0:col0 + nh], pb4[:, 0:nh]),
            reads=[pb4B], writes=[nBB], tag="nBcp")

    def phase_attn0(self):
        self.emit_attn(0)

    def phase_attn1(self):
        self.emit_attn(1)

    def emit_attn(self, layer):
        p, d, sb = self.p, self.d, self.sb
        pbig = self.pbig.ap()
        xs_src = d["xs_a"] if layer == 0 else d["xs_b"]
        xs_dst = d["xs_b"] if layer == 0 else d["xs_a"]
        xsrcB = self.B("xs_a" if layer == 0 else "xs_b")
        xdstB = self.B("xs_b" if layer == 0 else "xs_a")
        sc = 1.0 / 32.0 if layer == 0 else 0.125
        heads = 1 if layer == 0 else HEADS
        qB, kB, vB = self.B("q"), self.B("k"), self.B("v_tm")
        woB = self.B("wo_t")
        sums = sb["stats"].ap()
        pb5v = self.pb[5].ap().rearrange("p (a b) -> p a b", b=128)
        pb6v = self.pb[6].ap().rearrange("p (a b) -> p a b", b=128)
        items = [(qt, h) for qt in range(NQT) for h in range(heads)]
        state = {}

        def stageA(i):
            qt, h = items[i]
            par = i % 2
            L = (qt + 1) * 128
            kgs = (L + 511) // 512
            qs = slice(qt * 128, qt * 128 + 128)
            Pn = "P" if par == 0 else "P2"
            Pap = sb[Pn].ap()
            PB = self.take(Pn)
            smB = self.take(f"stats{par}")
            colB = 16 if layer == 0 else h
            if layer == 0:
                lq = [(c, 0, 128) for c in range(NCH)]
            else:
                lq = [(h // 2, (h % 2) * 64, 64)]
            pqs = []
            for kg in range(kgs):
                n = min(512, L - kg * 512)
                ks = slice(kg * 512, kg * 512 + n)
                qr = 2 * par + kg % 2
                pqB = self.takeQ(qr)
                pqs.append(pqB)
                for j, (c, poff, kw) in enumerate(lq):
                    p.op("T", lambda e, c=c, poff=poff, kw=kw, ks=ks, qs=qs,
                         qr=qr, j=j, n=n, nl=len(lq): e.matmul(
                             pbig[:, qr * 512:qr * 512 + n],
                             sb["q"].ap()[poff:poff + kw, c, qs],
                             sb["k"].ap()[poff:poff + kw, c, ks],
                             start=(j == 0), stop=(j == nl - 1)),
                         reads=[qB, kB], writes=[pqB],
                         tag=f"a{layer}s{qt}_{h}_{kg}")
                if kg == kgs - 1:
                    p.op("V", lambda e, qr=qr, n=n: e.tensor_tensor(
                        pbig[:, qr * 512 + n - 128:qr * 512 + n],
                        pbig[:, qr * 512 + n - 128:qr * 512 + n],
                        sb["maskdiag"].ap(), ALU.add),
                        reads=[self.B("maskdiag"), pqB], writes=[pqB],
                        tag=f"a{layer}mask{qt}_{h}")
                p.op("S", lambda e, qr=qr, n=n, ks=ks, kg=kg, colB=colB,
                     Pap=Pap, par=par: e.activation(
                         Pap[:, ks], pbig[:, qr * 512:qr * 512 + n], AF.Exp,
                         bias=sb["negB"].ap()[:, colB:colB + 1], scale=sc,
                         accum_out=sums[:, par * 4 + kg:par * 4 + kg + 1]),
                     reads=[pqB, self.B("negB")], writes=[PB, smB],
                     tag=f"a{layer}exp{qt}_{h}_{kg}")
            state[i] = (PB, smB, pqs)

        def stageB(i):
            qt, h = items[i]
            par = i % 2
            L = (qt + 1) * 128
            kgs = (L + 511) // 512
            nkc = L // 128
            qs = slice(qt * 128, qt * 128 + 128)
            Pn = "P" if par == 0 else "P2"
            Pap = sb[Pn].ap()
            PB, smB, pqs = state.pop(i)
            dgn = "diag" if par == 0 else "diag2"
            riB = self.take(f"rinv{par}")
            ri = sb["rinv"].ap()[:, par:par + 1]
            p.op("V", lambda e, kgs=kgs, par=par, ri=ri: e.reduce_sum(
                ri, sums[:, par * 4:par * 4 + kgs], AX.X),
                reads=[smB], writes=[riB], tag=f"a{layer}rs{qt}_{h}")
            p.op("V", lambda e, ri=ri: e.reciprocal(ri, ri),
                 reads=[riB], writes=[riB], tag=f"a{layer}ri{qt}_{h}")
            dgB = self.take(dgn)
            p.op("V", lambda e, dgn=dgn, ri=ri: e.tensor_scalar(
                sb[dgn].ap(), sb["ident"].ap(), ri, None, ALU.mult),
                reads=[riB, self.B("ident")], writes=[dgB],
                tag=f"a{layer}dg{qt}_{h}")
            trb = self.pb[4] if par == 0 else self.pb[7]
            trn = "pb4" if par == 0 else "pb7"
            trf = trb.ap()
            PTB = self.take("PT")
            for kc0 in range(0, nkc, 4):
                nb = min(4, nkc - kc0)
                p4B = self.take(trn)
                for j in range(nb):
                    p.op("T", lambda e, kc0=kc0, j=j, Pap=Pap, dgn=dgn, trf=trf:
                         e.matmul(trf[:, j * 128:(j + 1) * 128],
                                  Pap[:, (kc0 + j) * 128:(kc0 + j + 1) * 128],
                                  sb[dgn].ap(), start=True, stop=True),
                         reads=[PB, dgB], writes=[p4B],
                         tag=f"a{layer}tr{qt}_{h}_{kc0 + j}")
                p.op("V", lambda e, kc0=kc0, nb=nb, trf=trf: e.tensor_copy(
                    sb["PT"].ap()[:, kc0 * 128:(kc0 + nb) * 128],
                    trf[:, 0:nb * 128]),
                    reads=[p4B], writes=[PTB], tag=f"a{layer}tre{qt}_{h}_{kc0}")
            ovB = self.B("outv")
            if layer == 0:
                pvB = [self.take("pb5"), self.take("pb6")]
                for m in range(NCH):
                    pv = pb5v if m < 4 else pb6v
                    for kc in range(nkc):
                        p.op("T", lambda e, kc=kc, m=m, nkc=nkc, pv=pv: e.matmul(
                            pv[:, m % 4, :],
                            sb["v_tm"].ap()[:, kc, m * 128:(m + 1) * 128],
                            sb["PT"].ap()[:, kc * 128:(kc + 1) * 128],
                            start=(kc == 0), stop=(kc == nkc - 1)),
                            reads=[PTB, vB], writes=[pvB[m // 4]],
                            tag=f"a{layer}pv{qt}_{kc}_{m}")
                p.op("V", lambda e: e.tensor_copy(
                    sb["outv"].ap()[:, 0:4, :].rearrange("p c t -> p (c t)"),
                    self.pb[5].ap()),
                    reads=[pvB[0]], writes=[ovB], tag=f"a{layer}pve{qt}_a")
                p.op("V", lambda e: e.tensor_copy(
                    sb["outv"].ap()[:, 4:8, :].rearrange("p c t -> p (c t)"),
                    self.pb[6].ap()),
                    reads=[pvB[1]], writes=[ovB], tag=f"a{layer}pve{qt}_b")
            else:
                sl = par * 128
                pvB = self.take("pb5")
                for kc in range(nkc):
                    p.op("T", lambda e, kc=kc, h=h, sl=sl, nkc=nkc: e.matmul(
                        self.pb[5].ap()[0:64, sl:sl + 128],
                        sb["v_tm"].ap()[:, kc, h * 64:(h + 1) * 64],
                        sb["PT"].ap()[:, kc * 128:(kc + 1) * 128],
                        start=(kc == 0), stop=(kc == nkc - 1)),
                        reads=[PTB, vB], writes=[pvB],
                        tag=f"a{layer}pv{qt}_{h}_{kc}")
                p.op("V", lambda e, h=h, sl=sl: e.tensor_copy(
                    sb["outv"].ap()[(h % 2) * 64:(h % 2) * 64 + 64, h // 2, :],
                    self.pb[5].ap()[0:64, sl:sl + 128]),
                    reads=[pvB], writes=[ovB], tag=f"a{layer}pve{qt}_{h}")
            if h == heads - 1:
                xrB = self.take("xres")
                p.op("G", lambda e, qs=qs: e.dma_start(
                    sb["xres"].ap(),
                    xs_src.rearrange("(c p) t -> p c t", p=128)[:, :, qs]),
                    reads=[xsrcB], writes=[xrB], dma=True,
                    tag=f"a{layer}ldx{qt}")
                xnB = self.take("xnew")
                for w0 in (0, 4):
                    pyB = self.take("pb6")
                    for m in range(w0, w0 + 4):
                        for c in range(NCH):
                            p.op("T", lambda e, m=m, c=c: e.matmul(
                                pb6v[:, m % 4, :],
                                sb["wo_t"].ap()[:, c, m * 128:(m + 1) * 128],
                                sb["outv"].ap()[:, c, :], start=(c == 0),
                                stop=(c == NCH - 1)),
                                reads=[ovB, woB], writes=[pyB],
                                tag=f"a{layer}o{qt}_{m}")
                    for m in range(w0, w0 + 4):
                        if layer == 0:
                            p.op("V", lambda e, m=m: e.tensor_tensor(
                                sb["xnew"].ap()[:, m, :], pb6v[:, m % 4, :],
                                sb["xres"].ap()[:, m, :], ALU.add),
                                reads=[pyB, xrB], writes=[xnB],
                                tag=f"a{layer}r{qt}_{m}")
                        else:
                            p.op("V", lambda e, m=m: e.scalar_tensor_tensor(
                                sb["xnew"].ap()[:, m, :], pb6v[:, m % 4, :],
                                sb["bias_l1o"].ap()[:, m:m + 1],
                                sb["xres"].ap()[:, m, :], ALU.add, ALU.add),
                                reads=[pyB, xrB, self.B("bias_l1o")],
                                writes=[xnB], tag=f"a{layer}r{qt}_{m}")
                p.op("G", lambda e, qs=qs: e.dma_start(
                    xs_dst.rearrange("(c p) t -> p c t", p=128)[:, :, qs],
                    sb["xnew"].ap()),
                    reads=[xnB], writes=[xdstB], dma=True,
                    tag=f"a{layer}st{qt}")
                if qt % 4 == 3:
                    self.pump(2)

        stageA(0)
        for i in range(len(items)):
            if items[i][1] == 0:
                self.take("outv")
            if i + 1 < len(items):
                stageA(i + 1)
            stageB(i)
        self.xloc = "xs_b" if layer == 0 else "xs_a"

    def phase_ffn1a(self):
        self.emit_ffn1("a")

    def phase_ffn1b(self):
        self.emit_ffn1("b")

    def emit_ffn1(self, call):
        p, d, sb = self.p, self.d, self.sb
        pbig = self.pbig.ap()
        if call == "a":
            p.op("Q", lambda e: e.dma_start(
                d["xs_a"].rearrange("(c p) t -> p c t", p=128), sb["x"].ap()),
                reads=[self.B("x")], writes=[self.B("xs_a")], dma=True,
                tag="f1adump")
            xs_src, xsB = d["xs_a"], self.B("xs_a")
            xs_dst, xdB = d["xs_b"], self.B("xs_b")
        else:
            xs_src, xsB = d["xs_a"], self.B("xs_a")
            xs_dst, xdB = d["y"], self.B("y")
        fcB = self.take("wfcq")
        p.op("Q", lambda e: e.dma_start(
            sb["wfcq"].ap(), d["wfcq_d"].rearrange("(c p) m -> p c m", p=128)),
            reads=[self.B("wfcq_d_dram")], writes=[fcB], dma=True,
            tag=f"f1{call}ldfc")
        prB = self.take("wprq")
        p.op("Q", lambda e: e.dma_start(
            sb["wprq"].ap(), d["wprq_d"].rearrange("(c p) m -> p c m", p=128)),
            reads=[self.B("wprq_d_dram")], writes=[prB], dma=True,
            tag=f"f1{call}ldpr")
        sq2 = sb["sq128"].ap()
        for t in range(NT2):
            ts_ = slice(t * FT2, (t + 1) * FT2)
            xtB = self.take("xtile")
            p.op("G", lambda e, ts_=ts_: e.dma_start(
                sb["xtile"].ap(),
                xs_src.rearrange("(c p) t -> p c t", p=128)[:, :, ts_]),
                reads=[xsB], writes=[xtB], dma=True, tag=f"f1{call}ldx{t}")
            lcB = self.take("lnx_c")
            self.emit_ln(sb["xtile"].ap(), xtB, FT2, sb["lnx_c"].ap(), lcB,
                         f"f1{call}ln{t}", xb_name="xb2")
            hB = self.take("h")
            for m in range(32):
                q4 = m % 4
                ps_ = pbig[:, q4 * 512:q4 * 512 + 256]
                pqB = self.takeQ(q4)
                for c in range(NCH):
                    p.op("T", lambda e, m=m, c=c, ps_=ps_: e.matmul(
                        ps_, sb["wfcq"].ap()[:, c, m * 128:(m + 1) * 128],
                        sb["lnx_c"].ap()[:, c, :], start=(c == 0),
                        stop=(c == NCH - 1)),
                        reads=[lcB, fcB], writes=[pqB],
                        tag=f"f1{call}fc{t}_{m}")
                p.op("S", lambda e, m=m, ps_=ps_: e.activation(
                    sb["h"].ap()[:, m, :], ps_, AF.Gelu),
                    reads=[pqB], writes=[hB], tag=f"f1{call}fca{t}_{m}")
            pm = self.pb[5].ap()
            pmB = self.take("pb5")
            oc = sb["ones_col"].ap()
            for m in range(32):
                p.op("T", lambda e, m=m: e.matmul(
                    pm[0:1, 0:FT2], oc, sb["h"].ap()[:, m, :],
                    start=(m == 0), stop=(m == 31)),
                    reads=[hB, self.B("c_ones_col")], writes=[pmB],
                    tag=f"f1{call}hmu{t}")
            pv_ = self.pb[6].ap()
            pvB = self.take("pb6")
            for m in range(32):
                so = (m % 2) * FT2
                sqB = self.take("sqA" if m % 2 == 0 else "sqB")
                p.op("V", lambda e, m=m, so=so: e.tensor_tensor(
                    sq2[:, so:so + FT2], sb["h"].ap()[:, m, :],
                    sb["h"].ap()[:, m, :], ALU.mult),
                    reads=[hB], writes=[sqB], tag=f"f1{call}hsq{t}_{m}")
                p.op("T", lambda e, m=m, so=so: e.matmul(
                    pv_[0:1, 0:FT2], oc, sq2[:, so:so + FT2],
                    start=(m == 0), stop=(m == 31)),
                    reads=[sqB, self.B("c_ones_col")], writes=[pvB],
                    tag=f"f1{call}hvm{t}_{m}")
            sr = sb["srow"].ap()
            hD = 4096.0
            mu = sr[:, 0:FT2]
            var = sr[:, 512:512 + FT2]
            srB = self.take("srow")
            p.op("V", lambda e, mu=mu: e.tensor_scalar(
                mu, pm[0:1, 0:FT2], 1.0 / hD, None, ALU.mult),
                reads=[pmB], writes=[srB], tag=f"f1{call}mu{t}")
            p.op("V", lambda e, var=var: e.tensor_scalar(
                var, pv_[0:1, 0:FT2], 1.0 / hD, None, ALU.mult),
                reads=[pvB], writes=[srB], tag=f"f1{call}ex2{t}")
            ta = sb["tmpa"].ap()[0:1, 0:FT2]
            taB = self.take("tmpa")
            p.op("V", lambda e, ta=ta, mu=mu: e.tensor_tensor(ta, mu, mu, ALU.mult),
                 reads=[srB], writes=[taB], tag=f"f1{call}musq{t}")
            p.op("V", lambda e, var=var, ta=ta: e.tensor_tensor(
                var, var, ta, ALU.subtract),
                reads=[srB, taB], writes=[srB], tag=f"f1{call}var{t}")
            eps = sb["scal"].ap()[0:1, 0:1]
            p.op("S", lambda e, var=var: e.activation(var, var, AF.Ln, bias=eps),
                 reads=[srB, self.B("scal")], writes=[srB], tag=f"f1{call}lnv{t}")
            p.op("S", lambda e, var=var: e.activation(var, var, AF.Exp, scale=-0.5),
                 reads=[srB], writes=[srB], tag=f"f1{call}inv{t}")
            p.op("V", lambda e, mu=mu, var=var: e.tensor_tensor(
                mu, mu, var, ALU.mult),
                reads=[srB], writes=[srB], tag=f"f1{call}m2{t}")
            orf = sb["ones_row_f32"].ap()
            pb4 = self.pb[4].ap()
            for rowap, dst in ((var, "invb"), (mu, "m2b")):
                pb4B = self.take("pb4")
                p.op("T", lambda e, r=rowap: e.matmul(
                    pb4[:, 0:FT2], orf, r, start=True, stop=True),
                    reads=[srB, self.B("c_ones_row_f32")], writes=[pb4B],
                    tag=f"f1{call}bc{t}_{dst}")
                dB = self.take(dst)
                p.op("V", lambda e, dst=dst: e.tensor_copy(
                    sb[dst].ap()[:, 0:FT2], pb4[:, 0:FT2]),
                    reads=[pb4B], writes=[dB], tag=f"f1{call}bcc{t}_{dst}")
            invb = sb["invb"].ap()[:, 0:FT2]
            m2bv = sb["m2b"].ap()[:, 0:FT2]
            tb16 = sb["tmpa"].ap().bitcast(BF16)[:, 0:FT2]
            for m in range(32):
                taB = self.take("tmpa")
                p.op("V", lambda e, m=m: e.tensor_tensor(
                    tb16, sb["h"].ap()[:, m, :], invb, ALU.mult),
                    reads=[hB, self.B("invb")], writes=[taB],
                    tag=f"f1{call}hap1{t}_{m}")
                p.op("V", lambda e: e.tensor_tensor(tb16, tb16, m2bv,
                                                    ALU.subtract),
                     reads=[taB, self.B("m2b")], writes=[taB],
                     tag=f"f1{call}hap2{t}_{m}")
                p.op("S", lambda e, m=m: e.activation(
                    sb["h"].ap()[:, m, :], tb16, AF.Identity,
                    bias=sb["lnb"].ap()[:, m:m + 1],
                    scale=sb["lnw"].ap()[:, m:m + 1]),
                    reads=[taB, self.B("lnw"), self.B("lnb")], writes=[hB],
                    tag=f"f1{call}hap3{t}_{m}")
            ytB = self.take("ytile")
            for m in range(NCH):
                q4 = m % 4
                ps_ = pbig[:, q4 * 512 + 256:q4 * 512 + 512]
                pqB = self.takeQ(q4)
                for c in range(32):
                    p.op("T", lambda e, m=m, c=c, ps_=ps_: e.matmul(
                        ps_, sb["wprq"].ap()[:, c, m * 128:(m + 1) * 128],
                        sb["h"].ap()[:, c, :], start=(c == 0), stop=(c == 31)),
                        reads=[hB, prB], writes=[pqB],
                        tag=f"f1{call}pr{t}_{m}")
                p.op("V", lambda e, m=m, ps_=ps_: e.scalar_tensor_tensor(
                    sb["ytile"].ap()[:, m, :], ps_,
                    0.5, sb["xtile"].ap()[:, m, :], ALU.mult, ALU.add),
                    reads=[pqB, xtB], writes=[ytB], tag=f"f1{call}ev{t}_{m}")
            p.op("G", lambda e, ts_=ts_: e.dma_start(
                xs_dst.rearrange("(c p) t -> p c t", p=128)[:, :, ts_],
                sb["ytile"].ap()),
                reads=[ytB], writes=[xdB], dma=True, tag=f"f1{call}st{t}")
            self.pump(2)
        self.xloc = "xs_b" if call == "a" else "y"

    def phase_finish(self):
        p, d, sb = self.p, self.d, self.sb
        if self.xloc == "y":
            p.wait_event("Q", self.B("y").w[0])
            return
        if self.xloc == "x":
            ev = p.op("Q", lambda e: e.dma_start(
                d["y"].rearrange("(c p) t -> p c t", p=128), sb["x"].ap()),
                reads=[self.B("x")], writes=[self.B("y")], dma=True, tag="finx")
        elif self.xloc == "fout":
            ev = p.op("Q", lambda e: e.dma_start(
                d["y"].rearrange("(c p) t -> p c t", p=128), sb["fout"].ap()),
                reads=[self.B("fout")], writes=[self.B("y")], dma=True,
                tag="finf")
        else:
            ev = p.op("Q", lambda e, s=self.xloc: e.dma_start(
                d["y"].ap(), d[s].ap()),
                reads=[self.B(self.xloc)], writes=[self.B("y")], dma=True,
                tag="find")
        p.wait_event("Q", ev)

    def build(self):
        self.phase_init()
        if "qkv1" in self.phases:
            self.emit_bvb()
        for ph in self.phases:
            getattr(self, "phase_" + ph)()
        self.pump_all()
        if self.phases and self.phases[-1] in ("ln1", "ln3"):
            self.xloc = "fout"
        self.phase_finish()
        return self.nc


def kernel(**inputs):
    cores = prep_inputs(inputs)
    b = Builder(_PHASES)
    nc = b.build()
    b.p.emit()
    res = run_bass_kernel_spmd(nc, cores, core_ids=list(range(NCORES)))
    out = np.stack([np.ascontiguousarray(np.asarray(r["y"]).T)
                    for r in res.results])
    return out.astype(np.float32)
